# version 07
# baseline (speedup 1.0000x reference)
import numpy as np
import ml_dtypes

import concourse.bass as bass
import concourse.mybir as mybir
import concourse.bass_utils as _bu
from concourse.bass_utils import run_bass_kernel_spmd

F32 = mybir.dt.float32
BF16 = mybir.dt.bfloat16
I32 = mybir.dt.int32
U32 = mybir.dt.uint32
AF = mybir.ActivationFunctionType
OP = mybir.AluOpType
AX = mybir.AxisListType.X

NCORES = 8
D = 1024
HID = 4096
NH = 4
DK = 256
DV = 512
CH = 128
TPB = 8192
NT = TPB // CH
NTB = 32
E = 8
CAP = 1280
EPS = 1e-5
QSCALE = 0.0625
MAGIC = 0x5F3759DF
OOBBASE = 20000.0

GAMMAS = [1.0 - 2.0 ** (-5.0 - h) for h in range(NH)]

def _bvo_nobirsim(tmpdir, inp="bir.json", outp="file.neff", arch=None, *,
                  dve_root=None):
    from pathlib import Path
    from concourse.bass_utils import (get_walrus_driver, get_walrus_args,
                                      get_bir_arch, run_command)
    cmd = [
        get_walrus_driver(),
        "--pass", ",".join(["birverifier", "runtime_memory_reservation",
                            "lower_act", "lower_dve", "lower_ap_offset",
                            "codegen", "neff_packager"]),
        "-i", inp,
        "--neff-output-filename", outp,
        "--enable-birsim=false",
        "--mem-mode=physical", "--policy=0", "--enable-ldw-opt=false",
        "--assign-static-dmas-to-sp=false",
        "--dram-page-size=256", "--enable-neff-debug-info=true",
        "--jobs", "8",
        *get_walrus_args(get_bir_arch(tmpdir, inp) if arch is None else arch,
                         tmpdir, dve_root=dve_root),
    ]
    result = run_command(cmd, cwd=tmpdir)
    if result is not None:
        (Path(tmpdir) / "log.txt").write_text(result.stdout)
    return f"{tmpdir}/{outp}"

_bu.bir_verify_and_optimise = _bvo_nobirsim

ENGINES = ("pe", "dve", "act", "pool", "sp")
ENG_SEM = {e: "s_" + e for e in ENGINES}
BASE_SEMS = [ENG_SEM[e] for e in ENGINES] + ["cc0", "cc1", "cc2"]


class Buf:
    __slots__ = ("name", "w", "rs")

    def __init__(self, name=""):
        self.name, self.w, self.rs = name, None, []


class LM:
    __slots__ = ("eng", "sem", "val")

    def __init__(self, eng, sem, val):
        self.eng, self.sem, self.val = eng, sem, val


class Prog:
    def __init__(self, nc):
        self.nc = nc
        self.ops = {e: [] for e in ENGINES}
        self.cnt = {s: 0 for s in BASE_SEMS}

    def _sem(self, s):
        if s not in self.cnt:
            self.cnt[s] = 0
        return s

    def raw(self, eng, fn):
        self.ops[eng].append((fn, [], None, 0))

    def op(self, eng, fn, reads=(), writes=(), dma=None, cc=None, inc_n=1):
        deps = []
        own = ENG_SEM[eng]
        for b in reads:
            if b.w is not None:
                deps.append(b.w)
        for b in writes:
            deps.extend(b.rs)
            if b.w is not None:
                deps.append(b.w)
        if cc is not None:
            sem, amt = self._sem(cc), 1
        elif dma is not None:
            sem, amt = self._sem(dma), 16 * inc_n
        else:
            sem, amt = own, 1
        waits = {}
        for d in deps:
            if d.val > waits.get(d.sem, 0):
                waits[d.sem] = d.val
        self.cnt[sem] += amt
        lm = LM(eng, sem, self.cnt[sem])
        self.ops[eng].append((fn, sorted(waits.items()), sem, amt))
        for b in reads:
            b.rs.append(lm)
        for b in writes:
            b.w, b.rs = lm, []
        return lm

    def wait_all(self, eng, sems):
        marks = [(s, self.cnt[s]) for s in sems if self.cnt.get(s, 0) > 0]
        self.ops[eng].append((None, marks, None, 0))

    def phase_join(self, engines=ENGINES, dmas=None):
        if dmas is None:
            dmas = [s for s in self.cnt
                    if s.startswith("q_") or s in ("dact", "dpool", "dsp")]
        sl = [ENG_SEM[e] for e in engines] + list(dmas)
        for e in engines:
            self.wait_all(e, [s for s in sl if s != ENG_SEM[e]])

    def emit(self, sems):
        for s in self.cnt:
            if s not in sems:
                sems[s] = self._stack.enter_context(self.nc.semaphore(s))
        engmap = {"pe": "tensor", "dve": "vector", "act": "scalar",
                  "pool": "gpsimd", "sp": "sync"}
        with self.nc.Block() as blk:
            for ename in ENGINES:
                oplist = self.ops[ename]
                if not oplist:
                    continue

                def make(oplist=oplist):
                    def run(eng):
                        lw = {}
                        for (fn, waits, sem, amt) in oplist:
                            for (sname, val) in waits:
                                if lw.get(sname, 0) >= val:
                                    continue
                                eng.wait_ge(sems[sname], val)
                                lw[sname] = val
                            if fn is None:
                                continue
                            ins = fn(eng)
                            if sem is not None:
                                ins.then_inc(sems[sem], amt)
                    return run

                getattr(blk, engmap[ename])(make())


def _tables():
    t = {}
    inv = 1.0 / (10000.0 ** (np.arange(0, DK, 2, dtype=np.float64) / DK))
    fr = np.outer(np.arange(4096, dtype=np.float64), inv)
    t["cosT"] = np.ascontiguousarray(np.cos(fr).T).astype(ml_dtypes.bfloat16)
    t["sinT"] = np.ascontiguousarray(np.sin(fr).T).astype(ml_dtypes.bfloat16)
    j = np.arange(CH)
    DT = np.zeros((CH, NH * CH), np.float64)
    CQ = np.zeros((CH, NH * CH), np.float64)
    CK = np.zeros((CH, NH), np.float64)
    for h, g in enumerate(GAMMAS):
        rel = j[None, :] - j[:, None]
        DT[:, h * CH:(h + 1) * CH] = np.where(rel >= 0, g ** np.maximum(rel, 0), 0.0) * QSCALE
        CQ[:, h * CH:(h + 1) * CH] = np.broadcast_to((g ** (j + 1.0)) * QSCALE,
                                                     (CH, CH))
        CK[:, h] = g ** (CH - 1.0 - j)
    t["DT"] = DT.astype(ml_dtypes.bfloat16)
    t["CQ"] = CQ.astype(ml_dtypes.bfloat16)
    t["CK"] = CK.astype(np.float32)
    t["utri"] = np.triu(np.ones((128, 128))).astype(ml_dtypes.bfloat16)
    t["iden"] = np.eye(128).astype(ml_dtypes.bfloat16)
    t["onescol"] = np.ones((128, 1), ml_dtypes.bfloat16)
    return t


def build_program(debug=False):
    from contextlib import ExitStack
    nc = bass.Bass("TRN2", target_bir_lowering=False)

    def din(name, shape, dt=F32):
        return nc.dram_tensor(name, shape, dt, kind="ExternalInput")

    x_in = din("x", [TPB, D])
    wq_in = din("wq", [D, D]); wk_in = din("wk", [D, D])
    wv_in = din("wv", [D, 2048]); wg_in = din("wg", [D, 2048])
    wo_in = din("wo", [2048, D])
    wr_in = din("wr", [D, E]); rb_in = din("rb", [1, E])
    gnw16_in = din("gnw16", [128, 16])
    w1c_in = din("w1c", [D, HID]); w2c_in = din("w2c", [HID, D])
    b1c_in = din("b1c", [128, 32]); b2c_in = din("b2c", [1, D])
    ow1_in = din("ow1", [D, HID]); ow2_in = din("ow2", [HID, D])
    ob1_in = din("ob1", [128, 32]); ob2_in = din("ob2", [1, D])
    cos_in = din("cosT", [128, 4096], BF16); sin_in = din("sinT", [128, 4096], BF16)
    DT_in = din("DT", [128, NH * CH], BF16)
    CQ_in = din("CQ", [128, NH * CH], BF16)
    CK_in = din("CK", [128, NH])
    utri_in = din("utri", [128, 128], BF16)
    iden_in = din("iden", [128, 128], BF16)
    ones_in = din("onescol", [128, 1], BF16)
    cmask_in = din("cmask", [E, 1])
    io8_in = din("io8_in", [128, E])
    io8m8_in = din("io8m8_in", [128, E])
    iop_in = din("iop_in", [128, 1])

    out_dram = nc.dram_tensor("out", [TPB, D], F32, kind="ExternalOutput")
    dbg = {}
    if debug:
        dbg["xn"] = nc.dram_tensor("dbg_xn", [TPB, D], BF16, kind="ExternalOutput")
        dbg["eid"] = nc.dram_tensor("dbg_eid", [128, NT], F32, kind="ExternalOutput")
        dbg["lrank"] = nc.dram_tensor("dbg_lrank", [128, NT], F32, kind="ExternalOutput")
        dbg["inv"] = nc.dram_tensor("dbg_inv", [128, NT], F32, kind="ExternalOutput")
        dbg["idx"] = nc.dram_tensor("dbg_idx", [128, NT], U32, kind="ExternalOutput")
        dbg["osc"] = nc.dram_tensor("dbg_osc", [TPB, 2048], BF16, kind="ExternalOutput")
        dbg["s"] = nc.dram_tensor("dbg_s", [TPB, D], BF16, kind="ExternalOutput")
        dbg["yfull"] = nc.dram_tensor("dbg_yfull", [E * CAP, D], BF16, kind="ExternalOutput")
        dbg["bufc"] = nc.dram_tensor("dbg_bufc", [CAP, D], BF16, kind="ExternalOutput")
        dbg["coff"] = nc.dram_tensor("dbg_coff", [1, E], F32, kind="ExternalOutput")
        dbg["rstd"] = nc.dram_tensor("dbg_rstd", [128, NT], F32, kind="ExternalOutput")
        dbg["nmr"] = nc.dram_tensor("dbg_nmr", [128, NT], F32, kind="ExternalOutput")

    xn_dr = nc.dram_tensor("xn_dr", [TPB, D], BF16)
    xnf_dr = nc.dram_tensor("xnf_dr", [TPB, D], BF16)
    osc_dr = nc.dram_tensor("osc_dr", [TPB, 2048], BF16)
    s_dr = nc.dram_tensor("s_dr", [TPB, D], BF16)
    bufg_dr = nc.dram_tensor("bufg_dr", [E * CAP, D], BF16)
    bufc_dr = nc.dram_tensor("bufc_dr", [CAP, D], BF16)
    yc_dr = nc.dram_tensor("yc_dr", [CAP, D], BF16)
    yfull_dr = nc.dram_tensor("yfull_dr", [E * CAP, D], BF16)
    ccc_in = nc.dram_tensor("ccc_in", [1, E], F32)
    ccc_out = nc.dram_tensor("ccc_out", [E, E], F32)
    offs_dr = nc.dram_tensor("offs_dr", [NT, E], F32)
    coff_dr = nc.dram_tensor("coff_dr", [1, E], F32)

    stack = ExitStack()
    sems = {s: stack.enter_context(nc.semaphore(s)) for s in BASE_SEMS}
    bc_reg = stack.enter_context(nc.gpsimd.register("bc_reg"))
    p = Prog(nc)
    p._stack = stack
    B = {}

    def buf(name):
        if name not in B:
            B[name] = Buf(name)
        return B[name]

    def dma(eng, q, dst, src, rbufs=(), wbufs=()):
        stream = "q_" + wbufs[0].name if wbufs else q
        return p.op(eng, lambda e, d=dst, s=src: e.dma_start(d, s),
                    reads=rbufs, writes=wbufs, dma=stream)

    def mmgroup(psum_ap, pairs, reads, wbank):
        def emit(e, psum_ap=psum_ap, pairs=pairs):
            n = len(pairs)
            for i, (l, r) in enumerate(pairs):
                ins = e.matmul(psum_ap, l, r, start=(i == 0), stop=(i == n - 1))
            return ins
        return p.op("pe", emit, reads=reads, writes=[wbank])

    def rowchunks(t_in, cols, nch):
        return bass.AP(t_in, 0, [[cols, 128], [128 * cols, nch], [1, cols]])

    AS = nc.alloc_sbuf_tensor
    rstd_a = AS("rstd_a", [128, NT], F32)
    nmr_a = AS("nmr_a", [128, NT], F32)
    eid_a = AS("eid_a", [128, NT], F32)
    lrank_a = AS("lrank_a", [128, NT], F32)
    inv_a = AS("inv_a", [128, NT], F32)
    idx_a = AS("idx_a", [128, NT], U32)
    io8 = AS("io8", [128, E], F32)
    io8m8 = AS("io8m8", [128, E], F32)
    iop = AS("iop", [128, 1], F32)
    utri = AS("utri_s", [128, 128], BF16)
    iden = AS("iden_s", [128, 128], BF16)
    onescol = AS("onescol_s", [128, 1], BF16)
    rb_bc = AS("rb_bc", [128, E], F32)
    coff_bc = AS("coff_bc", [128, E], F32)
    offs_bc = AS("offs_bc", [128, NT * E], F32)
    runtot = AS("runtot", [1, E], F32)
    offs_row = AS("offs_row", [1, NT * E], F32)
    counts_T = AS("counts_T", [E, E], F32)
    cmrow = AS("cmrow", [E, E], F32)
    coff_col = AS("coff_col", [E, 1], F32)
    wr_sb = AS("wr_sb", [128, 8 * E], BF16)
    rxn = [AS(f"rxn{i}", [128, D], BF16) for i in range(2)]
    r8 = AS("r8", [128, E], F32)
    r8b = AS("r8b", [128, E], F32)
    rc1 = AS("rc1", [128, 1], F32)
    rc2 = AS("rc2", [128, 1], F32)
    rc3 = AS("rc3", [128, 1], F32)
    ytok = [AS(f"ytok{i}", [128, D], BF16) for i in range(2)]

    bank = [stack.enter_context(nc.psum_tensor(f"bank{i}", [128, 512], F32))
            for i in range(8)]
    bankB = [b.ap().bitcast(BF16) for b in bank]
    bb_ = [buf(f"bank{i}") for i in range(8)]

    x_t = x_in.ap().rearrange("(n p) d -> n p d", p=128)
    xn_t = xn_dr.ap().rearrange("(n p) d -> n p d", p=128)
    xnf_t = xnf_dr.ap().rearrange("(n p) d -> n p d", p=128)
    osc_t = osc_dr.ap().rearrange("(n p) d -> n p d", p=128)
    s_t = s_dr.ap().rearrange("(n p) d -> n p d", p=128)
    out_t = out_dram.ap().rearrange("(n p) d -> n p d", p=128)
    bufc_t = bufc_dr.ap().rearrange("(n p) d -> n p d", p=128)
    yc_t = yc_dr.ap().rearrange("(n p) d -> n p d", p=128)

    statsb = buf("stats")
    eidb = buf("eid_a")
    lrankb = buf("lrank_a")
    invb = buf("inv_a")
    idxb = buf("idx_a")

    dma("sp", "dsp", utri.ap(), utri_in[:, :], wbufs=[buf("utri")])
    dma("sp", "dsp", iden.ap(), iden_in[:, :], wbufs=[buf("iden")])
    dma("sp", "dsp", onescol.ap(), ones_in[:, :], wbufs=[buf("onescol")])
    dma("sp", "dsp", rb_bc.ap(), bass.AP(rb_in, 0, [[0, 128], [1, E]]),
        wbufs=[buf("rb_bc")])
    dma("pool", "dpool", wr_sb.ap(),
        rowchunks(wr_in, E, 8), wbufs=[buf("wr_sb")])
    dma("sp", "dsp", io8.ap(), io8_in[:, :], wbufs=[buf("io8")])
    dma("sp", "dsp", io8m8.ap(), io8m8_in[:, :], wbufs=[buf("io8m8")])
    dma("sp", "dsp", iop.ap(), iop_in[:, :], wbufs=[buf("iop")])
    p.op("pool", lambda e: e.memset(runtot[:], 0), writes=[buf("runtot")])
    p.raw("pool", lambda e: e.reg_mov(bc_reg, E * CAP - 1))

    c1 = ExitStack()
    S1 = lambda n, sh, d: c1.enter_context(nc.sbuf_tensor(n, sh, d))
    wq_sb = S1("wq_sb", [128, 8 * D], BF16)
    wk_sb = S1("wk_sb", [128, 8 * D], BF16)
    wv_sb = S1("wv_sb", [128, 8 * 2048], BF16)
    DTt = S1("DTt", [128, NH * CH], BF16)
    CQt = S1("CQt", [128, NH * CH], BF16)
    CKt = S1("CKt", [128, NH], F32)
    dma("sp", "dsp", DTt.ap(), DT_in[:, :], wbufs=[buf("DTt")])
    dma("sp", "dsp", CQt.ap(), CQ_in[:, :], wbufs=[buf("CQt")])
    dma("sp", "dsp", CKt.ap(), CK_in[:, :], wbufs=[buf("CKt")])
    dma("pool", "dpool", wq_sb.ap(),
        rowchunks(wq_in, D, 8), wbufs=[buf("wq")])
    dma("pool", "dpool", wk_sb.ap(),
        rowchunks(wk_in, D, 8), wbufs=[buf("wk")])
    dma("pool", "dpool", wv_sb.ap(),
        rowchunks(wv_in, 2048, 8), wbufs=[buf("wv")])

    ppctx = ExitStack()
    S0 = lambda n, sh, d: ppctx.enter_context(nc.sbuf_tensor(n, sh, d))
    zero2k = S0("zero2k", [128, 2048], BF16)
    p.op("pool", lambda e: e.memset(zero2k[:], 0), writes=[buf("zero2k")])
    bz = bufg_dr.ap().rearrange("(n p) (a d) -> n p (a d)", p=256, a=1).rearrange("n p d -> n (p d)").rearrange("n (p q) -> n p q", p=128)
    for i in range(40):
        dma("sp", "dsp", bz[i], zero2k[:], rbufs=[buf("zero2k")],
            wbufs=[buf("bufg")])
    xraw = [S0(f"ppx{i}", [128, D], F32) for i in range(8)]
    xnb = [S0(f"ppxn{i}", [128, D], BF16) for i in range(2)]
    xfmp = [S0(f"ppxf{i}", [128, D], BF16) for i in range(2)]
    xsqd = S0("ppsq", [128, D], BF16)
    sumsq8 = S0("ppsumsq", [128, 8], F32)
    sumx8 = S0("ppsumx", [128, 8], F32)
    mu8 = S0("ppmu", [128, 8], F32)
    vv8 = S0("ppvv", [128, 8], F32)
    lgb = S0("pplg", [128, E], F32)
    lg2 = S0("pplg2", [128, E], F32)
    mrow = S0("ppm", [128, 1], F32)
    ohbf = S0("ppoh", [128, E], BF16)

    def newton8(v_ap, out_ap, rb_, wb_, scr0, scr1, scri):
        n0, n1, ni = buf(scr0.handle_name), buf(scr1.handle_name), buf(scri.handle_name)
        p.op("dve", lambda e: e.tensor_single_scalar(
            scri.ap(), v_ap.bitcast(I32), 1, OP.arith_shift_right),
            reads=[rb_], writes=[ni])
        p.op("dve", lambda e: e.tensor_scalar(
            scri.ap(), scri.ap(), -1, MAGIC, OP.mult, OP.add),
            reads=[ni], writes=[ni])
        p.op("dve", lambda e: e.tensor_copy(scr0.ap(), scri.ap().bitcast(F32)),
             reads=[ni], writes=[n0])
        for _ in range(3):
            p.op("dve", lambda e: e.tensor_mul(scr1.ap(), scr0.ap(), scr0.ap()),
                 reads=[n0], writes=[n1])
            p.op("dve", lambda e: e.tensor_tensor(scr1.ap(), scr1.ap(), v_ap,
                                                  OP.mult),
                 reads=[n1, rb_], writes=[n1])
            p.op("dve", lambda e: e.tensor_scalar(scr1.ap(), scr1.ap(), -0.5,
                                                  1.5, OP.mult, OP.add),
                 reads=[n1], writes=[n1])
            p.op("dve", lambda e: e.tensor_mul(scr0.ap(), scr0.ap(), scr1.ap()),
                 reads=[n0, n1], writes=[n0])
        p.op("dve", lambda e: e.tensor_copy(out_ap, scr0.ap()),
             reads=[n0], writes=[wb_])

    nwt8 = [S0(f"ppnw{i}", [128, 8], F32) for i in range(2)]
    nwt8i = S0("ppnwi", [128, 8], I32)
    for tn in (nwt8[0], nwt8[1], nwt8i, sumsq8, sumx8, mu8, vv8):
        pass

    for g8 in range(NT // 8):
        ts8 = list(range(8 * g8, 8 * g8 + 8))
        for k, t in enumerate(ts8):
            bx = buf(f"ppx{k}")
            dma("sp", "dsp", xraw[k].ap(), x_t[t], wbufs=[bx])
            p.op("act", lambda e, k=k: e.activation(
                xsqd.ap(), xraw[k].ap(), AF.Square, accum_out=sumsq8[:, k:k + 1]),
                reads=[bx], writes=[buf("ppsq"), buf("ppsumsq")])
            p.op("act", lambda e, k=k: e.activation(
                xsqd.ap(), xraw[k].ap(), AF.Copy, accum_out=sumx8[:, k:k + 1]),
                reads=[bx], writes=[buf("ppsq"), buf("ppsumx")])
        p.op("dve", lambda e: e.tensor_scalar(mu8.ap(), sumx8.ap(), 1.0 / D,
                                              None, OP.mult),
             reads=[buf("ppsumx")], writes=[buf("ppmu")])
        p.op("dve", lambda e: e.tensor_scalar(vv8.ap(), sumsq8.ap(), 1.0 / D,
                                              None, OP.mult),
             reads=[buf("ppsumsq")], writes=[buf("ppvv")])
        p.op("dve", lambda e: e.tensor_mul(nwt8[0].ap(), mu8.ap(), mu8.ap()),
             reads=[buf("ppmu")], writes=[buf("ppnw0")])
        p.op("dve", lambda e: e.tensor_sub(vv8.ap(), vv8.ap(), nwt8[0].ap()),
             reads=[buf("ppvv"), buf("ppnw0")], writes=[buf("ppvv")])
        p.op("dve", lambda e: e.tensor_scalar(vv8.ap(), vv8.ap(), EPS, None,
                                              OP.add),
             reads=[buf("ppvv")], writes=[buf("ppvv")])
        class _H:
            pass
        s0 = _H(); s0.ap = nwt8[0].ap; s0.handle_name = "ppnw0"
        s1 = _H(); s1.ap = nwt8[1].ap; s1.handle_name = "ppnw1"
        si = _H(); si.ap = nwt8i.ap; si.handle_name = "ppnwi"
        newton8(vv8.ap(), rstd_a[:, 8 * g8:8 * g8 + 8], buf("ppvv"), statsb,
                s0, s1, si)
        p.op("dve", lambda e, g8=g8: e.scalar_tensor_tensor(
            nmr_a[:, 8 * g8:8 * g8 + 8], mu8.ap(), -1.0,
            rstd_a[:, 8 * g8:8 * g8 + 8], OP.mult, OP.mult),
            reads=[buf("ppmu"), statsb], writes=[statsb])
        for k, t in enumerate(ts8):
            par = t % 2
            bx, bxn, bxf = buf(f"ppx{k}"), buf(f"ppxn{par}"), buf(f"ppxf{par}")
            p.op("dve", lambda e, k=k, par=par, t=t: e.tensor_scalar(
                xnb[par].ap(), xraw[k].ap(), rstd_a[:, t:t + 1],
                nmr_a[:, t:t + 1], OP.mult, OP.add),
                reads=[bx, statsb], writes=[bxn])
            dma("sp", "dsp", xn_t[t], xnb[par].ap(), rbufs=[bxn],
                wbufs=[buf("xn_dr")])
            for ci in range(8):
                p.op("pe", lambda e, par=par, ci=ci: e.transpose(
                    bankB[7][:, ci * 128:(ci + 1) * 128],
                    xnb[par][:, ci * 128:(ci + 1) * 128], iden.ap()),
                    reads=[bxn, buf("iden")], writes=[bb_[7]])
            p.op("dve", lambda e, par=par: e.tensor_copy(xfmp[par].ap(),
                                                         bankB[7][:, 0:1024]),
                 reads=[bb_[7]], writes=[bxf])
            dma("sp", "dsp", xnf_t[t], xfmp[par].ap(), rbufs=[bxf],
                wbufs=[buf("xnf_dr")])
            mmgroup(bank[0][:, 0:E],
                    [(xfmp[par][:, ci * 128:(ci + 1) * 128],
                      wr_sb[:, 8 * ci:8 * ci + 8]) for ci in range(8)],
                    [bxf, buf("wr_sb")], bb_[0])
            p.op("dve", lambda e: e.tensor_add(lgb.ap(), bank[0][:, 0:E],
                                               rb_bc.ap()),
                 reads=[bb_[0], buf("rb_bc")], writes=[buf("pplg")])
            p.op("dve", lambda e: e.reduce_max(mrow.ap(), lgb.ap(), axis=AX),
                 reads=[buf("pplg")], writes=[buf("ppm")])
            p.op("dve", lambda e: e.tensor_scalar(lg2.ap(), lgb.ap(), mrow.ap(),
                                                  None, OP.is_equal),
                 reads=[buf("pplg"), buf("ppm")], writes=[buf("pplg2")])
            p.op("dve", lambda e: e.tensor_mul(lg2.ap(), lg2.ap(), io8m8.ap()),
                 reads=[buf("pplg2"), buf("io8m8")], writes=[buf("pplg2")])
            p.op("dve", lambda e: e.tensor_reduce(mrow.ap(), lg2.ap(), AX,
                                                  OP.min),
                 reads=[buf("pplg2")], writes=[buf("ppm")])
            p.op("dve", lambda e, t=t: e.tensor_scalar(
                eid_a[:, t:t + 1], mrow.ap(), 8.0, None, OP.add),
                reads=[buf("ppm")], writes=[eidb])
            p.op("dve", lambda e, t=t: e.tensor_scalar(
                ohbf.ap(), io8.ap(), eid_a[:, t:t + 1], None, OP.is_equal),
                reads=[buf("io8"), eidb], writes=[buf("ppoh")])
            mmgroup(bank[1][:, 0:E], [(utri.ap(), ohbf.ap())],
                    [buf("utri"), buf("ppoh")], bb_[1])
            mmgroup(bank[2][0:1, 0:E], [(onescol.ap(), ohbf.ap())],
                    [buf("onescol"), buf("ppoh")], bb_[2])
            p.op("dve", lambda e: e.tensor_tensor(lg2.ap(), bank[1][:, 0:E],
                                                  ohbf.ap(), OP.mult),
                 reads=[bb_[1], buf("ppoh")], writes=[buf("pplg2")])
            p.op("dve", lambda e: e.tensor_reduce(mrow.ap(), lg2.ap(), AX,
                                                  OP.add),
                 reads=[buf("pplg2")], writes=[buf("ppm")])
            p.op("dve", lambda e, t=t: e.tensor_scalar(
                lrank_a[:, t:t + 1], mrow.ap(), -1.0, None, OP.add),
                reads=[buf("ppm")], writes=[lrankb])
            p.op("dve", lambda e, t=t: e.tensor_copy(
                offs_row[:, E * t:E * (t + 1)], runtot.ap()),
                reads=[buf("runtot")], writes=[buf("offs_row")])
            p.op("dve", lambda e: e.tensor_add(runtot.ap(), runtot.ap(),
                                               bank[2][0:1, 0:E]),
                 reads=[buf("runtot"), bb_[2]], writes=[buf("runtot")])

    dma("act", "dact", bass.AP(offs_dr, 0, [[512, 1], [1, 512]]), offs_row.ap(),
        rbufs=[buf("offs_row")], wbufs=[buf("offs_dr")])
    dma("pool", "dpool", ccc_in[:, :], runtot.ap(), rbufs=[buf("runtot")],
        wbufs=[buf("ccc_in")])
    p.op("pool", lambda e: e.collective_compute(
        "AllGather", OP.bypass, replica_groups=[list(range(NCORES))],
        ins=[ccc_in.ap().opt()], outs=[ccc_out.ap().opt()]),
        reads=[buf("ccc_in")], writes=[buf("ccc_out")], cc="cc0")
    p.op("pool", lambda e: e.dma_start(
        counts_T.ap(), bass.AP(ccc_out, 0, [[1, E], [E, E]]),
        allow_slow_non_contiguous=True),
        reads=[buf("ccc_out")], writes=[buf("counts_T")], dma="q_counts_T")
    dma("pool", "dpool", cmrow.ap(),
        bass.AP(cmask_in, 0, [[0, E], [1, E]]), wbufs=[buf("cmrow")])
    p.op("pool", lambda e: e.tensor_mul(counts_T.ap(), counts_T.ap(), cmrow.ap()),
         reads=[buf("counts_T"), buf("cmrow")], writes=[buf("counts_T")])
    p.op("pool", lambda e: e.tensor_add(counts_T[:, 0:4], counts_T[:, 0:4],
                                        counts_T[:, 4:8]),
         reads=[buf("counts_T")], writes=[buf("counts_T")])
    p.op("pool", lambda e: e.tensor_add(counts_T[:, 0:2], counts_T[:, 0:2],
                                        counts_T[:, 2:4]),
         reads=[buf("counts_T")], writes=[buf("counts_T")])
    p.op("pool", lambda e: e.tensor_add(coff_col.ap(), counts_T[:, 0:1],
                                        counts_T[:, 1:2]),
         reads=[buf("counts_T")], writes=[buf("coff_col")])
    p.op("pool", lambda e: e.dma_start(
        bass.AP(coff_dr, 0, [[1, E], [1, 1]]), coff_col.ap(),
        allow_slow_non_contiguous=True),
        reads=[buf("coff_col")], writes=[buf("coff_dr")], dma="q_coff_dr")
    dma("pool", "dpool", coff_bc.ap(), bass.AP(coff_dr, 0, [[0, 128], [1, E]]),
        rbufs=[buf("coff_dr")], wbufs=[buf("coff_bc")])
    dma("pool", "dpool", offs_bc.ap(),
        bass.AP(offs_dr, 0, [[0, 128], [E, NT], [1, E]]),
        rbufs=[buf("offs_dr")], wbufs=[buf("offs_bc")])

    for t in range(NT):
        par = t % 2
        rb_ = buf(f"rxn{par}")
        dma("pool", "dpool", rxn[par].ap(), xn_t[t], rbufs=[buf("xn_dr")],
            wbufs=[rb_])
        p.op("pool", lambda e, t=t: e.tensor_scalar(
            r8.ap(), io8.ap(), eid_a[:, t:t + 1], None, OP.is_equal),
            reads=[buf("io8"), eidb], writes=[buf("r8")])
        p.op("pool", lambda e, t=t: e.tensor_add(
            r8b.ap(), coff_bc.ap(), offs_bc[:, E * t:E * (t + 1)]),
            reads=[buf("coff_bc"), buf("offs_bc")], writes=[buf("r8b")])
        p.op("pool", lambda e: e.tensor_mul(r8.ap(), r8.ap(), r8b.ap()),
             reads=[buf("r8"), buf("r8b")], writes=[buf("r8")])
        p.op("pool", lambda e: e.tensor_add(r8[:, 0:4], r8[:, 0:4], r8[:, 4:8]),
             reads=[buf("r8")], writes=[buf("r8")])
        p.op("pool", lambda e: e.tensor_add(r8[:, 0:2], r8[:, 0:2], r8[:, 2:4]),
             reads=[buf("r8")], writes=[buf("r8")])
        p.op("pool", lambda e: e.tensor_add(rc1.ap(), r8[:, 0:1], r8[:, 1:2]),
             reads=[buf("r8")], writes=[buf("rc1")])
        p.op("pool", lambda e, t=t: e.tensor_add(rc1.ap(), rc1.ap(),
                                                 lrank_a[:, t:t + 1]),
             reads=[buf("rc1"), lrankb], writes=[buf("rc1")])
        p.op("pool", lambda e, t=t: e.tensor_single_scalar(
            inv_a[:, t:t + 1], rc1.ap(), float(CAP), OP.is_ge),
            reads=[buf("rc1")], writes=[invb])
        p.op("pool", lambda e, t=t: e.tensor_scalar(
            rc2.ap(), eid_a[:, t:t + 1], float(CAP), None, OP.mult),
            reads=[eidb], writes=[buf("rc2")])
        p.op("pool", lambda e: e.tensor_add(rc2.ap(), rc2.ap(), rc1.ap()),
             reads=[buf("rc2"), buf("rc1")], writes=[buf("rc2")])
        p.op("pool", lambda e, t=t: e.tensor_scalar(
            rc3.ap(), iop.ap(), 1.0, OOBBASE + 128.0 * t, OP.mult, OP.add),
            reads=[buf("iop")], writes=[buf("rc3")])
        p.op("pool", lambda e: e.tensor_sub(rc3.ap(), rc3.ap(), rc2.ap()),
             reads=[buf("rc3"), buf("rc2")], writes=[buf("rc3")])
        p.op("pool", lambda e, t=t: e.tensor_scalar(
            rc3.ap(), rc3.ap(), inv_a[:, t:t + 1], None, OP.mult),
            reads=[buf("rc3"), invb], writes=[buf("rc3")])
        p.op("pool", lambda e: e.tensor_add(rc2.ap(), rc2.ap(), rc3.ap()),
             reads=[buf("rc2"), buf("rc3")], writes=[buf("rc2")])
        p.op("pool", lambda e, t=t: e.tensor_copy(idx_a[:, t:t + 1], rc2.ap()),
             reads=[buf("rc2")], writes=[idxb])
        p.op("pool", lambda e, t=t, par=par: e.indirect_dma_start(
            out=bufg_dr.ap(),
            out_offset=bass.IndirectOffsetOnAxis(idx_a[:, t:t + 1], 0),
            in_=rxn[par].ap(), in_offset=None,
            bounds_check=bc_reg, oob_is_err=False),
            reads=[rb_, idxb, buf("bufg")],
            writes=[buf("bufg")], dma=f"q_sc{t % 2}")
    p.wait_all("pool", ["q_sc0", "q_sc1"])
    p.op("pool", lambda e: e.collective_compute(
        "ReduceScatter", OP.add, replica_groups=[list(range(NCORES))],
        ins=[bufg_dr.ap().opt()], outs=[bufc_dr.ap().opt()]),
        reads=[buf("bufg")], writes=[buf("bufc")], cc="cc1")

    p.phase_join(engines=("pe", "dve", "act", "sp"),
                 dmas=[f"q_ppx{i}" for i in range(8)]
                 + ["q_xn_dr", "q_xnf_dr", "q_bufg"])
    ppctx.close()
    Ssb = [[S1(f"S{b}_{h}", [128, 2 * DV], F32) for h in range(NH)]
           for b in range(2)]
    Sbf = [[S1(f"Sb{b}_{h}", [128, 2 * DV], BF16) for h in range(NH)]
           for b in range(2)]
    p1xf = [S1(f"p1xf{i}", [128, 2048], BF16) for i in range(2)]
    qfm = [S1(f"qfm{i}", [128, 2048], BF16) for i in range(2)]
    kfm = [S1(f"kfm{i}", [128, 2048], BF16) for i in range(2)]
    vsb = [S1(f"vsb{i}", [128, 2 * 2048], BF16) for i in range(2)]
    att = [S1(f"att{i}", [128, 128], BF16) for i in range(2 * NH)]
    kckb = [S1(f"kck{i}", [128, DK], BF16) for i in range(2 * NH)]
    qqb = [S1(f"qq{i}", [128, DK], BF16) for i in range(2 * NH)]
    rt1 = S1("rt1", [128, 256], F32)
    rt2 = S1("rt2", [128, 256], F32)
    oscraw = [S1(f"oscr{i}", [128, 2048], BF16) for i in range(2)]
    cossl = [S1(f"cossl{i}", [128, 256], BF16) for i in range(2)]
    sinsl = [S1(f"sinsl{i}", [128, 256], BF16) for i in range(2)]
    ovar8 = S1("ovar8", [128, 8], F32)
    rstd8 = S1("rstd8", [128, 8], F32)
    vv8p = S1("vv8p", [128, 8], F32)
    n8 = [S1(f"n8_{i}", [128, 8], F32) for i in range(2)]
    n8i = S1("n8i", [128, 8], I32)
    osq = S1("osq", [128, DV], BF16)

    def wblk(wsb, ci, j):
        return wsb[:, ci * D + j * 128: ci * D + (j + 1) * 128]

    for g2 in range(NT // 2):
        bat, gg = g2 % 2, g2 // 2
        gp = bat
        sl = gg % 2
        ta = bat * NTB + 2 * gg
        tb = ta + 1
        pos0 = 2 * gg * 128
        if bat == 0:
            dma("sp", "dsp", cossl[sl].ap(), cos_in[:, pos0:pos0 + 256],
                wbufs=[buf(f"cossl{sl}")])
            dma("sp", "dsp", sinsl[sl].ap(), sin_in[:, pos0:pos0 + 256],
                wbufs=[buf(f"sinsl{sl}")])
        bxf = buf(f"p1xf{gp}")
        for (ti, a) in ((ta, 0), (tb, 1)):
            p.op("sp", lambda e, gp=gp, a=a, ti=ti: e.dma_start(
                bass.AP(p1xf[gp], a * 128, [[2048, 128], [256, 8], [1, 128]]),
                bass.AP(xnf_dr, ti * 128 * D, [[D, 128], [128, 8], [1, 128]])),
                reads=[buf("xnf_dr")], writes=[bxf], dma=f"q_p1xf{gp}")
        for (wsb, wb_name, dstl, dname) in ((wq_sb, "wq", qfm, "qfm"),
                                            (wk_sb, "wk", kfm, "kfm")):
            dst = dstl[gp]
            bdst = buf(f"{dname}{gp}")
            for h in range(NH):
                bbk = 4 + (h % 2)
                prs = []
                for half in range(2):
                    j = 2 * h + half
                    prs += [(wblk(wsb, ci, j),
                             p1xf[gp][:, ci * 256:(ci + 1) * 256])
                            for ci in range(8)]

                def emitq(e, prs=prs, bbk=bbk):
                    for i, (l, r) in enumerate(prs):
                        ins = e.matmul(bank[bbk][:, (i // 8) * 256:
                                                 (i // 8) * 256 + 256],
                                       l, r, start=(i % 8 == 0),
                                       stop=(i % 8 == 7))
                    return ins
                p.op("pe", emitq, reads=[bxf, buf(wb_name)], writes=[bb_[bbk]])
                cs = cossl[sl][:, 0:256]
                sn = sinsl[sl][:, 0:256]
                bcs, bsn = buf(f"cossl{sl}"), buf(f"sinsl{sl}")
                x1 = bank[bbk][:, 0:256]
                x2 = bank[bbk][:, 256:512]
                bk = bb_[bbk]
                p.op("dve", lambda e, x1=x1, cs=cs: e.tensor_tensor(
                    rt1.ap(), x1, cs, OP.mult),
                    reads=[bk, bcs], writes=[buf("rt1")])
                p.op("dve", lambda e, x2=x2, sn=sn: e.tensor_tensor(
                    rt2.ap(), x2, sn, OP.mult),
                    reads=[bk, bsn], writes=[buf("rt2")])
                p.op("dve", lambda e, dst=dst, h=h: e.tensor_sub(
                    dst[:, (2 * h) * 256:(2 * h) * 256 + 256],
                    rt1.ap(), rt2.ap()),
                    reads=[buf("rt1"), buf("rt2")], writes=[bdst])
                p.op("dve", lambda e, x2=x2, cs=cs: e.tensor_tensor(
                    rt1.ap(), x2, cs, OP.mult),
                    reads=[bk, bcs], writes=[buf("rt1")])
                p.op("dve", lambda e, x1=x1, sn=sn: e.tensor_tensor(
                    rt2.ap(), x1, sn, OP.mult),
                    reads=[bk, bsn], writes=[buf("rt2")])
                p.op("dve", lambda e, dst=dst, h=h: e.tensor_add(
                    dst[:, (2 * h + 1) * 256:(2 * h + 1) * 256 + 256],
                    rt1.ap(), rt2.ap()),
                    reads=[buf("rt1"), buf("rt2")], writes=[bdst])
        for a in range(2):
            for dv in range(4):
                bbk = 6 + ((a * 4 + dv) % 2)
                prs = [(p1xf[gp][:, ci * 256 + a * 128: ci * 256 + a * 128 + 128],
                        wv_sb[:, ci * 2048 + dv * 512: ci * 2048 + (dv + 1) * 512])
                       for ci in range(8)]
                mmgroup(bank[bbk][:, 0:512], prs, [bxf, buf("wv")], bb_[bbk])
                p.op("act", lambda e, gp=gp, a=a, dv=dv, bbk=bbk: e.copy(
                    vsb[gp][:, a * 2048 + dv * 512: a * 2048 + (dv + 1) * 512],
                    bank[bbk][:, 0:512]),
                    reads=[bb_[bbk]], writes=[buf(f"vsb{gp}")])
        for a in range(2):
            for h in range(NH):
                hb = a * NH + h
                for half in range(2):
                    p.op("dve", lambda e, gp=gp, a=a, h=h, half=half, hb=hb:
                         e.tensor_tensor(
                        qqb[hb][:, half * 128:(half + 1) * 128],
                        qfm[gp][:, (2 * h + half) * 256 + a * 128:
                                (2 * h + half) * 256 + a * 128 + 128],
                        CQt[:, h * 128:(h + 1) * 128], OP.mult),
                        reads=[buf(f"qfm{gp}"), buf("CQt")],
                        writes=[buf(f"qq{hb}")])
        for a, ti in ((0, ta), (1, tb)):
            c = ti % NTB
            for h in range(NH):
                hb = a * NH + h
                b_sc = 0 if h % 2 == 0 else 2
                b_o = 1 if h % 2 == 0 else 3
                battn, bkck = buf(f"att{hb}"), buf(f"kck{hb}")
                bqq = buf(f"qq{hb}")
                bS, bSb = buf(f"S{bat}_{h}"), buf(f"Sb{bat}_{h}")
                prs = [(kfm[gp][:, (2 * h + half) * 256 + a * 128:
                               (2 * h + half) * 256 + a * 128 + 128],
                        qfm[gp][:, (2 * h + half) * 256 + a * 128:
                               (2 * h + half) * 256 + a * 128 + 128])
                       for half in range(2)]
                mmgroup(bank[b_sc][:, 0:128], prs,
                        [buf(f"kfm{gp}"), buf(f"qfm{gp}")], bb_[b_sc])
                p.op("dve", lambda e, b_sc=b_sc, h=h, hb=hb: e.tensor_tensor(
                    att[hb].ap(), bank[b_sc][:, 0:128],
                    DTt[:, h * 128:(h + 1) * 128], OP.mult),
                    reads=[bb_[b_sc], buf("DTt")], writes=[battn])

                def emit_o(e, hb=hb, gp=gp, a=a, h=h, b_o=b_o, c=c, bat=bat):
                    ins = e.matmul(bank[b_o][:, 0:512], att[hb].ap(),
                                   vsb[gp][:, a * 2048 + h * 512:
                                           a * 2048 + (h + 1) * 512],
                                   start=True, stop=(c == 0))
                    if c > 0:
                        for half in range(2):
                            ins = e.matmul(
                                bank[b_o][:, 0:512],
                                qqb[hb][:, half * 128:(half + 1) * 128],
                                Sbf[bat][h][:, half * DV:(half + 1) * DV],
                                start=False, stop=(half == 1))
                    return ins
                rds = [battn, buf(f"vsb{gp}")] + ([bqq, bSb] if c > 0 else [])
                p.op("pe", emit_o, reads=rds, writes=[bb_[b_o]])
                if c < NTB - 1:
                    for half in range(2):
                        p.op("pe", lambda e, gp=gp, h=h, a=a, half=half,
                             b_sc=b_sc: e.transpose(
                            bankB[b_sc][:, 512 + half * 128:
                                        512 + (half + 1) * 128],
                            kfm[gp][:, (2 * h + half) * 256 + a * 128:
                                    (2 * h + half) * 256 + a * 128 + 128],
                            iden.ap()),
                            reads=[buf(f"kfm{gp}"), buf("iden")],
                            writes=[bb_[b_sc]])
                    p.op("act", lambda e, hb=hb, h=h, b_sc=b_sc: e.mul(
                        kckb[hb].ap(), bankB[b_sc][:, 512:768], CKt[:, h:h + 1]),
                        reads=[bb_[b_sc], buf("CKt")], writes=[bkck])
                    gC = GAMMAS[h] ** CH
                    for half in range(2):
                        p.op("pe", lambda e, hb=hb, gp=gp, a=a, h=h, half=half,
                             b_sc=b_sc: e.matmul(
                            bank[b_sc][:, 0:512],
                            kckb[hb][:, half * 128:(half + 1) * 128],
                            vsb[gp][:, a * 2048 + h * 512:
                                    a * 2048 + (h + 1) * 512],
                            start=True, stop=True),
                            reads=[bkck, buf(f"vsb{gp}")], writes=[bb_[b_sc]])
                        if c == 0:
                            p.op("dve", lambda e, h=h, half=half, b_sc=b_sc,
                                 bat=bat: e.tensor_copy(
                                Ssb[bat][h][:, half * DV:(half + 1) * DV],
                                bank[b_sc][:, 0:512]),
                                reads=[bb_[b_sc]], writes=[bS])
                        else:
                            p.op("dve", lambda e, h=h, half=half, b_sc=b_sc,
                                 gC=gC, bat=bat: e.scalar_tensor_tensor(
                                Ssb[bat][h][:, half * DV:(half + 1) * DV],
                                Ssb[bat][h][:, half * DV:(half + 1) * DV], gC,
                                bank[b_sc][:, 0:512], OP.mult, OP.add),
                                reads=[bb_[b_sc], bS], writes=[bS])
                    p.op("act", lambda e, h=h, bat=bat: e.copy(
                        Sbf[bat][h].ap(), Ssb[bat][h].ap()),
                        reads=[bS], writes=[bSb])
                p.op("act", lambda e, a=a, h=h, b_o=b_o: e.activation(
                    osq.ap(), bank[b_o][:, 0:512], AF.Square,
                    accum_out=ovar8[:, a * 4 + h:a * 4 + h + 1]),
                    reads=[bb_[b_o]], writes=[buf("osq"), buf("ovar8")])
                p.op("act", lambda e, a=a, h=h, b_o=b_o: e.copy(
                    oscraw[a][:, h * 512:(h + 1) * 512], bank[b_o][:, 0:512]),
                    reads=[bb_[b_o]], writes=[buf(f"oscr{a}")])
        p.op("dve", lambda e: e.tensor_scalar(
            vv8p.ap(), ovar8.ap(), 1.0 / DV, EPS, OP.mult, OP.add),
            reads=[buf("ovar8")], writes=[buf("vv8p")])
        class _H8:
            pass
        s0 = _H8(); s0.ap = n8[0].ap; s0.handle_name = "n8_0"
        s1 = _H8(); s1.ap = n8[1].ap; s1.handle_name = "n8_1"
        si = _H8(); si.ap = n8i.ap; si.handle_name = "n8i"
        newton8(vv8p.ap(), rstd8.ap(), buf("vv8p"), buf("rstd8"), s0, s1, si)
        for a, ti in ((0, ta), (1, tb)):
            for h in range(NH):
                col = a * 4 + h
                p.op("dve", lambda e, a=a, h=h, col=col: e.tensor_scalar(
                    oscraw[a][:, h * 512:(h + 1) * 512],
                    oscraw[a][:, h * 512:(h + 1) * 512],
                    rstd8[:, col:col + 1], None, OP.mult),
                    reads=[buf(f"oscr{a}"), buf("rstd8")],
                    writes=[buf(f"oscr{a}")])
            dma("sp", "dsp", osc_t[ti], oscraw[a].ap(),
                rbufs=[buf(f"oscr{a}")], wbufs=[buf("osc_dr")])
    c1.close()

    p.phase_join()

    c2 = ExitStack()
    S2 = lambda n, sh, d: c2.enter_context(nc.sbuf_tensor(n, sh, d))
    w1_sb = S2("w1_sb", [128, 8 * HID], BF16)
    w2_sb = S2("w2_sb", [128, 32 * D], BF16)
    b1_sb = S2("b1_sb", [128, 32], F32)
    b2bc = S2("b2bc", [128, D], F32)
    p2x = [S2(f"p2x{i}", [128, D], BF16) for i in range(2)]
    p2xf = S2("p2xf", [128, 2048], BF16)
    hfm = S2("hfm", [128, 32 * 256], BF16)
    ycsb = [S2(f"ycsb{i}", [128, D], BF16) for i in range(2)]

    dma("pool", "dpool", w1_sb.ap(),
        rowchunks(w1c_in, HID, 8), wbufs=[buf("w1")])
    dma("pool", "dpool", w2_sb.ap(),
        rowchunks(w2c_in, D, 32), wbufs=[buf("w2")])
    dma("sp", "dsp", b1_sb.ap(), b1c_in[:, :], wbufs=[buf("b1")])
    dma("sp", "dsp", b2bc.ap(), bass.AP(b2c_in, 0, [[0, 128], [1, D]]),
        wbufs=[buf("b2bc")])

    for gi in range(5):
        bxf = buf("p2xf")
        for a in range(2):
            dma("sp", "dsp", p2x[a].ap(), bufc_t[2 * gi + a],
                rbufs=[buf("bufc")], wbufs=[buf(f"p2x{a}")])
            bbk = 4 + a
            for ci in range(8):
                p.op("pe", lambda e, a=a, ci=ci, bbk=bbk: e.transpose(
                    bankB[bbk][:, ci * 128:(ci + 1) * 128],
                    p2x[a][:, ci * 128:(ci + 1) * 128], iden.ap()),
                    reads=[buf(f"p2x{a}"), buf("iden")], writes=[bb_[bbk]])
            p.op("dve", lambda e, a=a, bbk=bbk: e.tensor_copy(
                bass.AP(p2xf, a * 128, [[2048, 128], [256, 8], [1, 128]]),
                bankB[bbk][:, 0:1024]),
                reads=[bb_[bbk]], writes=[bxf])
        hbufs = []
        for j in range(32):
            bbk = j % 4
            prs = [(w1_sb[:, ci * HID + j * 128: ci * HID + (j + 1) * 128],
                    p2xf[:, ci * 256:(ci + 1) * 256]) for ci in range(8)]
            mmgroup(bank[bbk][:, 0:256], prs, [bxf, buf("w1")], bb_[bbk])
            hb = buf(f"hf2_{j}")
            hbufs.append(hb)
            p.op("act", lambda e, j=j, bbk=bbk: e.activation(
                hfm[:, j * 256:(j + 1) * 256], bank[bbk][:, 0:256], AF.Relu,
                bias=b1_sb[:, j:j + 1]),
                reads=[bb_[bbk], buf("b1")], writes=[hb])
        for ts in range(2):
            for dh in range(2):
                bbk = 4 + ts * 2 + dh

                def emity(e, ts=ts, dh=dh, bbk=bbk):
                    for j in range(32):
                        ins = e.matmul(
                            bank[bbk][:, 0:512],
                            hfm[:, j * 256 + ts * 128: j * 256 + ts * 128 + 128],
                            w2_sb[:, j * D + dh * 512: j * D + (dh + 1) * 512],
                            start=(j == 0), stop=(j == 31))
                    return ins
                p.op("pe", emity, reads=hbufs + [buf("w2")], writes=[bb_[bbk]])
            for dh in range(2):
                bbk = 4 + ts * 2 + dh
                p.op("dve", lambda e, ts=ts, dh=dh, bbk=bbk: e.tensor_tensor(
                    ycsb[ts][:, dh * 512:(dh + 1) * 512], bank[bbk][:, 0:512],
                    b2bc[:, dh * 512:(dh + 1) * 512], OP.add),
                    reads=[bb_[bbk], buf("b2bc")], writes=[buf(f"ycsb{ts}")])
        for a in range(2):
            dma("sp", "dsp", yc_t[2 * gi + a], ycsb[a].ap(),
                rbufs=[buf(f"ycsb{a}")], wbufs=[buf("yc_dr")])
    p.op("pool", lambda e: e.collective_compute(
        "AllGather", OP.bypass, replica_groups=[list(range(NCORES))],
        ins=[yc_dr.ap().opt()], outs=[yfull_dr.ap().opt()]),
        reads=[buf("yc_dr")], writes=[buf("yfull")], cc="cc2")
    c2.close()

    p.phase_join()

    c3 = ExitStack()
    S3 = lambda n, sh, d: c3.enter_context(nc.sbuf_tensor(n, sh, d))
    wg_sb = S3("wg_sb", [128, 8 * 2048], BF16)
    wo_sb = S3("wo_sb", [128, 16 * D], BF16)
    wos = S3("wos", [128, D], F32)
    gnw_sb = S3("gnw_sb", [128, 16], F32)
    a3xf = [S3(f"a3xf{i}", [128, 2048], BF16) for i in range(2)]
    a3g = [S3(f"a3g{i}", [128, 2048], BF16) for i in range(2)]
    a3osc = [S3(f"a3osc{i}", [128, 2048], BF16) for i in range(2)]
    a3og = [S3(f"a3og{i}", [128, 2048], BF16) for i in range(2)]
    a3ogf = S3("a3ogf", [128, 2048], BF16)
    a3s = [S3(f"a3s{i}", [128, D], BF16) for i in range(2)]

    dma("pool", "dpool", wg_sb.ap(),
        rowchunks(wg_in, 2048, 8), wbufs=[buf("wg")])
    dma("sp", "dsp", gnw_sb.ap(), gnw16_in[:, :], wbufs=[buf("gnw")])
    wo_r = wo_in.ap().rearrange("(c k) n -> c k n", k=128)
    for ci in range(16):
        dma("sp", "dsp", wos.ap(), wo_r[ci], rbufs=[buf("wo_pre")],
            wbufs=[buf("wo_pre")])
        p.op("act", lambda e, ci=ci: e.mul(
            wo_sb[:, ci * D:(ci + 1) * D], wos.ap(), gnw_sb[:, ci:ci + 1]),
            reads=[buf("wo_pre"), buf("gnw")], writes=[buf("wo")])

    for g in range(NT // 2):
        ta, tb = 2 * g, 2 * g + 1
        gp3 = g % 2
        bxf = buf(f"a3xf{gp3}")
        for (ti, a) in ((ta, 0), (tb, 1)):
            p.op("sp", lambda e, gp3=gp3, a=a, ti=ti: e.dma_start(
                bass.AP(a3xf[gp3], a * 128, [[2048, 128], [256, 8], [1, 128]]),
                bass.AP(xnf_dr, ti * 128 * D, [[D, 128], [128, 8], [1, 128]])),
                reads=[buf("xnf_dr")], writes=[bxf], dma=f"q_a3xf{gp3}")
            dma("sp", "dsp", a3osc[a].ap(), osc_t[ti], rbufs=[buf("osc_dr")],
                wbufs=[buf(f"a3osc{a}")])
        for a in range(2):
            for dv in range(4):
                bbk = dv
                prs = [(a3xf[gp3][:, ci * 256 + a * 128: ci * 256 + a * 128 + 128],
                        wg_sb[:, ci * 2048 + dv * 512: ci * 2048 + (dv + 1) * 512])
                       for ci in range(8)]
                mmgroup(bank[bbk][:, 0:512], prs, [bxf, buf("wg")], bb_[bbk])
                p.op("act", lambda e, a=a, dv=dv, bbk=bbk: e.activation(
                    a3g[a][:, dv * 512:(dv + 1) * 512], bank[bbk][:, 0:512],
                    AF.Silu),
                    reads=[bb_[bbk]], writes=[buf(f"a3g{a}")])
        for a, ti in ((0, ta), (1, tb)):
            p.op("dve", lambda e, a=a: e.tensor_mul(
                a3og[a].ap(), a3osc[a].ap(), a3g[a].ap()),
                reads=[buf(f"a3osc{a}"), buf(f"a3g{a}")],
                writes=[buf(f"a3og{a}")])
            for half in range(2):
                bbk = 4 + half
                for ci in range(8):
                    cc = half * 8 + ci
                    p.op("pe", lambda e, a=a, cc=cc, ci=ci, bbk=bbk: e.transpose(
                        bankB[bbk][:, ci * 128:(ci + 1) * 128],
                        a3og[a][:, cc * 128:(cc + 1) * 128], iden.ap()),
                        reads=[buf(f"a3og{a}"), buf("iden")], writes=[bb_[bbk]])
                p.op("dve", lambda e, half=half, bbk=bbk: e.tensor_copy(
                    a3ogf[:, half * 1024:(half + 1) * 1024],
                    bankB[bbk][:, 0:1024]),
                    reads=[bb_[bbk]], writes=[buf("a3ogf")])
            for dh in range(2):
                bbk = 6 + dh

                def emits(e, dh=dh, bbk=bbk):
                    for ci in range(16):
                        ins = e.matmul(
                            bank[bbk][:, 0:512],
                            a3ogf[:, ci * 128:(ci + 1) * 128],
                            wo_sb[:, ci * D + dh * 512: ci * D + (dh + 1) * 512],
                            start=(ci == 0), stop=(ci == 15))
                    return ins
                p.op("pe", emits, reads=[buf("a3ogf"), buf("wo")],
                     writes=[bb_[bbk]])
                p.op("dve", lambda e, a=a, dh=dh, bbk=bbk: e.tensor_copy(
                    a3s[a][:, dh * 512:(dh + 1) * 512], bank[bbk][:, 0:512]),
                    reads=[bb_[bbk]], writes=[buf(f"a3s{a}")])
            dma("sp", "dsp", s_t[ti], a3s[a].ap(), rbufs=[buf(f"a3s{a}")],
                wbufs=[buf("s_dr")])
    c3.close()

    p.phase_join()

    c4 = ExitStack()
    S4 = lambda n, sh, d: c4.enter_context(nc.sbuf_tensor(n, sh, d))
    o1_sb = S4("o1_sb", [128, 8 * HID], BF16)
    o2_sb = S4("o2_sb", [128, 32 * D], BF16)
    ob1_sb = S4("ob1_sb", [128, 32], F32)
    ob2bc = S4("ob2bc", [128, D], F32)
    b4x = [S4(f"b4x{i}", [128, D], F32) for i in range(2)]
    b4xf = [S4(f"b4xf{i}", [128, 2048], BF16) for i in range(2)]
    hfm4 = S4("hfm4", [128, 32 * 256], BF16)
    yov = [S4(f"yov{i}", [128, D], F32) for i in range(2)]
    b4s = [S4(f"b4s{i}", [128, D], BF16) for i in range(2)]
    b4f = S4("b4f", [128, D], F32)
    b4g = S4("b4g", [128, D], F32)
    b4o = [S4(f"b4o{i}", [128, D], F32) for i in range(2)]

    dma("pool", "dpool", o1_sb.ap(),
        rowchunks(ow1_in, HID, 8), wbufs=[buf("o1")])
    dma("pool", "dpool", o2_sb.ap(),
        rowchunks(ow2_in, D, 32), wbufs=[buf("o2")])
    dma("sp", "dsp", ob1_sb.ap(), ob1_in[:, :], wbufs=[buf("ob1")])
    dma("sp", "dsp", ob2bc.ap(), bass.AP(ob2_in, 0, [[0, 128], [1, D]]),
        wbufs=[buf("ob2bc")])

    for g in range(NT // 2):
        ta, tb = 2 * g, 2 * g + 1
        gp4 = g % 2
        bxf = buf(f"b4xf{gp4}")
        for (ti, a) in ((ta, 0), (tb, 1)):
            bx = buf(f"b4x{a}")
            dma("sp", "dsp", b4x[a].ap(), x_t[ti], wbufs=[bx])
            p.op("sp", lambda e, gp4=gp4, a=a, ti=ti: e.dma_start(
                bass.AP(b4xf[gp4], a * 128, [[2048, 128], [256, 8], [1, 128]]),
                bass.AP(xnf_dr, ti * 128 * D, [[D, 128], [128, 8], [1, 128]])),
                reads=[buf("xnf_dr")], writes=[bxf], dma=f"q_b4xf{gp4}")
            dma("sp", "dsp", b4s[a].ap(), s_t[ti], rbufs=[buf("s_dr")],
                wbufs=[buf(f"b4s{a}")])
            p.op("pool", lambda e, a=a: e.memset(ytok[a].ap(), 0),
                 reads=[], writes=[buf(f"ytok{a}")])
            p.op("pool", lambda e, a=a, ti=ti: e.indirect_dma_start(
                out=ytok[a].ap(), out_offset=None,
                in_=yfull_dr.ap(),
                in_offset=bass.IndirectOffsetOnAxis(idx_a[:, ti:ti + 1], 0),
                bounds_check=bc_reg, oob_is_err=False),
                reads=[buf("yfull"), idxb, buf(f"ytok{a}")],
                writes=[buf(f"ytok{a}")], dma=f"q_ytok{a}")
        hbufs = []
        for j in range(32):
            bbk = j % 4
            prs = [(o1_sb[:, ci * HID + j * 128: ci * HID + (j + 1) * 128],
                    b4xf[gp4][:, ci * 256:(ci + 1) * 256]) for ci in range(8)]
            mmgroup(bank[bbk][:, 0:256], prs, [bxf, buf("o1")], bb_[bbk])
            hb = buf(f"h4_{j}")
            hbufs.append(hb)
            p.op("act", lambda e, j=j, bbk=bbk: e.activation(
                hfm4[:, j * 256:(j + 1) * 256], bank[bbk][:, 0:256], AF.Relu,
                bias=ob1_sb[:, j:j + 1]),
                reads=[bb_[bbk], buf("ob1")], writes=[hb])
        for ts in range(2):
            for dh in range(2):
                bbk = 4 + ts * 2 + dh

                def emity(e, ts=ts, dh=dh, bbk=bbk):
                    for j in range(32):
                        ins = e.matmul(
                            bank[bbk][:, 0:512],
                            hfm4[:, j * 256 + ts * 128: j * 256 + ts * 128 + 128],
                            o2_sb[:, j * D + dh * 512: j * D + (dh + 1) * 512],
                            start=(j == 0), stop=(j == 31))
                    return ins
                p.op("pe", emity, reads=hbufs + [buf("o2")], writes=[bb_[bbk]])
            for dh in range(2):
                bbk = 4 + ts * 2 + dh
                p.op("dve", lambda e, ts=ts, dh=dh, bbk=bbk: e.tensor_tensor(
                    yov[ts][:, dh * 512:(dh + 1) * 512], bank[bbk][:, 0:512],
                    ob2bc[:, dh * 512:(dh + 1) * 512], OP.add),
                    reads=[bb_[bbk], buf("ob2bc")], writes=[buf(f"yov{ts}")])
        for a, ti in ((0, ta), (1, tb)):
            p.op("dve", lambda e, a=a, ti=ti: e.tensor_scalar(
                b4f.ap(), yov[a].ap(), inv_a[:, ti:ti + 1], None, OP.mult),
                reads=[buf(f"yov{a}"), invb], writes=[buf("b4f")])
            p.op("dve", lambda e, a=a: e.tensor_add(
                b4f.ap(), b4f.ap(), ytok[a].ap()),
                reads=[buf("b4f"), buf(f"ytok{a}")], writes=[buf("b4f")])
            p.op("act", lambda e: e.activation(b4g.ap(), b4f.ap(), AF.Silu),
                 reads=[buf("b4f")], writes=[buf("b4g")])
            p.op("dve", lambda e, a=a: e.tensor_mul(
                b4g.ap(), b4g.ap(), b4s[a].ap()),
                reads=[buf("b4g"), buf(f"b4s{a}")], writes=[buf("b4g")])
            p.op("dve", lambda e, a=a: e.tensor_add(
                b4o[a].ap(), b4g.ap(), b4x[a].ap()),
                reads=[buf("b4g"), buf(f"b4x{a}")], writes=[buf(f"b4o{a}")])
            dma("sp", "dsp", out_t[ti], b4o[a].ap(), rbufs=[buf(f"b4o{a}")],
                wbufs=[buf("outw")])
    c4.close()

    if debug:
        dma("sp", "dsp", dbg["xn"].ap(), xn_dr.ap(), rbufs=[buf("xn_dr")])
        dma("sp", "dsp", dbg["eid"].ap(), eid_a.ap(), rbufs=[eidb])
        dma("sp", "dsp", dbg["lrank"].ap(), lrank_a.ap(), rbufs=[lrankb])
        dma("sp", "dsp", dbg["inv"].ap(), inv_a.ap(), rbufs=[invb])
        dma("sp", "dsp", dbg["idx"].ap(), idx_a.ap(), rbufs=[idxb])
        dma("sp", "dsp", dbg["osc"].ap(), osc_dr.ap(), rbufs=[buf("osc_dr")])
        dma("sp", "dsp", dbg["s"].ap(), s_dr.ap(), rbufs=[buf("s_dr")])
        dma("sp", "dsp", dbg["yfull"].ap(), yfull_dr.ap(), rbufs=[buf("yfull")])
        dma("sp", "dsp", dbg["bufc"].ap(), bufc_dr.ap(), rbufs=[buf("bufc")])
        dma("act", "dact", dbg["coff"].ap(), coff_dr.ap(), rbufs=[buf("coff_dr")])
        dma("sp", "dsp", dbg["rstd"].ap(), rstd_a.ap(), rbufs=[statsb])
        dma("sp", "dsp", dbg["nmr"].ap(), nmr_a.ap(), rbufs=[statsb])

    p.wait_all("sp", [s for s in p.cnt if s.startswith("q_")
                      or s.startswith("cc") or s in ("dact", "dpool", "dsp")])

    p.emit(sems)
    stack.close()
    return nc


_PROG_CACHE = {}


def _get_program(debug=False):
    if debug not in _PROG_CACHE:
        _PROG_CACHE[debug] = build_program(debug=debug)
    return _PROG_CACHE[debug]


def make_in_maps(inputs):
    t = _tables()
    x = np.asarray(inputs["L0_attn_x"], np.float32).reshape(16 * 4096, D)
    w1 = np.asarray(inputs["L0_ffn_w1"], np.float32)
    b1 = np.asarray(inputs["L0_ffn_b1"], np.float32)
    w2 = np.asarray(inputs["L0_ffn_w2"], np.float32)
    b2 = np.asarray(inputs["L0_ffn_b2"], np.float32)
    gn = np.asarray(inputs["L0_attn_gn_weight"], np.float32)
    gnw16 = np.ascontiguousarray(
        np.tile(gn, NH).reshape(16, 128).T)
    shared = dict(
        wq=np.ascontiguousarray(inputs["L0_attn_Wq"], np.float32),
        wk=np.ascontiguousarray(inputs["L0_attn_Wk"], np.float32),
        wv=np.ascontiguousarray(inputs["L0_attn_Wv"], np.float32),
        wg=np.ascontiguousarray(inputs["L0_attn_Wg"], np.float32),
        wo=np.ascontiguousarray(inputs["L0_attn_Wo"], np.float32),
        wr=np.ascontiguousarray(inputs["L0_ffn_router_w"], np.float32),
        rb=np.ascontiguousarray(inputs["L0_ffn_router_b"],
                                np.float32).reshape(1, E),
        gnw16=gnw16,
        ow1=np.ascontiguousarray(inputs["L0_ffn_ow1"], np.float32),
        ow2=np.ascontiguousarray(inputs["L0_ffn_ow2"], np.float32),
        ob1=np.ascontiguousarray(
            np.asarray(inputs["L0_ffn_ob1"], np.float32).reshape(32, 128).T),
        ob2=np.ascontiguousarray(inputs["L0_ffn_ob2"],
                                 np.float32).reshape(1, D),
        cosT=t["cosT"], sinT=t["sinT"], DT=t["DT"], CQ=t["CQ"], CK=t["CK"],
        utri=t["utri"], iden=t["iden"], onescol=t["onescol"],
    )
    shared["io8_in"] = np.broadcast_to(np.arange(E, dtype=np.float32),
                                       (128, E)).copy()
    shared["io8m8_in"] = np.broadcast_to(np.arange(-8, 0, dtype=np.float32),
                                         (128, E)).copy()
    shared["iop_in"] = np.arange(128, dtype=np.float32).reshape(128, 1).copy()
    in_maps = []
    for c in range(NCORES):
        m = dict(shared)
        m["x"] = np.ascontiguousarray(x[c * TPB:(c + 1) * TPB])
        m["w1c"] = np.ascontiguousarray(w1[c])
        m["w2c"] = np.ascontiguousarray(w2[c])
        m["b1c"] = np.ascontiguousarray(b1[c].reshape(32, 128).T)
        m["b2c"] = np.ascontiguousarray(b2[c].reshape(1, D))
        cm = np.zeros((E, 1), np.float32)
        cm[:c] = 1.0
        m["cmask"] = cm
        in_maps.append(m)
    return in_maps


def kernel(**inputs):
    nc = _get_program(debug=False)
    in_maps = make_in_maps(inputs)
    res = run_bass_kernel_spmd(nc, in_maps, core_ids=list(range(NCORES)))
    out = np.concatenate([np.asarray(r["out"]) for r in res.results], axis=0)
    return out.reshape(16, 4096, D)


# version 11
# speedup vs baseline: 1.0036x; 1.0036x over previous
import numpy as np
import ml_dtypes

import concourse.bass as bass
import concourse.mybir as mybir
import concourse.bass_utils as _bu
from concourse.bass_utils import run_bass_kernel_spmd

F32 = mybir.dt.float32
BF16 = mybir.dt.bfloat16
I32 = mybir.dt.int32
U32 = mybir.dt.uint32
AF = mybir.ActivationFunctionType
OP = mybir.AluOpType
AX = mybir.AxisListType.X

NCORES = 8
D = 1024
HID = 4096
NH = 4
DK = 256
DV = 512
CH = 128
TPB = 8192
NT = TPB // CH
NTB = 32
E = 8
CAP = 1280
EPS = 1e-5
QSCALE = 0.0625
MAGIC = 0x5F3759DF
OOBBASE = 20000.0

GAMMAS = [1.0 - 2.0 ** (-5.0 - h) for h in range(NH)]

def _bvo_nobirsim(tmpdir, inp="bir.json", outp="file.neff", arch=None, *,
                  dve_root=None):
    from pathlib import Path
    from concourse.bass_utils import (get_walrus_driver, get_walrus_args,
                                      get_bir_arch, run_command)
    cmd = [
        get_walrus_driver(),
        "--pass", ",".join(["birverifier", "runtime_memory_reservation",
                            "lower_act", "lower_dve", "lower_ap_offset",
                            "codegen", "neff_packager"]),
        "-i", inp,
        "--neff-output-filename", outp,
        "--enable-birsim=false",
        "--mem-mode=physical", "--policy=0", "--enable-ldw-opt=false",
        "--assign-static-dmas-to-sp=false",
        "--dram-page-size=256", "--enable-neff-debug-info=true",
        "--jobs", "8",
        *get_walrus_args(get_bir_arch(tmpdir, inp) if arch is None else arch,
                         tmpdir, dve_root=dve_root),
    ]
    result = run_command(cmd, cwd=tmpdir)
    if result is not None:
        (Path(tmpdir) / "log.txt").write_text(result.stdout)
    return f"{tmpdir}/{outp}"

_bu.bir_verify_and_optimise = _bvo_nobirsim

ENGINES = ("pe", "dve", "act", "pool", "sp")
ENG_SEM = {e: "s_" + e for e in ENGINES}
BASE_SEMS = [ENG_SEM[e] for e in ENGINES] + ["cc0", "cc1", "cc2"]


class Buf:
    __slots__ = ("name", "w", "rs")

    def __init__(self, name=""):
        self.name, self.w, self.rs = name, None, []


class LM:
    __slots__ = ("eng", "sem", "val")

    def __init__(self, eng, sem, val):
        self.eng, self.sem, self.val = eng, sem, val


class Prog:
    def __init__(self, nc):
        self.nc = nc
        self.ops = {e: [] for e in ENGINES}
        self.cnt = {s: 0 for s in BASE_SEMS}

    def _sem(self, s):
        if s not in self.cnt:
            self.cnt[s] = 0
        return s

    def raw(self, eng, fn):
        self.ops[eng].append((fn, [], None, 0))

    def op(self, eng, fn, reads=(), writes=(), dma=None, cc=None, inc_n=1):
        deps = []
        own = ENG_SEM[eng]
        for b in reads:
            if b.w is not None:
                deps.append(b.w)
        for b in writes:
            deps.extend(b.rs)
            if b.w is not None:
                deps.append(b.w)
        if cc is not None:
            sem, amt = self._sem(cc), 1
        elif dma is not None:
            sem, amt = self._sem(dma), 16 * inc_n
        else:
            sem, amt = own, 1
        waits = {}
        for d in deps:
            if d.val > waits.get(d.sem, 0):
                waits[d.sem] = d.val
        self.cnt[sem] += amt
        lm = LM(eng, sem, self.cnt[sem])
        self.ops[eng].append((fn, sorted(waits.items()), sem, amt))
        for b in reads:
            b.rs.append(lm)
        for b in writes:
            b.w, b.rs = lm, []
        return lm

    def wait_all(self, eng, sems):
        marks = [(s, self.cnt[s]) for s in sems if self.cnt.get(s, 0) > 0]
        self.ops[eng].append((None, marks, None, 0))

    def phase_join(self, engines=ENGINES, dmas=None):
        if dmas is None:
            dmas = [s for s in self.cnt
                    if s.startswith("q_") or s in ("dact", "dpool", "dsp")]
        sl = [ENG_SEM[e] for e in engines] + list(dmas)
        for e in engines:
            self.wait_all(e, [s for s in sl if s != ENG_SEM[e]])

    def emit(self, sems):
        for s in self.cnt:
            if s not in sems:
                sems[s] = self._stack.enter_context(self.nc.semaphore(s))
        engmap = {"pe": "tensor", "dve": "vector", "act": "scalar",
                  "pool": "gpsimd", "sp": "sync"}
        with self.nc.Block() as blk:
            for ename in ENGINES:
                oplist = self.ops[ename]
                if not oplist:
                    continue

                def make(oplist=oplist):
                    def run(eng):
                        lw = {}
                        for (fn, waits, sem, amt) in oplist:
                            for (sname, val) in waits:
                                if lw.get(sname, 0) >= val:
                                    continue
                                eng.wait_ge(sems[sname], val)
                                lw[sname] = val
                            if fn is None:
                                continue
                            ins = fn(eng)
                            if sem is not None:
                                ins.then_inc(sems[sem], amt)
                    return run

                getattr(blk, engmap[ename])(make())


def _tables():
    t = {}
    inv = 1.0 / (10000.0 ** (np.arange(0, DK, 2, dtype=np.float64) / DK))
    fr = np.outer(np.arange(4096, dtype=np.float64), inv)
    cosT = np.cos(fr).T
    sinT = np.sin(fr).T
    NW = 4096 // 256
    cc2 = np.zeros((128, NW * 512))
    ss2 = np.zeros((128, NW * 512))
    for w in range(NW):
        cw = cosT[:, w * 256:(w + 1) * 256]
        sw = sinT[:, w * 256:(w + 1) * 256]
        cc2[:, w * 512:w * 512 + 256] = cw
        cc2[:, w * 512 + 256:(w + 1) * 512] = cw
        ss2[:, w * 512:w * 512 + 256] = -sw
        ss2[:, w * 512 + 256:(w + 1) * 512] = sw
    t["cc2T"] = cc2.astype(ml_dtypes.bfloat16)
    t["ss2T"] = ss2.astype(ml_dtypes.bfloat16)
    j = np.arange(CH)
    DT = np.zeros((CH, NH * CH), np.float64)
    CQ = np.zeros((CH, NH * 2 * CH), np.float64)
    CK = np.zeros((CH, NH), np.float64)
    for h, g in enumerate(GAMMAS):
        rel = j[None, :] - j[:, None]
        DT[:, h * CH:(h + 1) * CH] = np.where(rel >= 0, g ** np.maximum(rel, 0), 0.0) * QSCALE
        cqrow = np.broadcast_to((g ** (j + 1.0)) * QSCALE, (CH, CH))
        CQ[:, h * 2 * CH:h * 2 * CH + CH] = cqrow
        CQ[:, h * 2 * CH + CH:(h + 1) * 2 * CH] = cqrow
        CK[:, h] = g ** (CH - 1.0 - j)
    t["DT"] = DT.astype(ml_dtypes.bfloat16)
    t["CQ"] = CQ.astype(ml_dtypes.bfloat16)
    t["CK"] = CK.astype(np.float32)
    t["utri"] = np.triu(np.ones((128, 128))).astype(ml_dtypes.bfloat16)
    t["iden"] = np.eye(128).astype(ml_dtypes.bfloat16)
    t["onescol"] = np.ones((128, 1), ml_dtypes.bfloat16)
    return t


def build_program(debug=False):
    from contextlib import ExitStack
    nc = bass.Bass("TRN2", target_bir_lowering=False)

    def din(name, shape, dt=F32):
        return nc.dram_tensor(name, shape, dt, kind="ExternalInput")

    x_in = din("x", [TPB, D])
    wq_in = din("wq", [D, D]); wk_in = din("wk", [D, D])
    wv_in = din("wv", [D, 2048]); wg_in = din("wg", [D, 2048])
    wo_in = din("wo", [2048, D])
    wr_in = din("wr", [D, E]); rb_in = din("rb", [1, E])
    gnw16_in = din("gnw16", [128, 16])
    w1c_in = din("w1c", [D, HID]); w2c_in = din("w2c", [HID, D])
    b1c_in = din("b1c", [128, 32]); b2c_in = din("b2c", [1, D])
    ow1_in = din("ow1", [D, HID]); ow2_in = din("ow2", [HID, D])
    ob1_in = din("ob1", [128, 32]); ob2_in = din("ob2", [1, D])
    cc2_in = din("cc2T", [128, 16 * 512], BF16)
    ss2_in = din("ss2T", [128, 16 * 512], BF16)
    DT_in = din("DT", [128, NH * CH], BF16)
    CQ_in = din("CQ", [128, NH * 2 * CH], BF16)
    CK_in = din("CK", [128, NH])
    utri_in = din("utri", [128, 128], BF16)
    iden_in = din("iden", [128, 128], BF16)
    ones_in = din("onescol", [128, 1], BF16)
    cmask_in = din("cmask", [E, 1])
    io8_in = din("io8_in", [128, E])
    io8m8_in = din("io8m8_in", [128, E])
    iop_in = din("iop_in", [128, 1])

    out_dram = nc.dram_tensor("out", [TPB, D], F32, kind="ExternalOutput")
    dbg = {}
    if debug:
        dbg["xn"] = nc.dram_tensor("dbg_xn", [TPB, D], BF16, kind="ExternalOutput")
        dbg["eid"] = nc.dram_tensor("dbg_eid", [128, NT], F32, kind="ExternalOutput")
        dbg["lrank"] = nc.dram_tensor("dbg_lrank", [128, NT], F32, kind="ExternalOutput")
        dbg["inv"] = nc.dram_tensor("dbg_inv", [128, NT], F32, kind="ExternalOutput")
        dbg["idx"] = nc.dram_tensor("dbg_idx", [128, NT], U32, kind="ExternalOutput")
        dbg["osc"] = nc.dram_tensor("dbg_osc", [TPB, 2048], BF16, kind="ExternalOutput")
        dbg["s"] = nc.dram_tensor("dbg_s", [TPB, D], BF16, kind="ExternalOutput")
        dbg["yfull"] = nc.dram_tensor("dbg_yfull", [E * CAP, D], BF16, kind="ExternalOutput")
        dbg["bufc"] = nc.dram_tensor("dbg_bufc", [CAP, D], BF16, kind="ExternalOutput")
        dbg["coff"] = nc.dram_tensor("dbg_coff", [1, E], F32, kind="ExternalOutput")
        dbg["rstd"] = nc.dram_tensor("dbg_rstd", [128, NT], F32, kind="ExternalOutput")
        dbg["nmr"] = nc.dram_tensor("dbg_nmr", [128, NT], F32, kind="ExternalOutput")

    xn_dr = nc.dram_tensor("xn_dr", [TPB, D], BF16)
    xnf_dr = nc.dram_tensor("xnf_dr", [TPB, D], BF16)
    g_dr = nc.dram_tensor("g_dr", [TPB, 2048], BF16)
    osc_dr = nc.dram_tensor("osc_dr", [TPB, 2048], BF16)
    s_dr = nc.dram_tensor("s_dr", [TPB, D], BF16)
    bufg_dr = nc.dram_tensor("bufg_dr", [E * CAP, D], BF16)
    bufc_dr = nc.dram_tensor("bufc_dr", [CAP, D], BF16)
    yc_dr = nc.dram_tensor("yc_dr", [CAP, D], BF16)
    yfull_dr = nc.dram_tensor("yfull_dr", [E * CAP, D], BF16)
    ccc_in = nc.dram_tensor("ccc_in", [1, E], F32)
    ccc_out = nc.dram_tensor("ccc_out", [E, E], F32)
    offs_dr = nc.dram_tensor("offs_dr", [NT, E], F32)
    coff_dr = nc.dram_tensor("coff_dr", [1, E], F32)

    stack = ExitStack()
    sems = {s: stack.enter_context(nc.semaphore(s)) for s in BASE_SEMS}
    bc_reg = stack.enter_context(nc.gpsimd.register("bc_reg"))
    p = Prog(nc)
    p._stack = stack
    B = {}

    def buf(name):
        if name not in B:
            B[name] = Buf(name)
        return B[name]

    def dma(eng, q, dst, src, rbufs=(), wbufs=()):
        stream = "q_" + wbufs[0].name if wbufs else q
        return p.op(eng, lambda e, d=dst, s=src: e.dma_start(d, s),
                    reads=rbufs, writes=wbufs, dma=stream)

    def mmgroup(psum_ap, pairs, reads, wbank):
        def emit(e, psum_ap=psum_ap, pairs=pairs):
            n = len(pairs)
            for i, (l, r) in enumerate(pairs):
                ins = e.matmul(psum_ap, l, r, start=(i == 0), stop=(i == n - 1))
            return ins
        return p.op("pe", emit, reads=reads, writes=[wbank])

    def rowchunks(t_in, cols, nch):
        return bass.AP(t_in, 0, [[cols, 128], [128 * cols, nch], [1, cols]])

    AS = nc.alloc_sbuf_tensor
    rstd_a = AS("rstd_a", [128, NT], F32)
    nmr_a = AS("nmr_a", [128, NT], F32)
    eid_a = AS("eid_a", [128, NT], F32)
    lrank_a = AS("lrank_a", [128, NT], F32)
    inv_a = AS("inv_a", [128, NT], F32)
    idx_a = AS("idx_a", [128, NT], U32)
    io8 = AS("io8", [128, E], F32)
    io8m8 = AS("io8m8", [128, E], F32)
    iop = AS("iop", [128, 1], F32)
    utri = AS("utri_s", [128, 128], BF16)
    iden = AS("iden_s", [128, 128], BF16)
    onescol = AS("onescol_s", [128, 1], BF16)
    rb_bc = AS("rb_bc", [128, E], F32)
    coff_bc = AS("coff_bc", [128, E], F32)
    offs_bc = AS("offs_bc", [128, NT * E], F32)
    runtot = AS("runtot", [1, E], F32)
    offs_row = AS("offs_row", [1, NT * E], F32)
    counts_T = AS("counts_T", [E, E], F32)
    cmrow = AS("cmrow", [E, E], F32)
    coff_col = AS("coff_col", [E, 1], F32)
    wr_sb = AS("wr_sb", [128, 8 * E], BF16)
    rxn = [AS(f"rxn{i}", [128, D], BF16) for i in range(2)]
    r8 = AS("r8", [128, E], F32)
    r8b = AS("r8b", [128, E], F32)
    rc1 = AS("rc1", [128, 1], F32)
    rc2 = AS("rc2", [128, 1], F32)
    rc3 = AS("rc3", [128, 1], F32)
    ytok = [AS(f"ytok{i}", [128, D], BF16) for i in range(2)]

    bank = [stack.enter_context(nc.psum_tensor(f"bank{i}", [128, 512], F32))
            for i in range(8)]
    bankB = [b.ap().bitcast(BF16) for b in bank]
    bb_ = [buf(f"bank{i}") for i in range(8)]

    x_t = x_in.ap().rearrange("(n p) d -> n p d", p=128)
    xn_t = xn_dr.ap().rearrange("(n p) d -> n p d", p=128)
    xnf_t = xnf_dr.ap().rearrange("(n p) d -> n p d", p=128)
    g_t = g_dr.ap().rearrange("(n p) d -> n p d", p=128)
    osc_t = osc_dr.ap().rearrange("(n p) d -> n p d", p=128)
    s_t = s_dr.ap().rearrange("(n p) d -> n p d", p=128)
    out_t = out_dram.ap().rearrange("(n p) d -> n p d", p=128)
    bufc_t = bufc_dr.ap().rearrange("(n p) d -> n p d", p=128)
    yc_t = yc_dr.ap().rearrange("(n p) d -> n p d", p=128)

    statsb = buf("stats")
    eidb = buf("eid_a")
    lrankb = buf("lrank_a")
    invb = buf("inv_a")
    idxb = buf("idx_a")

    dma("sp", "dsp", utri.ap(), utri_in[:, :], wbufs=[buf("utri")])
    dma("sp", "dsp", iden.ap(), iden_in[:, :], wbufs=[buf("iden")])
    dma("sp", "dsp", onescol.ap(), ones_in[:, :], wbufs=[buf("onescol")])
    dma("sp", "dsp", rb_bc.ap(), bass.AP(rb_in, 0, [[0, 128], [1, E]]),
        wbufs=[buf("rb_bc")])
    dma("pool", "dpool", wr_sb.ap(),
        rowchunks(wr_in, E, 8), wbufs=[buf("wr_sb")])
    dma("sp", "dsp", io8.ap(), io8_in[:, :], wbufs=[buf("io8")])
    dma("sp", "dsp", io8m8.ap(), io8m8_in[:, :], wbufs=[buf("io8m8")])
    dma("sp", "dsp", iop.ap(), iop_in[:, :], wbufs=[buf("iop")])
    p.op("pool", lambda e: e.memset(runtot[:], 0), writes=[buf("runtot")])
    p.raw("pool", lambda e: e.reg_mov(bc_reg, E * CAP - 1))

    c1 = ExitStack()
    S1 = lambda n, sh, d: c1.enter_context(nc.sbuf_tensor(n, sh, d))
    wq_sb = S1("wq_sb", [128, 8 * D], BF16)
    wk_sb = S1("wk_sb", [128, 8 * D], BF16)
    wv_sb = S1("wv_sb", [128, 8 * 2048], BF16)
    DTt = S1("DTt", [128, NH * CH], BF16)
    CQt = S1("CQt", [128, NH * 2 * CH], BF16)
    CKt = S1("CKt", [128, NH], F32)
    dma("sp", "dsp", DTt.ap(), DT_in[:, :], wbufs=[buf("DTt")])
    dma("sp", "dsp", CQt.ap(), CQ_in[:, :], wbufs=[buf("CQt")])
    dma("sp", "dsp", CKt.ap(), CK_in[:, :], wbufs=[buf("CKt")])
    dma("pool", "dpool", wq_sb.ap(),
        rowchunks(wq_in, D, 8), wbufs=[buf("wq")])
    dma("pool", "dpool", wk_sb.ap(),
        rowchunks(wk_in, D, 8), wbufs=[buf("wk")])
    dma("pool", "dpool", wv_sb.ap(),
        rowchunks(wv_in, 2048, 8), wbufs=[buf("wv")])

    ppctx = ExitStack()
    S0 = lambda n, sh, d: ppctx.enter_context(nc.sbuf_tensor(n, sh, d))
    zero2k = S0("zero2k", [128, 2048], BF16)
    p.op("pool", lambda e: e.memset(zero2k[:], 0), writes=[buf("zero2k")])
    bz = bufg_dr.ap().rearrange("(n p) (a d) -> n p (a d)", p=256, a=1).rearrange("n p d -> n (p d)").rearrange("n (p q) -> n p q", p=128)
    for i in range(40):
        dma("sp", "dsp", bz[i], zero2k[:], rbufs=[buf("zero2k")],
            wbufs=[buf("bufg")])
    xraw = [S0(f"ppx{i}", [128, D], F32) for i in range(8)]
    xnb = [S0(f"ppxn{i}", [128, D], BF16) for i in range(2)]
    xfmp = [S0(f"ppxf{i}", [128, D], BF16) for i in range(2)]
    wg_sb = S0("wg_sb", [128, 8 * 2048], BF16)
    gsb = [S0(f"ppg{i}", [128, 2048], BF16) for i in range(2)]
    dma("pool", "dpool", wg_sb.ap(),
        rowchunks(wg_in, 2048, 8), wbufs=[buf("wg")])
    xsqd = S0("ppsq", [128, D], BF16)
    sumsq8 = S0("ppsumsq", [128, 8], F32)
    sumx8 = S0("ppsumx", [128, 8], F32)
    mu8 = S0("ppmu", [128, 8], F32)
    vv8 = S0("ppvv", [128, 8], F32)
    lgb = S0("pplg", [128, E], F32)
    lg2 = S0("pplg2", [128, E], F32)
    mrow = S0("ppm", [128, 1], F32)
    ohbf = S0("ppoh", [128, E], BF16)

    def newton8(v_ap, out_ap, rb_, wb_, scr0, scr1, scri):
        n0, n1, ni = buf(scr0.handle_name), buf(scr1.handle_name), buf(scri.handle_name)
        p.op("dve", lambda e: e.tensor_single_scalar(
            scri.ap(), v_ap.bitcast(I32), 1, OP.arith_shift_right),
            reads=[rb_], writes=[ni])
        p.op("dve", lambda e: e.tensor_scalar(
            scri.ap(), scri.ap(), -1, MAGIC, OP.mult, OP.add),
            reads=[ni], writes=[ni])
        p.op("dve", lambda e: e.tensor_copy(scr0.ap(), scri.ap().bitcast(F32)),
             reads=[ni], writes=[n0])
        for _ in range(2):
            p.op("dve", lambda e: e.tensor_mul(scr1.ap(), scr0.ap(), scr0.ap()),
                 reads=[n0], writes=[n1])
            p.op("dve", lambda e: e.tensor_tensor(scr1.ap(), scr1.ap(), v_ap,
                                                  OP.mult),
                 reads=[n1, rb_], writes=[n1])
            p.op("dve", lambda e: e.tensor_scalar(scr1.ap(), scr1.ap(), -0.5,
                                                  1.5, OP.mult, OP.add),
                 reads=[n1], writes=[n1])
            p.op("dve", lambda e: e.tensor_mul(scr0.ap(), scr0.ap(), scr1.ap()),
                 reads=[n0, n1], writes=[n0])
        p.op("dve", lambda e: e.tensor_copy(out_ap, scr0.ap()),
             reads=[n0], writes=[wb_])

    nwt8 = [S0(f"ppnw{i}", [128, 8], F32) for i in range(2)]
    nwt8i = S0("ppnwi", [128, 8], I32)
    for tn in (nwt8[0], nwt8[1], nwt8i, sumsq8, sumx8, mu8, vv8):
        pass

    for g8 in range(NT // 8):
        ts8 = list(range(8 * g8, 8 * g8 + 8))
        for k, t in enumerate(ts8):
            bx = buf(f"ppx{k}")
            dma("sp", "dsp", xraw[k].ap(), x_t[t], wbufs=[bx])
            p.op("act", lambda e, k=k: e.activation(
                xsqd.ap(), xraw[k].ap(), AF.Square, accum_out=sumsq8[:, k:k + 1]),
                reads=[bx], writes=[buf("ppsq"), buf("ppsumsq")])
            p.op("act", lambda e, k=k: e.activation(
                xsqd.ap(), xraw[k].ap(), AF.Copy, accum_out=sumx8[:, k:k + 1]),
                reads=[bx], writes=[buf("ppsq"), buf("ppsumx")])
        p.op("dve", lambda e: e.tensor_scalar(mu8.ap(), sumx8.ap(), 1.0 / D,
                                              None, OP.mult),
             reads=[buf("ppsumx")], writes=[buf("ppmu")])
        p.op("dve", lambda e: e.tensor_scalar(vv8.ap(), sumsq8.ap(), 1.0 / D,
                                              None, OP.mult),
             reads=[buf("ppsumsq")], writes=[buf("ppvv")])
        p.op("dve", lambda e: e.tensor_mul(nwt8[0].ap(), mu8.ap(), mu8.ap()),
             reads=[buf("ppmu")], writes=[buf("ppnw0")])
        p.op("dve", lambda e: e.tensor_sub(vv8.ap(), vv8.ap(), nwt8[0].ap()),
             reads=[buf("ppvv"), buf("ppnw0")], writes=[buf("ppvv")])
        p.op("dve", lambda e: e.tensor_scalar(vv8.ap(), vv8.ap(), EPS, None,
                                              OP.add),
             reads=[buf("ppvv")], writes=[buf("ppvv")])
        class _H:
            pass
        s0 = _H(); s0.ap = nwt8[0].ap; s0.handle_name = "ppnw0"
        s1 = _H(); s1.ap = nwt8[1].ap; s1.handle_name = "ppnw1"
        si = _H(); si.ap = nwt8i.ap; si.handle_name = "ppnwi"
        newton8(vv8.ap(), rstd_a[:, 8 * g8:8 * g8 + 8], buf("ppvv"), statsb,
                s0, s1, si)
        p.op("dve", lambda e, g8=g8: e.scalar_tensor_tensor(
            nmr_a[:, 8 * g8:8 * g8 + 8], mu8.ap(), -1.0,
            rstd_a[:, 8 * g8:8 * g8 + 8], OP.mult, OP.mult),
            reads=[buf("ppmu"), statsb], writes=[statsb])
        for k, t in enumerate(ts8):
            par = t % 2
            bx, bxn, bxf = buf(f"ppx{k}"), buf(f"ppxn{par}"), buf(f"ppxf{par}")
            p.op("dve", lambda e, k=k, par=par, t=t: e.tensor_scalar(
                xnb[par].ap(), xraw[k].ap(), rstd_a[:, t:t + 1],
                nmr_a[:, t:t + 1], OP.mult, OP.add),
                reads=[bx, statsb], writes=[bxn])
            dma("sp", "dsp", xn_t[t], xnb[par].ap(), rbufs=[bxn],
                wbufs=[buf("xn_dr")])
            for ci in range(8):
                p.op("pe", lambda e, par=par, ci=ci: e.transpose(
                    bankB[7][:, ci * 128:(ci + 1) * 128],
                    xnb[par][:, ci * 128:(ci + 1) * 128], iden.ap()),
                    reads=[bxn, buf("iden")], writes=[bb_[7]])
            p.op("dve", lambda e, par=par: e.tensor_copy(xfmp[par].ap(),
                                                         bankB[7][:, 0:1024]),
                 reads=[bb_[7]], writes=[bxf])
            dma("sp", "dsp", xnf_t[t], xfmp[par].ap(), rbufs=[bxf],
                wbufs=[buf("xnf_dr")])
            bg = buf(f"ppg{par}")
            for dv in range(4):
                bbk = 3 + dv
                prs = [(xfmp[par][:, ci * 128:(ci + 1) * 128],
                        wg_sb[:, ci * 2048 + dv * 512: ci * 2048 + (dv + 1) * 512])
                       for ci in range(8)]
                mmgroup(bank[bbk][:, 0:512], prs, [bxf, buf("wg")], bb_[bbk])
                p.op("act", lambda e, par=par, dv=dv, bbk=bbk: e.activation(
                    gsb[par][:, dv * 512:(dv + 1) * 512], bank[bbk][:, 0:512],
                    AF.Silu),
                    reads=[bb_[bbk]], writes=[bg])
            dma("act", "dact", g_t[t], gsb[par].ap(), rbufs=[bg],
                wbufs=[buf("g_dr")])
            mmgroup(bank[0][:, 0:E],
                    [(xfmp[par][:, ci * 128:(ci + 1) * 128],
                      wr_sb[:, 8 * ci:8 * ci + 8]) for ci in range(8)],
                    [bxf, buf("wr_sb")], bb_[0])
            p.op("dve", lambda e: e.tensor_add(lgb.ap(), bank[0][:, 0:E],
                                               rb_bc.ap()),
                 reads=[bb_[0], buf("rb_bc")], writes=[buf("pplg")])
            p.op("dve", lambda e: e.reduce_max(mrow.ap(), lgb.ap(), axis=AX),
                 reads=[buf("pplg")], writes=[buf("ppm")])
            p.op("dve", lambda e: e.tensor_scalar(lg2.ap(), lgb.ap(), mrow.ap(),
                                                  None, OP.is_equal),
                 reads=[buf("pplg"), buf("ppm")], writes=[buf("pplg2")])
            p.op("dve", lambda e: e.tensor_mul(lg2.ap(), lg2.ap(), io8m8.ap()),
                 reads=[buf("pplg2"), buf("io8m8")], writes=[buf("pplg2")])
            p.op("dve", lambda e: e.tensor_reduce(mrow.ap(), lg2.ap(), AX,
                                                  OP.min),
                 reads=[buf("pplg2")], writes=[buf("ppm")])
            p.op("dve", lambda e, t=t: e.tensor_scalar(
                eid_a[:, t:t + 1], mrow.ap(), 8.0, None, OP.add),
                reads=[buf("ppm")], writes=[eidb])
            p.op("dve", lambda e, t=t: e.tensor_scalar(
                ohbf.ap(), io8.ap(), eid_a[:, t:t + 1], None, OP.is_equal),
                reads=[buf("io8"), eidb], writes=[buf("ppoh")])
            mmgroup(bank[1][:, 0:E], [(utri.ap(), ohbf.ap())],
                    [buf("utri"), buf("ppoh")], bb_[1])
            mmgroup(bank[2][0:1, 0:E], [(onescol.ap(), ohbf.ap())],
                    [buf("onescol"), buf("ppoh")], bb_[2])
            p.op("dve", lambda e: e.tensor_tensor(lg2.ap(), bank[1][:, 0:E],
                                                  ohbf.ap(), OP.mult),
                 reads=[bb_[1], buf("ppoh")], writes=[buf("pplg2")])
            p.op("dve", lambda e: e.tensor_reduce(mrow.ap(), lg2.ap(), AX,
                                                  OP.add),
                 reads=[buf("pplg2")], writes=[buf("ppm")])
            p.op("dve", lambda e, t=t: e.tensor_scalar(
                lrank_a[:, t:t + 1], mrow.ap(), -1.0, None, OP.add),
                reads=[buf("ppm")], writes=[lrankb])
            p.op("dve", lambda e, t=t: e.tensor_copy(
                offs_row[:, E * t:E * (t + 1)], runtot.ap()),
                reads=[buf("runtot")], writes=[buf("offs_row")])
            p.op("dve", lambda e: e.tensor_add(runtot.ap(), runtot.ap(),
                                               bank[2][0:1, 0:E]),
                 reads=[buf("runtot"), bb_[2]], writes=[buf("runtot")])

    dma("act", "dact", bass.AP(offs_dr, 0, [[512, 1], [1, 512]]), offs_row.ap(),
        rbufs=[buf("offs_row")], wbufs=[buf("offs_dr")])
    dma("pool", "dpool", ccc_in[:, :], runtot.ap(), rbufs=[buf("runtot")],
        wbufs=[buf("ccc_in")])
    p.op("pool", lambda e: e.collective_compute(
        "AllGather", OP.bypass, replica_groups=[list(range(NCORES))],
        ins=[ccc_in.ap().opt()], outs=[ccc_out.ap().opt()]),
        reads=[buf("ccc_in")], writes=[buf("ccc_out")], cc="cc0")
    p.op("pool", lambda e: e.dma_start(
        counts_T.ap(), bass.AP(ccc_out, 0, [[1, E], [E, E]]),
        allow_slow_non_contiguous=True),
        reads=[buf("ccc_out")], writes=[buf("counts_T")], dma="q_counts_T")
    dma("pool", "dpool", cmrow.ap(),
        bass.AP(cmask_in, 0, [[0, E], [1, E]]), wbufs=[buf("cmrow")])
    p.op("pool", lambda e: e.tensor_mul(counts_T.ap(), counts_T.ap(), cmrow.ap()),
         reads=[buf("counts_T"), buf("cmrow")], writes=[buf("counts_T")])
    p.op("pool", lambda e: e.tensor_add(counts_T[:, 0:4], counts_T[:, 0:4],
                                        counts_T[:, 4:8]),
         reads=[buf("counts_T")], writes=[buf("counts_T")])
    p.op("pool", lambda e: e.tensor_add(counts_T[:, 0:2], counts_T[:, 0:2],
                                        counts_T[:, 2:4]),
         reads=[buf("counts_T")], writes=[buf("counts_T")])
    p.op("pool", lambda e: e.tensor_add(coff_col.ap(), counts_T[:, 0:1],
                                        counts_T[:, 1:2]),
         reads=[buf("counts_T")], writes=[buf("coff_col")])
    p.op("pool", lambda e: e.dma_start(
        bass.AP(coff_dr, 0, [[1, E], [1, 1]]), coff_col.ap(),
        allow_slow_non_contiguous=True),
        reads=[buf("coff_col")], writes=[buf("coff_dr")], dma="q_coff_dr")
    dma("pool", "dpool", coff_bc.ap(), bass.AP(coff_dr, 0, [[0, 128], [1, E]]),
        rbufs=[buf("coff_dr")], wbufs=[buf("coff_bc")])
    dma("pool", "dpool", offs_bc.ap(),
        bass.AP(offs_dr, 0, [[0, 128], [E, NT], [1, E]]),
        rbufs=[buf("offs_dr")], wbufs=[buf("offs_bc")])

    for t in range(NT):
        par = t % 2
        rb_ = buf(f"rxn{par}")
        dma("pool", "dpool", rxn[par].ap(), xn_t[t], rbufs=[buf("xn_dr")],
            wbufs=[rb_])
        p.op("pool", lambda e, t=t: e.tensor_scalar(
            r8.ap(), io8.ap(), eid_a[:, t:t + 1], None, OP.is_equal),
            reads=[buf("io8"), eidb], writes=[buf("r8")])
        p.op("pool", lambda e, t=t: e.tensor_add(
            r8b.ap(), coff_bc.ap(), offs_bc[:, E * t:E * (t + 1)]),
            reads=[buf("coff_bc"), buf("offs_bc")], writes=[buf("r8b")])
        p.op("pool", lambda e: e.tensor_mul(r8.ap(), r8.ap(), r8b.ap()),
             reads=[buf("r8"), buf("r8b")], writes=[buf("r8")])
        p.op("pool", lambda e: e.tensor_add(r8[:, 0:4], r8[:, 0:4], r8[:, 4:8]),
             reads=[buf("r8")], writes=[buf("r8")])
        p.op("pool", lambda e: e.tensor_add(r8[:, 0:2], r8[:, 0:2], r8[:, 2:4]),
             reads=[buf("r8")], writes=[buf("r8")])
        p.op("pool", lambda e: e.tensor_add(rc1.ap(), r8[:, 0:1], r8[:, 1:2]),
             reads=[buf("r8")], writes=[buf("rc1")])
        p.op("pool", lambda e, t=t: e.tensor_add(rc1.ap(), rc1.ap(),
                                                 lrank_a[:, t:t + 1]),
             reads=[buf("rc1"), lrankb], writes=[buf("rc1")])
        p.op("pool", lambda e, t=t: e.tensor_single_scalar(
            inv_a[:, t:t + 1], rc1.ap(), float(CAP), OP.is_ge),
            reads=[buf("rc1")], writes=[invb])
        p.op("pool", lambda e, t=t: e.tensor_scalar(
            rc2.ap(), eid_a[:, t:t + 1], float(CAP), None, OP.mult),
            reads=[eidb], writes=[buf("rc2")])
        p.op("pool", lambda e: e.tensor_add(rc2.ap(), rc2.ap(), rc1.ap()),
             reads=[buf("rc2"), buf("rc1")], writes=[buf("rc2")])
        p.op("pool", lambda e, t=t: e.tensor_scalar(
            rc3.ap(), iop.ap(), 1.0, OOBBASE + 128.0 * t, OP.mult, OP.add),
            reads=[buf("iop")], writes=[buf("rc3")])
        p.op("pool", lambda e: e.tensor_sub(rc3.ap(), rc3.ap(), rc2.ap()),
             reads=[buf("rc3"), buf("rc2")], writes=[buf("rc3")])
        p.op("pool", lambda e, t=t: e.tensor_scalar(
            rc3.ap(), rc3.ap(), inv_a[:, t:t + 1], None, OP.mult),
            reads=[buf("rc3"), invb], writes=[buf("rc3")])
        p.op("pool", lambda e: e.tensor_add(rc2.ap(), rc2.ap(), rc3.ap()),
             reads=[buf("rc2"), buf("rc3")], writes=[buf("rc2")])
        p.op("pool", lambda e, t=t: e.tensor_copy(idx_a[:, t:t + 1], rc2.ap()),
             reads=[buf("rc2")], writes=[idxb])
        p.op("pool", lambda e, t=t, par=par: e.indirect_dma_start(
            out=bufg_dr.ap(),
            out_offset=bass.IndirectOffsetOnAxis(idx_a[:, t:t + 1], 0),
            in_=rxn[par].ap(), in_offset=None,
            bounds_check=bc_reg, oob_is_err=False),
            reads=[rb_, idxb, buf("bufg")],
            writes=[buf("bufg")], dma=f"q_sc{t % 2}")
    p.wait_all("pool", ["q_sc0", "q_sc1"])
    p.op("pool", lambda e: e.collective_compute(
        "ReduceScatter", OP.add, replica_groups=[list(range(NCORES))],
        ins=[bufg_dr.ap().opt()], outs=[bufc_dr.ap().opt()]),
        reads=[buf("bufg")], writes=[buf("bufc")], cc="cc1")

    p.phase_join(engines=("pe", "dve", "act", "sp"),
                 dmas=[f"q_ppx{i}" for i in range(8)]
                 + ["q_xn_dr", "q_xnf_dr", "q_g_dr", "q_bufg"])
    ppctx.close()
    Ssb = [[S1(f"S{b}_{h}", [128, 2 * DV], F32) for h in range(NH)]
           for b in range(2)]
    Sbf = [[S1(f"Sb{b}_{h}", [128, 2 * DV], BF16) for h in range(NH)]
           for b in range(2)]
    p1xf = [S1(f"p1xf{i}", [128, 2048], BF16) for i in range(2)]
    qfm = [S1(f"qfm{i}", [128, 2048], BF16) for i in range(2)]
    kfm = [S1(f"kfm{i}", [128, 2048], BF16) for i in range(2)]
    vsb = [S1(f"vsb{i}", [128, 2 * 2048], BF16) for i in range(2)]
    att = [S1(f"att{i}", [128, 128], BF16) for i in range(2 * NH)]
    kckb = [S1(f"kck{i}", [128, DK], BF16) for i in range(2 * NH)]
    qqb = [S1(f"qq{i}", [128, DK], BF16) for i in range(2 * NH)]
    rt1 = S1("rt1", [128, 512], F32)
    rt2 = S1("rt2", [128, 512], F32)
    oscraw = [S1(f"oscr{i}", [128, 2048], BF16) for i in range(2)]
    ccsl = [S1(f"ccsl{i}", [128, 512], BF16) for i in range(2)]
    sssl = [S1(f"sssl{i}", [128, 512], BF16) for i in range(2)]
    ovar8 = S1("ovar8", [128, 8], F32)
    rstd8 = S1("rstd8", [128, 8], F32)
    vv8p = S1("vv8p", [128, 8], F32)
    n8 = [S1(f"n8_{i}", [128, 8], F32) for i in range(2)]
    n8i = S1("n8i", [128, 8], I32)
    osq = S1("osq", [128, DV], BF16)

    def wblk(wsb, ci, j):
        return wsb[:, ci * D + j * 128: ci * D + (j + 1) * 128]

    def emit_qkv(g2):
        bat, gg = g2 % 2, g2 // 2
        gp = bat
        sl = gg % 2
        ta = bat * NTB + 2 * gg
        tb = ta + 1
        if bat == 0:
            dma("sp", "dsp", ccsl[sl].ap(), cc2_in[:, gg * 512:(gg + 1) * 512],
                wbufs=[buf(f"ccsl{sl}")])
            dma("sp", "dsp", sssl[sl].ap(), ss2_in[:, gg * 512:(gg + 1) * 512],
                wbufs=[buf(f"sssl{sl}")])
        bxf = buf(f"p1xf{gp}")
        for (ti, a) in ((ta, 0), (tb, 1)):
            p.op("sp", lambda e, gp=gp, a=a, ti=ti: e.dma_start(
                bass.AP(p1xf[gp], a * 128, [[2048, 128], [256, 8], [1, 128]]),
                bass.AP(xnf_dr, ti * 128 * D, [[D, 128], [128, 8], [1, 128]])),
                reads=[buf("xnf_dr")], writes=[bxf], dma=f"q_p1xf{gp}")
        for (wsb, wb_name, dstl, dname) in ((wq_sb, "wq", qfm, "qfm"),
                                            (wk_sb, "wk", kfm, "kfm")):
            dst = dstl[gp]
            bdst = buf(f"{dname}{gp}")
            for h in range(NH):
                bbk = 4 + (h % 2)
                prs = []
                for half in range(2):
                    j = 2 * h + half
                    prs += [(wblk(wsb, ci, j),
                             p1xf[gp][:, ci * 256:(ci + 1) * 256])
                            for ci in range(8)]

                def emitq(e, prs=prs, bbk=bbk):
                    for i, (l, r) in enumerate(prs):
                        ins = e.matmul(bank[bbk][:, (i // 8) * 256:
                                                 (i // 8) * 256 + 256],
                                       l, r, start=(i % 8 == 0),
                                       stop=(i % 8 == 7))
                    return ins
                p.op("pe", emitq, reads=[bxf, buf(wb_name)], writes=[bb_[bbk]])
                bcs, bsn = buf(f"ccsl{sl}"), buf(f"sssl{sl}")
                bk = bb_[bbk]
                p.op("dve", lambda e, bbk=bbk, sl=sl: e.tensor_tensor(
                    rt1.ap(), bank[bbk][:, 0:512], ccsl[sl].ap(), OP.mult),
                    reads=[bk, bcs], writes=[buf("rt1")])
                p.op("dve", lambda e, bbk=bbk, sl=sl: e.tensor_tensor(
                    rt2.ap(), bank[bbk][:, 0:512], sssl[sl].ap(), OP.mult),
                    reads=[bk, bsn], writes=[buf("rt2")])
                p.op("dve", lambda e, dst=dst, h=h: e.tensor_sub(
                    dst[:, (2 * h) * 256:(2 * h) * 256 + 256],
                    rt1[:, 0:256], rt2[:, 256:512]),
                    reads=[buf("rt1"), buf("rt2")], writes=[bdst])
                p.op("dve", lambda e, dst=dst, h=h: e.tensor_sub(
                    dst[:, (2 * h + 1) * 256:(2 * h + 1) * 256 + 256],
                    rt1[:, 256:512], rt2[:, 0:256]),
                    reads=[buf("rt1"), buf("rt2")], writes=[bdst])
        for a in range(2):
            for dv in range(4):
                bbk = 6 + ((a * 4 + dv) % 2)
                prs = [(p1xf[gp][:, ci * 256 + a * 128: ci * 256 + a * 128 + 128],
                        wv_sb[:, ci * 2048 + dv * 512: ci * 2048 + (dv + 1) * 512])
                       for ci in range(8)]
                mmgroup(bank[bbk][:, 0:512], prs, [bxf, buf("wv")], bb_[bbk])
                p.op("act", lambda e, gp=gp, a=a, dv=dv, bbk=bbk: e.copy(
                    vsb[gp][:, a * 2048 + dv * 512: a * 2048 + (dv + 1) * 512],
                    bank[bbk][:, 0:512]),
                    reads=[bb_[bbk]], writes=[buf(f"vsb{gp}")])
    def emit_ret(g2):
        bat, gg = g2 % 2, g2 // 2
        gp = bat
        ta = bat * NTB + 2 * gg
        tb = ta + 1
        for a in range(2):
            for h in range(NH):
                hb = a * NH + h
                for half in range(2):
                    p.op("dve", lambda e, gp=gp, a=a, h=h, half=half, hb=hb:
                         e.tensor_tensor(
                        qqb[hb][:, half * 128:(half + 1) * 128],
                        qfm[gp][:, (2 * h + half) * 256 + a * 128:
                                (2 * h + half) * 256 + a * 128 + 128],
                        CQt[:, h * 256 + half * 128:
                            h * 256 + (half + 1) * 128], OP.mult),
                        reads=[buf(f"qfm{gp}"), buf("CQt")],
                        writes=[buf(f"qq{hb}")])
        for a, ti in ((0, ta), (1, tb)):
            c = ti % NTB
            for h in range(NH):
                hb = a * NH + h
                b_sc = 0 if h % 2 == 0 else 2
                b_o = 1 if h % 2 == 0 else 3
                battn, bkck = buf(f"att{hb}"), buf(f"kck{hb}")
                bqq = buf(f"qq{hb}")
                bS, bSb = buf(f"S{bat}_{h}"), buf(f"Sb{bat}_{h}")
                prs = [(kfm[gp][:, (2 * h + half) * 256 + a * 128:
                               (2 * h + half) * 256 + a * 128 + 128],
                        qfm[gp][:, (2 * h + half) * 256 + a * 128:
                               (2 * h + half) * 256 + a * 128 + 128])
                       for half in range(2)]
                mmgroup(bank[b_sc][:, 0:128], prs,
                        [buf(f"kfm{gp}"), buf(f"qfm{gp}")], bb_[b_sc])
                p.op("dve", lambda e, b_sc=b_sc, h=h, hb=hb: e.tensor_tensor(
                    att[hb].ap(), bank[b_sc][:, 0:128],
                    DTt[:, h * 128:(h + 1) * 128], OP.mult),
                    reads=[bb_[b_sc], buf("DTt")], writes=[battn])

                def emit_o(e, hb=hb, gp=gp, a=a, h=h, b_o=b_o, c=c, bat=bat):
                    ins = e.matmul(bank[b_o][:, 0:512], att[hb].ap(),
                                   vsb[gp][:, a * 2048 + h * 512:
                                           a * 2048 + (h + 1) * 512],
                                   start=True, stop=(c == 0))
                    if c > 0:
                        for half in range(2):
                            ins = e.matmul(
                                bank[b_o][:, 0:512],
                                qqb[hb][:, half * 128:(half + 1) * 128],
                                Sbf[bat][h][:, half * DV:(half + 1) * DV],
                                start=False, stop=(half == 1))
                    return ins
                rds = [battn, buf(f"vsb{gp}")] + ([bqq, bSb] if c > 0 else [])
                p.op("pe", emit_o, reads=rds, writes=[bb_[b_o]])
                p.op("act", lambda e, a=a, h=h, b_o=b_o: e.activation(
                    osq.ap(), bank[b_o][:, 0:512], AF.Square,
                    accum_out=ovar8[:, a * 4 + h:a * 4 + h + 1]),
                    reads=[bb_[b_o]], writes=[buf("osq"), buf("ovar8")])
                p.op("act", lambda e, a=a, h=h, b_o=b_o: e.copy(
                    oscraw[a][:, h * 512:(h + 1) * 512], bank[b_o][:, 0:512]),
                    reads=[bb_[b_o]], writes=[buf(f"oscr{a}")])
                if c < NTB - 1:
                    for half in range(2):
                        p.op("pe", lambda e, gp=gp, h=h, a=a, half=half,
                             b_sc=b_sc: e.transpose(
                            bankB[b_sc][:, 512 + half * 128:
                                        512 + (half + 1) * 128],
                            kfm[gp][:, (2 * h + half) * 256 + a * 128:
                                    (2 * h + half) * 256 + a * 128 + 128],
                            iden.ap()),
                            reads=[buf(f"kfm{gp}"), buf("iden")],
                            writes=[bb_[b_sc]])
                    p.op("act", lambda e, hb=hb, h=h, b_sc=b_sc: e.mul(
                        kckb[hb].ap(), bankB[b_sc][:, 512:768], CKt[:, h:h + 1]),
                        reads=[bb_[b_sc], buf("CKt")], writes=[bkck])
                    gC = GAMMAS[h] ** CH
                    for half in range(2):
                        b_u = b_sc if half == 0 else b_o
                        p.op("pe", lambda e, hb=hb, gp=gp, a=a, h=h, half=half,
                             b_u=b_u: e.matmul(
                            bank[b_u][:, 0:512],
                            kckb[hb][:, half * 128:(half + 1) * 128],
                            vsb[gp][:, a * 2048 + h * 512:
                                    a * 2048 + (h + 1) * 512],
                            start=True, stop=True),
                            reads=[bkck, buf(f"vsb{gp}")], writes=[bb_[b_u]])
                    for half in range(2):
                        b_u = b_sc if half == 0 else b_o
                        if c == 0:
                            p.op("dve", lambda e, h=h, half=half, b_u=b_u,
                                 bat=bat: e.tensor_copy(
                                Ssb[bat][h][:, half * DV:(half + 1) * DV],
                                bank[b_u][:, 0:512]),
                                reads=[bb_[b_u]], writes=[bS])
                        else:
                            p.op("dve", lambda e, h=h, half=half, b_u=b_u,
                                 gC=gC, bat=bat: e.scalar_tensor_tensor(
                                Ssb[bat][h][:, half * DV:(half + 1) * DV],
                                Ssb[bat][h][:, half * DV:(half + 1) * DV], gC,
                                bank[b_u][:, 0:512], OP.mult, OP.add),
                                reads=[bb_[b_u], bS], writes=[bS])
                    p.op("act", lambda e, h=h, bat=bat: e.copy(
                        Sbf[bat][h].ap(), Ssb[bat][h].ap()),
                        reads=[bS], writes=[bSb])
        p.op("dve", lambda e: e.tensor_scalar(
            vv8p.ap(), ovar8.ap(), 1.0 / DV, EPS, OP.mult, OP.add),
            reads=[buf("ovar8")], writes=[buf("vv8p")])
        class _H8:
            pass
        s0 = _H8(); s0.ap = n8[0].ap; s0.handle_name = "n8_0"
        s1 = _H8(); s1.ap = n8[1].ap; s1.handle_name = "n8_1"
        si = _H8(); si.ap = n8i.ap; si.handle_name = "n8i"
        newton8(vv8p.ap(), rstd8.ap(), buf("vv8p"), buf("rstd8"), s0, s1, si)
        for a, ti in ((0, ta), (1, tb)):
            for h in range(NH):
                col = a * 4 + h
                p.op("dve", lambda e, a=a, h=h, col=col: e.tensor_scalar(
                    oscraw[a][:, h * 512:(h + 1) * 512],
                    oscraw[a][:, h * 512:(h + 1) * 512],
                    rstd8[:, col:col + 1], None, OP.mult),
                    reads=[buf(f"oscr{a}"), buf("rstd8")],
                    writes=[buf(f"oscr{a}")])
            dma("sp", "dsp", osc_t[ti], oscraw[a].ap(),
                rbufs=[buf(f"oscr{a}")], wbufs=[buf("osc_dr")])

    for g2 in range(NT // 2):
        emit_qkv(g2)
        if g2 >= 1:
            emit_ret(g2 - 1)
    emit_ret(NT // 2 - 1)
    c1.close()

    p.phase_join()

    c2 = ExitStack()
    S2 = lambda n, sh, d: c2.enter_context(nc.sbuf_tensor(n, sh, d))
    w1_sb = S2("w1_sb", [128, 8 * HID], BF16)
    w2_sb = S2("w2_sb", [128, 32 * D], BF16)
    b1_sb = S2("b1_sb", [128, 32], F32)
    b2bc = S2("b2bc", [128, D], F32)
    p2x = [S2(f"p2x{i}", [128, D], BF16) for i in range(2)]
    p2xf = S2("p2xf", [128, 2048], BF16)
    hfm = S2("hfm", [128, 32 * 256], BF16)
    ycsb = [S2(f"ycsb{i}", [128, D], BF16) for i in range(2)]

    dma("pool", "dpool", w1_sb.ap(),
        rowchunks(w1c_in, HID, 8), wbufs=[buf("w1")])
    dma("pool", "dpool", w2_sb.ap(),
        rowchunks(w2c_in, D, 32), wbufs=[buf("w2")])
    dma("sp", "dsp", b1_sb.ap(), b1c_in[:, :], wbufs=[buf("b1")])
    dma("sp", "dsp", b2bc.ap(), bass.AP(b2c_in, 0, [[0, 128], [1, D]]),
        wbufs=[buf("b2bc")])

    for gi in range(5):
        bxf = buf("p2xf")
        for a in range(2):
            dma("sp", "dsp", p2x[a].ap(), bufc_t[2 * gi + a],
                rbufs=[buf("bufc")], wbufs=[buf(f"p2x{a}")])
            bbk = 4 + a
            for ci in range(8):
                p.op("pe", lambda e, a=a, ci=ci, bbk=bbk: e.transpose(
                    bankB[bbk][:, ci * 128:(ci + 1) * 128],
                    p2x[a][:, ci * 128:(ci + 1) * 128], iden.ap()),
                    reads=[buf(f"p2x{a}"), buf("iden")], writes=[bb_[bbk]])
            p.op("dve", lambda e, a=a, bbk=bbk: e.tensor_copy(
                bass.AP(p2xf, a * 128, [[2048, 128], [256, 8], [1, 128]]),
                bankB[bbk][:, 0:1024]),
                reads=[bb_[bbk]], writes=[bxf])
        hbufs = []
        for j in range(32):
            bbk = j % 4
            prs = [(w1_sb[:, ci * HID + j * 128: ci * HID + (j + 1) * 128],
                    p2xf[:, ci * 256:(ci + 1) * 256]) for ci in range(8)]
            mmgroup(bank[bbk][:, 0:256], prs, [bxf, buf("w1")], bb_[bbk])
            hb = buf(f"hf2_{j}")
            hbufs.append(hb)
            p.op("act", lambda e, j=j, bbk=bbk: e.activation(
                hfm[:, j * 256:(j + 1) * 256], bank[bbk][:, 0:256], AF.Relu,
                bias=b1_sb[:, j:j + 1]),
                reads=[bb_[bbk], buf("b1")], writes=[hb])
        for ts in range(2):
            for dh in range(2):
                bbk = 4 + ts * 2 + dh

                def emity(e, ts=ts, dh=dh, bbk=bbk):
                    for j in range(32):
                        ins = e.matmul(
                            bank[bbk][:, 0:512],
                            hfm[:, j * 256 + ts * 128: j * 256 + ts * 128 + 128],
                            w2_sb[:, j * D + dh * 512: j * D + (dh + 1) * 512],
                            start=(j == 0), stop=(j == 31))
                    return ins
                p.op("pe", emity, reads=hbufs + [buf("w2")], writes=[bb_[bbk]])
            for dh in range(2):
                bbk = 4 + ts * 2 + dh
                p.op("dve", lambda e, ts=ts, dh=dh, bbk=bbk: e.tensor_tensor(
                    ycsb[ts][:, dh * 512:(dh + 1) * 512], bank[bbk][:, 0:512],
                    b2bc[:, dh * 512:(dh + 1) * 512], OP.add),
                    reads=[bb_[bbk], buf("b2bc")], writes=[buf(f"ycsb{ts}")])
        for a in range(2):
            dma("sp", "dsp", yc_t[2 * gi + a], ycsb[a].ap(),
                rbufs=[buf(f"ycsb{a}")], wbufs=[buf("yc_dr")])
    p.op("pool", lambda e: e.collective_compute(
        "AllGather", OP.bypass, replica_groups=[list(range(NCORES))],
        ins=[yc_dr.ap().opt()], outs=[yfull_dr.ap().opt()]),
        reads=[buf("yc_dr")], writes=[buf("yfull")], cc="cc2")
    c2.close()

    p.phase_join()

    c3 = ExitStack()
    S3 = lambda n, sh, d: c3.enter_context(nc.sbuf_tensor(n, sh, d))
    wo_sb = S3("wo_sb", [128, 16 * D], BF16)
    wos = S3("wos", [128, D], F32)
    gnw_sb = S3("gnw_sb", [128, 16], F32)
    a3g = [S3(f"a3g{i}", [128, 2048], BF16) for i in range(2)]
    a3osc = [S3(f"a3osc{i}", [128, 2048], BF16) for i in range(2)]
    a3og = [S3(f"a3og{i}", [128, 2048], BF16) for i in range(2)]
    a3ogf = [S3(f"a3ogf{i}", [128, 2048], BF16) for i in range(2)]
    a3s = [S3(f"a3s{i}", [128, D], BF16) for i in range(2)]

    dma("sp", "dsp", gnw_sb.ap(), gnw16_in[:, :], wbufs=[buf("gnw")])
    wo_r = wo_in.ap().rearrange("(c k) n -> c k n", k=128)
    for ci in range(16):
        dma("sp", "dsp", wos.ap(), wo_r[ci], rbufs=[buf("wo_pre")],
            wbufs=[buf("wo_pre")])
        p.op("act", lambda e, ci=ci: e.mul(
            wo_sb[:, ci * D:(ci + 1) * D], wos.ap(), gnw_sb[:, ci:ci + 1]),
            reads=[buf("wo_pre"), buf("gnw")], writes=[buf("wo")])

    for g in range(NT // 2):
        ta, tb = 2 * g, 2 * g + 1
        for (ti, a) in ((ta, 0), (tb, 1)):
            dma("sp", "dsp", a3g[a].ap(), g_t[ti], rbufs=[buf("g_dr")],
                wbufs=[buf(f"a3g{a}")])
            dma("sp", "dsp", a3osc[a].ap(), osc_t[ti], rbufs=[buf("osc_dr")],
                wbufs=[buf(f"a3osc{a}")])
        for a, ti in ((0, ta), (1, tb)):
            p.op("dve", lambda e, a=a: e.tensor_mul(
                a3og[a].ap(), a3osc[a].ap(), a3g[a].ap()),
                reads=[buf(f"a3osc{a}"), buf(f"a3g{a}")],
                writes=[buf(f"a3og{a}")])
            for half in range(2):
                bbk = 4 + ((a * 2 + half) % 2)
                for ci in range(8):
                    cc = half * 8 + ci
                    p.op("pe", lambda e, a=a, cc=cc, ci=ci, bbk=bbk: e.transpose(
                        bankB[bbk][:, ci * 128:(ci + 1) * 128],
                        a3og[a][:, cc * 128:(cc + 1) * 128], iden.ap()),
                        reads=[buf(f"a3og{a}"), buf("iden")], writes=[bb_[bbk]])
                p.op("dve", lambda e, a=a, half=half, bbk=bbk: e.tensor_copy(
                    a3ogf[a][:, half * 1024:(half + 1) * 1024],
                    bankB[bbk][:, 0:1024]),
                    reads=[bb_[bbk]], writes=[buf(f"a3ogf{a}")])
            for dh in range(2):
                bbk = 6 + dh

                def emits(e, a=a, dh=dh, bbk=bbk):
                    for ci in range(16):
                        ins = e.matmul(
                            bank[bbk][:, 0:512],
                            a3ogf[a][:, ci * 128:(ci + 1) * 128],
                            wo_sb[:, ci * D + dh * 512: ci * D + (dh + 1) * 512],
                            start=(ci == 0), stop=(ci == 15))
                    return ins
                p.op("pe", emits, reads=[buf(f"a3ogf{a}"), buf("wo")],
                     writes=[bb_[bbk]])
                p.op("dve", lambda e, a=a, dh=dh, bbk=bbk: e.tensor_copy(
                    a3s[a][:, dh * 512:(dh + 1) * 512], bank[bbk][:, 0:512]),
                    reads=[bb_[bbk]], writes=[buf(f"a3s{a}")])
            dma("sp", "dsp", s_t[ti], a3s[a].ap(), rbufs=[buf(f"a3s{a}")],
                wbufs=[buf("s_dr")])
    c3.close()

    p.phase_join()

    c4 = ExitStack()
    S4 = lambda n, sh, d: c4.enter_context(nc.sbuf_tensor(n, sh, d))
    o1_sb = S4("o1_sb", [128, 8 * HID], BF16)
    o2_sb = S4("o2_sb", [128, 32 * D], BF16)
    ob1_sb = S4("ob1_sb", [128, 32], F32)
    ob2bc = S4("ob2bc", [128, D], F32)
    b4x = [S4(f"b4x{i}", [128, D], F32) for i in range(2)]
    b4xf = [S4(f"b4xf{i}", [128, 2048], BF16) for i in range(2)]
    hfm4 = S4("hfm4", [128, 32 * 256], BF16)
    yov = [S4(f"yov{i}", [128, D], F32) for i in range(2)]
    b4s = [S4(f"b4s{i}", [128, D], BF16) for i in range(2)]
    b4f = S4("b4f", [128, D], F32)
    b4g = S4("b4g", [128, D], F32)
    b4o = [S4(f"b4o{i}", [128, D], F32) for i in range(2)]

    dma("pool", "dpool", o1_sb.ap(),
        rowchunks(ow1_in, HID, 8), wbufs=[buf("o1")])
    dma("pool", "dpool", o2_sb.ap(),
        rowchunks(ow2_in, D, 32), wbufs=[buf("o2")])
    dma("sp", "dsp", ob1_sb.ap(), ob1_in[:, :], wbufs=[buf("ob1")])
    dma("sp", "dsp", ob2bc.ap(), bass.AP(ob2_in, 0, [[0, 128], [1, D]]),
        wbufs=[buf("ob2bc")])

    for g in range(NT // 2):
        ta, tb = 2 * g, 2 * g + 1
        gp4 = g % 2
        bxf = buf(f"b4xf{gp4}")
        for (ti, a) in ((ta, 0), (tb, 1)):
            bx = buf(f"b4x{a}")
            dma("sp", "dsp", b4x[a].ap(), x_t[ti], wbufs=[bx])
            p.op("sp", lambda e, gp4=gp4, a=a, ti=ti: e.dma_start(
                bass.AP(b4xf[gp4], a * 128, [[2048, 128], [256, 8], [1, 128]]),
                bass.AP(xnf_dr, ti * 128 * D, [[D, 128], [128, 8], [1, 128]])),
                reads=[buf("xnf_dr")], writes=[bxf], dma=f"q_b4xf{gp4}")
            dma("sp", "dsp", b4s[a].ap(), s_t[ti], rbufs=[buf("s_dr")],
                wbufs=[buf(f"b4s{a}")])
            p.op("pool", lambda e, a=a: e.memset(ytok[a].ap(), 0),
                 reads=[], writes=[buf(f"ytok{a}")])
            p.op("pool", lambda e, a=a, ti=ti: e.indirect_dma_start(
                out=ytok[a].ap(), out_offset=None,
                in_=yfull_dr.ap(),
                in_offset=bass.IndirectOffsetOnAxis(idx_a[:, ti:ti + 1], 0),
                bounds_check=bc_reg, oob_is_err=False),
                reads=[buf("yfull"), idxb, buf(f"ytok{a}")],
                writes=[buf(f"ytok{a}")], dma=f"q_ytok{a}")
        hbufs = []
        for j in range(32):
            bbk = j % 4
            prs = [(o1_sb[:, ci * HID + j * 128: ci * HID + (j + 1) * 128],
                    b4xf[gp4][:, ci * 256:(ci + 1) * 256]) for ci in range(8)]
            mmgroup(bank[bbk][:, 0:256], prs, [bxf, buf("o1")], bb_[bbk])
            hb = buf(f"h4_{j}")
            hbufs.append(hb)
            p.op("act", lambda e, j=j, bbk=bbk: e.activation(
                hfm4[:, j * 256:(j + 1) * 256], bank[bbk][:, 0:256], AF.Relu,
                bias=ob1_sb[:, j:j + 1]),
                reads=[bb_[bbk], buf("ob1")], writes=[hb])
        for ts in range(2):
            for dh in range(2):
                bbk = 4 + ts * 2 + dh

                def emity(e, ts=ts, dh=dh, bbk=bbk):
                    for j in range(32):
                        ins = e.matmul(
                            bank[bbk][:, 0:512],
                            hfm4[:, j * 256 + ts * 128: j * 256 + ts * 128 + 128],
                            o2_sb[:, j * D + dh * 512: j * D + (dh + 1) * 512],
                            start=(j == 0), stop=(j == 31))
                    return ins
                p.op("pe", emity, reads=hbufs + [buf("o2")], writes=[bb_[bbk]])
            for dh in range(2):
                bbk = 4 + ts * 2 + dh
                p.op("dve", lambda e, ts=ts, dh=dh, bbk=bbk: e.tensor_tensor(
                    yov[ts][:, dh * 512:(dh + 1) * 512], bank[bbk][:, 0:512],
                    ob2bc[:, dh * 512:(dh + 1) * 512], OP.add),
                    reads=[bb_[bbk], buf("ob2bc")], writes=[buf(f"yov{ts}")])
        for a, ti in ((0, ta), (1, tb)):
            p.op("dve", lambda e, a=a, ti=ti: e.tensor_scalar(
                b4f.ap(), yov[a].ap(), inv_a[:, ti:ti + 1], None, OP.mult),
                reads=[buf(f"yov{a}"), invb], writes=[buf("b4f")])
            p.op("dve", lambda e, a=a: e.tensor_add(
                b4f.ap(), b4f.ap(), ytok[a].ap()),
                reads=[buf("b4f"), buf(f"ytok{a}")], writes=[buf("b4f")])
            p.op("act", lambda e: e.activation(b4g.ap(), b4f.ap(), AF.Silu),
                 reads=[buf("b4f")], writes=[buf("b4g")])
            p.op("dve", lambda e, a=a: e.tensor_mul(
                b4g.ap(), b4g.ap(), b4s[a].ap()),
                reads=[buf("b4g"), buf(f"b4s{a}")], writes=[buf("b4g")])
            p.op("dve", lambda e, a=a: e.tensor_add(
                b4o[a].ap(), b4g.ap(), b4x[a].ap()),
                reads=[buf("b4g"), buf(f"b4x{a}")], writes=[buf(f"b4o{a}")])
            dma("sp", "dsp", out_t[ti], b4o[a].ap(), rbufs=[buf(f"b4o{a}")],
                wbufs=[buf("outw")])
    c4.close()

    if debug:
        dma("sp", "dsp", dbg["xn"].ap(), xn_dr.ap(), rbufs=[buf("xn_dr")])
        dma("sp", "dsp", dbg["eid"].ap(), eid_a.ap(), rbufs=[eidb])
        dma("sp", "dsp", dbg["lrank"].ap(), lrank_a.ap(), rbufs=[lrankb])
        dma("sp", "dsp", dbg["inv"].ap(), inv_a.ap(), rbufs=[invb])
        dma("sp", "dsp", dbg["idx"].ap(), idx_a.ap(), rbufs=[idxb])
        dma("sp", "dsp", dbg["osc"].ap(), osc_dr.ap(), rbufs=[buf("osc_dr")])
        dma("sp", "dsp", dbg["s"].ap(), s_dr.ap(), rbufs=[buf("s_dr")])
        dma("sp", "dsp", dbg["yfull"].ap(), yfull_dr.ap(), rbufs=[buf("yfull")])
        dma("sp", "dsp", dbg["bufc"].ap(), bufc_dr.ap(), rbufs=[buf("bufc")])
        dma("act", "dact", dbg["coff"].ap(), coff_dr.ap(), rbufs=[buf("coff_dr")])
        dma("sp", "dsp", dbg["rstd"].ap(), rstd_a.ap(), rbufs=[statsb])
        dma("sp", "dsp", dbg["nmr"].ap(), nmr_a.ap(), rbufs=[statsb])

    p.wait_all("sp", [s for s in p.cnt if s.startswith("q_")
                      or s.startswith("cc") or s in ("dact", "dpool", "dsp")])

    p.emit(sems)
    stack.close()
    return nc


_PROG_CACHE = {}


def _get_program(debug=False):
    if debug not in _PROG_CACHE:
        _PROG_CACHE[debug] = build_program(debug=debug)
    return _PROG_CACHE[debug]


def make_in_maps(inputs):
    t = _tables()
    x = np.asarray(inputs["L0_attn_x"], np.float32).reshape(16 * 4096, D)
    w1 = np.asarray(inputs["L0_ffn_w1"], np.float32)
    b1 = np.asarray(inputs["L0_ffn_b1"], np.float32)
    w2 = np.asarray(inputs["L0_ffn_w2"], np.float32)
    b2 = np.asarray(inputs["L0_ffn_b2"], np.float32)
    gn = np.asarray(inputs["L0_attn_gn_weight"], np.float32)
    gnw16 = np.ascontiguousarray(
        np.tile(gn, NH).reshape(16, 128).T)
    shared = dict(
        wq=np.ascontiguousarray(inputs["L0_attn_Wq"], np.float32),
        wk=np.ascontiguousarray(inputs["L0_attn_Wk"], np.float32),
        wv=np.ascontiguousarray(inputs["L0_attn_Wv"], np.float32),
        wg=np.ascontiguousarray(inputs["L0_attn_Wg"], np.float32),
        wo=np.ascontiguousarray(inputs["L0_attn_Wo"], np.float32),
        wr=np.ascontiguousarray(inputs["L0_ffn_router_w"], np.float32),
        rb=np.ascontiguousarray(inputs["L0_ffn_router_b"],
                                np.float32).reshape(1, E),
        gnw16=gnw16,
        ow1=np.ascontiguousarray(inputs["L0_ffn_ow1"], np.float32),
        ow2=np.ascontiguousarray(inputs["L0_ffn_ow2"], np.float32),
        ob1=np.ascontiguousarray(
            np.asarray(inputs["L0_ffn_ob1"], np.float32).reshape(32, 128).T),
        ob2=np.ascontiguousarray(inputs["L0_ffn_ob2"],
                                 np.float32).reshape(1, D),
        cc2T=t["cc2T"], ss2T=t["ss2T"], DT=t["DT"], CQ=t["CQ"], CK=t["CK"],
        utri=t["utri"], iden=t["iden"], onescol=t["onescol"],
    )
    shared["io8_in"] = np.broadcast_to(np.arange(E, dtype=np.float32),
                                       (128, E)).copy()
    shared["io8m8_in"] = np.broadcast_to(np.arange(-8, 0, dtype=np.float32),
                                         (128, E)).copy()
    shared["iop_in"] = np.arange(128, dtype=np.float32).reshape(128, 1).copy()
    in_maps = []
    for c in range(NCORES):
        m = dict(shared)
        m["x"] = np.ascontiguousarray(x[c * TPB:(c + 1) * TPB])
        m["w1c"] = np.ascontiguousarray(w1[c])
        m["w2c"] = np.ascontiguousarray(w2[c])
        m["b1c"] = np.ascontiguousarray(b1[c].reshape(32, 128).T)
        m["b2c"] = np.ascontiguousarray(b2[c].reshape(1, D))
        cm = np.zeros((E, 1), np.float32)
        cm[:c] = 1.0
        m["cmask"] = cm
        in_maps.append(m)
    return in_maps


def kernel(**inputs):
    nc = _get_program(debug=False)
    in_maps = make_in_maps(inputs)
    res = run_bass_kernel_spmd(nc, in_maps, core_ids=list(range(NCORES)))
    out = np.concatenate([np.asarray(r["out"]) for r in res.results], axis=0)
    return out.reshape(16, 4096, D)


# version 12
# speedup vs baseline: 1.0073x; 1.0037x over previous
import numpy as np
import ml_dtypes

import concourse.bass as bass
import concourse.mybir as mybir
import concourse.bass_utils as _bu
from concourse.bass_utils import run_bass_kernel_spmd

F32 = mybir.dt.float32
BF16 = mybir.dt.bfloat16
I32 = mybir.dt.int32
U32 = mybir.dt.uint32
AF = mybir.ActivationFunctionType
OP = mybir.AluOpType
AX = mybir.AxisListType.X

NCORES = 8
D = 1024
HID = 4096
NH = 4
DK = 256
DV = 512
CH = 128
TPB = 8192
NT = TPB // CH
NTB = 32
E = 8
CAP = 1280
EPS = 1e-5
QSCALE = 0.0625
MAGIC = 0x5F3759DF
OOBBASE = 20000.0

GAMMAS = [1.0 - 2.0 ** (-5.0 - h) for h in range(NH)]

def _bvo_nobirsim(tmpdir, inp="bir.json", outp="file.neff", arch=None, *,
                  dve_root=None):
    from pathlib import Path
    from concourse.bass_utils import (get_walrus_driver, get_walrus_args,
                                      get_bir_arch, run_command)
    cmd = [
        get_walrus_driver(),
        "--pass", ",".join(["birverifier", "runtime_memory_reservation",
                            "lower_act", "lower_dve", "lower_ap_offset",
                            "codegen", "neff_packager"]),
        "-i", inp,
        "--neff-output-filename", outp,
        "--enable-birsim=false",
        "--mem-mode=physical", "--policy=0", "--enable-ldw-opt=false",
        "--assign-static-dmas-to-sp=false",
        "--dram-page-size=256", "--enable-neff-debug-info=true",
        "--jobs", "8",
        *get_walrus_args(get_bir_arch(tmpdir, inp) if arch is None else arch,
                         tmpdir, dve_root=dve_root),
    ]
    result = run_command(cmd, cwd=tmpdir)
    if result is not None:
        (Path(tmpdir) / "log.txt").write_text(result.stdout)
    return f"{tmpdir}/{outp}"

_bu.bir_verify_and_optimise = _bvo_nobirsim

ENGINES = ("pe", "dve", "act", "pool", "sp")
ENG_SEM = {e: "s_" + e for e in ENGINES}
BASE_SEMS = [ENG_SEM[e] for e in ENGINES] + ["cc0", "cc1", "cc2"]


class Buf:
    __slots__ = ("name", "w", "rs")

    def __init__(self, name=""):
        self.name, self.w, self.rs = name, None, []


class LM:
    __slots__ = ("eng", "sem", "val")

    def __init__(self, eng, sem, val):
        self.eng, self.sem, self.val = eng, sem, val


class Prog:
    def __init__(self, nc):
        self.nc = nc
        self.ops = {e: [] for e in ENGINES}
        self.cnt = {s: 0 for s in BASE_SEMS}

    def _sem(self, s):
        if s not in self.cnt:
            self.cnt[s] = 0
        return s

    def raw(self, eng, fn):
        self.ops[eng].append((fn, [], None, 0))

    def op(self, eng, fn, reads=(), writes=(), dma=None, cc=None, inc_n=1):
        deps = []
        own = ENG_SEM[eng]
        for b in reads:
            if b.w is not None:
                deps.append(b.w)
        for b in writes:
            deps.extend(b.rs)
            if b.w is not None:
                deps.append(b.w)
        if cc is not None:
            sem, amt = self._sem(cc), 1
        elif dma is not None:
            sem, amt = self._sem(dma), 16 * inc_n
        else:
            sem, amt = own, 1
        waits = {}
        for d in deps:
            if d.val > waits.get(d.sem, 0):
                waits[d.sem] = d.val
        self.cnt[sem] += amt
        lm = LM(eng, sem, self.cnt[sem])
        self.ops[eng].append((fn, sorted(waits.items()), sem, amt))
        for b in reads:
            b.rs.append(lm)
        for b in writes:
            b.w, b.rs = lm, []
        return lm

    def wait_all(self, eng, sems):
        marks = [(s, self.cnt[s]) for s in sems if self.cnt.get(s, 0) > 0]
        self.ops[eng].append((None, marks, None, 0))

    def phase_join(self, engines=ENGINES, dmas=None):
        if dmas is None:
            dmas = [s for s in self.cnt
                    if s.startswith("q_") or s in ("dact", "dpool", "dsp")]
        sl = [ENG_SEM[e] for e in engines] + list(dmas)
        for e in engines:
            self.wait_all(e, [s for s in sl if s != ENG_SEM[e]])

    def emit(self, sems):
        for s in self.cnt:
            if s not in sems:
                sems[s] = self._stack.enter_context(self.nc.semaphore(s))
        engmap = {"pe": "tensor", "dve": "vector", "act": "scalar",
                  "pool": "gpsimd", "sp": "sync"}
        with self.nc.Block() as blk:
            for ename in ENGINES:
                oplist = self.ops[ename]
                if not oplist:
                    continue

                def make(oplist=oplist):
                    def run(eng):
                        lw = {}
                        for (fn, waits, sem, amt) in oplist:
                            for (sname, val) in waits:
                                if lw.get(sname, 0) >= val:
                                    continue
                                eng.wait_ge(sems[sname], val)
                                lw[sname] = val
                            if fn is None:
                                continue
                            ins = fn(eng)
                            if sem is not None:
                                ins.then_inc(sems[sem], amt)
                    return run

                getattr(blk, engmap[ename])(make())


def _tables():
    t = {}
    inv = 1.0 / (10000.0 ** (np.arange(0, DK, 2, dtype=np.float64) / DK))
    fr = np.outer(np.arange(4096, dtype=np.float64), inv)
    cosT = np.cos(fr).T
    sinT = np.sin(fr).T
    NW = 4096 // 256
    cc2 = np.zeros((128, NW * 512))
    ss2 = np.zeros((128, NW * 512))
    for w in range(NW):
        cw = cosT[:, w * 256:(w + 1) * 256]
        sw = sinT[:, w * 256:(w + 1) * 256]
        cc2[:, w * 512:w * 512 + 256] = cw
        cc2[:, w * 512 + 256:(w + 1) * 512] = cw
        ss2[:, w * 512:w * 512 + 256] = -sw
        ss2[:, w * 512 + 256:(w + 1) * 512] = sw
    t["cc2T"] = cc2.astype(ml_dtypes.bfloat16)
    t["ss2T"] = ss2.astype(ml_dtypes.bfloat16)
    j = np.arange(CH)
    DT = np.zeros((CH, NH * CH), np.float64)
    CQ = np.zeros((CH, NH * 2 * CH), np.float64)
    CK = np.zeros((CH, NH), np.float64)
    for h, g in enumerate(GAMMAS):
        rel = j[None, :] - j[:, None]
        DT[:, h * CH:(h + 1) * CH] = np.where(rel >= 0, g ** np.maximum(rel, 0), 0.0) * QSCALE
        cqrow = np.broadcast_to((g ** (j + 1.0)) * QSCALE, (CH, CH))
        CQ[:, h * 2 * CH:h * 2 * CH + CH] = cqrow
        CQ[:, h * 2 * CH + CH:(h + 1) * 2 * CH] = cqrow
        CK[:, h] = g ** (CH - 1.0 - j)
    t["DT"] = DT.astype(ml_dtypes.bfloat16)
    t["CQ"] = CQ.astype(ml_dtypes.bfloat16)
    t["CK"] = CK.astype(np.float32)
    t["utri"] = np.triu(np.ones((128, 128))).astype(ml_dtypes.bfloat16)
    t["iden"] = np.eye(128).astype(ml_dtypes.bfloat16)
    t["onescol"] = np.ones((128, 1), ml_dtypes.bfloat16)
    return t


def build_program(debug=False):
    from contextlib import ExitStack
    nc = bass.Bass("TRN2", target_bir_lowering=False)

    def din(name, shape, dt=F32):
        return nc.dram_tensor(name, shape, dt, kind="ExternalInput")

    x_in = din("x", [TPB, D])
    wq_in = din("wq", [D, D]); wk_in = din("wk", [D, D])
    wv_in = din("wv", [D, 2048]); wg_in = din("wg", [D, 2048])
    wo_in = din("wo", [2048, D])
    wr_in = din("wr", [D, E]); rb_in = din("rb", [1, E])
    gnw16_in = din("gnw16", [128, 16])
    w1c_in = din("w1c", [D, HID]); w2c_in = din("w2c", [HID, D])
    b1c_in = din("b1c", [128, 32]); b2c_in = din("b2c", [1, D])
    ow1_in = din("ow1", [D, HID]); ow2_in = din("ow2", [HID, D])
    ob1_in = din("ob1", [128, 32]); ob2_in = din("ob2", [1, D])
    cc2_in = din("cc2T", [128, 16 * 512], BF16)
    ss2_in = din("ss2T", [128, 16 * 512], BF16)
    DT_in = din("DT", [128, NH * CH], BF16)
    CQ_in = din("CQ", [128, NH * 2 * CH], BF16)
    CK_in = din("CK", [128, NH])
    utri_in = din("utri", [128, 128], BF16)
    iden_in = din("iden", [128, 128], BF16)
    ones_in = din("onescol", [128, 1], BF16)
    cmask_in = din("cmask", [E, 1])
    io8_in = din("io8_in", [128, E])
    io8m8_in = din("io8m8_in", [128, E])
    iop_in = din("iop_in", [128, 1])

    out_dram = nc.dram_tensor("out", [TPB, D], F32, kind="ExternalOutput")
    dbg = {}
    if debug:
        dbg["xn"] = nc.dram_tensor("dbg_xn", [TPB, D], BF16, kind="ExternalOutput")
        dbg["eid"] = nc.dram_tensor("dbg_eid", [128, NT], F32, kind="ExternalOutput")
        dbg["lrank"] = nc.dram_tensor("dbg_lrank", [128, NT], F32, kind="ExternalOutput")
        dbg["inv"] = nc.dram_tensor("dbg_inv", [128, NT], F32, kind="ExternalOutput")
        dbg["idx"] = nc.dram_tensor("dbg_idx", [128, NT], U32, kind="ExternalOutput")
        dbg["osc"] = nc.dram_tensor("dbg_osc", [TPB, 2048], BF16, kind="ExternalOutput")
        dbg["s"] = nc.dram_tensor("dbg_s", [TPB, D], BF16, kind="ExternalOutput")
        dbg["yfull"] = nc.dram_tensor("dbg_yfull", [E * CAP, D], BF16, kind="ExternalOutput")
        dbg["bufc"] = nc.dram_tensor("dbg_bufc", [CAP, D], BF16, kind="ExternalOutput")
        dbg["coff"] = nc.dram_tensor("dbg_coff", [1, E], F32, kind="ExternalOutput")
        dbg["rstd"] = nc.dram_tensor("dbg_rstd", [128, NT], F32, kind="ExternalOutput")
        dbg["nmr"] = nc.dram_tensor("dbg_nmr", [128, NT], F32, kind="ExternalOutput")

    xn_dr = nc.dram_tensor("xn_dr", [TPB, D], BF16)
    xnf_dr = nc.dram_tensor("xnf_dr", [TPB, D], BF16)
    g_dr = nc.dram_tensor("g_dr", [TPB, 2048], BF16)
    osc_dr = nc.dram_tensor("osc_dr", [TPB, 2048], BF16)
    s_dr = nc.dram_tensor("s_dr", [TPB, D], BF16)
    bufg_dr = nc.dram_tensor("bufg_dr", [E * CAP, D], BF16)
    bufc_dr = nc.dram_tensor("bufc_dr", [CAP, D], BF16)
    yc_dr = nc.dram_tensor("yc_dr", [CAP, D], BF16)
    yfull_dr = nc.dram_tensor("yfull_dr", [E * CAP, D], BF16)
    ccc_in = nc.dram_tensor("ccc_in", [1, E], F32)
    ccc_out = nc.dram_tensor("ccc_out", [E, E], F32)
    offs_dr = nc.dram_tensor("offs_dr", [NT, E], F32)
    coff_dr = nc.dram_tensor("coff_dr", [1, E], F32)

    stack = ExitStack()
    sems = {s: stack.enter_context(nc.semaphore(s)) for s in BASE_SEMS}
    bc_reg = stack.enter_context(nc.gpsimd.register("bc_reg"))
    p = Prog(nc)
    p._stack = stack
    B = {}

    def buf(name):
        if name not in B:
            B[name] = Buf(name)
        return B[name]

    def dma(eng, q, dst, src, rbufs=(), wbufs=()):
        stream = "q_" + wbufs[0].name if wbufs else q
        return p.op(eng, lambda e, d=dst, s=src: e.dma_start(d, s),
                    reads=rbufs, writes=wbufs, dma=stream)

    def mmgroup(psum_ap, pairs, reads, wbank):
        def emit(e, psum_ap=psum_ap, pairs=pairs):
            n = len(pairs)
            for i, (l, r) in enumerate(pairs):
                ins = e.matmul(psum_ap, l, r, start=(i == 0), stop=(i == n - 1))
            return ins
        return p.op("pe", emit, reads=reads, writes=[wbank])

    def rowchunks(t_in, cols, nch):
        return bass.AP(t_in, 0, [[cols, 128], [128 * cols, nch], [1, cols]])

    AS = nc.alloc_sbuf_tensor
    rstd_a = AS("rstd_a", [128, NT], F32)
    nmr_a = AS("nmr_a", [128, NT], F32)
    eid_a = AS("eid_a", [128, NT], F32)
    lrank_a = AS("lrank_a", [128, NT], F32)
    inv_a = AS("inv_a", [128, NT], F32)
    idx_a = AS("idx_a", [128, NT], U32)
    io8 = AS("io8", [128, E], F32)
    io8m8 = AS("io8m8", [128, E], F32)
    iop = AS("iop", [128, 1], F32)
    utri = AS("utri_s", [128, 128], BF16)
    iden = AS("iden_s", [128, 128], BF16)
    onescol = AS("onescol_s", [128, 1], BF16)
    rb_bc = AS("rb_bc", [128, E], F32)
    coff_bc = AS("coff_bc", [128, E], F32)
    offs_bc = AS("offs_bc", [128, NT * E], F32)
    runtot = AS("runtot", [1, E], F32)
    offs_row = AS("offs_row", [1, NT * E], F32)
    counts_T = AS("counts_T", [E, E], F32)
    cmrow = AS("cmrow", [E, E], F32)
    coff_col = AS("coff_col", [E, 1], F32)
    wr_sb = AS("wr_sb", [128, 8 * E], BF16)
    rxn = [AS(f"rxn{i}", [128, D], BF16) for i in range(2)]
    r8 = AS("r8", [128, E], F32)
    r8b = AS("r8b", [128, E], F32)
    rc1 = AS("rc1", [128, 1], F32)
    rc2 = AS("rc2", [128, 1], F32)
    rc3 = AS("rc3", [128, 1], F32)
    ytok = [AS(f"ytok{i}", [128, D], BF16) for i in range(2)]

    bank = [stack.enter_context(nc.psum_tensor(f"bank{i}", [128, 512], F32))
            for i in range(8)]
    bankB = [b.ap().bitcast(BF16) for b in bank]
    bb_ = [buf(f"bank{i}") for i in range(8)]

    x_t = x_in.ap().rearrange("(n p) d -> n p d", p=128)
    xn_t = xn_dr.ap().rearrange("(n p) d -> n p d", p=128)
    xnf_t = xnf_dr.ap().rearrange("(n p) d -> n p d", p=128)
    g_t = g_dr.ap().rearrange("(n p) d -> n p d", p=128)
    osc_t = osc_dr.ap().rearrange("(n p) d -> n p d", p=128)
    s_t = s_dr.ap().rearrange("(n p) d -> n p d", p=128)
    out_t = out_dram.ap().rearrange("(n p) d -> n p d", p=128)
    bufc_t = bufc_dr.ap().rearrange("(n p) d -> n p d", p=128)
    yc_t = yc_dr.ap().rearrange("(n p) d -> n p d", p=128)

    statsb = buf("stats")
    eidb = buf("eid_a")
    lrankb = buf("lrank_a")
    invb = buf("inv_a")
    idxb = buf("idx_a")

    dma("sp", "dsp", utri.ap(), utri_in[:, :], wbufs=[buf("utri")])
    dma("sp", "dsp", iden.ap(), iden_in[:, :], wbufs=[buf("iden")])
    dma("sp", "dsp", onescol.ap(), ones_in[:, :], wbufs=[buf("onescol")])
    dma("sp", "dsp", rb_bc.ap(), bass.AP(rb_in, 0, [[0, 128], [1, E]]),
        wbufs=[buf("rb_bc")])
    dma("pool", "dpool", wr_sb.ap(),
        rowchunks(wr_in, E, 8), wbufs=[buf("wr_sb")])
    dma("sp", "dsp", io8.ap(), io8_in[:, :], wbufs=[buf("io8")])
    dma("sp", "dsp", io8m8.ap(), io8m8_in[:, :], wbufs=[buf("io8m8")])
    dma("sp", "dsp", iop.ap(), iop_in[:, :], wbufs=[buf("iop")])
    p.op("pool", lambda e: e.memset(runtot[:], 0), writes=[buf("runtot")])
    p.raw("pool", lambda e: e.reg_mov(bc_reg, E * CAP - 1))

    c1 = ExitStack()
    S1 = lambda n, sh, d: c1.enter_context(nc.sbuf_tensor(n, sh, d))
    wq_sb = S1("wq_sb", [128, 8 * D], BF16)
    wk_sb = S1("wk_sb", [128, 8 * D], BF16)
    wv_sb = S1("wv_sb", [128, 8 * 2048], BF16)
    DTt = S1("DTt", [128, NH * CH], BF16)
    CQt = S1("CQt", [128, NH * 2 * CH], BF16)
    CKt = S1("CKt", [128, NH], F32)
    dma("sp", "dsp", DTt.ap(), DT_in[:, :], wbufs=[buf("DTt")])
    dma("sp", "dsp", CQt.ap(), CQ_in[:, :], wbufs=[buf("CQt")])
    dma("sp", "dsp", CKt.ap(), CK_in[:, :], wbufs=[buf("CKt")])
    dma("pool", "dpool", wq_sb.ap(),
        rowchunks(wq_in, D, 8), wbufs=[buf("wq")])
    dma("pool", "dpool", wk_sb.ap(),
        rowchunks(wk_in, D, 8), wbufs=[buf("wk")])
    dma("pool", "dpool", wv_sb.ap(),
        rowchunks(wv_in, 2048, 8), wbufs=[buf("wv")])

    ppctx = ExitStack()
    S0 = lambda n, sh, d: ppctx.enter_context(nc.sbuf_tensor(n, sh, d))
    zero2k = S0("zero2k", [128, 2048], BF16)
    p.op("pool", lambda e: e.memset(zero2k[:], 0), writes=[buf("zero2k")])
    bz = bufg_dr.ap().rearrange("(n p) (a d) -> n p (a d)", p=256, a=1).rearrange("n p d -> n (p d)").rearrange("n (p q) -> n p q", p=128)
    for i in range(40):
        dma("sp", "dsp", bz[i], zero2k[:], rbufs=[buf("zero2k")],
            wbufs=[buf("bufg")])
    xraw = [S0(f"ppx{i}", [128, D], F32) for i in range(8)]
    xnb = [S0(f"ppxn{i}", [128, D], BF16) for i in range(4)]
    xfmp = [S0(f"ppxf{i}", [128, D], BF16) for i in range(4)]
    wg_sb = S0("wg_sb", [128, 8 * 2048], BF16)
    gsb = [S0(f"ppg{i}", [128, 2048], BF16) for i in range(4)]
    dma("pool", "dpool", wg_sb.ap(),
        rowchunks(wg_in, 2048, 8), wbufs=[buf("wg")])
    xsqd = S0("ppsq", [128, D], BF16)
    sumsq8 = S0("ppsumsq", [128, 8], F32)
    sumx8 = S0("ppsumx", [128, 8], F32)
    mu8 = S0("ppmu", [128, 8], F32)
    vv8 = S0("ppvv", [128, 8], F32)
    lgb = S0("pplg", [128, E], F32)
    lg2 = S0("pplg2", [128, E], F32)
    mrow = S0("ppm", [128, 1], F32)
    ohbf = [S0(f"ppoh{i}", [128, E], BF16) for i in range(2)]
    lgps = [S0(f"pplgp{i}", [128, E], F32) for i in range(2)]
    cumsb = [S0(f"ppcum{i}", [128, E], F32) for i in range(2)]
    cntsb = [S0(f"ppcnt{i}", [1, E], F32) for i in range(2)]

    def newton8(v_ap, out_ap, rb_, wb_, scr0, scr1, scri):
        n0, n1, ni = buf(scr0.handle_name), buf(scr1.handle_name), buf(scri.handle_name)
        p.op("dve", lambda e: e.tensor_single_scalar(
            scri.ap(), v_ap.bitcast(I32), 1, OP.arith_shift_right),
            reads=[rb_], writes=[ni])
        p.op("dve", lambda e: e.tensor_scalar(
            scri.ap(), scri.ap(), -1, MAGIC, OP.mult, OP.add),
            reads=[ni], writes=[ni])
        p.op("dve", lambda e: e.tensor_copy(scr0.ap(), scri.ap().bitcast(F32)),
             reads=[ni], writes=[n0])
        for _ in range(2):
            p.op("dve", lambda e: e.tensor_mul(scr1.ap(), scr0.ap(), scr0.ap()),
                 reads=[n0], writes=[n1])
            p.op("dve", lambda e: e.tensor_tensor(scr1.ap(), scr1.ap(), v_ap,
                                                  OP.mult),
                 reads=[n1, rb_], writes=[n1])
            p.op("dve", lambda e: e.tensor_scalar(scr1.ap(), scr1.ap(), -0.5,
                                                  1.5, OP.mult, OP.add),
                 reads=[n1], writes=[n1])
            p.op("dve", lambda e: e.tensor_mul(scr0.ap(), scr0.ap(), scr1.ap()),
                 reads=[n0, n1], writes=[n0])
        p.op("dve", lambda e: e.tensor_copy(out_ap, scr0.ap()),
             reads=[n0], writes=[wb_])

    nwt8 = [S0(f"ppnw{i}", [128, 8], F32) for i in range(2)]
    nwt8i = S0("ppnwi", [128, 8], I32)
    for tn in (nwt8[0], nwt8[1], nwt8i, sumsq8, sumx8, mu8, vv8):
        pass

    for g8 in range(NT // 8):
        ts8 = list(range(8 * g8, 8 * g8 + 8))
        for k, t in enumerate(ts8):
            bx = buf(f"ppx{k}")
            dma("sp", "dsp", xraw[k].ap(), x_t[t], wbufs=[bx])
            p.op("act", lambda e, k=k: e.activation(
                xsqd.ap(), xraw[k].ap(), AF.Square, accum_out=sumsq8[:, k:k + 1]),
                reads=[bx], writes=[buf("ppsq"), buf("ppsumsq")])
            p.op("act", lambda e, k=k: e.activation(
                xsqd.ap(), xraw[k].ap(), AF.Copy, accum_out=sumx8[:, k:k + 1]),
                reads=[bx], writes=[buf("ppsq"), buf("ppsumx")])
        p.op("dve", lambda e: e.tensor_scalar(mu8.ap(), sumx8.ap(), 1.0 / D,
                                              None, OP.mult),
             reads=[buf("ppsumx")], writes=[buf("ppmu")])
        p.op("dve", lambda e: e.tensor_scalar(vv8.ap(), sumsq8.ap(), 1.0 / D,
                                              None, OP.mult),
             reads=[buf("ppsumsq")], writes=[buf("ppvv")])
        p.op("dve", lambda e: e.tensor_mul(nwt8[0].ap(), mu8.ap(), mu8.ap()),
             reads=[buf("ppmu")], writes=[buf("ppnw0")])
        p.op("dve", lambda e: e.tensor_sub(vv8.ap(), vv8.ap(), nwt8[0].ap()),
             reads=[buf("ppvv"), buf("ppnw0")], writes=[buf("ppvv")])
        p.op("dve", lambda e: e.tensor_scalar(vv8.ap(), vv8.ap(), EPS, None,
                                              OP.add),
             reads=[buf("ppvv")], writes=[buf("ppvv")])
        class _H:
            pass
        s0 = _H(); s0.ap = nwt8[0].ap; s0.handle_name = "ppnw0"
        s1 = _H(); s1.ap = nwt8[1].ap; s1.handle_name = "ppnw1"
        si = _H(); si.ap = nwt8i.ap; si.handle_name = "ppnwi"
        newton8(vv8.ap(), rstd_a[:, 8 * g8:8 * g8 + 8], buf("ppvv"), statsb,
                s0, s1, si)
        p.op("dve", lambda e, g8=g8: e.scalar_tensor_tensor(
            nmr_a[:, 8 * g8:8 * g8 + 8], mu8.ap(), -1.0,
            rstd_a[:, 8 * g8:8 * g8 + 8], OP.mult, OP.mult),
            reads=[buf("ppmu"), statsb], writes=[statsb])
        for k, t in enumerate(ts8):
            par = t % 4
            tbk = 7 - (t % 2)
            bx, bxn, bxf = buf(f"ppx{k}"), buf(f"ppxn{par}"), buf(f"ppxf{par}")
            p.op("dve", lambda e, k=k, par=par, t=t: e.tensor_scalar(
                xnb[par].ap(), xraw[k].ap(), rstd_a[:, t:t + 1],
                nmr_a[:, t:t + 1], OP.mult, OP.add),
                reads=[bx, statsb], writes=[bxn])
            dma("sp", "dsp", xn_t[t], xnb[par].ap(), rbufs=[bxn],
                wbufs=[buf("xn_dr")])
            for ci in range(8):
                p.op("pe", lambda e, par=par, ci=ci, tbk=tbk: e.transpose(
                    bankB[tbk][:, ci * 128:(ci + 1) * 128],
                    xnb[par][:, ci * 128:(ci + 1) * 128], iden.ap()),
                    reads=[bxn, buf("iden")], writes=[bb_[tbk]])
            p.op("dve", lambda e, par=par, tbk=tbk: e.tensor_copy(
                xfmp[par].ap(), bankB[tbk][:, 0:1024]),
                reads=[bb_[tbk]], writes=[bxf])
            dma("sp", "dsp", xnf_t[t], xfmp[par].ap(), rbufs=[bxf],
                wbufs=[buf("xnf_dr")])
            bg = buf(f"ppg{par}")
            for dv in range(4):
                bbk = 3 + (dv % 2)
                prs = [(xfmp[par][:, ci * 128:(ci + 1) * 128],
                        wg_sb[:, ci * 2048 + dv * 512: ci * 2048 + (dv + 1) * 512])
                       for ci in range(8)]
                mmgroup(bank[bbk][:, 0:512], prs, [bxf, buf("wg")], bb_[bbk])
                p.op("act", lambda e, par=par, dv=dv, bbk=bbk: e.activation(
                    gsb[par][:, dv * 512:(dv + 1) * 512], bank[bbk][:, 0:512],
                    AF.Silu),
                    reads=[bb_[bbk]], writes=[bg])
            dma("act", "dact", g_t[t], gsb[par].ap(), rbufs=[bg],
                wbufs=[buf("g_dr")])
            mmgroup(bank[0][:, 0:E],
                    [(xfmp[par][:, ci * 128:(ci + 1) * 128],
                      wr_sb[:, 8 * ci:8 * ci + 8]) for ci in range(8)],
                    [bxf, buf("wr_sb")], bb_[0])
            pp2 = t % 2
            blg, boh = buf(f"pplgp{pp2}"), buf(f"ppoh{pp2}")
            bcu, bcn = buf(f"ppcum{pp2}"), buf(f"ppcnt{pp2}")
            p.op("dve", lambda e, pp2=pp2: e.tensor_add(
                lgps[pp2].ap(), bank[0][:, 0:E], rb_bc.ap()),
                reads=[bb_[0], buf("rb_bc")], writes=[blg])
            p.op("dve", lambda e, pp2=pp2: e.reduce_max(
                mrow.ap(), lgps[pp2].ap(), axis=AX),
                reads=[blg], writes=[buf("ppm")])
            p.op("pool", lambda e, pp2=pp2: e.tensor_scalar(
                lg2.ap(), lgps[pp2].ap(), mrow.ap(), None, OP.is_equal),
                reads=[blg, buf("ppm")], writes=[buf("pplg2")])
            p.op("pool", lambda e: e.tensor_mul(lg2.ap(), lg2.ap(), io8m8.ap()),
                 reads=[buf("pplg2"), buf("io8m8")], writes=[buf("pplg2")])
            p.op("dve", lambda e: e.tensor_reduce(mrow.ap(), lg2.ap(), AX,
                                                  OP.min),
                 reads=[buf("pplg2")], writes=[buf("ppm")])
            p.op("pool", lambda e, t=t: e.tensor_scalar(
                eid_a[:, t:t + 1], mrow.ap(), 8.0, None, OP.add),
                reads=[buf("ppm")], writes=[eidb])
            p.op("pool", lambda e, t=t, pp2=pp2: e.tensor_scalar(
                ohbf[pp2].ap(), io8.ap(), eid_a[:, t:t + 1], None, OP.is_equal),
                reads=[buf("io8"), eidb], writes=[boh])
            mmgroup(bank[1][:, 0:E], [(utri.ap(), ohbf[pp2].ap())],
                    [buf("utri"), boh], bb_[1])
            mmgroup(bank[2][0:1, 0:E], [(onescol.ap(), ohbf[pp2].ap())],
                    [buf("onescol"), boh], bb_[2])
            p.op("dve", lambda e, pp2=pp2: e.tensor_tensor(
                cumsb[pp2].ap(), bank[1][:, 0:E], ohbf[pp2].ap(), OP.mult),
                reads=[bb_[1], boh], writes=[bcu])
            p.op("dve", lambda e, pp2=pp2: e.tensor_copy(
                cntsb[pp2].ap(), bank[2][0:1, 0:E]),
                reads=[bb_[2]], writes=[bcn])
            p.op("dve", lambda e, pp2=pp2: e.tensor_reduce(
                mrow.ap(), cumsb[pp2].ap(), AX, OP.add),
                reads=[bcu], writes=[buf("ppm")])
            p.op("pool", lambda e, t=t: e.tensor_scalar(
                lrank_a[:, t:t + 1], mrow.ap(), -1.0, None, OP.add),
                reads=[buf("ppm")], writes=[lrankb])
            p.op("pool", lambda e, t=t: e.tensor_copy(
                offs_row[:, E * t:E * (t + 1)], runtot.ap()),
                reads=[buf("runtot")], writes=[buf("offs_row")])
            p.op("pool", lambda e, pp2=pp2: e.tensor_add(
                runtot.ap(), runtot.ap(), cntsb[pp2].ap()),
                reads=[buf("runtot"), bcn], writes=[buf("runtot")])

    dma("act", "dact", bass.AP(offs_dr, 0, [[512, 1], [1, 512]]), offs_row.ap(),
        rbufs=[buf("offs_row")], wbufs=[buf("offs_dr")])
    dma("pool", "dpool", ccc_in[:, :], runtot.ap(), rbufs=[buf("runtot")],
        wbufs=[buf("ccc_in")])
    p.op("pool", lambda e: e.collective_compute(
        "AllGather", OP.bypass, replica_groups=[list(range(NCORES))],
        ins=[ccc_in.ap().opt()], outs=[ccc_out.ap().opt()]),
        reads=[buf("ccc_in")], writes=[buf("ccc_out")], cc="cc0")
    p.op("pool", lambda e: e.dma_start(
        counts_T.ap(), bass.AP(ccc_out, 0, [[1, E], [E, E]]),
        allow_slow_non_contiguous=True),
        reads=[buf("ccc_out")], writes=[buf("counts_T")], dma="q_counts_T")
    dma("pool", "dpool", cmrow.ap(),
        bass.AP(cmask_in, 0, [[0, E], [1, E]]), wbufs=[buf("cmrow")])
    p.op("pool", lambda e: e.tensor_mul(counts_T.ap(), counts_T.ap(), cmrow.ap()),
         reads=[buf("counts_T"), buf("cmrow")], writes=[buf("counts_T")])
    p.op("pool", lambda e: e.tensor_add(counts_T[:, 0:4], counts_T[:, 0:4],
                                        counts_T[:, 4:8]),
         reads=[buf("counts_T")], writes=[buf("counts_T")])
    p.op("pool", lambda e: e.tensor_add(counts_T[:, 0:2], counts_T[:, 0:2],
                                        counts_T[:, 2:4]),
         reads=[buf("counts_T")], writes=[buf("counts_T")])
    p.op("pool", lambda e: e.tensor_add(coff_col.ap(), counts_T[:, 0:1],
                                        counts_T[:, 1:2]),
         reads=[buf("counts_T")], writes=[buf("coff_col")])
    p.op("pool", lambda e: e.dma_start(
        bass.AP(coff_dr, 0, [[1, E], [1, 1]]), coff_col.ap(),
        allow_slow_non_contiguous=True),
        reads=[buf("coff_col")], writes=[buf("coff_dr")], dma="q_coff_dr")
    dma("pool", "dpool", coff_bc.ap(), bass.AP(coff_dr, 0, [[0, 128], [1, E]]),
        rbufs=[buf("coff_dr")], wbufs=[buf("coff_bc")])
    dma("pool", "dpool", offs_bc.ap(),
        bass.AP(offs_dr, 0, [[0, 128], [E, NT], [1, E]]),
        rbufs=[buf("offs_dr")], wbufs=[buf("offs_bc")])

    for t in range(NT):
        par = t % 2
        rb_ = buf(f"rxn{par}")
        dma("pool", "dpool", rxn[par].ap(), xn_t[t], rbufs=[buf("xn_dr")],
            wbufs=[rb_])
        p.op("pool", lambda e, t=t: e.tensor_scalar(
            r8.ap(), io8.ap(), eid_a[:, t:t + 1], None, OP.is_equal),
            reads=[buf("io8"), eidb], writes=[buf("r8")])
        p.op("pool", lambda e, t=t: e.tensor_add(
            r8b.ap(), coff_bc.ap(), offs_bc[:, E * t:E * (t + 1)]),
            reads=[buf("coff_bc"), buf("offs_bc")], writes=[buf("r8b")])
        p.op("pool", lambda e: e.tensor_mul(r8.ap(), r8.ap(), r8b.ap()),
             reads=[buf("r8"), buf("r8b")], writes=[buf("r8")])
        p.op("pool", lambda e: e.tensor_add(r8[:, 0:4], r8[:, 0:4], r8[:, 4:8]),
             reads=[buf("r8")], writes=[buf("r8")])
        p.op("pool", lambda e: e.tensor_add(r8[:, 0:2], r8[:, 0:2], r8[:, 2:4]),
             reads=[buf("r8")], writes=[buf("r8")])
        p.op("pool", lambda e: e.tensor_add(rc1.ap(), r8[:, 0:1], r8[:, 1:2]),
             reads=[buf("r8")], writes=[buf("rc1")])
        p.op("pool", lambda e, t=t: e.tensor_add(rc1.ap(), rc1.ap(),
                                                 lrank_a[:, t:t + 1]),
             reads=[buf("rc1"), lrankb], writes=[buf("rc1")])
        p.op("pool", lambda e, t=t: e.tensor_single_scalar(
            inv_a[:, t:t + 1], rc1.ap(), float(CAP), OP.is_ge),
            reads=[buf("rc1")], writes=[invb])
        p.op("pool", lambda e, t=t: e.tensor_scalar(
            rc2.ap(), eid_a[:, t:t + 1], float(CAP), None, OP.mult),
            reads=[eidb], writes=[buf("rc2")])
        p.op("pool", lambda e: e.tensor_add(rc2.ap(), rc2.ap(), rc1.ap()),
             reads=[buf("rc2"), buf("rc1")], writes=[buf("rc2")])
        p.op("pool", lambda e, t=t: e.tensor_scalar(
            rc3.ap(), iop.ap(), 1.0, OOBBASE + 128.0 * t, OP.mult, OP.add),
            reads=[buf("iop")], writes=[buf("rc3")])
        p.op("pool", lambda e: e.tensor_sub(rc3.ap(), rc3.ap(), rc2.ap()),
             reads=[buf("rc3"), buf("rc2")], writes=[buf("rc3")])
        p.op("pool", lambda e, t=t: e.tensor_scalar(
            rc3.ap(), rc3.ap(), inv_a[:, t:t + 1], None, OP.mult),
            reads=[buf("rc3"), invb], writes=[buf("rc3")])
        p.op("pool", lambda e: e.tensor_add(rc2.ap(), rc2.ap(), rc3.ap()),
             reads=[buf("rc2"), buf("rc3")], writes=[buf("rc2")])
        p.op("pool", lambda e, t=t: e.tensor_copy(idx_a[:, t:t + 1], rc2.ap()),
             reads=[buf("rc2")], writes=[idxb])
        p.op("pool", lambda e, t=t, par=par: e.indirect_dma_start(
            out=bufg_dr.ap(),
            out_offset=bass.IndirectOffsetOnAxis(idx_a[:, t:t + 1], 0),
            in_=rxn[par].ap(), in_offset=None,
            bounds_check=bc_reg, oob_is_err=False),
            reads=[rb_, idxb, buf("bufg")],
            writes=[buf("bufg")], dma=f"q_sc{t % 2}")
    p.wait_all("pool", ["q_sc0", "q_sc1"])
    p.op("pool", lambda e: e.collective_compute(
        "ReduceScatter", OP.add, replica_groups=[list(range(NCORES))],
        ins=[bufg_dr.ap().opt()], outs=[bufc_dr.ap().opt()]),
        reads=[buf("bufg")], writes=[buf("bufc")], cc="cc1")

    p.phase_join(engines=("pe", "dve", "act", "sp"),
                 dmas=[f"q_ppx{i}" for i in range(8)]
                 + ["q_xn_dr", "q_xnf_dr", "q_g_dr", "q_bufg"])
    ppctx.close()
    Ssb = [[S1(f"S{b}_{h}", [128, 2 * DV], F32) for h in range(NH)]
           for b in range(2)]
    Sbf = [[S1(f"Sb{b}_{h}", [128, 2 * DV], BF16) for h in range(NH)]
           for b in range(2)]
    p1xf = [S1(f"p1xf{i}", [128, 2048], BF16) for i in range(2)]
    qfm = [S1(f"qfm{i}", [128, 2048], BF16) for i in range(2)]
    kfm = [S1(f"kfm{i}", [128, 2048], BF16) for i in range(2)]
    vsb = [S1(f"vsb{i}", [128, 2 * 2048], BF16) for i in range(2)]
    att = [S1(f"att{i}", [128, 128], BF16) for i in range(2 * NH)]
    kckb = [S1(f"kck{i}", [128, DK], BF16) for i in range(2 * NH)]
    qqb = [S1(f"qq{i}", [128, DK], BF16) for i in range(2 * NH)]
    rt1 = S1("rt1", [128, 512], F32)
    rt2 = S1("rt2", [128, 512], F32)
    oscraw = [S1(f"oscr{i}", [128, 2048], BF16) for i in range(2)]
    ccsl = [S1(f"ccsl{i}", [128, 512], BF16) for i in range(2)]
    sssl = [S1(f"sssl{i}", [128, 512], BF16) for i in range(2)]
    ovar8 = S1("ovar8", [128, 8], F32)
    rstd8 = S1("rstd8", [128, 8], F32)
    vv8p = S1("vv8p", [128, 8], F32)
    n8 = [S1(f"n8_{i}", [128, 8], F32) for i in range(2)]
    n8i = S1("n8i", [128, 8], I32)
    osq = S1("osq", [128, DV], BF16)

    def wblk(wsb, ci, j):
        return wsb[:, ci * D + j * 128: ci * D + (j + 1) * 128]

    def emit_qkv(g2):
        bat, gg = g2 % 2, g2 // 2
        gp = bat
        sl = gg % 2
        ta = bat * NTB + 2 * gg
        tb = ta + 1
        if bat == 0:
            dma("sp", "dsp", ccsl[sl].ap(), cc2_in[:, gg * 512:(gg + 1) * 512],
                wbufs=[buf(f"ccsl{sl}")])
            dma("sp", "dsp", sssl[sl].ap(), ss2_in[:, gg * 512:(gg + 1) * 512],
                wbufs=[buf(f"sssl{sl}")])
        bxf = buf(f"p1xf{gp}")
        for (ti, a) in ((ta, 0), (tb, 1)):
            p.op("sp", lambda e, gp=gp, a=a, ti=ti: e.dma_start(
                bass.AP(p1xf[gp], a * 128, [[2048, 128], [256, 8], [1, 128]]),
                bass.AP(xnf_dr, ti * 128 * D, [[D, 128], [128, 8], [1, 128]])),
                reads=[buf("xnf_dr")], writes=[bxf], dma=f"q_p1xf{gp}")
        for (wsb, wb_name, dstl, dname) in ((wq_sb, "wq", qfm, "qfm"),
                                            (wk_sb, "wk", kfm, "kfm")):
            dst = dstl[gp]
            bdst = buf(f"{dname}{gp}")
            for h in range(NH):
                bbk = 4 + (h % 2)
                prs = []
                for half in range(2):
                    j = 2 * h + half
                    prs += [(wblk(wsb, ci, j),
                             p1xf[gp][:, ci * 256:(ci + 1) * 256])
                            for ci in range(8)]

                def emitq(e, prs=prs, bbk=bbk):
                    for i, (l, r) in enumerate(prs):
                        ins = e.matmul(bank[bbk][:, (i // 8) * 256:
                                                 (i // 8) * 256 + 256],
                                       l, r, start=(i % 8 == 0),
                                       stop=(i % 8 == 7))
                    return ins
                p.op("pe", emitq, reads=[bxf, buf(wb_name)], writes=[bb_[bbk]])
                bcs, bsn = buf(f"ccsl{sl}"), buf(f"sssl{sl}")
                bk = bb_[bbk]
                p.op("dve", lambda e, bbk=bbk, sl=sl: e.tensor_tensor(
                    rt1.ap(), bank[bbk][:, 0:512], ccsl[sl].ap(), OP.mult),
                    reads=[bk, bcs], writes=[buf("rt1")])
                p.op("dve", lambda e, bbk=bbk, sl=sl: e.tensor_tensor(
                    rt2.ap(), bank[bbk][:, 0:512], sssl[sl].ap(), OP.mult),
                    reads=[bk, bsn], writes=[buf("rt2")])
                p.op("dve", lambda e, dst=dst, h=h: e.tensor_sub(
                    dst[:, (2 * h) * 256:(2 * h) * 256 + 256],
                    rt1[:, 0:256], rt2[:, 256:512]),
                    reads=[buf("rt1"), buf("rt2")], writes=[bdst])
                p.op("dve", lambda e, dst=dst, h=h: e.tensor_sub(
                    dst[:, (2 * h + 1) * 256:(2 * h + 1) * 256 + 256],
                    rt1[:, 256:512], rt2[:, 0:256]),
                    reads=[buf("rt1"), buf("rt2")], writes=[bdst])
        for a in range(2):
            for dv in range(4):
                bbk = 6 + ((a * 4 + dv) % 2)
                prs = [(p1xf[gp][:, ci * 256 + a * 128: ci * 256 + a * 128 + 128],
                        wv_sb[:, ci * 2048 + dv * 512: ci * 2048 + (dv + 1) * 512])
                       for ci in range(8)]
                mmgroup(bank[bbk][:, 0:512], prs, [bxf, buf("wv")], bb_[bbk])
                p.op("act", lambda e, gp=gp, a=a, dv=dv, bbk=bbk: e.copy(
                    vsb[gp][:, a * 2048 + dv * 512: a * 2048 + (dv + 1) * 512],
                    bank[bbk][:, 0:512]),
                    reads=[bb_[bbk]], writes=[buf(f"vsb{gp}")])
    def emit_ret(g2):
        bat, gg = g2 % 2, g2 // 2
        gp = bat
        ta = bat * NTB + 2 * gg
        tb = ta + 1
        for a in range(2):
            for h in range(NH):
                hb = a * NH + h
                for half in range(2):
                    p.op("dve", lambda e, gp=gp, a=a, h=h, half=half, hb=hb:
                         e.tensor_tensor(
                        qqb[hb][:, half * 128:(half + 1) * 128],
                        qfm[gp][:, (2 * h + half) * 256 + a * 128:
                                (2 * h + half) * 256 + a * 128 + 128],
                        CQt[:, h * 256 + half * 128:
                            h * 256 + (half + 1) * 128], OP.mult),
                        reads=[buf(f"qfm{gp}"), buf("CQt")],
                        writes=[buf(f"qq{hb}")])
        for a, ti in ((0, ta), (1, tb)):
            c = ti % NTB
            for h in range(NH):
                hb = a * NH + h
                b_sc = 0 if h % 2 == 0 else 2
                b_o = 1 if h % 2 == 0 else 3
                battn, bkck = buf(f"att{hb}"), buf(f"kck{hb}")
                bqq = buf(f"qq{hb}")
                bS, bSb = buf(f"S{bat}_{h}"), buf(f"Sb{bat}_{h}")
                prs = [(kfm[gp][:, (2 * h + half) * 256 + a * 128:
                               (2 * h + half) * 256 + a * 128 + 128],
                        qfm[gp][:, (2 * h + half) * 256 + a * 128:
                               (2 * h + half) * 256 + a * 128 + 128])
                       for half in range(2)]
                mmgroup(bank[b_sc][:, 0:128], prs,
                        [buf(f"kfm{gp}"), buf(f"qfm{gp}")], bb_[b_sc])
                p.op("dve", lambda e, b_sc=b_sc, h=h, hb=hb: e.tensor_tensor(
                    att[hb].ap(), bank[b_sc][:, 0:128],
                    DTt[:, h * 128:(h + 1) * 128], OP.mult),
                    reads=[bb_[b_sc], buf("DTt")], writes=[battn])

                def emit_o(e, hb=hb, gp=gp, a=a, h=h, b_o=b_o, c=c, bat=bat):
                    ins = e.matmul(bank[b_o][:, 0:512], att[hb].ap(),
                                   vsb[gp][:, a * 2048 + h * 512:
                                           a * 2048 + (h + 1) * 512],
                                   start=True, stop=(c == 0))
                    if c > 0:
                        for half in range(2):
                            ins = e.matmul(
                                bank[b_o][:, 0:512],
                                qqb[hb][:, half * 128:(half + 1) * 128],
                                Sbf[bat][h][:, half * DV:(half + 1) * DV],
                                start=False, stop=(half == 1))
                    return ins
                rds = [battn, buf(f"vsb{gp}")] + ([bqq, bSb] if c > 0 else [])
                p.op("pe", emit_o, reads=rds, writes=[bb_[b_o]])
                p.op("act", lambda e, a=a, h=h, b_o=b_o: e.activation(
                    osq.ap(), bank[b_o][:, 0:512], AF.Square,
                    accum_out=ovar8[:, a * 4 + h:a * 4 + h + 1]),
                    reads=[bb_[b_o]], writes=[buf("osq"), buf("ovar8")])
                p.op("act", lambda e, a=a, h=h, b_o=b_o: e.copy(
                    oscraw[a][:, h * 512:(h + 1) * 512], bank[b_o][:, 0:512]),
                    reads=[bb_[b_o]], writes=[buf(f"oscr{a}")])
                if c < NTB - 1:
                    for half in range(2):
                        p.op("pe", lambda e, gp=gp, h=h, a=a, half=half,
                             b_sc=b_sc: e.transpose(
                            bankB[b_sc][:, 512 + half * 128:
                                        512 + (half + 1) * 128],
                            kfm[gp][:, (2 * h + half) * 256 + a * 128:
                                    (2 * h + half) * 256 + a * 128 + 128],
                            iden.ap()),
                            reads=[buf(f"kfm{gp}"), buf("iden")],
                            writes=[bb_[b_sc]])
                    p.op("act", lambda e, hb=hb, h=h, b_sc=b_sc: e.mul(
                        kckb[hb].ap(), bankB[b_sc][:, 512:768], CKt[:, h:h + 1]),
                        reads=[bb_[b_sc], buf("CKt")], writes=[bkck])
                    gC = GAMMAS[h] ** CH
                    for half in range(2):
                        b_u = b_sc if half == 0 else b_o
                        p.op("pe", lambda e, hb=hb, gp=gp, a=a, h=h, half=half,
                             b_u=b_u: e.matmul(
                            bank[b_u][:, 0:512],
                            kckb[hb][:, half * 128:(half + 1) * 128],
                            vsb[gp][:, a * 2048 + h * 512:
                                    a * 2048 + (h + 1) * 512],
                            start=True, stop=True),
                            reads=[bkck, buf(f"vsb{gp}")], writes=[bb_[b_u]])
                    for half in range(2):
                        b_u = b_sc if half == 0 else b_o
                        if c == 0:
                            p.op("dve", lambda e, h=h, half=half, b_u=b_u,
                                 bat=bat: e.tensor_copy(
                                Ssb[bat][h][:, half * DV:(half + 1) * DV],
                                bank[b_u][:, 0:512]),
                                reads=[bb_[b_u]], writes=[bS])
                        else:
                            p.op("dve", lambda e, h=h, half=half, b_u=b_u,
                                 gC=gC, bat=bat: e.scalar_tensor_tensor(
                                Ssb[bat][h][:, half * DV:(half + 1) * DV],
                                Ssb[bat][h][:, half * DV:(half + 1) * DV], gC,
                                bank[b_u][:, 0:512], OP.mult, OP.add),
                                reads=[bb_[b_u], bS], writes=[bS])
                    p.op("act", lambda e, h=h, bat=bat: e.copy(
                        Sbf[bat][h].ap(), Ssb[bat][h].ap()),
                        reads=[bS], writes=[bSb])
        p.op("dve", lambda e: e.tensor_scalar(
            vv8p.ap(), ovar8.ap(), 1.0 / DV, EPS, OP.mult, OP.add),
            reads=[buf("ovar8")], writes=[buf("vv8p")])
        class _H8:
            pass
        s0 = _H8(); s0.ap = n8[0].ap; s0.handle_name = "n8_0"
        s1 = _H8(); s1.ap = n8[1].ap; s1.handle_name = "n8_1"
        si = _H8(); si.ap = n8i.ap; si.handle_name = "n8i"
        newton8(vv8p.ap(), rstd8.ap(), buf("vv8p"), buf("rstd8"), s0, s1, si)
        for a, ti in ((0, ta), (1, tb)):
            for h in range(NH):
                col = a * 4 + h
                p.op("dve", lambda e, a=a, h=h, col=col: e.tensor_scalar(
                    oscraw[a][:, h * 512:(h + 1) * 512],
                    oscraw[a][:, h * 512:(h + 1) * 512],
                    rstd8[:, col:col + 1], None, OP.mult),
                    reads=[buf(f"oscr{a}"), buf("rstd8")],
                    writes=[buf(f"oscr{a}")])
            dma("sp", "dsp", osc_t[ti], oscraw[a].ap(),
                rbufs=[buf(f"oscr{a}")], wbufs=[buf("osc_dr")])

    for g2 in range(NT // 2):
        emit_qkv(g2)
        if g2 >= 1:
            emit_ret(g2 - 1)
    emit_ret(NT // 2 - 1)
    c1.close()

    p.phase_join()

    c2 = ExitStack()
    S2 = lambda n, sh, d: c2.enter_context(nc.sbuf_tensor(n, sh, d))
    w1_sb = S2("w1_sb", [128, 8 * HID], BF16)
    w2_sb = S2("w2_sb", [128, 32 * D], BF16)
    b1_sb = S2("b1_sb", [128, 32], F32)
    b2bc = S2("b2bc", [128, D], F32)
    p2x = [S2(f"p2x{i}", [128, D], BF16) for i in range(2)]
    p2xf = S2("p2xf", [128, 2048], BF16)
    hfm = S2("hfm", [128, 32 * 256], BF16)
    ycsb = [S2(f"ycsb{i}", [128, D], BF16) for i in range(2)]

    dma("pool", "dpool", w1_sb.ap(),
        rowchunks(w1c_in, HID, 8), wbufs=[buf("w1")])
    dma("pool", "dpool", w2_sb.ap(),
        rowchunks(w2c_in, D, 32), wbufs=[buf("w2")])
    dma("sp", "dsp", b1_sb.ap(), b1c_in[:, :], wbufs=[buf("b1")])
    dma("sp", "dsp", b2bc.ap(), bass.AP(b2c_in, 0, [[0, 128], [1, D]]),
        wbufs=[buf("b2bc")])

    for gi in range(5):
        bxf = buf("p2xf")
        for a in range(2):
            dma("sp", "dsp", p2x[a].ap(), bufc_t[2 * gi + a],
                rbufs=[buf("bufc")], wbufs=[buf(f"p2x{a}")])
            bbk = 4 + a
            for ci in range(8):
                p.op("pe", lambda e, a=a, ci=ci, bbk=bbk: e.transpose(
                    bankB[bbk][:, ci * 128:(ci + 1) * 128],
                    p2x[a][:, ci * 128:(ci + 1) * 128], iden.ap()),
                    reads=[buf(f"p2x{a}"), buf("iden")], writes=[bb_[bbk]])
            p.op("dve", lambda e, a=a, bbk=bbk: e.tensor_copy(
                bass.AP(p2xf, a * 128, [[2048, 128], [256, 8], [1, 128]]),
                bankB[bbk][:, 0:1024]),
                reads=[bb_[bbk]], writes=[bxf])
        hbufs = []
        for j in range(32):
            bbk = j % 4
            prs = [(w1_sb[:, ci * HID + j * 128: ci * HID + (j + 1) * 128],
                    p2xf[:, ci * 256:(ci + 1) * 256]) for ci in range(8)]
            mmgroup(bank[bbk][:, 0:256], prs, [bxf, buf("w1")], bb_[bbk])
            hb = buf(f"hf2_{j}")
            hbufs.append(hb)
            p.op("act", lambda e, j=j, bbk=bbk: e.activation(
                hfm[:, j * 256:(j + 1) * 256], bank[bbk][:, 0:256], AF.Relu,
                bias=b1_sb[:, j:j + 1]),
                reads=[bb_[bbk], buf("b1")], writes=[hb])
        for ts in range(2):
            for dh in range(2):
                bbk = 4 + ts * 2 + dh

                def emity(e, ts=ts, dh=dh, bbk=bbk):
                    for j in range(32):
                        ins = e.matmul(
                            bank[bbk][:, 0:512],
                            hfm[:, j * 256 + ts * 128: j * 256 + ts * 128 + 128],
                            w2_sb[:, j * D + dh * 512: j * D + (dh + 1) * 512],
                            start=(j == 0), stop=(j == 31))
                    return ins
                p.op("pe", emity, reads=hbufs + [buf("w2")], writes=[bb_[bbk]])
            for dh in range(2):
                bbk = 4 + ts * 2 + dh
                p.op("dve", lambda e, ts=ts, dh=dh, bbk=bbk: e.tensor_tensor(
                    ycsb[ts][:, dh * 512:(dh + 1) * 512], bank[bbk][:, 0:512],
                    b2bc[:, dh * 512:(dh + 1) * 512], OP.add),
                    reads=[bb_[bbk], buf("b2bc")], writes=[buf(f"ycsb{ts}")])
        for a in range(2):
            dma("sp", "dsp", yc_t[2 * gi + a], ycsb[a].ap(),
                rbufs=[buf(f"ycsb{a}")], wbufs=[buf("yc_dr")])
    p.op("pool", lambda e: e.collective_compute(
        "AllGather", OP.bypass, replica_groups=[list(range(NCORES))],
        ins=[yc_dr.ap().opt()], outs=[yfull_dr.ap().opt()]),
        reads=[buf("yc_dr")], writes=[buf("yfull")], cc="cc2")
    c2.close()

    p.phase_join()

    c3 = ExitStack()
    S3 = lambda n, sh, d: c3.enter_context(nc.sbuf_tensor(n, sh, d))
    wo_sb = S3("wo_sb", [128, 16 * D], BF16)
    wos = S3("wos", [128, D], F32)
    gnw_sb = S3("gnw_sb", [128, 16], F32)
    a3g = [S3(f"a3g{i}", [128, 2048], BF16) for i in range(2)]
    a3osc = [S3(f"a3osc{i}", [128, 2048], BF16) for i in range(2)]
    a3og = [S3(f"a3og{i}", [128, 2048], BF16) for i in range(2)]
    a3ogf = [S3(f"a3ogf{i}", [128, 2048], BF16) for i in range(2)]
    a3s = [S3(f"a3s{i}", [128, D], BF16) for i in range(2)]

    dma("sp", "dsp", gnw_sb.ap(), gnw16_in[:, :], wbufs=[buf("gnw")])
    wo_r = wo_in.ap().rearrange("(c k) n -> c k n", k=128)
    for ci in range(16):
        dma("sp", "dsp", wos.ap(), wo_r[ci], rbufs=[buf("wo_pre")],
            wbufs=[buf("wo_pre")])
        p.op("act", lambda e, ci=ci: e.mul(
            wo_sb[:, ci * D:(ci + 1) * D], wos.ap(), gnw_sb[:, ci:ci + 1]),
            reads=[buf("wo_pre"), buf("gnw")], writes=[buf("wo")])

    for g in range(NT // 2):
        ta, tb = 2 * g, 2 * g + 1
        for (ti, a) in ((ta, 0), (tb, 1)):
            dma("sp", "dsp", a3g[a].ap(), g_t[ti], rbufs=[buf("g_dr")],
                wbufs=[buf(f"a3g{a}")])
            dma("sp", "dsp", a3osc[a].ap(), osc_t[ti], rbufs=[buf("osc_dr")],
                wbufs=[buf(f"a3osc{a}")])
        for a, ti in ((0, ta), (1, tb)):
            p.op("dve", lambda e, a=a: e.tensor_mul(
                a3og[a].ap(), a3osc[a].ap(), a3g[a].ap()),
                reads=[buf(f"a3osc{a}"), buf(f"a3g{a}")],
                writes=[buf(f"a3og{a}")])
            for half in range(2):
                bbk = 4 + ((a * 2 + half) % 2)
                for ci in range(8):
                    cc = half * 8 + ci
                    p.op("pe", lambda e, a=a, cc=cc, ci=ci, bbk=bbk: e.transpose(
                        bankB[bbk][:, ci * 128:(ci + 1) * 128],
                        a3og[a][:, cc * 128:(cc + 1) * 128], iden.ap()),
                        reads=[buf(f"a3og{a}"), buf("iden")], writes=[bb_[bbk]])
                p.op("dve", lambda e, a=a, half=half, bbk=bbk: e.tensor_copy(
                    a3ogf[a][:, half * 1024:(half + 1) * 1024],
                    bankB[bbk][:, 0:1024]),
                    reads=[bb_[bbk]], writes=[buf(f"a3ogf{a}")])
            for dh in range(2):
                bbk = 6 + dh

                def emits(e, a=a, dh=dh, bbk=bbk):
                    for ci in range(16):
                        ins = e.matmul(
                            bank[bbk][:, 0:512],
                            a3ogf[a][:, ci * 128:(ci + 1) * 128],
                            wo_sb[:, ci * D + dh * 512: ci * D + (dh + 1) * 512],
                            start=(ci == 0), stop=(ci == 15))
                    return ins
                p.op("pe", emits, reads=[buf(f"a3ogf{a}"), buf("wo")],
                     writes=[bb_[bbk]])
                p.op("dve", lambda e, a=a, dh=dh, bbk=bbk: e.tensor_copy(
                    a3s[a][:, dh * 512:(dh + 1) * 512], bank[bbk][:, 0:512]),
                    reads=[bb_[bbk]], writes=[buf(f"a3s{a}")])
            dma("sp", "dsp", s_t[ti], a3s[a].ap(), rbufs=[buf(f"a3s{a}")],
                wbufs=[buf("s_dr")])
    c3.close()

    p.phase_join()

    c4 = ExitStack()
    S4 = lambda n, sh, d: c4.enter_context(nc.sbuf_tensor(n, sh, d))
    o1_sb = S4("o1_sb", [128, 8 * HID], BF16)
    o2_sb = S4("o2_sb", [128, 32 * D], BF16)
    ob1_sb = S4("ob1_sb", [128, 32], F32)
    ob2bc = S4("ob2bc", [128, D], F32)
    b4x = [S4(f"b4x{i}", [128, D], F32) for i in range(2)]
    b4xf = [S4(f"b4xf{i}", [128, 2048], BF16) for i in range(2)]
    hfm4 = S4("hfm4", [128, 32 * 256], BF16)
    yov = [S4(f"yov{i}", [128, D], F32) for i in range(2)]
    b4s = [S4(f"b4s{i}", [128, D], BF16) for i in range(2)]
    b4f = S4("b4f", [128, D], F32)
    b4g = S4("b4g", [128, D], F32)
    b4o = [S4(f"b4o{i}", [128, D], F32) for i in range(2)]

    dma("pool", "dpool", o1_sb.ap(),
        rowchunks(ow1_in, HID, 8), wbufs=[buf("o1")])
    dma("pool", "dpool", o2_sb.ap(),
        rowchunks(ow2_in, D, 32), wbufs=[buf("o2")])
    dma("sp", "dsp", ob1_sb.ap(), ob1_in[:, :], wbufs=[buf("ob1")])
    dma("sp", "dsp", ob2bc.ap(), bass.AP(ob2_in, 0, [[0, 128], [1, D]]),
        wbufs=[buf("ob2bc")])

    for g in range(NT // 2):
        ta, tb = 2 * g, 2 * g + 1
        gp4 = g % 2
        bxf = buf(f"b4xf{gp4}")
        for (ti, a) in ((ta, 0), (tb, 1)):
            bx = buf(f"b4x{a}")
            dma("sp", "dsp", b4x[a].ap(), x_t[ti], wbufs=[bx])
            p.op("sp", lambda e, gp4=gp4, a=a, ti=ti: e.dma_start(
                bass.AP(b4xf[gp4], a * 128, [[2048, 128], [256, 8], [1, 128]]),
                bass.AP(xnf_dr, ti * 128 * D, [[D, 128], [128, 8], [1, 128]])),
                reads=[buf("xnf_dr")], writes=[bxf], dma=f"q_b4xf{gp4}")
            dma("sp", "dsp", b4s[a].ap(), s_t[ti], rbufs=[buf("s_dr")],
                wbufs=[buf(f"b4s{a}")])
            p.op("pool", lambda e, a=a: e.memset(ytok[a].ap(), 0),
                 reads=[], writes=[buf(f"ytok{a}")])
            p.op("pool", lambda e, a=a, ti=ti: e.indirect_dma_start(
                out=ytok[a].ap(), out_offset=None,
                in_=yfull_dr.ap(),
                in_offset=bass.IndirectOffsetOnAxis(idx_a[:, ti:ti + 1], 0),
                bounds_check=bc_reg, oob_is_err=False),
                reads=[buf("yfull"), idxb, buf(f"ytok{a}")],
                writes=[buf(f"ytok{a}")], dma=f"q_ytok{a}")
        hbufs = []
        for j in range(32):
            bbk = j % 4
            prs = [(o1_sb[:, ci * HID + j * 128: ci * HID + (j + 1) * 128],
                    b4xf[gp4][:, ci * 256:(ci + 1) * 256]) for ci in range(8)]
            mmgroup(bank[bbk][:, 0:256], prs, [bxf, buf("o1")], bb_[bbk])
            hb = buf(f"h4_{j}")
            hbufs.append(hb)
            p.op("act", lambda e, j=j, bbk=bbk: e.activation(
                hfm4[:, j * 256:(j + 1) * 256], bank[bbk][:, 0:256], AF.Relu,
                bias=ob1_sb[:, j:j + 1]),
                reads=[bb_[bbk], buf("ob1")], writes=[hb])
        for ts in range(2):
            for dh in range(2):
                bbk = 4 + ts * 2 + dh

                def emity(e, ts=ts, dh=dh, bbk=bbk):
                    for j in range(32):
                        ins = e.matmul(
                            bank[bbk][:, 0:512],
                            hfm4[:, j * 256 + ts * 128: j * 256 + ts * 128 + 128],
                            o2_sb[:, j * D + dh * 512: j * D + (dh + 1) * 512],
                            start=(j == 0), stop=(j == 31))
                    return ins
                p.op("pe", emity, reads=hbufs + [buf("o2")], writes=[bb_[bbk]])
            for dh in range(2):
                bbk = 4 + ts * 2 + dh
                p.op("dve", lambda e, ts=ts, dh=dh, bbk=bbk: e.tensor_tensor(
                    yov[ts][:, dh * 512:(dh + 1) * 512], bank[bbk][:, 0:512],
                    ob2bc[:, dh * 512:(dh + 1) * 512], OP.add),
                    reads=[bb_[bbk], buf("ob2bc")], writes=[buf(f"yov{ts}")])
        for a, ti in ((0, ta), (1, tb)):
            p.op("dve", lambda e, a=a, ti=ti: e.tensor_scalar(
                b4f.ap(), yov[a].ap(), inv_a[:, ti:ti + 1], None, OP.mult),
                reads=[buf(f"yov{a}"), invb], writes=[buf("b4f")])
            p.op("dve", lambda e, a=a: e.tensor_add(
                b4f.ap(), b4f.ap(), ytok[a].ap()),
                reads=[buf("b4f"), buf(f"ytok{a}")], writes=[buf("b4f")])
            p.op("act", lambda e: e.activation(b4g.ap(), b4f.ap(), AF.Silu),
                 reads=[buf("b4f")], writes=[buf("b4g")])
            p.op("dve", lambda e, a=a: e.tensor_mul(
                b4g.ap(), b4g.ap(), b4s[a].ap()),
                reads=[buf("b4g"), buf(f"b4s{a}")], writes=[buf("b4g")])
            p.op("dve", lambda e, a=a: e.tensor_add(
                b4o[a].ap(), b4g.ap(), b4x[a].ap()),
                reads=[buf("b4g"), buf(f"b4x{a}")], writes=[buf(f"b4o{a}")])
            dma("sp", "dsp", out_t[ti], b4o[a].ap(), rbufs=[buf(f"b4o{a}")],
                wbufs=[buf("outw")])
    c4.close()

    if debug:
        dma("sp", "dsp", dbg["xn"].ap(), xn_dr.ap(), rbufs=[buf("xn_dr")])
        dma("sp", "dsp", dbg["eid"].ap(), eid_a.ap(), rbufs=[eidb])
        dma("sp", "dsp", dbg["lrank"].ap(), lrank_a.ap(), rbufs=[lrankb])
        dma("sp", "dsp", dbg["inv"].ap(), inv_a.ap(), rbufs=[invb])
        dma("sp", "dsp", dbg["idx"].ap(), idx_a.ap(), rbufs=[idxb])
        dma("sp", "dsp", dbg["osc"].ap(), osc_dr.ap(), rbufs=[buf("osc_dr")])
        dma("sp", "dsp", dbg["s"].ap(), s_dr.ap(), rbufs=[buf("s_dr")])
        dma("sp", "dsp", dbg["yfull"].ap(), yfull_dr.ap(), rbufs=[buf("yfull")])
        dma("sp", "dsp", dbg["bufc"].ap(), bufc_dr.ap(), rbufs=[buf("bufc")])
        dma("act", "dact", dbg["coff"].ap(), coff_dr.ap(), rbufs=[buf("coff_dr")])
        dma("sp", "dsp", dbg["rstd"].ap(), rstd_a.ap(), rbufs=[statsb])
        dma("sp", "dsp", dbg["nmr"].ap(), nmr_a.ap(), rbufs=[statsb])

    p.wait_all("sp", [s for s in p.cnt if s.startswith("q_")
                      or s.startswith("cc") or s in ("dact", "dpool", "dsp")])

    p.emit(sems)
    stack.close()
    return nc


_PROG_CACHE = {}


def _get_program(debug=False):
    if debug not in _PROG_CACHE:
        _PROG_CACHE[debug] = build_program(debug=debug)
    return _PROG_CACHE[debug]


def make_in_maps(inputs):
    t = _tables()
    x = np.asarray(inputs["L0_attn_x"], np.float32).reshape(16 * 4096, D)
    w1 = np.asarray(inputs["L0_ffn_w1"], np.float32)
    b1 = np.asarray(inputs["L0_ffn_b1"], np.float32)
    w2 = np.asarray(inputs["L0_ffn_w2"], np.float32)
    b2 = np.asarray(inputs["L0_ffn_b2"], np.float32)
    gn = np.asarray(inputs["L0_attn_gn_weight"], np.float32)
    gnw16 = np.ascontiguousarray(
        np.tile(gn, NH).reshape(16, 128).T)
    shared = dict(
        wq=np.ascontiguousarray(inputs["L0_attn_Wq"], np.float32),
        wk=np.ascontiguousarray(inputs["L0_attn_Wk"], np.float32),
        wv=np.ascontiguousarray(inputs["L0_attn_Wv"], np.float32),
        wg=np.ascontiguousarray(inputs["L0_attn_Wg"], np.float32),
        wo=np.ascontiguousarray(inputs["L0_attn_Wo"], np.float32),
        wr=np.ascontiguousarray(inputs["L0_ffn_router_w"], np.float32),
        rb=np.ascontiguousarray(inputs["L0_ffn_router_b"],
                                np.float32).reshape(1, E),
        gnw16=gnw16,
        ow1=np.ascontiguousarray(inputs["L0_ffn_ow1"], np.float32),
        ow2=np.ascontiguousarray(inputs["L0_ffn_ow2"], np.float32),
        ob1=np.ascontiguousarray(
            np.asarray(inputs["L0_ffn_ob1"], np.float32).reshape(32, 128).T),
        ob2=np.ascontiguousarray(inputs["L0_ffn_ob2"],
                                 np.float32).reshape(1, D),
        cc2T=t["cc2T"], ss2T=t["ss2T"], DT=t["DT"], CQ=t["CQ"], CK=t["CK"],
        utri=t["utri"], iden=t["iden"], onescol=t["onescol"],
    )
    shared["io8_in"] = np.broadcast_to(np.arange(E, dtype=np.float32),
                                       (128, E)).copy()
    shared["io8m8_in"] = np.broadcast_to(np.arange(-8, 0, dtype=np.float32),
                                         (128, E)).copy()
    shared["iop_in"] = np.arange(128, dtype=np.float32).reshape(128, 1).copy()
    in_maps = []
    for c in range(NCORES):
        m = dict(shared)
        m["x"] = np.ascontiguousarray(x[c * TPB:(c + 1) * TPB])
        m["w1c"] = np.ascontiguousarray(w1[c])
        m["w2c"] = np.ascontiguousarray(w2[c])
        m["b1c"] = np.ascontiguousarray(b1[c].reshape(32, 128).T)
        m["b2c"] = np.ascontiguousarray(b2[c].reshape(1, D))
        cm = np.zeros((E, 1), np.float32)
        cm[:c] = 1.0
        m["cmask"] = cm
        in_maps.append(m)
    return in_maps


def kernel(**inputs):
    nc = _get_program(debug=False)
    in_maps = make_in_maps(inputs)
    res = run_bass_kernel_spmd(nc, in_maps, core_ids=list(range(NCORES)))
    out = np.concatenate([np.asarray(r["out"]) for r in res.results], axis=0)
    return out.reshape(16, 4096, D)


# version 13
# speedup vs baseline: 1.0407x; 1.0332x over previous
import numpy as np
import ml_dtypes

import concourse.bass as bass
import concourse.mybir as mybir
import concourse.bass_utils as _bu
from concourse.bass_utils import run_bass_kernel_spmd

F32 = mybir.dt.float32
BF16 = mybir.dt.bfloat16
I32 = mybir.dt.int32
U32 = mybir.dt.uint32
AF = mybir.ActivationFunctionType
OP = mybir.AluOpType
AX = mybir.AxisListType.X

NCORES = 8
D = 1024
HID = 4096
NH = 4
DK = 256
DV = 512
CH = 128
TPB = 8192
NT = TPB // CH
NTB = 32
E = 8
CAP = 1280
EPS = 1e-5
QSCALE = 0.0625
MAGIC = 0x5F3759DF
OOBBASE = 20000.0

GAMMAS = [1.0 - 2.0 ** (-5.0 - h) for h in range(NH)]

def _bvo_nobirsim(tmpdir, inp="bir.json", outp="file.neff", arch=None, *,
                  dve_root=None):
    from pathlib import Path
    from concourse.bass_utils import (get_walrus_driver, get_walrus_args,
                                      get_bir_arch, run_command)
    cmd = [
        get_walrus_driver(),
        "--pass", ",".join(["birverifier", "runtime_memory_reservation",
                            "lower_act", "lower_dve", "lower_ap_offset",
                            "codegen", "neff_packager"]),
        "-i", inp,
        "--neff-output-filename", outp,
        "--enable-birsim=false",
        "--mem-mode=physical", "--policy=0", "--enable-ldw-opt=false",
        "--assign-static-dmas-to-sp=false",
        "--dram-page-size=256", "--enable-neff-debug-info=true",
        "--jobs", "8",
        *get_walrus_args(get_bir_arch(tmpdir, inp) if arch is None else arch,
                         tmpdir, dve_root=dve_root),
    ]
    result = run_command(cmd, cwd=tmpdir)
    if result is not None:
        (Path(tmpdir) / "log.txt").write_text(result.stdout)
    return f"{tmpdir}/{outp}"

_bu.bir_verify_and_optimise = _bvo_nobirsim

ENGINES = ("pe", "dve", "act", "pool", "sp")
ENG_SEM = {e: "s_" + e for e in ENGINES}
BASE_SEMS = [ENG_SEM[e] for e in ENGINES] + ["cc0", "cc1", "cc2"]


class Buf:
    __slots__ = ("name", "w", "rs")

    def __init__(self, name=""):
        self.name, self.w, self.rs = name, None, []


class LM:
    __slots__ = ("eng", "sem", "val")

    def __init__(self, eng, sem, val):
        self.eng, self.sem, self.val = eng, sem, val


class Prog:
    def __init__(self, nc):
        self.nc = nc
        self.ops = {e: [] for e in ENGINES}
        self.cnt = {s: 0 for s in BASE_SEMS}

    def _sem(self, s):
        if s not in self.cnt:
            self.cnt[s] = 0
        return s

    def raw(self, eng, fn):
        self.ops[eng].append((fn, [], None, 0))

    def op(self, eng, fn, reads=(), writes=(), dma=None, cc=None, inc_n=1):
        deps = []
        own = ENG_SEM[eng]
        for b in reads:
            if b.w is not None:
                deps.append(b.w)
        for b in writes:
            deps.extend(b.rs)
            if b.w is not None:
                deps.append(b.w)
        if cc is not None:
            sem, amt = self._sem(cc), 1
        elif dma is not None:
            sem, amt = self._sem(dma), 16 * inc_n
        else:
            sem, amt = own, 1
        waits = {}
        for d in deps:
            if d.val > waits.get(d.sem, 0):
                waits[d.sem] = d.val
        self.cnt[sem] += amt
        lm = LM(eng, sem, self.cnt[sem])
        self.ops[eng].append((fn, sorted(waits.items()), sem, amt))
        for b in reads:
            b.rs.append(lm)
        for b in writes:
            b.w, b.rs = lm, []
        return lm

    def wait_all(self, eng, sems):
        marks = [(s, self.cnt[s]) for s in sems if self.cnt.get(s, 0) > 0]
        self.ops[eng].append((None, marks, None, 0))

    def phase_join(self, engines=ENGINES, dmas=None):
        if dmas is None:
            dmas = [s for s in self.cnt
                    if s.startswith("q_") or s in ("dact", "dpool", "dsp")]
        sl = [ENG_SEM[e] for e in engines] + list(dmas)
        for e in engines:
            self.wait_all(e, [s for s in sl if s != ENG_SEM[e]])

    def emit(self, sems):
        for s in self.cnt:
            if s not in sems:
                sems[s] = self._stack.enter_context(self.nc.semaphore(s))
        engmap = {"pe": "tensor", "dve": "vector", "act": "scalar",
                  "pool": "gpsimd", "sp": "sync"}
        with self.nc.Block() as blk:
            for ename in ENGINES:
                oplist = self.ops[ename]
                if not oplist:
                    continue

                def make(oplist=oplist):
                    def run(eng):
                        lw = {}
                        for (fn, waits, sem, amt) in oplist:
                            for (sname, val) in waits:
                                if lw.get(sname, 0) >= val:
                                    continue
                                eng.wait_ge(sems[sname], val)
                                lw[sname] = val
                            if fn is None:
                                continue
                            ins = fn(eng)
                            if sem is not None:
                                ins.then_inc(sems[sem], amt)
                    return run

                getattr(blk, engmap[ename])(make())


def _tables():
    t = {}
    inv = 1.0 / (10000.0 ** (np.arange(0, DK, 2, dtype=np.float64) / DK))
    fr = np.outer(np.arange(4096, dtype=np.float64), inv)
    cosT = np.cos(fr).T
    sinT = np.sin(fr).T
    NW = 4096 // 256
    cc2 = np.zeros((128, NW * 512))
    ss2 = np.zeros((128, NW * 512))
    for w in range(NW):
        cw = cosT[:, w * 256:(w + 1) * 256]
        sw = sinT[:, w * 256:(w + 1) * 256]
        cc2[:, w * 512:w * 512 + 256] = cw
        cc2[:, w * 512 + 256:(w + 1) * 512] = cw
        ss2[:, w * 512:w * 512 + 256] = -sw
        ss2[:, w * 512 + 256:(w + 1) * 512] = sw
    t["cc2T"] = cc2.astype(ml_dtypes.bfloat16)
    t["ss2T"] = ss2.astype(ml_dtypes.bfloat16)
    j = np.arange(CH)
    DT = np.zeros((CH, NH * CH), np.float64)
    CQ = np.zeros((CH, NH * 2 * CH), np.float64)
    CK = np.zeros((CH, NH), np.float64)
    for h, g in enumerate(GAMMAS):
        rel = j[None, :] - j[:, None]
        DT[:, h * CH:(h + 1) * CH] = np.where(rel >= 0, g ** np.maximum(rel, 0), 0.0) * QSCALE
        cqrow = np.broadcast_to((g ** (j + 1.0)) * QSCALE, (CH, CH))
        CQ[:, h * 2 * CH:h * 2 * CH + CH] = cqrow
        CQ[:, h * 2 * CH + CH:(h + 1) * 2 * CH] = cqrow
        CK[:, h] = g ** (CH - 1.0 - j)
    t["DT"] = DT.astype(ml_dtypes.bfloat16)
    t["CQ"] = CQ.astype(ml_dtypes.bfloat16)
    t["CK"] = CK.astype(np.float32)
    t["utri"] = np.triu(np.ones((128, 128))).astype(ml_dtypes.bfloat16)
    t["iden"] = np.eye(128).astype(ml_dtypes.bfloat16)
    t["onescol"] = np.ones((128, 1), ml_dtypes.bfloat16)
    return t


def build_program(debug=False):
    from contextlib import ExitStack
    nc = bass.Bass("TRN2", target_bir_lowering=False)

    def din(name, shape, dt=F32):
        return nc.dram_tensor(name, shape, dt, kind="ExternalInput")

    x_in = din("x", [TPB, D])
    wq_in = din("wq", [D, D]); wk_in = din("wk", [D, D])
    wv_in = din("wv", [D, 2048]); wg_in = din("wg", [D, 2048])
    wo_in = din("wo", [2048, D])
    wr_in = din("wr", [D, E]); rb_in = din("rb", [1, E])
    gnw16_in = din("gnw16", [128, 16])
    w1c_in = din("w1c", [D, HID]); w2c_in = din("w2c", [HID, D])
    b1c_in = din("b1c", [128, 32]); b2c_in = din("b2c", [1, D])
    ow1_in = din("ow1", [D, HID]); ow2_in = din("ow2", [HID, D])
    ob1_in = din("ob1", [128, 32]); ob2_in = din("ob2", [1, D], BF16)
    cc2_in = din("cc2T", [128, 16 * 512], BF16)
    ss2_in = din("ss2T", [128, 16 * 512], BF16)
    DT_in = din("DT", [128, NH * CH], BF16)
    CQ_in = din("CQ", [128, NH * 2 * CH], BF16)
    CK_in = din("CK", [128, NH])
    utri_in = din("utri", [128, 128], BF16)
    iden_in = din("iden", [128, 128], BF16)
    ones_in = din("onescol", [128, 1], BF16)
    cmask_in = din("cmask", [E, 1])
    io8_in = din("io8_in", [128, E])
    io8m8_in = din("io8m8_in", [128, E])
    iop_in = din("iop_in", [128, 1])

    out_dram = nc.dram_tensor("out", [TPB, D], F32, kind="ExternalOutput")
    dbg = {}
    if debug:
        dbg["xn"] = nc.dram_tensor("dbg_xn", [TPB, D], BF16, kind="ExternalOutput")
        dbg["eid"] = nc.dram_tensor("dbg_eid", [128, NT], F32, kind="ExternalOutput")
        dbg["lrank"] = nc.dram_tensor("dbg_lrank", [128, NT], F32, kind="ExternalOutput")
        dbg["inv"] = nc.dram_tensor("dbg_inv", [128, NT], F32, kind="ExternalOutput")
        dbg["idx"] = nc.dram_tensor("dbg_idx", [128, NT], U32, kind="ExternalOutput")
        dbg["osc"] = nc.dram_tensor("dbg_osc", [TPB, 2048], BF16, kind="ExternalOutput")
        dbg["s"] = nc.dram_tensor("dbg_s", [TPB, D], BF16, kind="ExternalOutput")
        dbg["yfull"] = nc.dram_tensor("dbg_yfull", [E * CAP, D], BF16, kind="ExternalOutput")
        dbg["bufc"] = nc.dram_tensor("dbg_bufc", [CAP, D], BF16, kind="ExternalOutput")
        dbg["coff"] = nc.dram_tensor("dbg_coff", [1, E], F32, kind="ExternalOutput")
        dbg["rstd"] = nc.dram_tensor("dbg_rstd", [128, NT], F32, kind="ExternalOutput")
        dbg["nmr"] = nc.dram_tensor("dbg_nmr", [128, NT], F32, kind="ExternalOutput")

    xn_dr = nc.dram_tensor("xn_dr", [TPB, D], BF16)
    xnf_dr = nc.dram_tensor("xnf_dr", [TPB, D], BF16)
    g_dr = nc.dram_tensor("g_dr", [TPB, 2048], BF16)
    osc_dr = nc.dram_tensor("osc_dr", [TPB, 2048], BF16)
    s_dr = nc.dram_tensor("s_dr", [TPB, D], BF16)
    bufg_dr = nc.dram_tensor("bufg_dr", [E * CAP, D], BF16)
    bufc_dr = nc.dram_tensor("bufc_dr", [CAP, D], BF16)
    yc_dr = nc.dram_tensor("yc_dr", [CAP, D], BF16)
    yfull_dr = nc.dram_tensor("yfull_dr", [E * CAP, D], BF16)
    ccc_in = nc.dram_tensor("ccc_in", [1, E], F32)
    ccc_out = nc.dram_tensor("ccc_out", [E, E], F32)
    offs_dr = nc.dram_tensor("offs_dr", [NT, E], F32)
    coff_dr = nc.dram_tensor("coff_dr", [1, E], F32)

    stack = ExitStack()
    sems = {s: stack.enter_context(nc.semaphore(s)) for s in BASE_SEMS}
    bc_reg = stack.enter_context(nc.gpsimd.register("bc_reg"))
    p = Prog(nc)
    p._stack = stack
    B = {}

    def buf(name):
        if name not in B:
            B[name] = Buf(name)
        return B[name]

    def dma(eng, q, dst, src, rbufs=(), wbufs=()):
        stream = "q_" + wbufs[0].name if wbufs else q
        return p.op(eng, lambda e, d=dst, s=src: e.dma_start(d, s),
                    reads=rbufs, writes=wbufs, dma=stream)

    def mmgroup(psum_ap, pairs, reads, wbank):
        def emit(e, psum_ap=psum_ap, pairs=pairs):
            n = len(pairs)
            for i, (l, r) in enumerate(pairs):
                ins = e.matmul(psum_ap, l, r, start=(i == 0), stop=(i == n - 1))
            return ins
        return p.op("pe", emit, reads=reads, writes=[wbank])

    def rowchunks(t_in, cols, nch):
        return bass.AP(t_in, 0, [[cols, 128], [128 * cols, nch], [1, cols]])

    AS = nc.alloc_sbuf_tensor
    rstd_a = AS("rstd_a", [128, NT], F32)
    nmr_a = AS("nmr_a", [128, NT], F32)
    eid_a = AS("eid_a", [128, NT], F32)
    lrank_a = AS("lrank_a", [128, NT], F32)
    inv_a = AS("inv_a", [128, NT], F32)
    idx_a = AS("idx_a", [128, NT], U32)
    io8 = AS("io8", [128, E], F32)
    io8m8 = AS("io8m8", [128, E], F32)
    iop = AS("iop", [128, 1], F32)
    utri = AS("utri_s", [128, 128], BF16)
    iden = AS("iden_s", [128, 128], BF16)
    onescol = AS("onescol_s", [128, 1], BF16)
    rb_bc = AS("rb_bc", [128, E], F32)
    coff_bc = AS("coff_bc", [128, E], F32)
    offs_bc = AS("offs_bc", [128, NT * E], F32)
    runtot = AS("runtot", [1, E], F32)
    offs_row = AS("offs_row", [1, NT * E], F32)
    counts_T = AS("counts_T", [E, E], F32)
    cmrow = AS("cmrow", [E, E], F32)
    coff_col = AS("coff_col", [E, 1], F32)
    wr_sb = AS("wr_sb", [128, 8 * E], BF16)
    rxn = [AS(f"rxn{i}", [128, D], BF16) for i in range(2)]
    r8 = AS("r8", [128, E], F32)
    r8b = AS("r8b", [128, E], F32)
    rc1 = AS("rc1", [128, 1], F32)
    rc2 = AS("rc2", [128, 1], F32)
    rc3 = AS("rc3", [128, 1], F32)
    ytok = [AS(f"ytok{i}", [128, D], BF16) for i in range(2)]

    bank = [stack.enter_context(nc.psum_tensor(f"bank{i}", [128, 512], F32))
            for i in range(8)]
    bankB = [b.ap().bitcast(BF16) for b in bank]
    bb_ = [buf(f"bank{i}") for i in range(8)]

    x_t = x_in.ap().rearrange("(n p) d -> n p d", p=128)
    xn_t = xn_dr.ap().rearrange("(n p) d -> n p d", p=128)
    xnf_t = xnf_dr.ap().rearrange("(n p) d -> n p d", p=128)
    g_t = g_dr.ap().rearrange("(n p) d -> n p d", p=128)
    osc_t = osc_dr.ap().rearrange("(n p) d -> n p d", p=128)
    s_t = s_dr.ap().rearrange("(n p) d -> n p d", p=128)
    out_t = out_dram.ap().rearrange("(n p) d -> n p d", p=128)
    bufc_t = bufc_dr.ap().rearrange("(n p) d -> n p d", p=128)
    yc_t = yc_dr.ap().rearrange("(n p) d -> n p d", p=128)

    statsb = buf("stats")
    eidb = buf("eid_a")
    lrankb = buf("lrank_a")
    invb = buf("inv_a")
    idxb = buf("idx_a")

    dma("sp", "dsp", utri.ap(), utri_in[:, :], wbufs=[buf("utri")])
    dma("sp", "dsp", iden.ap(), iden_in[:, :], wbufs=[buf("iden")])
    dma("sp", "dsp", onescol.ap(), ones_in[:, :], wbufs=[buf("onescol")])
    dma("sp", "dsp", rb_bc.ap(), bass.AP(rb_in, 0, [[0, 128], [1, E]]),
        wbufs=[buf("rb_bc")])
    dma("pool", "dpool", wr_sb.ap(),
        rowchunks(wr_in, E, 8), wbufs=[buf("wr_sb")])
    dma("sp", "dsp", io8.ap(), io8_in[:, :], wbufs=[buf("io8")])
    dma("sp", "dsp", io8m8.ap(), io8m8_in[:, :], wbufs=[buf("io8m8")])
    dma("sp", "dsp", iop.ap(), iop_in[:, :], wbufs=[buf("iop")])
    p.op("pool", lambda e: e.memset(runtot[:], 0), writes=[buf("runtot")])
    p.raw("pool", lambda e: e.reg_mov(bc_reg, E * CAP - 1))

    c1 = ExitStack()
    S1 = lambda n, sh, d: c1.enter_context(nc.sbuf_tensor(n, sh, d))
    wq_sb = S1("wq_sb", [128, 8 * D], BF16)
    wk_sb = S1("wk_sb", [128, 8 * D], BF16)
    wv_sb = S1("wv_sb", [128, 8 * 2048], BF16)
    DTt = S1("DTt", [128, NH * CH], BF16)
    CQt = S1("CQt", [128, NH * 2 * CH], BF16)
    CKt = S1("CKt", [128, NH], F32)
    dma("sp", "dsp", DTt.ap(), DT_in[:, :], wbufs=[buf("DTt")])
    dma("sp", "dsp", CQt.ap(), CQ_in[:, :], wbufs=[buf("CQt")])
    dma("sp", "dsp", CKt.ap(), CK_in[:, :], wbufs=[buf("CKt")])
    dma("pool", "dpool", wq_sb.ap(),
        rowchunks(wq_in, D, 8), wbufs=[buf("wq")])
    dma("pool", "dpool", wk_sb.ap(),
        rowchunks(wk_in, D, 8), wbufs=[buf("wk")])
    dma("pool", "dpool", wv_sb.ap(),
        rowchunks(wv_in, 2048, 8), wbufs=[buf("wv")])

    ppctx = ExitStack()
    S0 = lambda n, sh, d: ppctx.enter_context(nc.sbuf_tensor(n, sh, d))
    zero2k = S0("zero2k", [128, 2048], BF16)
    p.op("pool", lambda e: e.memset(zero2k[:], 0), writes=[buf("zero2k")])
    bz = bufg_dr.ap().rearrange("(n p) (a d) -> n p (a d)", p=256, a=1).rearrange("n p d -> n (p d)").rearrange("n (p q) -> n p q", p=128)
    for i in range(40):
        dma("sp", "dsp", bz[i], zero2k[:], rbufs=[buf("zero2k")],
            wbufs=[buf("bufg")])
    xraw = [S0(f"ppx{i}", [128, D], F32) for i in range(8)]
    xnb = [S0(f"ppxn{i}", [128, D], BF16) for i in range(4)]
    xfmp = [S0(f"ppxf{i}", [128, D], BF16) for i in range(4)]
    wg_sb = S0("wg_sb", [128, 8 * 2048], BF16)
    gsb = [S0(f"ppg{i}", [128, 2048], BF16) for i in range(4)]
    dma("pool", "dpool", wg_sb.ap(),
        rowchunks(wg_in, 2048, 8), wbufs=[buf("wg")])
    xsqd = S0("ppsq", [128, D], BF16)
    sumsq8 = S0("ppsumsq", [128, 8], F32)
    sumx8 = S0("ppsumx", [128, 8], F32)
    mu8 = S0("ppmu", [128, 8], F32)
    vv8 = S0("ppvv", [128, 8], F32)
    lgb = S0("pplg", [128, E], F32)
    lg2 = S0("pplg2", [128, E], F32)
    mrow = S0("ppm", [128, 1], F32)
    ohbf = S0("ppoh", [128, E], BF16)

    def newton8(v_ap, out_ap, rb_, wb_, scr0, scr1, scri):
        n0, n1, ni = buf(scr0.handle_name), buf(scr1.handle_name), buf(scri.handle_name)
        p.op("dve", lambda e: e.tensor_single_scalar(
            scri.ap(), v_ap.bitcast(I32), 1, OP.arith_shift_right),
            reads=[rb_], writes=[ni])
        p.op("dve", lambda e: e.tensor_scalar(
            scri.ap(), scri.ap(), -1, MAGIC, OP.mult, OP.add),
            reads=[ni], writes=[ni])
        p.op("dve", lambda e: e.tensor_copy(scr0.ap(), scri.ap().bitcast(F32)),
             reads=[ni], writes=[n0])
        for _ in range(2):
            p.op("dve", lambda e: e.tensor_mul(scr1.ap(), scr0.ap(), scr0.ap()),
                 reads=[n0], writes=[n1])
            p.op("dve", lambda e: e.tensor_tensor(scr1.ap(), scr1.ap(), v_ap,
                                                  OP.mult),
                 reads=[n1, rb_], writes=[n1])
            p.op("dve", lambda e: e.tensor_scalar(scr1.ap(), scr1.ap(), -0.5,
                                                  1.5, OP.mult, OP.add),
                 reads=[n1], writes=[n1])
            p.op("dve", lambda e: e.tensor_mul(scr0.ap(), scr0.ap(), scr1.ap()),
                 reads=[n0, n1], writes=[n0])
        p.op("dve", lambda e: e.tensor_copy(out_ap, scr0.ap()),
             reads=[n0], writes=[wb_])

    nwt8 = [S0(f"ppnw{i}", [128, 8], F32) for i in range(2)]
    nwt8i = S0("ppnwi", [128, 8], I32)
    for tn in (nwt8[0], nwt8[1], nwt8i, sumsq8, sumx8, mu8, vv8):
        pass

    for g8 in range(NT // 8):
        ts8 = list(range(8 * g8, 8 * g8 + 8))
        for k, t in enumerate(ts8):
            bx = buf(f"ppx{k}")
            dma("sp", "dsp", xraw[k].ap(), x_t[t], wbufs=[bx])
            p.op("act", lambda e, k=k: e.activation(
                xsqd.ap(), xraw[k].ap(), AF.Square, accum_out=sumsq8[:, k:k + 1]),
                reads=[bx], writes=[buf("ppsq"), buf("ppsumsq")])
            p.op("act", lambda e, k=k: e.activation(
                xsqd.ap(), xraw[k].ap(), AF.Copy, accum_out=sumx8[:, k:k + 1]),
                reads=[bx], writes=[buf("ppsq"), buf("ppsumx")])
        p.op("dve", lambda e: e.tensor_scalar(mu8.ap(), sumx8.ap(), 1.0 / D,
                                              None, OP.mult),
             reads=[buf("ppsumx")], writes=[buf("ppmu")])
        p.op("dve", lambda e: e.tensor_scalar(vv8.ap(), sumsq8.ap(), 1.0 / D,
                                              None, OP.mult),
             reads=[buf("ppsumsq")], writes=[buf("ppvv")])
        p.op("dve", lambda e: e.tensor_mul(nwt8[0].ap(), mu8.ap(), mu8.ap()),
             reads=[buf("ppmu")], writes=[buf("ppnw0")])
        p.op("dve", lambda e: e.tensor_sub(vv8.ap(), vv8.ap(), nwt8[0].ap()),
             reads=[buf("ppvv"), buf("ppnw0")], writes=[buf("ppvv")])
        p.op("dve", lambda e: e.tensor_scalar(vv8.ap(), vv8.ap(), EPS, None,
                                              OP.add),
             reads=[buf("ppvv")], writes=[buf("ppvv")])
        class _H:
            pass
        s0 = _H(); s0.ap = nwt8[0].ap; s0.handle_name = "ppnw0"
        s1 = _H(); s1.ap = nwt8[1].ap; s1.handle_name = "ppnw1"
        si = _H(); si.ap = nwt8i.ap; si.handle_name = "ppnwi"
        newton8(vv8.ap(), rstd_a[:, 8 * g8:8 * g8 + 8], buf("ppvv"), statsb,
                s0, s1, si)
        p.op("dve", lambda e, g8=g8: e.scalar_tensor_tensor(
            nmr_a[:, 8 * g8:8 * g8 + 8], mu8.ap(), -1.0,
            rstd_a[:, 8 * g8:8 * g8 + 8], OP.mult, OP.mult),
            reads=[buf("ppmu"), statsb], writes=[statsb])
        for k, t in enumerate(ts8):
            par = t % 4
            tbk = 7 - (t % 2)
            bx, bxn, bxf = buf(f"ppx{k}"), buf(f"ppxn{par}"), buf(f"ppxf{par}")
            p.op("dve", lambda e, k=k, par=par, t=t: e.tensor_scalar(
                xnb[par].ap(), xraw[k].ap(), rstd_a[:, t:t + 1],
                nmr_a[:, t:t + 1], OP.mult, OP.add),
                reads=[bx, statsb], writes=[bxn])
            dma("sp", "dsp", xn_t[t], xnb[par].ap(), rbufs=[bxn],
                wbufs=[buf("xn_dr")])
            for ci in range(8):
                p.op("pe", lambda e, par=par, ci=ci, tbk=tbk: e.transpose(
                    bankB[tbk][:, ci * 128:(ci + 1) * 128],
                    xnb[par][:, ci * 128:(ci + 1) * 128], iden.ap()),
                    reads=[bxn, buf("iden")], writes=[bb_[tbk]])
            p.op("dve", lambda e, par=par, tbk=tbk: e.tensor_copy(
                xfmp[par].ap(), bankB[tbk][:, 0:1024]),
                reads=[bb_[tbk]], writes=[bxf])
            dma("sp", "dsp", xnf_t[t], xfmp[par].ap(), rbufs=[bxf],
                wbufs=[buf("xnf_dr")])
            bg = buf(f"ppg{par}")
            for dv in range(4):
                bbk = 3 + (dv % 2)
                prs = [(xfmp[par][:, ci * 128:(ci + 1) * 128],
                        wg_sb[:, ci * 2048 + dv * 512: ci * 2048 + (dv + 1) * 512])
                       for ci in range(8)]
                mmgroup(bank[bbk][:, 0:512], prs, [bxf, buf("wg")], bb_[bbk])
                p.op("act", lambda e, par=par, dv=dv, bbk=bbk: e.activation(
                    gsb[par][:, dv * 512:(dv + 1) * 512], bank[bbk][:, 0:512],
                    AF.Silu),
                    reads=[bb_[bbk]], writes=[bg])
            dma("act", "dact", g_t[t], gsb[par].ap(), rbufs=[bg],
                wbufs=[buf("g_dr")])
            mmgroup(bank[0][:, 0:E],
                    [(xfmp[par][:, ci * 128:(ci + 1) * 128],
                      wr_sb[:, 8 * ci:8 * ci + 8]) for ci in range(8)],
                    [bxf, buf("wr_sb")], bb_[0])
            p.op("dve", lambda e: e.tensor_add(lgb.ap(), bank[0][:, 0:E],
                                               rb_bc.ap()),
                 reads=[bb_[0], buf("rb_bc")], writes=[buf("pplg")])
            p.op("dve", lambda e: e.reduce_max(mrow.ap(), lgb.ap(), axis=AX),
                 reads=[buf("pplg")], writes=[buf("ppm")])
            p.op("dve", lambda e: e.tensor_scalar(lg2.ap(), lgb.ap(), mrow.ap(),
                                                  None, OP.is_equal),
                 reads=[buf("pplg"), buf("ppm")], writes=[buf("pplg2")])
            p.op("dve", lambda e: e.tensor_mul(lg2.ap(), lg2.ap(), io8m8.ap()),
                 reads=[buf("pplg2"), buf("io8m8")], writes=[buf("pplg2")])
            p.op("dve", lambda e: e.tensor_reduce(mrow.ap(), lg2.ap(), AX,
                                                  OP.min),
                 reads=[buf("pplg2")], writes=[buf("ppm")])
            p.op("dve", lambda e, t=t: e.tensor_scalar(
                eid_a[:, t:t + 1], mrow.ap(), 8.0, None, OP.add),
                reads=[buf("ppm")], writes=[eidb])
            p.op("dve", lambda e, t=t: e.tensor_scalar(
                ohbf.ap(), io8.ap(), eid_a[:, t:t + 1], None, OP.is_equal),
                reads=[buf("io8"), eidb], writes=[buf("ppoh")])
            mmgroup(bank[1][:, 0:E], [(utri.ap(), ohbf.ap())],
                    [buf("utri"), buf("ppoh")], bb_[1])
            mmgroup(bank[2][0:1, 0:E], [(onescol.ap(), ohbf.ap())],
                    [buf("onescol"), buf("ppoh")], bb_[2])
            p.op("dve", lambda e: e.tensor_tensor(lg2.ap(), bank[1][:, 0:E],
                                                  ohbf.ap(), OP.mult),
                 reads=[bb_[1], buf("ppoh")], writes=[buf("pplg2")])
            p.op("dve", lambda e: e.tensor_reduce(mrow.ap(), lg2.ap(), AX,
                                                  OP.add),
                 reads=[buf("pplg2")], writes=[buf("ppm")])
            p.op("dve", lambda e, t=t: e.tensor_scalar(
                lrank_a[:, t:t + 1], mrow.ap(), -1.0, None, OP.add),
                reads=[buf("ppm")], writes=[lrankb])
            p.op("dve", lambda e, t=t: e.tensor_copy(
                offs_row[:, E * t:E * (t + 1)], runtot.ap()),
                reads=[buf("runtot")], writes=[buf("offs_row")])
            p.op("dve", lambda e: e.tensor_add(runtot.ap(), runtot.ap(),
                                               bank[2][0:1, 0:E]),
                 reads=[buf("runtot"), bb_[2]], writes=[buf("runtot")])

    dma("act", "dact", bass.AP(offs_dr, 0, [[512, 1], [1, 512]]), offs_row.ap(),
        rbufs=[buf("offs_row")], wbufs=[buf("offs_dr")])
    dma("pool", "dpool", ccc_in[:, :], runtot.ap(), rbufs=[buf("runtot")],
        wbufs=[buf("ccc_in")])
    p.op("pool", lambda e: e.collective_compute(
        "AllGather", OP.bypass, replica_groups=[list(range(NCORES))],
        ins=[ccc_in.ap().opt()], outs=[ccc_out.ap().opt()]),
        reads=[buf("ccc_in")], writes=[buf("ccc_out")], cc="cc0")
    p.op("pool", lambda e: e.dma_start(
        counts_T.ap(), bass.AP(ccc_out, 0, [[1, E], [E, E]]),
        allow_slow_non_contiguous=True),
        reads=[buf("ccc_out")], writes=[buf("counts_T")], dma="q_counts_T")
    dma("pool", "dpool", cmrow.ap(),
        bass.AP(cmask_in, 0, [[0, E], [1, E]]), wbufs=[buf("cmrow")])
    p.op("pool", lambda e: e.tensor_mul(counts_T.ap(), counts_T.ap(), cmrow.ap()),
         reads=[buf("counts_T"), buf("cmrow")], writes=[buf("counts_T")])
    p.op("pool", lambda e: e.tensor_add(counts_T[:, 0:4], counts_T[:, 0:4],
                                        counts_T[:, 4:8]),
         reads=[buf("counts_T")], writes=[buf("counts_T")])
    p.op("pool", lambda e: e.tensor_add(counts_T[:, 0:2], counts_T[:, 0:2],
                                        counts_T[:, 2:4]),
         reads=[buf("counts_T")], writes=[buf("counts_T")])
    p.op("pool", lambda e: e.tensor_add(coff_col.ap(), counts_T[:, 0:1],
                                        counts_T[:, 1:2]),
         reads=[buf("counts_T")], writes=[buf("coff_col")])
    p.op("pool", lambda e: e.dma_start(
        bass.AP(coff_dr, 0, [[1, E], [1, 1]]), coff_col.ap(),
        allow_slow_non_contiguous=True),
        reads=[buf("coff_col")], writes=[buf("coff_dr")], dma="q_coff_dr")
    dma("pool", "dpool", coff_bc.ap(), bass.AP(coff_dr, 0, [[0, 128], [1, E]]),
        rbufs=[buf("coff_dr")], wbufs=[buf("coff_bc")])
    dma("pool", "dpool", offs_bc.ap(),
        bass.AP(offs_dr, 0, [[0, 128], [E, NT], [1, E]]),
        rbufs=[buf("offs_dr")], wbufs=[buf("offs_bc")])

    for t in range(NT):
        par = t % 2
        rb_ = buf(f"rxn{par}")
        dma("pool", "dpool", rxn[par].ap(), xn_t[t], rbufs=[buf("xn_dr")],
            wbufs=[rb_])
        p.op("pool", lambda e, t=t: e.tensor_scalar(
            r8.ap(), io8.ap(), eid_a[:, t:t + 1], None, OP.is_equal),
            reads=[buf("io8"), eidb], writes=[buf("r8")])
        p.op("pool", lambda e, t=t: e.tensor_add(
            r8b.ap(), coff_bc.ap(), offs_bc[:, E * t:E * (t + 1)]),
            reads=[buf("coff_bc"), buf("offs_bc")], writes=[buf("r8b")])
        p.op("pool", lambda e: e.tensor_mul(r8.ap(), r8.ap(), r8b.ap()),
             reads=[buf("r8"), buf("r8b")], writes=[buf("r8")])
        p.op("pool", lambda e: e.tensor_add(r8[:, 0:4], r8[:, 0:4], r8[:, 4:8]),
             reads=[buf("r8")], writes=[buf("r8")])
        p.op("pool", lambda e: e.tensor_add(r8[:, 0:2], r8[:, 0:2], r8[:, 2:4]),
             reads=[buf("r8")], writes=[buf("r8")])
        p.op("pool", lambda e: e.tensor_add(rc1.ap(), r8[:, 0:1], r8[:, 1:2]),
             reads=[buf("r8")], writes=[buf("rc1")])
        p.op("pool", lambda e, t=t: e.tensor_add(rc1.ap(), rc1.ap(),
                                                 lrank_a[:, t:t + 1]),
             reads=[buf("rc1"), lrankb], writes=[buf("rc1")])
        p.op("pool", lambda e, t=t: e.tensor_single_scalar(
            inv_a[:, t:t + 1], rc1.ap(), float(CAP), OP.is_ge),
            reads=[buf("rc1")], writes=[invb])
        p.op("pool", lambda e, t=t: e.tensor_scalar(
            rc2.ap(), eid_a[:, t:t + 1], float(CAP), None, OP.mult),
            reads=[eidb], writes=[buf("rc2")])
        p.op("pool", lambda e: e.tensor_add(rc2.ap(), rc2.ap(), rc1.ap()),
             reads=[buf("rc2"), buf("rc1")], writes=[buf("rc2")])
        p.op("pool", lambda e, t=t: e.tensor_scalar(
            rc3.ap(), iop.ap(), 1.0, OOBBASE + 128.0 * t, OP.mult, OP.add),
            reads=[buf("iop")], writes=[buf("rc3")])
        p.op("pool", lambda e: e.tensor_sub(rc3.ap(), rc3.ap(), rc2.ap()),
             reads=[buf("rc3"), buf("rc2")], writes=[buf("rc3")])
        p.op("pool", lambda e, t=t: e.tensor_scalar(
            rc3.ap(), rc3.ap(), inv_a[:, t:t + 1], None, OP.mult),
            reads=[buf("rc3"), invb], writes=[buf("rc3")])
        p.op("pool", lambda e: e.tensor_add(rc2.ap(), rc2.ap(), rc3.ap()),
             reads=[buf("rc2"), buf("rc3")], writes=[buf("rc2")])
        p.op("pool", lambda e, t=t: e.tensor_copy(idx_a[:, t:t + 1], rc2.ap()),
             reads=[buf("rc2")], writes=[idxb])
        p.op("pool", lambda e, t=t, par=par: e.indirect_dma_start(
            out=bufg_dr.ap(),
            out_offset=bass.IndirectOffsetOnAxis(idx_a[:, t:t + 1], 0),
            in_=rxn[par].ap(), in_offset=None,
            bounds_check=bc_reg, oob_is_err=False),
            reads=[rb_, idxb, buf("bufg")],
            writes=[buf("bufg")], dma=f"q_sc{t % 2}")
    p.wait_all("pool", ["q_sc0", "q_sc1"])
    p.op("pool", lambda e: e.collective_compute(
        "ReduceScatter", OP.add, replica_groups=[list(range(NCORES))],
        ins=[bufg_dr.ap().opt()], outs=[bufc_dr.ap().opt()]),
        reads=[buf("bufg")], writes=[buf("bufc")], cc="cc1")

    p.phase_join(engines=("pe", "dve", "act", "sp"),
                 dmas=[f"q_ppx{i}" for i in range(8)]
                 + ["q_xn_dr", "q_xnf_dr", "q_g_dr", "q_bufg"])
    ppctx.close()
    Ssb = [[S1(f"S{b}_{h}", [128, 2 * DV], F32) for h in range(NH)]
           for b in range(2)]
    Sbf = [[S1(f"Sb{b}_{h}", [128, 2 * DV], BF16) for h in range(NH)]
           for b in range(2)]
    p1xf = [S1(f"p1xf{i}", [128, 2048], BF16) for i in range(2)]
    qfm = [S1(f"qfm{i}", [128, 2048], BF16) for i in range(2)]
    kfm = [S1(f"kfm{i}", [128, 2048], BF16) for i in range(2)]
    vsb = [S1(f"vsb{i}", [128, 2 * 2048], BF16) for i in range(2)]
    att = [S1(f"att{i}", [128, 128], BF16) for i in range(2 * NH)]
    kckb = [S1(f"kck{i}", [128, DK], BF16) for i in range(2 * NH)]
    qqb = [S1(f"qq{i}", [128, DK], BF16) for i in range(2 * NH)]
    rt1 = S1("rt1", [128, 512], F32)
    rt2 = S1("rt2", [128, 512], F32)
    oscraw = [S1(f"oscr{i}", [128, 2048], BF16) for i in range(2)]
    ccsl = [S1(f"ccsl{i}", [128, 512], BF16) for i in range(2)]
    sssl = [S1(f"sssl{i}", [128, 512], BF16) for i in range(2)]
    ovar8 = S1("ovar8", [128, 8], F32)
    rstd8 = S1("rstd8", [128, 8], F32)
    vv8p = S1("vv8p", [128, 8], F32)
    n8 = [S1(f"n8_{i}", [128, 8], F32) for i in range(2)]
    n8i = S1("n8i", [128, 8], I32)
    osq = S1("osq", [128, DV], BF16)

    def wblk(wsb, ci, j):
        return wsb[:, ci * D + j * 128: ci * D + (j + 1) * 128]

    def emit_qkv(g2):
        bat, gg = g2 % 2, g2 // 2
        gp = bat
        sl = gg % 2
        ta = bat * NTB + 2 * gg
        tb = ta + 1
        if bat == 0:
            dma("sp", "dsp", ccsl[sl].ap(), cc2_in[:, gg * 512:(gg + 1) * 512],
                wbufs=[buf(f"ccsl{sl}")])
            dma("sp", "dsp", sssl[sl].ap(), ss2_in[:, gg * 512:(gg + 1) * 512],
                wbufs=[buf(f"sssl{sl}")])
        bxf = buf(f"p1xf{gp}")
        for (ti, a) in ((ta, 0), (tb, 1)):
            p.op("sp", lambda e, gp=gp, a=a, ti=ti: e.dma_start(
                bass.AP(p1xf[gp], a * 128, [[2048, 128], [256, 8], [1, 128]]),
                bass.AP(xnf_dr, ti * 128 * D, [[D, 128], [128, 8], [1, 128]])),
                reads=[buf("xnf_dr")], writes=[bxf], dma=f"q_p1xf{gp}")
        for (wsb, wb_name, dstl, dname) in ((wq_sb, "wq", qfm, "qfm"),
                                            (wk_sb, "wk", kfm, "kfm")):
            dst = dstl[gp]
            bdst = buf(f"{dname}{gp}")
            for h in range(NH):
                bbk = 4 + (h % 2)
                prs = []
                for half in range(2):
                    j = 2 * h + half
                    prs += [(wblk(wsb, ci, j),
                             p1xf[gp][:, ci * 256:(ci + 1) * 256])
                            for ci in range(8)]

                def emitq(e, prs=prs, bbk=bbk):
                    for i, (l, r) in enumerate(prs):
                        ins = e.matmul(bank[bbk][:, (i // 8) * 256:
                                                 (i // 8) * 256 + 256],
                                       l, r, start=(i % 8 == 0),
                                       stop=(i % 8 == 7))
                    return ins
                p.op("pe", emitq, reads=[bxf, buf(wb_name)], writes=[bb_[bbk]])
                bcs, bsn = buf(f"ccsl{sl}"), buf(f"sssl{sl}")
                bk = bb_[bbk]
                p.op("dve", lambda e, bbk=bbk, sl=sl: e.tensor_tensor(
                    rt1.ap(), bank[bbk][:, 0:512], ccsl[sl].ap(), OP.mult),
                    reads=[bk, bcs], writes=[buf("rt1")])
                p.op("dve", lambda e, bbk=bbk, sl=sl: e.tensor_tensor(
                    rt2.ap(), bank[bbk][:, 0:512], sssl[sl].ap(), OP.mult),
                    reads=[bk, bsn], writes=[buf("rt2")])
                p.op("dve", lambda e, dst=dst, h=h: e.tensor_sub(
                    dst[:, (2 * h) * 256:(2 * h) * 256 + 256],
                    rt1[:, 0:256], rt2[:, 256:512]),
                    reads=[buf("rt1"), buf("rt2")], writes=[bdst])
                p.op("dve", lambda e, dst=dst, h=h: e.tensor_sub(
                    dst[:, (2 * h + 1) * 256:(2 * h + 1) * 256 + 256],
                    rt1[:, 256:512], rt2[:, 0:256]),
                    reads=[buf("rt1"), buf("rt2")], writes=[bdst])
        for a in range(2):
            for dv in range(4):
                bbk = 6 + ((a * 4 + dv) % 2)
                prs = [(p1xf[gp][:, ci * 256 + a * 128: ci * 256 + a * 128 + 128],
                        wv_sb[:, ci * 2048 + dv * 512: ci * 2048 + (dv + 1) * 512])
                       for ci in range(8)]
                mmgroup(bank[bbk][:, 0:512], prs, [bxf, buf("wv")], bb_[bbk])
                p.op("act", lambda e, gp=gp, a=a, dv=dv, bbk=bbk: e.copy(
                    vsb[gp][:, a * 2048 + dv * 512: a * 2048 + (dv + 1) * 512],
                    bank[bbk][:, 0:512]),
                    reads=[bb_[bbk]], writes=[buf(f"vsb{gp}")])
    def emit_ret(g2):
        bat, gg = g2 % 2, g2 // 2
        gp = bat
        ta = bat * NTB + 2 * gg
        tb = ta + 1
        for a in range(2):
            for h in range(NH):
                hb = a * NH + h
                for half in range(2):
                    p.op("dve", lambda e, gp=gp, a=a, h=h, half=half, hb=hb:
                         e.tensor_tensor(
                        qqb[hb][:, half * 128:(half + 1) * 128],
                        qfm[gp][:, (2 * h + half) * 256 + a * 128:
                                (2 * h + half) * 256 + a * 128 + 128],
                        CQt[:, h * 256 + half * 128:
                            h * 256 + (half + 1) * 128], OP.mult),
                        reads=[buf(f"qfm{gp}"), buf("CQt")],
                        writes=[buf(f"qq{hb}")])
        for a, ti in ((0, ta), (1, tb)):
            c = ti % NTB
            for h in range(NH):
                hb = a * NH + h
                b_sc = 0 if h % 2 == 0 else 2
                b_o = 1 if h % 2 == 0 else 3
                battn, bkck = buf(f"att{hb}"), buf(f"kck{hb}")
                bqq = buf(f"qq{hb}")
                bS, bSb = buf(f"S{bat}_{h}"), buf(f"Sb{bat}_{h}")
                prs = [(kfm[gp][:, (2 * h + half) * 256 + a * 128:
                               (2 * h + half) * 256 + a * 128 + 128],
                        qfm[gp][:, (2 * h + half) * 256 + a * 128:
                               (2 * h + half) * 256 + a * 128 + 128])
                       for half in range(2)]
                mmgroup(bank[b_sc][:, 0:128], prs,
                        [buf(f"kfm{gp}"), buf(f"qfm{gp}")], bb_[b_sc])
                p.op("dve", lambda e, b_sc=b_sc, h=h, hb=hb: e.tensor_tensor(
                    att[hb].ap(), bank[b_sc][:, 0:128],
                    DTt[:, h * 128:(h + 1) * 128], OP.mult),
                    reads=[bb_[b_sc], buf("DTt")], writes=[battn])

                def emit_o(e, hb=hb, gp=gp, a=a, h=h, b_o=b_o, c=c, bat=bat):
                    ins = e.matmul(bank[b_o][:, 0:512], att[hb].ap(),
                                   vsb[gp][:, a * 2048 + h * 512:
                                           a * 2048 + (h + 1) * 512],
                                   start=True, stop=(c == 0))
                    if c > 0:
                        for half in range(2):
                            ins = e.matmul(
                                bank[b_o][:, 0:512],
                                qqb[hb][:, half * 128:(half + 1) * 128],
                                Sbf[bat][h][:, half * DV:(half + 1) * DV],
                                start=False, stop=(half == 1))
                    return ins
                rds = [battn, buf(f"vsb{gp}")] + ([bqq, bSb] if c > 0 else [])
                p.op("pe", emit_o, reads=rds, writes=[bb_[b_o]])
                p.op("act", lambda e, a=a, h=h, b_o=b_o: e.activation(
                    osq.ap(), bank[b_o][:, 0:512], AF.Square,
                    accum_out=ovar8[:, a * 4 + h:a * 4 + h + 1]),
                    reads=[bb_[b_o]], writes=[buf("osq"), buf("ovar8")])
                p.op("act", lambda e, a=a, h=h, b_o=b_o: e.copy(
                    oscraw[a][:, h * 512:(h + 1) * 512], bank[b_o][:, 0:512]),
                    reads=[bb_[b_o]], writes=[buf(f"oscr{a}")])
                if c < NTB - 1:
                    for half in range(2):
                        p.op("pe", lambda e, gp=gp, h=h, a=a, half=half,
                             b_sc=b_sc: e.transpose(
                            bankB[b_sc][:, 512 + half * 128:
                                        512 + (half + 1) * 128],
                            kfm[gp][:, (2 * h + half) * 256 + a * 128:
                                    (2 * h + half) * 256 + a * 128 + 128],
                            iden.ap()),
                            reads=[buf(f"kfm{gp}"), buf("iden")],
                            writes=[bb_[b_sc]])
                    p.op("act", lambda e, hb=hb, h=h, b_sc=b_sc: e.mul(
                        kckb[hb].ap(), bankB[b_sc][:, 512:768], CKt[:, h:h + 1]),
                        reads=[bb_[b_sc], buf("CKt")], writes=[bkck])
                    gC = GAMMAS[h] ** CH
                    for half in range(2):
                        b_u = b_sc if half == 0 else b_o
                        p.op("pe", lambda e, hb=hb, gp=gp, a=a, h=h, half=half,
                             b_u=b_u: e.matmul(
                            bank[b_u][:, 0:512],
                            kckb[hb][:, half * 128:(half + 1) * 128],
                            vsb[gp][:, a * 2048 + h * 512:
                                    a * 2048 + (h + 1) * 512],
                            start=True, stop=True),
                            reads=[bkck, buf(f"vsb{gp}")], writes=[bb_[b_u]])
                    for half in range(2):
                        b_u = b_sc if half == 0 else b_o
                        if c == 0:
                            p.op("dve", lambda e, h=h, half=half, b_u=b_u,
                                 bat=bat: e.tensor_copy(
                                Ssb[bat][h][:, half * DV:(half + 1) * DV],
                                bank[b_u][:, 0:512]),
                                reads=[bb_[b_u]], writes=[bS])
                        else:
                            p.op("dve", lambda e, h=h, half=half, b_u=b_u,
                                 gC=gC, bat=bat: e.scalar_tensor_tensor(
                                Ssb[bat][h][:, half * DV:(half + 1) * DV],
                                Ssb[bat][h][:, half * DV:(half + 1) * DV], gC,
                                bank[b_u][:, 0:512], OP.mult, OP.add),
                                reads=[bb_[b_u], bS], writes=[bS])
                    p.op("act", lambda e, h=h, bat=bat: e.copy(
                        Sbf[bat][h].ap(), Ssb[bat][h].ap()),
                        reads=[bS], writes=[bSb])
        p.op("dve", lambda e: e.tensor_scalar(
            vv8p.ap(), ovar8.ap(), 1.0 / DV, EPS, OP.mult, OP.add),
            reads=[buf("ovar8")], writes=[buf("vv8p")])
        class _H8:
            pass
        s0 = _H8(); s0.ap = n8[0].ap; s0.handle_name = "n8_0"
        s1 = _H8(); s1.ap = n8[1].ap; s1.handle_name = "n8_1"
        si = _H8(); si.ap = n8i.ap; si.handle_name = "n8i"
        newton8(vv8p.ap(), rstd8.ap(), buf("vv8p"), buf("rstd8"), s0, s1, si)
        for a, ti in ((0, ta), (1, tb)):
            for h in range(NH):
                col = a * 4 + h
                p.op("dve", lambda e, a=a, h=h, col=col: e.tensor_scalar(
                    oscraw[a][:, h * 512:(h + 1) * 512],
                    oscraw[a][:, h * 512:(h + 1) * 512],
                    rstd8[:, col:col + 1], None, OP.mult),
                    reads=[buf(f"oscr{a}"), buf("rstd8")],
                    writes=[buf(f"oscr{a}")])
            dma("sp", "dsp", osc_t[ti], oscraw[a].ap(),
                rbufs=[buf(f"oscr{a}")], wbufs=[buf("osc_dr")])

    for g2 in range(NT // 2):
        emit_qkv(g2)
        if g2 >= 1:
            emit_ret(g2 - 1)
    emit_ret(NT // 2 - 1)
    c1.close()

    p.phase_join()

    c2 = ExitStack()
    S2 = lambda n, sh, d: c2.enter_context(nc.sbuf_tensor(n, sh, d))
    w1_sb = S2("w1_sb", [128, 8 * HID], BF16)
    w2_sb = S2("w2_sb", [128, 32 * D], BF16)
    b1_sb = S2("b1_sb", [128, 32], F32)
    b2bc = S2("b2bc", [128, D], F32)
    p2x = [S2(f"p2x{i}", [128, D], BF16) for i in range(2)]
    p2xf = S2("p2xf", [128, 2048], BF16)
    hfm = S2("hfm", [128, 32 * 256], BF16)
    ycsb = [S2(f"ycsb{i}", [128, D], BF16) for i in range(2)]

    dma("pool", "dpool", w1_sb.ap(),
        rowchunks(w1c_in, HID, 8), wbufs=[buf("w1")])
    dma("pool", "dpool", w2_sb.ap(),
        rowchunks(w2c_in, D, 32), wbufs=[buf("w2")])
    dma("sp", "dsp", b1_sb.ap(), b1c_in[:, :], wbufs=[buf("b1")])
    dma("sp", "dsp", b2bc.ap(), bass.AP(b2c_in, 0, [[0, 128], [1, D]]),
        wbufs=[buf("b2bc")])

    for gi in range(5):
        bxf = buf("p2xf")
        for a in range(2):
            dma("sp", "dsp", p2x[a].ap(), bufc_t[2 * gi + a],
                rbufs=[buf("bufc")], wbufs=[buf(f"p2x{a}")])
            bbk = 4 + a
            for ci in range(8):
                p.op("pe", lambda e, a=a, ci=ci, bbk=bbk: e.transpose(
                    bankB[bbk][:, ci * 128:(ci + 1) * 128],
                    p2x[a][:, ci * 128:(ci + 1) * 128], iden.ap()),
                    reads=[buf(f"p2x{a}"), buf("iden")], writes=[bb_[bbk]])
            p.op("dve", lambda e, a=a, bbk=bbk: e.tensor_copy(
                bass.AP(p2xf, a * 128, [[2048, 128], [256, 8], [1, 128]]),
                bankB[bbk][:, 0:1024]),
                reads=[bb_[bbk]], writes=[bxf])
        hbufs = []
        for j in range(32):
            bbk = j % 4
            prs = [(w1_sb[:, ci * HID + j * 128: ci * HID + (j + 1) * 128],
                    p2xf[:, ci * 256:(ci + 1) * 256]) for ci in range(8)]
            mmgroup(bank[bbk][:, 0:256], prs, [bxf, buf("w1")], bb_[bbk])
            hb = buf(f"hf2_{j}")
            hbufs.append(hb)
            p.op("act", lambda e, j=j, bbk=bbk: e.activation(
                hfm[:, j * 256:(j + 1) * 256], bank[bbk][:, 0:256], AF.Relu,
                bias=b1_sb[:, j:j + 1]),
                reads=[bb_[bbk], buf("b1")], writes=[hb])
        for ts in range(2):
            for dh in range(2):
                bbk = 4 + ts * 2 + dh

                def emity(e, ts=ts, dh=dh, bbk=bbk):
                    for j in range(32):
                        ins = e.matmul(
                            bank[bbk][:, 0:512],
                            hfm[:, j * 256 + ts * 128: j * 256 + ts * 128 + 128],
                            w2_sb[:, j * D + dh * 512: j * D + (dh + 1) * 512],
                            start=(j == 0), stop=(j == 31))
                    return ins
                p.op("pe", emity, reads=hbufs + [buf("w2")], writes=[bb_[bbk]])
            for dh in range(2):
                bbk = 4 + ts * 2 + dh
                p.op("dve", lambda e, ts=ts, dh=dh, bbk=bbk: e.tensor_tensor(
                    ycsb[ts][:, dh * 512:(dh + 1) * 512], bank[bbk][:, 0:512],
                    b2bc[:, dh * 512:(dh + 1) * 512], OP.add),
                    reads=[bb_[bbk], buf("b2bc")], writes=[buf(f"ycsb{ts}")])
        for a in range(2):
            dma("sp", "dsp", yc_t[2 * gi + a], ycsb[a].ap(),
                rbufs=[buf(f"ycsb{a}")], wbufs=[buf("yc_dr")])
    p.op("pool", lambda e: e.collective_compute(
        "AllGather", OP.bypass, replica_groups=[list(range(NCORES))],
        ins=[yc_dr.ap().opt()], outs=[yfull_dr.ap().opt()]),
        reads=[buf("yc_dr")], writes=[buf("yfull")], cc="cc2")
    c2.close()

    p.phase_join()

    c3 = ExitStack()
    S3 = lambda n, sh, d: c3.enter_context(nc.sbuf_tensor(n, sh, d))
    wo_sb = S3("wo_sb", [128, 16 * D], BF16)
    wos = S3("wos", [128, D], F32)
    gnw_sb = S3("gnw_sb", [128, 16], F32)
    a3g = [S3(f"a3g{i}", [128, 2048], BF16) for i in range(2)]
    a3osc = [S3(f"a3osc{i}", [128, 2048], BF16) for i in range(2)]
    a3og = [S3(f"a3og{i}", [128, 2048], BF16) for i in range(2)]
    a3ogf = [S3(f"a3ogf{i}", [128, 2048], BF16) for i in range(2)]
    a3s = [S3(f"a3s{i}", [128, D], BF16) for i in range(2)]

    dma("sp", "dsp", gnw_sb.ap(), gnw16_in[:, :], wbufs=[buf("gnw")])
    wo_r = wo_in.ap().rearrange("(c k) n -> c k n", k=128)
    for ci in range(16):
        dma("sp", "dsp", wos.ap(), wo_r[ci], rbufs=[buf("wo_pre")],
            wbufs=[buf("wo_pre")])
        p.op("act", lambda e, ci=ci: e.mul(
            wo_sb[:, ci * D:(ci + 1) * D], wos.ap(), gnw_sb[:, ci:ci + 1]),
            reads=[buf("wo_pre"), buf("gnw")], writes=[buf("wo")])

    for g in range(NT // 2):
        ta, tb = 2 * g, 2 * g + 1
        for (ti, a) in ((ta, 0), (tb, 1)):
            dma("sp", "dsp", a3g[a].ap(), g_t[ti], rbufs=[buf("g_dr")],
                wbufs=[buf(f"a3g{a}")])
            dma("sp", "dsp", a3osc[a].ap(), osc_t[ti], rbufs=[buf("osc_dr")],
                wbufs=[buf(f"a3osc{a}")])
        for a, ti in ((0, ta), (1, tb)):
            p.op("dve", lambda e, a=a: e.tensor_mul(
                a3og[a].ap(), a3osc[a].ap(), a3g[a].ap()),
                reads=[buf(f"a3osc{a}"), buf(f"a3g{a}")],
                writes=[buf(f"a3og{a}")])
            for half in range(2):
                bbk = 4 + ((a * 2 + half) % 2)
                for ci in range(8):
                    cc = half * 8 + ci
                    p.op("pe", lambda e, a=a, cc=cc, ci=ci, bbk=bbk: e.transpose(
                        bankB[bbk][:, ci * 128:(ci + 1) * 128],
                        a3og[a][:, cc * 128:(cc + 1) * 128], iden.ap()),
                        reads=[buf(f"a3og{a}"), buf("iden")], writes=[bb_[bbk]])
                p.op("dve", lambda e, a=a, half=half, bbk=bbk: e.tensor_copy(
                    a3ogf[a][:, half * 1024:(half + 1) * 1024],
                    bankB[bbk][:, 0:1024]),
                    reads=[bb_[bbk]], writes=[buf(f"a3ogf{a}")])
            for dh in range(2):
                bbk = 6 + dh

                def emits(e, a=a, dh=dh, bbk=bbk):
                    for ci in range(16):
                        ins = e.matmul(
                            bank[bbk][:, 0:512],
                            a3ogf[a][:, ci * 128:(ci + 1) * 128],
                            wo_sb[:, ci * D + dh * 512: ci * D + (dh + 1) * 512],
                            start=(ci == 0), stop=(ci == 15))
                    return ins
                p.op("pe", emits, reads=[buf(f"a3ogf{a}"), buf("wo")],
                     writes=[bb_[bbk]])
                p.op("dve", lambda e, a=a, dh=dh, bbk=bbk: e.tensor_copy(
                    a3s[a][:, dh * 512:(dh + 1) * 512], bank[bbk][:, 0:512]),
                    reads=[bb_[bbk]], writes=[buf(f"a3s{a}")])
            dma("sp", "dsp", s_t[ti], a3s[a].ap(), rbufs=[buf(f"a3s{a}")],
                wbufs=[buf("s_dr")])
    c3.close()

    p.phase_join()

    c4 = ExitStack()
    S4 = lambda n, sh, d: c4.enter_context(nc.sbuf_tensor(n, sh, d))
    o1_sb = S4("o1_sb", [128, 8 * HID], BF16)
    o2_sb = S4("o2_sb", [128, 32 * D], BF16)
    ob1_sb = S4("ob1_sb", [128, 32], F32)
    ob2bc = S4("ob2bc", [128, D], BF16)
    b4x = [S4(f"b4x{i}", [128, D], F32) for i in range(2)]
    b4xf = S4("b4xf", [128, 4096], BF16)
    hfm4 = S4("hfm4", [128, 32 * 512], BF16)
    yov = S4("yov", [128, D], F32)
    b4s = [S4(f"b4s{i}", [128, D], BF16) for i in range(2)]
    b4g = S4("b4g", [128, D], BF16)
    b4o = [S4("b4o0", [128, D], F32)]

    dma("pool", "dpool", o1_sb.ap(),
        rowchunks(ow1_in, HID, 8), wbufs=[buf("o1")])
    dma("pool", "dpool", o2_sb.ap(),
        rowchunks(ow2_in, D, 32), wbufs=[buf("o2")])
    dma("sp", "dsp", ob1_sb.ap(), ob1_in[:, :], wbufs=[buf("ob1")])
    dma("sp", "dsp", ob2bc.ap(), bass.AP(ob2_in, 0, [[0, 128], [1, D]]),
        wbufs=[buf("ob2bc")])

    def b4_loads(ti, a2):
        dma("pool", "dpool", b4x[a2].ap(), x_t[ti], wbufs=[buf(f"b4x{a2}")])
        dma("pool", "dpool", b4s[a2].ap(), s_t[ti], rbufs=[buf("s_dr")],
            wbufs=[buf(f"b4s{a2}")])
        p.op("pool", lambda e, a2=a2: e.memset(ytok[a2].ap(), 0),
             reads=[], writes=[buf(f"ytok{a2}")])
        p.op("pool", lambda e, a2=a2, ti=ti: e.indirect_dma_start(
            out=ytok[a2].ap(), out_offset=None,
            in_=yfull_dr.ap(),
            in_offset=bass.IndirectOffsetOnAxis(idx_a[:, ti:ti + 1], 0),
            bounds_check=bc_reg, oob_is_err=False),
            reads=[buf("yfull"), idxb, buf(f"ytok{a2}")],
            writes=[buf(f"ytok{a2}")], dma=f"q_ytok{a2}")

    def b4_combine(ti, a2):
        p.op("dve", lambda e, ti=ti: e.tensor_scalar(
            yov.ap(), yov.ap(), inv_a[:, ti:ti + 1], None, OP.mult),
            reads=[buf("yov"), invb], writes=[buf("yov")])
        p.op("dve", lambda e, a2=a2: e.tensor_add(
            yov.ap(), yov.ap(), ytok[a2].ap()),
            reads=[buf("yov"), buf(f"ytok{a2}")], writes=[buf("yov")])
        p.op("act", lambda e: e.activation(b4g.ap(), yov.ap(), AF.Silu),
             reads=[buf("yov")], writes=[buf("b4g")])
        p.op("dve", lambda e, a2=a2: e.tensor_mul(
            b4g.ap(), b4g.ap(), b4s[a2].ap()),
            reads=[buf("b4g"), buf(f"b4s{a2}")], writes=[buf("b4g")])
        p.op("dve", lambda e, a2=a2: e.tensor_add(
            b4o[0].ap(), b4g.ap(), b4x[a2].ap()),
            reads=[buf("b4g"), buf(f"b4x{a2}")], writes=[buf("b4o0")])
        dma("sp", "dsp", out_t[ti], b4o[0].ap(), rbufs=[buf("b4o0")],
            wbufs=[buf("outw")])

    for g in range(NT // 4):
        t0 = 4 * g
        bxf = buf("b4xf")
        for a in range(4):
            p.op("sp", lambda e, a=a, ti=t0 + a: e.dma_start(
                bass.AP(b4xf, a * 128, [[4096, 128], [512, 8], [1, 128]]),
                bass.AP(xnf_dr, ti * 128 * D, [[D, 128], [128, 8], [1, 128]])),
                reads=[buf("xnf_dr")], writes=[bxf], dma="q_b4xf")
        for a in range(2):
            b4_loads(t0 + a, a)
        hbufs = []
        for j in range(32):
            bbk = j % 4
            prs = [(o1_sb[:, ci * HID + j * 128: ci * HID + (j + 1) * 128],
                    b4xf[:, ci * 512:(ci + 1) * 512]) for ci in range(8)]
            mmgroup(bank[bbk][:, 0:512], prs, [bxf, buf("o1")], bb_[bbk])
            hb = buf(f"h4_{j}")
            hbufs.append(hb)
            p.op("act", lambda e, j=j, bbk=bbk: e.activation(
                hfm4[:, j * 512:(j + 1) * 512], bank[bbk][:, 0:512],
                AF.Relu, bias=ob1_sb[:, j:j + 1]),
                reads=[bb_[bbk], buf("ob1")], writes=[hb])
        for ts in range(4):
            a2 = ts % 2
            for dh in range(2):
                bbk = 4 + (ts * 2 + dh) % 4

                def emity(e, ts=ts, dh=dh, bbk=bbk):
                    for j in range(32):
                        ins = e.matmul(
                            bank[bbk][:, 0:512],
                            hfm4[:, j * 512 + ts * 128:
                                 j * 512 + ts * 128 + 128],
                            o2_sb[:, j * D + dh * 512: j * D + (dh + 1) * 512],
                            start=(j == 0), stop=(j == 31))
                    return ins
                p.op("pe", emity, reads=hbufs + [buf("o2")], writes=[bb_[bbk]])
            for dh in range(2):
                bbk = 4 + (ts * 2 + dh) % 4
                p.op("dve", lambda e, dh=dh, bbk=bbk: e.tensor_tensor(
                    yov[:, dh * 512:(dh + 1) * 512], bank[bbk][:, 0:512],
                    ob2bc[:, dh * 512:(dh + 1) * 512], OP.add),
                    reads=[bb_[bbk], buf("ob2bc")], writes=[buf("yov")])
            b4_combine(t0 + ts, a2)
            if ts == 1:
                for a in range(2):
                    b4_loads(t0 + 2 + a, a)
    c4.close()

    if debug:
        dma("sp", "dsp", dbg["xn"].ap(), xn_dr.ap(), rbufs=[buf("xn_dr")])
        dma("sp", "dsp", dbg["eid"].ap(), eid_a.ap(), rbufs=[eidb])
        dma("sp", "dsp", dbg["lrank"].ap(), lrank_a.ap(), rbufs=[lrankb])
        dma("sp", "dsp", dbg["inv"].ap(), inv_a.ap(), rbufs=[invb])
        dma("sp", "dsp", dbg["idx"].ap(), idx_a.ap(), rbufs=[idxb])
        dma("sp", "dsp", dbg["osc"].ap(), osc_dr.ap(), rbufs=[buf("osc_dr")])
        dma("sp", "dsp", dbg["s"].ap(), s_dr.ap(), rbufs=[buf("s_dr")])
        dma("sp", "dsp", dbg["yfull"].ap(), yfull_dr.ap(), rbufs=[buf("yfull")])
        dma("sp", "dsp", dbg["bufc"].ap(), bufc_dr.ap(), rbufs=[buf("bufc")])
        dma("act", "dact", dbg["coff"].ap(), coff_dr.ap(), rbufs=[buf("coff_dr")])
        dma("sp", "dsp", dbg["rstd"].ap(), rstd_a.ap(), rbufs=[statsb])
        dma("sp", "dsp", dbg["nmr"].ap(), nmr_a.ap(), rbufs=[statsb])

    p.wait_all("sp", [s for s in p.cnt if s.startswith("q_")
                      or s.startswith("cc") or s in ("dact", "dpool", "dsp")])

    p.emit(sems)
    stack.close()
    return nc


_PROG_CACHE = {}


def _get_program(debug=False):
    if debug not in _PROG_CACHE:
        _PROG_CACHE[debug] = build_program(debug=debug)
    return _PROG_CACHE[debug]


def make_in_maps(inputs):
    t = _tables()
    x = np.asarray(inputs["L0_attn_x"], np.float32).reshape(16 * 4096, D)
    w1 = np.asarray(inputs["L0_ffn_w1"], np.float32)
    b1 = np.asarray(inputs["L0_ffn_b1"], np.float32)
    w2 = np.asarray(inputs["L0_ffn_w2"], np.float32)
    b2 = np.asarray(inputs["L0_ffn_b2"], np.float32)
    gn = np.asarray(inputs["L0_attn_gn_weight"], np.float32)
    gnw16 = np.ascontiguousarray(
        np.tile(gn, NH).reshape(16, 128).T)
    shared = dict(
        wq=np.ascontiguousarray(inputs["L0_attn_Wq"], np.float32),
        wk=np.ascontiguousarray(inputs["L0_attn_Wk"], np.float32),
        wv=np.ascontiguousarray(inputs["L0_attn_Wv"], np.float32),
        wg=np.ascontiguousarray(inputs["L0_attn_Wg"], np.float32),
        wo=np.ascontiguousarray(inputs["L0_attn_Wo"], np.float32),
        wr=np.ascontiguousarray(inputs["L0_ffn_router_w"], np.float32),
        rb=np.ascontiguousarray(inputs["L0_ffn_router_b"],
                                np.float32).reshape(1, E),
        gnw16=gnw16,
        ow1=np.ascontiguousarray(inputs["L0_ffn_ow1"], np.float32),
        ow2=np.ascontiguousarray(inputs["L0_ffn_ow2"], np.float32),
        ob1=np.ascontiguousarray(
            np.asarray(inputs["L0_ffn_ob1"], np.float32).reshape(32, 128).T),
        ob2=np.ascontiguousarray(inputs["L0_ffn_ob2"],
                                 np.float32).reshape(1, D).astype(
                                     ml_dtypes.bfloat16),
        cc2T=t["cc2T"], ss2T=t["ss2T"], DT=t["DT"], CQ=t["CQ"], CK=t["CK"],
        utri=t["utri"], iden=t["iden"], onescol=t["onescol"],
    )
    shared["io8_in"] = np.broadcast_to(np.arange(E, dtype=np.float32),
                                       (128, E)).copy()
    shared["io8m8_in"] = np.broadcast_to(np.arange(-8, 0, dtype=np.float32),
                                         (128, E)).copy()
    shared["iop_in"] = np.arange(128, dtype=np.float32).reshape(128, 1).copy()
    in_maps = []
    for c in range(NCORES):
        m = dict(shared)
        m["x"] = np.ascontiguousarray(x[c * TPB:(c + 1) * TPB])
        m["w1c"] = np.ascontiguousarray(w1[c])
        m["w2c"] = np.ascontiguousarray(w2[c])
        m["b1c"] = np.ascontiguousarray(b1[c].reshape(32, 128).T)
        m["b2c"] = np.ascontiguousarray(b2[c].reshape(1, D))
        cm = np.zeros((E, 1), np.float32)
        cm[:c] = 1.0
        m["cmask"] = cm
        in_maps.append(m)
    return in_maps


def kernel(**inputs):
    nc = _get_program(debug=False)
    in_maps = make_in_maps(inputs)
    res = run_bass_kernel_spmd(nc, in_maps, core_ids=list(range(NCORES)))
    out = np.concatenate([np.asarray(r["out"]) for r in res.results], axis=0)
    return out.reshape(16, 4096, D)


# version 14 (final)
# speedup vs baseline: 1.1364x; 1.0919x over previous
import numpy as np
import ml_dtypes

import concourse.bass as bass
import concourse.mybir as mybir
import concourse.bass_utils as _bu
from concourse.bass_utils import run_bass_kernel_spmd

F32 = mybir.dt.float32
BF16 = mybir.dt.bfloat16
I32 = mybir.dt.int32
U32 = mybir.dt.uint32
AF = mybir.ActivationFunctionType
OP = mybir.AluOpType
AX = mybir.AxisListType.X

NCORES = 8
D = 1024
HID = 4096
NH = 4
DK = 256
DV = 512
CH = 128
TPB = 8192
NT = TPB // CH
NTB = 32
E = 8
CAP = 1280
EPS = 1e-5
QSCALE = 0.0625
MAGIC = 0x5F3759DF
OOBBASE = 20000.0

GAMMAS = [1.0 - 2.0 ** (-5.0 - h) for h in range(NH)]

def _bvo_nobirsim(tmpdir, inp="bir.json", outp="file.neff", arch=None, *,
                  dve_root=None):
    from pathlib import Path
    from concourse.bass_utils import (get_walrus_driver, get_walrus_args,
                                      get_bir_arch, run_command)
    cmd = [
        get_walrus_driver(),
        "--pass", ",".join(["birverifier", "runtime_memory_reservation",
                            "lower_act", "lower_dve", "lower_ap_offset",
                            "codegen", "neff_packager"]),
        "-i", inp,
        "--neff-output-filename", outp,
        "--enable-birsim=false",
        "--mem-mode=physical", "--policy=0", "--enable-ldw-opt=false",
        "--assign-static-dmas-to-sp=false",
        "--dram-page-size=256", "--enable-neff-debug-info=true",
        "--jobs", "8",
        *get_walrus_args(get_bir_arch(tmpdir, inp) if arch is None else arch,
                         tmpdir, dve_root=dve_root),
    ]
    result = run_command(cmd, cwd=tmpdir)
    if result is not None:
        (Path(tmpdir) / "log.txt").write_text(result.stdout)
    return f"{tmpdir}/{outp}"

_bu.bir_verify_and_optimise = _bvo_nobirsim

ENGINES = ("pe", "dve", "act", "pool", "sp")
ENG_SEM = {e: "s_" + e for e in ENGINES}
BASE_SEMS = [ENG_SEM[e] for e in ENGINES] + ["cc0", "cc1", "cc2"]


class Buf:
    __slots__ = ("name", "w", "rs")

    def __init__(self, name=""):
        self.name, self.w, self.rs = name, None, []


class LM:
    __slots__ = ("eng", "sem", "val")

    def __init__(self, eng, sem, val):
        self.eng, self.sem, self.val = eng, sem, val


class Prog:
    def __init__(self, nc):
        self.nc = nc
        self.ops = {e: [] for e in ENGINES}
        self.cnt = {s: 0 for s in BASE_SEMS}

    def _sem(self, s):
        if s not in self.cnt:
            self.cnt[s] = 0
        return s

    def raw(self, eng, fn):
        self.ops[eng].append((fn, [], None, 0))

    def op(self, eng, fn, reads=(), writes=(), dma=None, cc=None, inc_n=1):
        deps = []
        own = ENG_SEM[eng]
        for b in reads:
            if b.w is not None:
                deps.append(b.w)
        for b in writes:
            deps.extend(b.rs)
            if b.w is not None:
                deps.append(b.w)
        if cc is not None:
            sem, amt = self._sem(cc), 1
        elif dma is not None:
            sem, amt = self._sem(dma), 16 * inc_n
        else:
            sem, amt = own, 1
        waits = {}
        for d in deps:
            if d.val > waits.get(d.sem, 0):
                waits[d.sem] = d.val
        self.cnt[sem] += amt
        lm = LM(eng, sem, self.cnt[sem])
        self.ops[eng].append((fn, sorted(waits.items()), sem, amt))
        for b in reads:
            b.rs.append(lm)
        for b in writes:
            b.w, b.rs = lm, []
        return lm

    def wait_all(self, eng, sems):
        marks = [(s, self.cnt[s]) for s in sems if self.cnt.get(s, 0) > 0]
        self.ops[eng].append((None, marks, None, 0))

    def phase_join(self, engines=ENGINES, dmas=None):
        if dmas is None:
            dmas = [s for s in self.cnt
                    if s.startswith("q_") or s in ("dact", "dpool", "dsp")]
        sl = [ENG_SEM[e] for e in engines] + list(dmas)
        for e in engines:
            self.wait_all(e, [s for s in sl if s != ENG_SEM[e]])

    def emit(self, sems):
        for s in self.cnt:
            if s not in sems:
                sems[s] = self._stack.enter_context(self.nc.semaphore(s))
        engmap = {"pe": "tensor", "dve": "vector", "act": "scalar",
                  "pool": "gpsimd", "sp": "sync"}
        with self.nc.Block() as blk:
            for ename in ENGINES:
                oplist = self.ops[ename]
                if not oplist:
                    continue

                def make(oplist=oplist):
                    def run(eng):
                        lw = {}
                        for (fn, waits, sem, amt) in oplist:
                            for (sname, val) in waits:
                                if lw.get(sname, 0) >= val:
                                    continue
                                eng.wait_ge(sems[sname], val)
                                lw[sname] = val
                            if fn is None:
                                continue
                            ins = fn(eng)
                            if sem is not None:
                                ins.then_inc(sems[sem], amt)
                    return run

                getattr(blk, engmap[ename])(make())


def _tables():
    t = {}
    inv = 1.0 / (10000.0 ** (np.arange(0, DK, 2, dtype=np.float64) / DK))
    fr = np.outer(np.arange(4096, dtype=np.float64), inv)
    cosT = np.cos(fr).T
    sinT = np.sin(fr).T
    NW = 4096 // 256
    cc2 = np.zeros((128, NW * 512))
    ss2 = np.zeros((128, NW * 512))
    for w in range(NW):
        cw = cosT[:, w * 256:(w + 1) * 256]
        sw = sinT[:, w * 256:(w + 1) * 256]
        cc2[:, w * 512:w * 512 + 256] = cw
        cc2[:, w * 512 + 256:(w + 1) * 512] = cw
        ss2[:, w * 512:w * 512 + 256] = -sw
        ss2[:, w * 512 + 256:(w + 1) * 512] = sw
    t["cc2T"] = cc2.astype(ml_dtypes.bfloat16)
    t["ss2T"] = ss2.astype(ml_dtypes.bfloat16)
    j = np.arange(CH)
    DT = np.zeros((CH, NH * CH), np.float64)
    CQ = np.zeros((CH, NH * 2 * CH), np.float64)
    CK = np.zeros((CH, NH), np.float64)
    for h, g in enumerate(GAMMAS):
        rel = j[None, :] - j[:, None]
        DT[:, h * CH:(h + 1) * CH] = np.where(rel >= 0, g ** np.maximum(rel, 0), 0.0) * QSCALE
        cqrow = np.broadcast_to((g ** (j + 1.0)) * QSCALE, (CH, CH))
        CQ[:, h * 2 * CH:h * 2 * CH + CH] = cqrow
        CQ[:, h * 2 * CH + CH:(h + 1) * 2 * CH] = cqrow
        CK[:, h] = g ** (CH - 1.0 - j)
    t["DT"] = DT.astype(ml_dtypes.bfloat16)
    t["CQ"] = CQ.astype(ml_dtypes.bfloat16)
    t["CK"] = CK.astype(np.float32)
    t["utri"] = np.triu(np.ones((128, 128))).astype(ml_dtypes.bfloat16)
    t["iden"] = np.eye(128).astype(ml_dtypes.bfloat16)
    t["onescol"] = np.ones((128, 1), ml_dtypes.bfloat16)
    return t


def build_program(debug=False):
    from contextlib import ExitStack
    nc = bass.Bass("TRN2", target_bir_lowering=False)

    def din(name, shape, dt=F32):
        return nc.dram_tensor(name, shape, dt, kind="ExternalInput")

    x_in = din("x", [TPB, D])
    wq_in = din("wq", [D, D]); wk_in = din("wk", [D, D])
    wv_in = din("wv", [D, 2048]); wg_in = din("wg", [D, 2048])
    wo_in = din("wo", [2048, D])
    wr_in = din("wr", [D, E]); rb_in = din("rb", [1, E])
    gnw16_in = din("gnw16", [128, 16])
    w1c_in = din("w1c", [D, HID]); w2c_in = din("w2c", [HID, D])
    b1c_in = din("b1c", [128, 32]); b2c_in = din("b2c", [1, D], BF16)
    ow1_in = din("ow1", [D, HID]); ow2_in = din("ow2", [HID, D])
    ob1_in = din("ob1", [128, 32]); ob2_in = din("ob2", [1, D], BF16)
    cc2_in = din("cc2T", [128, 16 * 512], BF16)
    ss2_in = din("ss2T", [128, 16 * 512], BF16)
    DT_in = din("DT", [128, NH * CH], BF16)
    CQ_in = din("CQ", [128, NH * 2 * CH], BF16)
    CK_in = din("CK", [128, NH])
    utri_in = din("utri", [128, 128], BF16)
    iden_in = din("iden", [128, 128], BF16)
    ones_in = din("onescol", [128, 1], BF16)
    cmask_in = din("cmask", [E, 1])
    io8_in = din("io8_in", [128, E])
    io8m8_in = din("io8m8_in", [128, E])
    iop_in = din("iop_in", [128, 1])

    out_dram = nc.dram_tensor("out", [TPB, D], F32, kind="ExternalOutput")
    dbg = {}
    if debug:
        dbg["xn"] = nc.dram_tensor("dbg_xn", [TPB, D], BF16, kind="ExternalOutput")
        dbg["eid"] = nc.dram_tensor("dbg_eid", [128, NT], F32, kind="ExternalOutput")
        dbg["lrank"] = nc.dram_tensor("dbg_lrank", [128, NT], F32, kind="ExternalOutput")
        dbg["inv"] = nc.dram_tensor("dbg_inv", [128, NT], F32, kind="ExternalOutput")
        dbg["idx"] = nc.dram_tensor("dbg_idx", [128, NT], U32, kind="ExternalOutput")
        dbg["osc"] = nc.dram_tensor("dbg_osc", [TPB, 2048], BF16, kind="ExternalOutput")
        dbg["s"] = nc.dram_tensor("dbg_s", [TPB, D], BF16, kind="ExternalOutput")
        dbg["yfull"] = nc.dram_tensor("dbg_yfull", [E * CAP, D], BF16, kind="ExternalOutput")
        dbg["bufc"] = nc.dram_tensor("dbg_bufc", [CAP, D], BF16, kind="ExternalOutput")
        dbg["coff"] = nc.dram_tensor("dbg_coff", [1, E], F32, kind="ExternalOutput")
        dbg["rstd"] = nc.dram_tensor("dbg_rstd", [128, NT], F32, kind="ExternalOutput")
        dbg["nmr"] = nc.dram_tensor("dbg_nmr", [128, NT], F32, kind="ExternalOutput")

    xn_dr = nc.dram_tensor("xn_dr", [TPB, D], BF16)
    xnf_dr = nc.dram_tensor("xnf_dr", [TPB, D], BF16)
    g_dr = nc.dram_tensor("g_dr", [TPB, 2048], BF16)
    osc_dr = nc.dram_tensor("osc_dr", [TPB, 2048], BF16)
    s_dr = nc.dram_tensor("s_dr", [TPB, D], BF16)
    bufg_dr = nc.dram_tensor("bufg_dr", [E * CAP, D], BF16)
    bufc_dr = nc.dram_tensor("bufc_dr", [CAP, D], BF16)
    yc_dr = nc.dram_tensor("yc_dr", [CAP, D], BF16)
    yfull_dr = nc.dram_tensor("yfull_dr", [E * CAP, D], BF16)
    ccc_in = nc.dram_tensor("ccc_in", [1, E], F32)
    ccc_out = nc.dram_tensor("ccc_out", [E, E], F32)
    offs_dr = nc.dram_tensor("offs_dr", [NT, E], F32)
    coff_dr = nc.dram_tensor("coff_dr", [1, E], F32)

    stack = ExitStack()
    sems = {s: stack.enter_context(nc.semaphore(s)) for s in BASE_SEMS}
    bc_reg = stack.enter_context(nc.gpsimd.register("bc_reg"))
    p = Prog(nc)
    p._stack = stack
    B = {}

    def buf(name):
        if name not in B:
            B[name] = Buf(name)
        return B[name]

    def dma(eng, q, dst, src, rbufs=(), wbufs=()):
        stream = "q_" + wbufs[0].name if wbufs else q
        return p.op(eng, lambda e, d=dst, s=src: e.dma_start(d, s),
                    reads=rbufs, writes=wbufs, dma=stream)

    def mmgroup(psum_ap, pairs, reads, wbank):
        def emit(e, psum_ap=psum_ap, pairs=pairs):
            n = len(pairs)
            for i, (l, r) in enumerate(pairs):
                ins = e.matmul(psum_ap, l, r, start=(i == 0), stop=(i == n - 1))
            return ins
        return p.op("pe", emit, reads=reads, writes=[wbank])

    def rowchunks(t_in, cols, nch):
        return bass.AP(t_in, 0, [[cols, 128], [128 * cols, nch], [1, cols]])

    AS = nc.alloc_sbuf_tensor
    rstd_a = AS("rstd_a", [128, NT], F32)
    nmr_a = AS("nmr_a", [128, NT], F32)
    eid_a = AS("eid_a", [128, NT], F32)
    lrank_a = AS("lrank_a", [128, NT], F32)
    inv_a = AS("inv_a", [128, NT], F32)
    idx_a = AS("idx_a", [128, NT], U32)
    io8 = AS("io8", [128, E], F32)
    io8m8 = AS("io8m8", [128, E], F32)
    iop = AS("iop", [128, 1], F32)
    utri = AS("utri_s", [128, 128], BF16)
    iden = AS("iden_s", [128, 128], BF16)
    onescol = AS("onescol_s", [128, 1], BF16)
    rb_bc = AS("rb_bc", [128, E], F32)
    coff_bc = AS("coff_bc", [128, E], F32)
    offs_bc = AS("offs_bc", [128, NT * E], F32)
    runtot = AS("runtot", [1, E], F32)
    offs_row = AS("offs_row", [1, NT * E], F32)
    counts_T = AS("counts_T", [E, E], F32)
    cmrow = AS("cmrow", [E, E], F32)
    coff_col = AS("coff_col", [E, 1], F32)
    wr_sb = AS("wr_sb", [128, 8 * E], BF16)
    rxn = [AS(f"rxn{i}", [128, D], BF16) for i in range(2)]
    r8 = AS("r8", [128, E], F32)
    r8b = AS("r8b", [128, E], F32)
    rc1 = AS("rc1", [128, 1], F32)
    rc2 = AS("rc2", [128, 1], F32)
    rc3 = AS("rc3", [128, 1], F32)
    ytok = [AS(f"ytok{i}", [128, D], BF16) for i in range(2)]

    bank = [stack.enter_context(nc.psum_tensor(f"bank{i}", [128, 512], F32))
            for i in range(8)]
    bankB = [b.ap().bitcast(BF16) for b in bank]
    bb_ = [buf(f"bank{i}") for i in range(8)]

    x_t = x_in.ap().rearrange("(n p) d -> n p d", p=128)
    xn_t = xn_dr.ap().rearrange("(n p) d -> n p d", p=128)
    xnf_t = xnf_dr.ap().rearrange("(n p) d -> n p d", p=128)
    g_t = g_dr.ap().rearrange("(n p) d -> n p d", p=128)
    osc_t = osc_dr.ap().rearrange("(n p) d -> n p d", p=128)
    s_t = s_dr.ap().rearrange("(n p) d -> n p d", p=128)
    out_t = out_dram.ap().rearrange("(n p) d -> n p d", p=128)
    bufc_t = bufc_dr.ap().rearrange("(n p) d -> n p d", p=128)
    yc_t = yc_dr.ap().rearrange("(n p) d -> n p d", p=128)

    statsb = buf("stats")
    eidb = buf("eid_a")
    lrankb = buf("lrank_a")
    invb = buf("inv_a")
    idxb = buf("idx_a")

    dma("sp", "dsp", utri.ap(), utri_in[:, :], wbufs=[buf("utri")])
    dma("sp", "dsp", iden.ap(), iden_in[:, :], wbufs=[buf("iden")])
    dma("sp", "dsp", onescol.ap(), ones_in[:, :], wbufs=[buf("onescol")])
    dma("sp", "dsp", rb_bc.ap(), bass.AP(rb_in, 0, [[0, 128], [1, E]]),
        wbufs=[buf("rb_bc")])
    dma("pool", "dpool", wr_sb.ap(),
        rowchunks(wr_in, E, 8), wbufs=[buf("wr_sb")])
    dma("sp", "dsp", io8.ap(), io8_in[:, :], wbufs=[buf("io8")])
    dma("sp", "dsp", io8m8.ap(), io8m8_in[:, :], wbufs=[buf("io8m8")])
    dma("sp", "dsp", iop.ap(), iop_in[:, :], wbufs=[buf("iop")])
    p.op("pool", lambda e: e.memset(runtot[:], 0), writes=[buf("runtot")])
    p.raw("pool", lambda e: e.reg_mov(bc_reg, E * CAP - 1))

    c1 = ExitStack()
    S1 = lambda n, sh, d: c1.enter_context(nc.sbuf_tensor(n, sh, d))
    wq_sb = S1("wq_sb", [128, 8 * D], BF16)
    wk_sb = S1("wk_sb", [128, 8 * D], BF16)
    wv_sb = S1("wv_sb", [128, 8 * 2048], BF16)
    DTt = S1("DTt", [128, NH * CH], BF16)
    CQt = S1("CQt", [128, NH * 2 * CH], BF16)
    CKt = S1("CKt", [128, NH], F32)
    dma("sp", "dsp", DTt.ap(), DT_in[:, :], wbufs=[buf("DTt")])
    dma("sp", "dsp", CQt.ap(), CQ_in[:, :], wbufs=[buf("CQt")])
    dma("sp", "dsp", CKt.ap(), CK_in[:, :], wbufs=[buf("CKt")])
    dma("pool", "dpool", wq_sb.ap(),
        rowchunks(wq_in, D, 8), wbufs=[buf("wq")])
    dma("pool", "dpool", wk_sb.ap(),
        rowchunks(wk_in, D, 8), wbufs=[buf("wk")])
    dma("pool", "dpool", wv_sb.ap(),
        rowchunks(wv_in, 2048, 8), wbufs=[buf("wv")])

    ppctx = ExitStack()
    S0 = lambda n, sh, d: ppctx.enter_context(nc.sbuf_tensor(n, sh, d))
    zero2k = S0("zero2k", [128, 2048], BF16)
    p.op("pool", lambda e: e.memset(zero2k[:], 0), writes=[buf("zero2k")])
    bz = bufg_dr.ap().rearrange("(n p) (a d) -> n p (a d)", p=256, a=1).rearrange("n p d -> n (p d)").rearrange("n (p q) -> n p q", p=128)
    for i in range(40):
        dma("sp", "dsp", bz[i], zero2k[:], rbufs=[buf("zero2k")],
            wbufs=[buf("bufg")])
    xraw = [S0(f"ppx{i}", [128, D], F32) for i in range(8)]
    xnb = [S0(f"ppxn{i}", [128, D], BF16) for i in range(4)]
    xfmp = [S0(f"ppxf{i}", [128, D], BF16) for i in range(4)]
    wg_sb = S0("wg_sb", [128, 8 * 2048], BF16)
    gsb = [S0(f"ppg{i}", [128, 2048], BF16) for i in range(4)]
    dma("pool", "dpool", wg_sb.ap(),
        rowchunks(wg_in, 2048, 8), wbufs=[buf("wg")])
    xsqd = S0("ppsq", [128, D], BF16)
    sumsq8 = S0("ppsumsq", [128, 8], F32)
    sumx8 = S0("ppsumx", [128, 8], F32)
    mu8 = S0("ppmu", [128, 8], F32)
    vv8 = S0("ppvv", [128, 8], F32)
    lgb = S0("pplg", [128, E], F32)
    lg2 = S0("pplg2", [128, E], F32)
    mrow = S0("ppm", [128, 1], F32)
    ohbf = S0("ppoh", [128, E], BF16)

    def newton8(v_ap, out_ap, rb_, wb_, scr0, scr1, scri):
        n0, n1, ni = buf(scr0.handle_name), buf(scr1.handle_name), buf(scri.handle_name)
        p.op("dve", lambda e: e.tensor_single_scalar(
            scri.ap(), v_ap.bitcast(I32), 1, OP.arith_shift_right),
            reads=[rb_], writes=[ni])
        p.op("dve", lambda e: e.tensor_scalar(
            scri.ap(), scri.ap(), -1, MAGIC, OP.mult, OP.add),
            reads=[ni], writes=[ni])
        p.op("dve", lambda e: e.tensor_copy(scr0.ap(), scri.ap().bitcast(F32)),
             reads=[ni], writes=[n0])
        for _ in range(2):
            p.op("dve", lambda e: e.tensor_mul(scr1.ap(), scr0.ap(), scr0.ap()),
                 reads=[n0], writes=[n1])
            p.op("dve", lambda e: e.tensor_tensor(scr1.ap(), scr1.ap(), v_ap,
                                                  OP.mult),
                 reads=[n1, rb_], writes=[n1])
            p.op("dve", lambda e: e.tensor_scalar(scr1.ap(), scr1.ap(), -0.5,
                                                  1.5, OP.mult, OP.add),
                 reads=[n1], writes=[n1])
            p.op("dve", lambda e: e.tensor_mul(scr0.ap(), scr0.ap(), scr1.ap()),
                 reads=[n0, n1], writes=[n0])
        p.op("dve", lambda e: e.tensor_copy(out_ap, scr0.ap()),
             reads=[n0], writes=[wb_])

    nwt8 = [S0(f"ppnw{i}", [128, 8], F32) for i in range(2)]
    nwt8i = S0("ppnwi", [128, 8], I32)
    for tn in (nwt8[0], nwt8[1], nwt8i, sumsq8, sumx8, mu8, vv8):
        pass

    for g8 in range(NT // 8):
        ts8 = list(range(8 * g8, 8 * g8 + 8))
        for k, t in enumerate(ts8):
            bx = buf(f"ppx{k}")
            dma("sp", "dsp", xraw[k].ap(), x_t[t], wbufs=[bx])
            p.op("act", lambda e, k=k: e.activation(
                xsqd.ap(), xraw[k].ap(), AF.Square, accum_out=sumsq8[:, k:k + 1]),
                reads=[bx], writes=[buf("ppsq"), buf("ppsumsq")])
            p.op("act", lambda e, k=k: e.activation(
                xsqd.ap(), xraw[k].ap(), AF.Copy, accum_out=sumx8[:, k:k + 1]),
                reads=[bx], writes=[buf("ppsq"), buf("ppsumx")])
        p.op("dve", lambda e: e.tensor_scalar(mu8.ap(), sumx8.ap(), 1.0 / D,
                                              None, OP.mult),
             reads=[buf("ppsumx")], writes=[buf("ppmu")])
        p.op("dve", lambda e: e.tensor_scalar(vv8.ap(), sumsq8.ap(), 1.0 / D,
                                              None, OP.mult),
             reads=[buf("ppsumsq")], writes=[buf("ppvv")])
        p.op("dve", lambda e: e.tensor_mul(nwt8[0].ap(), mu8.ap(), mu8.ap()),
             reads=[buf("ppmu")], writes=[buf("ppnw0")])
        p.op("dve", lambda e: e.tensor_sub(vv8.ap(), vv8.ap(), nwt8[0].ap()),
             reads=[buf("ppvv"), buf("ppnw0")], writes=[buf("ppvv")])
        p.op("dve", lambda e: e.tensor_scalar(vv8.ap(), vv8.ap(), EPS, None,
                                              OP.add),
             reads=[buf("ppvv")], writes=[buf("ppvv")])
        class _H:
            pass
        s0 = _H(); s0.ap = nwt8[0].ap; s0.handle_name = "ppnw0"
        s1 = _H(); s1.ap = nwt8[1].ap; s1.handle_name = "ppnw1"
        si = _H(); si.ap = nwt8i.ap; si.handle_name = "ppnwi"
        newton8(vv8.ap(), rstd_a[:, 8 * g8:8 * g8 + 8], buf("ppvv"), statsb,
                s0, s1, si)
        p.op("dve", lambda e, g8=g8: e.scalar_tensor_tensor(
            nmr_a[:, 8 * g8:8 * g8 + 8], mu8.ap(), -1.0,
            rstd_a[:, 8 * g8:8 * g8 + 8], OP.mult, OP.mult),
            reads=[buf("ppmu"), statsb], writes=[statsb])
        for k, t in enumerate(ts8):
            par = t % 4
            tbk = 7 - (t % 2)
            bx, bxn, bxf = buf(f"ppx{k}"), buf(f"ppxn{par}"), buf(f"ppxf{par}")
            p.op("dve", lambda e, k=k, par=par, t=t: e.tensor_scalar(
                xnb[par].ap(), xraw[k].ap(), rstd_a[:, t:t + 1],
                nmr_a[:, t:t + 1], OP.mult, OP.add),
                reads=[bx, statsb], writes=[bxn])
            dma("sp", "dsp", xn_t[t], xnb[par].ap(), rbufs=[bxn],
                wbufs=[buf("xn_dr")])
            for ci in range(8):
                p.op("pe", lambda e, par=par, ci=ci, tbk=tbk: e.transpose(
                    bankB[tbk][:, ci * 128:(ci + 1) * 128],
                    xnb[par][:, ci * 128:(ci + 1) * 128], iden.ap()),
                    reads=[bxn, buf("iden")], writes=[bb_[tbk]])
            p.op("dve", lambda e, par=par, tbk=tbk: e.tensor_copy(
                xfmp[par].ap(), bankB[tbk][:, 0:1024]),
                reads=[bb_[tbk]], writes=[bxf])
            dma("sp", "dsp", xnf_t[t], xfmp[par].ap(), rbufs=[bxf],
                wbufs=[buf("xnf_dr")])
            bg = buf(f"ppg{par}")
            for dv in range(4):
                bbk = 3 + (dv % 2)
                prs = [(xfmp[par][:, ci * 128:(ci + 1) * 128],
                        wg_sb[:, ci * 2048 + dv * 512: ci * 2048 + (dv + 1) * 512])
                       for ci in range(8)]
                mmgroup(bank[bbk][:, 0:512], prs, [bxf, buf("wg")], bb_[bbk])
                p.op("act", lambda e, par=par, dv=dv, bbk=bbk: e.activation(
                    gsb[par][:, dv * 512:(dv + 1) * 512], bank[bbk][:, 0:512],
                    AF.Silu),
                    reads=[bb_[bbk]], writes=[bg])
            dma("act", "dact", g_t[t], gsb[par].ap(), rbufs=[bg],
                wbufs=[buf("g_dr")])
            mmgroup(bank[0][:, 0:E],
                    [(xfmp[par][:, ci * 128:(ci + 1) * 128],
                      wr_sb[:, 8 * ci:8 * ci + 8]) for ci in range(8)],
                    [bxf, buf("wr_sb")], bb_[0])
            p.op("dve", lambda e: e.tensor_add(lgb.ap(), bank[0][:, 0:E],
                                               rb_bc.ap()),
                 reads=[bb_[0], buf("rb_bc")], writes=[buf("pplg")])
            p.op("dve", lambda e: e.reduce_max(mrow.ap(), lgb.ap(), axis=AX),
                 reads=[buf("pplg")], writes=[buf("ppm")])
            p.op("dve", lambda e: e.tensor_scalar(lg2.ap(), lgb.ap(), mrow.ap(),
                                                  None, OP.is_equal),
                 reads=[buf("pplg"), buf("ppm")], writes=[buf("pplg2")])
            p.op("dve", lambda e: e.tensor_mul(lg2.ap(), lg2.ap(), io8m8.ap()),
                 reads=[buf("pplg2"), buf("io8m8")], writes=[buf("pplg2")])
            p.op("dve", lambda e: e.tensor_reduce(mrow.ap(), lg2.ap(), AX,
                                                  OP.min),
                 reads=[buf("pplg2")], writes=[buf("ppm")])
            p.op("dve", lambda e, t=t: e.tensor_scalar(
                eid_a[:, t:t + 1], mrow.ap(), 8.0, None, OP.add),
                reads=[buf("ppm")], writes=[eidb])
            p.op("dve", lambda e, t=t: e.tensor_scalar(
                ohbf.ap(), io8.ap(), eid_a[:, t:t + 1], None, OP.is_equal),
                reads=[buf("io8"), eidb], writes=[buf("ppoh")])
            mmgroup(bank[1][:, 0:E], [(utri.ap(), ohbf.ap())],
                    [buf("utri"), buf("ppoh")], bb_[1])
            mmgroup(bank[2][0:1, 0:E], [(onescol.ap(), ohbf.ap())],
                    [buf("onescol"), buf("ppoh")], bb_[2])
            p.op("dve", lambda e: e.tensor_tensor(lg2.ap(), bank[1][:, 0:E],
                                                  ohbf.ap(), OP.mult),
                 reads=[bb_[1], buf("ppoh")], writes=[buf("pplg2")])
            p.op("dve", lambda e: e.tensor_reduce(mrow.ap(), lg2.ap(), AX,
                                                  OP.add),
                 reads=[buf("pplg2")], writes=[buf("ppm")])
            p.op("dve", lambda e, t=t: e.tensor_scalar(
                lrank_a[:, t:t + 1], mrow.ap(), -1.0, None, OP.add),
                reads=[buf("ppm")], writes=[lrankb])
            p.op("dve", lambda e, t=t: e.tensor_copy(
                offs_row[:, E * t:E * (t + 1)], runtot.ap()),
                reads=[buf("runtot")], writes=[buf("offs_row")])
            p.op("dve", lambda e: e.tensor_add(runtot.ap(), runtot.ap(),
                                               bank[2][0:1, 0:E]),
                 reads=[buf("runtot"), bb_[2]], writes=[buf("runtot")])

    dma("act", "dact", bass.AP(offs_dr, 0, [[512, 1], [1, 512]]), offs_row.ap(),
        rbufs=[buf("offs_row")], wbufs=[buf("offs_dr")])
    dma("pool", "dpool", ccc_in[:, :], runtot.ap(), rbufs=[buf("runtot")],
        wbufs=[buf("ccc_in")])
    p.op("pool", lambda e: e.collective_compute(
        "AllGather", OP.bypass, replica_groups=[list(range(NCORES))],
        ins=[ccc_in.ap().opt()], outs=[ccc_out.ap().opt()]),
        reads=[buf("ccc_in")], writes=[buf("ccc_out")], cc="cc0")
    p.op("pool", lambda e: e.dma_start(
        counts_T.ap(), bass.AP(ccc_out, 0, [[1, E], [E, E]]),
        allow_slow_non_contiguous=True),
        reads=[buf("ccc_out")], writes=[buf("counts_T")], dma="q_counts_T")
    dma("pool", "dpool", cmrow.ap(),
        bass.AP(cmask_in, 0, [[0, E], [1, E]]), wbufs=[buf("cmrow")])
    p.op("pool", lambda e: e.tensor_mul(counts_T.ap(), counts_T.ap(), cmrow.ap()),
         reads=[buf("counts_T"), buf("cmrow")], writes=[buf("counts_T")])
    p.op("pool", lambda e: e.tensor_add(counts_T[:, 0:4], counts_T[:, 0:4],
                                        counts_T[:, 4:8]),
         reads=[buf("counts_T")], writes=[buf("counts_T")])
    p.op("pool", lambda e: e.tensor_add(counts_T[:, 0:2], counts_T[:, 0:2],
                                        counts_T[:, 2:4]),
         reads=[buf("counts_T")], writes=[buf("counts_T")])
    p.op("pool", lambda e: e.tensor_add(coff_col.ap(), counts_T[:, 0:1],
                                        counts_T[:, 1:2]),
         reads=[buf("counts_T")], writes=[buf("coff_col")])
    p.op("pool", lambda e: e.dma_start(
        bass.AP(coff_dr, 0, [[1, E], [1, 1]]), coff_col.ap(),
        allow_slow_non_contiguous=True),
        reads=[buf("coff_col")], writes=[buf("coff_dr")], dma="q_coff_dr")
    dma("pool", "dpool", coff_bc.ap(), bass.AP(coff_dr, 0, [[0, 128], [1, E]]),
        rbufs=[buf("coff_dr")], wbufs=[buf("coff_bc")])
    dma("pool", "dpool", offs_bc.ap(),
        bass.AP(offs_dr, 0, [[0, 128], [E, NT], [1, E]]),
        rbufs=[buf("offs_dr")], wbufs=[buf("offs_bc")])

    for t in range(NT):
        par = t % 2
        rb_ = buf(f"rxn{par}")
        dma("pool", "dpool", rxn[par].ap(), xn_t[t], rbufs=[buf("xn_dr")],
            wbufs=[rb_])
        p.op("pool", lambda e, t=t: e.tensor_scalar(
            r8.ap(), io8.ap(), eid_a[:, t:t + 1], None, OP.is_equal),
            reads=[buf("io8"), eidb], writes=[buf("r8")])
        p.op("pool", lambda e, t=t: e.tensor_add(
            r8b.ap(), coff_bc.ap(), offs_bc[:, E * t:E * (t + 1)]),
            reads=[buf("coff_bc"), buf("offs_bc")], writes=[buf("r8b")])
        p.op("pool", lambda e: e.tensor_mul(r8.ap(), r8.ap(), r8b.ap()),
             reads=[buf("r8"), buf("r8b")], writes=[buf("r8")])
        p.op("pool", lambda e: e.tensor_add(r8[:, 0:4], r8[:, 0:4], r8[:, 4:8]),
             reads=[buf("r8")], writes=[buf("r8")])
        p.op("pool", lambda e: e.tensor_add(r8[:, 0:2], r8[:, 0:2], r8[:, 2:4]),
             reads=[buf("r8")], writes=[buf("r8")])
        p.op("pool", lambda e: e.tensor_add(rc1.ap(), r8[:, 0:1], r8[:, 1:2]),
             reads=[buf("r8")], writes=[buf("rc1")])
        p.op("pool", lambda e, t=t: e.tensor_add(rc1.ap(), rc1.ap(),
                                                 lrank_a[:, t:t + 1]),
             reads=[buf("rc1"), lrankb], writes=[buf("rc1")])
        p.op("pool", lambda e, t=t: e.tensor_single_scalar(
            inv_a[:, t:t + 1], rc1.ap(), float(CAP), OP.is_ge),
            reads=[buf("rc1")], writes=[invb])
        p.op("pool", lambda e, t=t: e.tensor_scalar(
            rc2.ap(), eid_a[:, t:t + 1], float(CAP), None, OP.mult),
            reads=[eidb], writes=[buf("rc2")])
        p.op("pool", lambda e: e.tensor_add(rc2.ap(), rc2.ap(), rc1.ap()),
             reads=[buf("rc2"), buf("rc1")], writes=[buf("rc2")])
        p.op("pool", lambda e, t=t: e.tensor_scalar(
            rc3.ap(), iop.ap(), 1.0, OOBBASE + 128.0 * t, OP.mult, OP.add),
            reads=[buf("iop")], writes=[buf("rc3")])
        p.op("pool", lambda e: e.tensor_sub(rc3.ap(), rc3.ap(), rc2.ap()),
             reads=[buf("rc3"), buf("rc2")], writes=[buf("rc3")])
        p.op("pool", lambda e, t=t: e.tensor_scalar(
            rc3.ap(), rc3.ap(), inv_a[:, t:t + 1], None, OP.mult),
            reads=[buf("rc3"), invb], writes=[buf("rc3")])
        p.op("pool", lambda e: e.tensor_add(rc2.ap(), rc2.ap(), rc3.ap()),
             reads=[buf("rc2"), buf("rc3")], writes=[buf("rc2")])
        p.op("pool", lambda e, t=t: e.tensor_copy(idx_a[:, t:t + 1], rc2.ap()),
             reads=[buf("rc2")], writes=[idxb])
        p.op("pool", lambda e, t=t, par=par: e.indirect_dma_start(
            out=bufg_dr.ap(),
            out_offset=bass.IndirectOffsetOnAxis(idx_a[:, t:t + 1], 0),
            in_=rxn[par].ap(), in_offset=None,
            bounds_check=bc_reg, oob_is_err=False),
            reads=[rb_, idxb, buf("bufg")],
            writes=[buf("bufg")], dma=f"q_sc{t % 2}")
    p.wait_all("pool", ["q_sc0", "q_sc1"])
    p.op("pool", lambda e: e.collective_compute(
        "ReduceScatter", OP.add, replica_groups=[list(range(NCORES))],
        ins=[bufg_dr.ap().opt()], outs=[bufc_dr.ap().opt()]),
        reads=[buf("bufg")], writes=[buf("bufc")], cc="cc1")

    p.phase_join(engines=("pe", "dve", "act", "sp"),
                 dmas=[f"q_ppx{i}" for i in range(8)]
                 + ["q_xn_dr", "q_xnf_dr", "q_g_dr", "q_bufg"])
    ppctx.close()
    Ssb = [[S1(f"S{b}_{h}", [128, 2 * DV], F32) for h in range(NH)]
           for b in range(2)]
    Sbf = [[S1(f"Sb{b}_{h}", [128, 2 * DV], BF16) for h in range(NH)]
           for b in range(2)]
    p1xf = [S1(f"p1xf{i}", [128, 2048], BF16) for i in range(2)]
    qfm = [S1(f"qfm{i}", [128, 2048], BF16) for i in range(2)]
    kfm = [S1(f"kfm{i}", [128, 2048], BF16) for i in range(2)]
    vsb = [S1(f"vsb{i}", [128, 2 * 2048], BF16) for i in range(2)]
    att4 = [S1(f"att4_{i}", [128, 512], BF16) for i in range(2)]
    kckb = [S1(f"kck{i}", [128, DK], BF16) for i in range(2 * NH)]
    qqb = [S1(f"qq{i}", [128, DK], BF16) for i in range(2 * NH)]
    rt1 = S1("rt1", [128, 512], F32)
    rt2 = S1("rt2", [128, 512], F32)
    oscraw = [S1(f"oscr{i}", [128, 2048], BF16) for i in range(2)]
    ccsl = [S1(f"ccsl{i}", [128, 512], BF16) for i in range(2)]
    sssl = [S1(f"sssl{i}", [128, 512], BF16) for i in range(2)]
    ovar8 = S1("ovar8", [128, 8], F32)
    rstd8 = S1("rstd8", [128, 8], F32)
    vv8p = S1("vv8p", [128, 8], F32)
    n8 = [S1(f"n8_{i}", [128, 8], F32) for i in range(2)]
    n8i = S1("n8i", [128, 8], I32)
    osq = S1("osq", [128, DV], BF16)

    def wblk(wsb, ci, j):
        return wsb[:, ci * D + j * 128: ci * D + (j + 1) * 128]

    _rr = [0]

    def nxt():
        _rr[0] = (_rr[0] + 1) % 8
        return _rr[0]

    def emit_qkv(g2):
        bat, gg = g2 % 2, g2 // 2
        gp = bat
        sl = gg % 2
        ta = bat * NTB + 2 * gg
        tb = ta + 1
        if bat == 0:
            dma("sp", "dsp", ccsl[sl].ap(), cc2_in[:, gg * 512:(gg + 1) * 512],
                wbufs=[buf(f"ccsl{sl}")])
            dma("sp", "dsp", sssl[sl].ap(), ss2_in[:, gg * 512:(gg + 1) * 512],
                wbufs=[buf(f"sssl{sl}")])
        bxf = buf(f"p1xf{gp}")
        for (ti, a) in ((ta, 0), (tb, 1)):
            p.op("sp", lambda e, gp=gp, a=a, ti=ti: e.dma_start(
                bass.AP(p1xf[gp], a * 128, [[2048, 128], [256, 8], [1, 128]]),
                bass.AP(xnf_dr, ti * 128 * D, [[D, 128], [128, 8], [1, 128]])),
                reads=[buf("xnf_dr")], writes=[bxf], dma=f"q_p1xf{gp}")
        for (wsb, wb_name, dstl, dname) in ((wq_sb, "wq", qfm, "qfm"),
                                            (wk_sb, "wk", kfm, "kfm")):
            dst = dstl[gp]
            bdst = buf(f"{dname}{gp}")
            for h in range(NH):
                bbk = nxt()
                prs = []
                for half in range(2):
                    j = 2 * h + half
                    prs += [(wblk(wsb, ci, j),
                             p1xf[gp][:, ci * 256:(ci + 1) * 256])
                            for ci in range(8)]

                def emitq(e, prs=prs, bbk=bbk):
                    for i, (l, r) in enumerate(prs):
                        ins = e.matmul(bank[bbk][:, (i // 8) * 256:
                                                 (i // 8) * 256 + 256],
                                       l, r, start=(i % 8 == 0),
                                       stop=(i % 8 == 7))
                    return ins
                p.op("pe", emitq, reads=[bxf, buf(wb_name)], writes=[bb_[bbk]])
                bcs, bsn = buf(f"ccsl{sl}"), buf(f"sssl{sl}")
                bk = bb_[bbk]
                p.op("dve", lambda e, bbk=bbk, sl=sl: e.tensor_tensor(
                    rt1.ap(), bank[bbk][:, 0:512], ccsl[sl].ap(), OP.mult),
                    reads=[bk, bcs], writes=[buf("rt1")])
                p.op("dve", lambda e, bbk=bbk, sl=sl: e.tensor_tensor(
                    rt2.ap(), bank[bbk][:, 0:512], sssl[sl].ap(), OP.mult),
                    reads=[bk, bsn], writes=[buf("rt2")])
                p.op("dve", lambda e, dst=dst, h=h: e.tensor_sub(
                    dst[:, (2 * h) * 256:(2 * h) * 256 + 256],
                    rt1[:, 0:256], rt2[:, 256:512]),
                    reads=[buf("rt1"), buf("rt2")], writes=[bdst])
                p.op("dve", lambda e, dst=dst, h=h: e.tensor_sub(
                    dst[:, (2 * h + 1) * 256:(2 * h + 1) * 256 + 256],
                    rt1[:, 256:512], rt2[:, 0:256]),
                    reads=[buf("rt1"), buf("rt2")], writes=[bdst])
        for a in range(2):
            for dv in range(4):
                bbk = nxt()
                prs = [(p1xf[gp][:, ci * 256 + a * 128: ci * 256 + a * 128 + 128],
                        wv_sb[:, ci * 2048 + dv * 512: ci * 2048 + (dv + 1) * 512])
                       for ci in range(8)]
                mmgroup(bank[bbk][:, 0:512], prs, [bxf, buf("wv")], bb_[bbk])
                p.op("act", lambda e, gp=gp, a=a, dv=dv, bbk=bbk: e.copy(
                    vsb[gp][:, a * 2048 + dv * 512: a * 2048 + (dv + 1) * 512],
                    bank[bbk][:, 0:512]),
                    reads=[bb_[bbk]], writes=[buf(f"vsb{gp}")])
    def emit_ret(g2):
        bat, gg = g2 % 2, g2 // 2
        gp = bat
        ta = bat * NTB + 2 * gg
        tb = ta + 1
        for a in range(2):
            for h in range(NH):
                hb = a * NH + h
                for half in range(2):
                    p.op("dve", lambda e, gp=gp, a=a, h=h, half=half, hb=hb:
                         e.tensor_tensor(
                        qqb[hb][:, half * 128:(half + 1) * 128],
                        qfm[gp][:, (2 * h + half) * 256 + a * 128:
                                (2 * h + half) * 256 + a * 128 + 128],
                        CQt[:, h * 256 + half * 128:
                            h * 256 + (half + 1) * 128], OP.mult),
                        reads=[buf(f"qfm{gp}"), buf("CQt")],
                        writes=[buf(f"qq{hb}")])
        for a, ti in ((0, ta), (1, tb)):
            c = ti % NTB
            batt = buf(f"att4_{a}")
            bk_at = nxt()

            def emit_attn(e, gp=gp, a=a, bk_at=bk_at):
                for h in range(NH):
                    for half in range(2):
                        ins = e.matmul(
                            bank[bk_at][:, h * 128:(h + 1) * 128],
                            kfm[gp][:, (2 * h + half) * 256 + a * 128:
                                    (2 * h + half) * 256 + a * 128 + 128],
                            qfm[gp][:, (2 * h + half) * 256 + a * 128:
                                    (2 * h + half) * 256 + a * 128 + 128],
                            start=(half == 0), stop=(half == 1))
                return ins
            p.op("pe", emit_attn, reads=[buf(f"kfm{gp}"), buf(f"qfm{gp}")],
                 writes=[bb_[bk_at]])
            p.op("dve", lambda e, a=a, bk_at=bk_at: e.tensor_tensor(
                att4[a].ap(), bank[bk_at][:, 0:512], DTt[:, 0:512], OP.mult),
                reads=[bb_[bk_at], buf("DTt")], writes=[batt])
            for h in range(NH):
                hb = a * NH + h
                bqq = buf(f"qq{hb}")
                bSb = buf(f"Sb{bat}_{h}")
                b_o = nxt()

                def emit_o(e, hb=hb, gp=gp, a=a, h=h, b_o=b_o, c=c, bat=bat):
                    ins = e.matmul(bank[b_o][:, 0:512],
                                   att4[a][:, h * 128:(h + 1) * 128],
                                   vsb[gp][:, a * 2048 + h * 512:
                                           a * 2048 + (h + 1) * 512],
                                   start=True, stop=(c == 0))
                    if c > 0:
                        for half in range(2):
                            ins = e.matmul(
                                bank[b_o][:, 0:512],
                                qqb[hb][:, half * 128:(half + 1) * 128],
                                Sbf[bat][h][:, half * DV:(half + 1) * DV],
                                start=False, stop=(half == 1))
                    return ins
                rds = [batt, buf(f"vsb{gp}")] + ([bqq, bSb] if c > 0 else [])
                p.op("pe", emit_o, reads=rds, writes=[bb_[b_o]])
                p.op("act", lambda e, a=a, h=h, b_o=b_o: e.activation(
                    osq.ap(), bank[b_o][:, 0:512], AF.Square,
                    accum_out=ovar8[:, a * 4 + h:a * 4 + h + 1]),
                    reads=[bb_[b_o]], writes=[buf("osq"), buf("ovar8")])
                p.op("act", lambda e, a=a, h=h, b_o=b_o: e.copy(
                    oscraw[a][:, h * 512:(h + 1) * 512], bank[b_o][:, 0:512]),
                    reads=[bb_[b_o]], writes=[buf(f"oscr{a}")])
            if c < NTB - 1:
                bk_t = nxt()

                def emit_kt(e, gp=gp, a=a, bk_t=bk_t):
                    for h in range(NH):
                        for half in range(2):
                            j2 = 2 * h + half
                            ins = e.transpose(
                                bankB[bk_t][:, j2 * 128:(j2 + 1) * 128],
                                kfm[gp][:, j2 * 256 + a * 128:
                                        j2 * 256 + a * 128 + 128],
                                iden.ap())
                    return ins
                p.op("pe", emit_kt, reads=[buf(f"kfm{gp}"), buf("iden")],
                     writes=[bb_[bk_t]])
                for h in range(NH):
                    hb = a * NH + h
                    p.op("act", lambda e, hb=hb, h=h, bk_t=bk_t: e.mul(
                        kckb[hb].ap(), bankB[bk_t][:, h * 256:(h + 1) * 256],
                        CKt[:, h:h + 1]),
                        reads=[bb_[bk_t], buf("CKt")], writes=[buf(f"kck{hb}")])
                for h in range(NH):
                    hb = a * NH + h
                    bkck = buf(f"kck{hb}")
                    bS = buf(f"S{bat}_{h}")
                    bSb = buf(f"Sb{bat}_{h}")
                    gC = GAMMAS[h] ** CH
                    ubanks = [nxt(), nxt()]
                    for half in range(2):
                        b_u = ubanks[half]
                        p.op("pe", lambda e, hb=hb, gp=gp, a=a, h=h, half=half,
                             b_u=b_u: e.matmul(
                            bank[b_u][:, 0:512],
                            kckb[hb][:, half * 128:(half + 1) * 128],
                            vsb[gp][:, a * 2048 + h * 512:
                                    a * 2048 + (h + 1) * 512],
                            start=True, stop=True),
                            reads=[bkck, buf(f"vsb{gp}")], writes=[bb_[b_u]])
                    for half in range(2):
                        b_u = ubanks[half]
                        if c == 0:
                            p.op("dve", lambda e, h=h, half=half, b_u=b_u,
                                 bat=bat: e.tensor_copy(
                                Ssb[bat][h][:, half * DV:(half + 1) * DV],
                                bank[b_u][:, 0:512]),
                                reads=[bb_[b_u]], writes=[bS])
                        else:
                            p.op("dve", lambda e, h=h, half=half, b_u=b_u,
                                 gC=gC, bat=bat: e.scalar_tensor_tensor(
                                Ssb[bat][h][:, half * DV:(half + 1) * DV],
                                Ssb[bat][h][:, half * DV:(half + 1) * DV], gC,
                                bank[b_u][:, 0:512], OP.mult, OP.add),
                                reads=[bb_[b_u], bS], writes=[bS])
                    p.op("act", lambda e, h=h, bat=bat: e.copy(
                        Sbf[bat][h].ap(), Ssb[bat][h].ap()),
                        reads=[bS], writes=[bSb])
        p.op("dve", lambda e: e.tensor_scalar(
            vv8p.ap(), ovar8.ap(), 1.0 / DV, EPS, OP.mult, OP.add),
            reads=[buf("ovar8")], writes=[buf("vv8p")])
        class _H8:
            pass
        s0 = _H8(); s0.ap = n8[0].ap; s0.handle_name = "n8_0"
        s1 = _H8(); s1.ap = n8[1].ap; s1.handle_name = "n8_1"
        si = _H8(); si.ap = n8i.ap; si.handle_name = "n8i"
        newton8(vv8p.ap(), rstd8.ap(), buf("vv8p"), buf("rstd8"), s0, s1, si)
        for a, ti in ((0, ta), (1, tb)):
            for h in range(NH):
                col = a * 4 + h
                p.op("dve", lambda e, a=a, h=h, col=col: e.tensor_scalar(
                    oscraw[a][:, h * 512:(h + 1) * 512],
                    oscraw[a][:, h * 512:(h + 1) * 512],
                    rstd8[:, col:col + 1], None, OP.mult),
                    reads=[buf(f"oscr{a}"), buf("rstd8")],
                    writes=[buf(f"oscr{a}")])
            dma("sp", "dsp", osc_t[ti], oscraw[a].ap(),
                rbufs=[buf(f"oscr{a}")], wbufs=[buf("osc_dr")])

    for g2 in range(NT // 2):
        emit_qkv(g2)
        if g2 >= 1:
            emit_ret(g2 - 1)
    emit_ret(NT // 2 - 1)
    c1.close()

    p.phase_join()

    c3 = ExitStack()
    S3 = lambda n, sh, d: c3.enter_context(nc.sbuf_tensor(n, sh, d))
    wo_sb = S3("wo_sb", [128, 16 * D], BF16)
    wos = S3("wos", [128, 512], F32)
    gnw_sb = S3("gnw_sb", [128, 16], F32)
    dma("sp", "dsp", gnw_sb.ap(), gnw16_in[:, :], wbufs=[buf("gnw")])
    wo_r2 = wo_in.ap().rearrange("(c k) (m n) -> c m k n", k=128, n=512)
    for ci in range(16):
        for m in range(2):
            dma("sp", "dsp", wos.ap(), wo_r2[ci, m], rbufs=[buf("wo_pre")],
                wbufs=[buf("wo_pre")])
            p.op("act", lambda e, ci=ci, m=m: e.mul(
                wo_sb[:, ci * D + m * 512:ci * D + (m + 1) * 512], wos.ap(),
                gnw_sb[:, ci:ci + 1]),
                reads=[buf("wo_pre"), buf("gnw")], writes=[buf("wo")])

    c2 = ExitStack()
    S2 = lambda n, sh, d: c2.enter_context(nc.sbuf_tensor(n, sh, d))
    w1_sb = S2("w1_sb", [128, 8 * HID], BF16)
    w2_sb = S2("w2_sb", [128, 32 * D], BF16)
    b1_sb = S2("b1_sb", [128, 32], F32)
    b2bc = S2("b2bc", [128, D], BF16)
    p2x = [S2(f"p2x{i}", [128, D], BF16) for i in range(2)]
    p2xf = S2("p2xf", [128, 2048], BF16)
    hfm = S2("hfm", [128, 32 * 256], BF16)
    ycsb = [S2(f"ycsb{i}", [128, D], BF16) for i in range(2)]

    dma("pool", "dpool", w1_sb.ap(),
        rowchunks(w1c_in, HID, 8), wbufs=[buf("w1")])
    dma("pool", "dpool", w2_sb.ap(),
        rowchunks(w2c_in, D, 32), wbufs=[buf("w2")])
    dma("sp", "dsp", b1_sb.ap(), b1c_in[:, :], wbufs=[buf("b1")])
    dma("sp", "dsp", b2bc.ap(), bass.AP(b2c_in, 0, [[0, 128], [1, D]]),
        wbufs=[buf("b2bc")])

    for gi in range(5):
        bxf = buf("p2xf")
        for a in range(2):
            dma("sp", "dsp", p2x[a].ap(), bufc_t[2 * gi + a],
                rbufs=[buf("bufc")], wbufs=[buf(f"p2x{a}")])
            bbk = 4 + a
            for ci in range(8):
                p.op("pe", lambda e, a=a, ci=ci, bbk=bbk: e.transpose(
                    bankB[bbk][:, ci * 128:(ci + 1) * 128],
                    p2x[a][:, ci * 128:(ci + 1) * 128], iden.ap()),
                    reads=[buf(f"p2x{a}"), buf("iden")], writes=[bb_[bbk]])
            p.op("dve", lambda e, a=a, bbk=bbk: e.tensor_copy(
                bass.AP(p2xf, a * 128, [[2048, 128], [256, 8], [1, 128]]),
                bankB[bbk][:, 0:1024]),
                reads=[bb_[bbk]], writes=[bxf])
        hbufs = []
        for j in range(32):
            bbk = j % 4
            prs = [(w1_sb[:, ci * HID + j * 128: ci * HID + (j + 1) * 128],
                    p2xf[:, ci * 256:(ci + 1) * 256]) for ci in range(8)]
            mmgroup(bank[bbk][:, 0:256], prs, [bxf, buf("w1")], bb_[bbk])
            hb = buf(f"hf2_{j}")
            hbufs.append(hb)
            p.op("act", lambda e, j=j, bbk=bbk: e.activation(
                hfm[:, j * 256:(j + 1) * 256], bank[bbk][:, 0:256], AF.Relu,
                bias=b1_sb[:, j:j + 1]),
                reads=[bb_[bbk], buf("b1")], writes=[hb])
        for ts in range(2):
            for dh in range(2):
                bbk = 4 + ts * 2 + dh

                def emity(e, ts=ts, dh=dh, bbk=bbk):
                    for j in range(32):
                        ins = e.matmul(
                            bank[bbk][:, 0:512],
                            hfm[:, j * 256 + ts * 128: j * 256 + ts * 128 + 128],
                            w2_sb[:, j * D + dh * 512: j * D + (dh + 1) * 512],
                            start=(j == 0), stop=(j == 31))
                    return ins
                p.op("pe", emity, reads=hbufs + [buf("w2")], writes=[bb_[bbk]])
            for dh in range(2):
                bbk = 4 + ts * 2 + dh
                p.op("dve", lambda e, ts=ts, dh=dh, bbk=bbk: e.tensor_tensor(
                    ycsb[ts][:, dh * 512:(dh + 1) * 512], bank[bbk][:, 0:512],
                    b2bc[:, dh * 512:(dh + 1) * 512], OP.add),
                    reads=[bb_[bbk], buf("b2bc")], writes=[buf(f"ycsb{ts}")])
        for a in range(2):
            dma("sp", "dsp", yc_t[2 * gi + a], ycsb[a].ap(),
                rbufs=[buf(f"ycsb{a}")], wbufs=[buf("yc_dr")])
    p.op("pool", lambda e: e.collective_compute(
        "AllGather", OP.bypass, replica_groups=[list(range(NCORES))],
        ins=[yc_dr.ap().opt()], outs=[yfull_dr.ap().opt()]),
        reads=[buf("yc_dr")], writes=[buf("yfull")], cc="cc2")
    c2.close()

    p.phase_join()

    c3b = ExitStack()
    S3b = lambda n, sh, d: c3b.enter_context(nc.sbuf_tensor(n, sh, d))
    a3g = [S3b(f"a3g{i}", [128, 2048], BF16) for i in range(2)]
    a3osc = [S3b(f"a3osc{i}", [128, 2048], BF16) for i in range(2)]
    a3og = [S3b(f"a3og{i}", [128, 2048], BF16) for i in range(2)]
    a3ogf = [S3b(f"a3ogf{i}", [128, 2048], BF16) for i in range(2)]
    a3s = [S3b(f"a3s{i}", [128, D], BF16) for i in range(2)]

    for g in range(NT // 2):
        ta, tb = 2 * g, 2 * g + 1
        for (ti, a) in ((ta, 0), (tb, 1)):
            dma("act", "dact", a3g[a].ap(), g_t[ti], rbufs=[buf("g_dr")],
                wbufs=[buf(f"a3g{a}")])
            dma("pool", "dpool", a3osc[a].ap(), osc_t[ti],
                rbufs=[buf("osc_dr")], wbufs=[buf(f"a3osc{a}")])
        for a, ti in ((0, ta), (1, tb)):
            p.op("dve", lambda e, a=a: e.tensor_mul(
                a3og[a].ap(), a3osc[a].ap(), a3g[a].ap()),
                reads=[buf(f"a3osc{a}"), buf(f"a3g{a}")],
                writes=[buf(f"a3og{a}")])
            for half in range(2):
                bbk = 4 + ((a * 2 + half) % 2)
                for ci in range(8):
                    cc = half * 8 + ci
                    p.op("pe", lambda e, a=a, cc=cc, ci=ci, bbk=bbk: e.transpose(
                        bankB[bbk][:, ci * 128:(ci + 1) * 128],
                        a3og[a][:, cc * 128:(cc + 1) * 128], iden.ap()),
                        reads=[buf(f"a3og{a}"), buf("iden")], writes=[bb_[bbk]])
                p.op("dve", lambda e, a=a, half=half, bbk=bbk: e.tensor_copy(
                    a3ogf[a][:, half * 1024:(half + 1) * 1024],
                    bankB[bbk][:, 0:1024]),
                    reads=[bb_[bbk]], writes=[buf(f"a3ogf{a}")])
            for dh in range(2):
                bbk = 6 + dh

                def emits(e, a=a, dh=dh, bbk=bbk):
                    for ci in range(16):
                        ins = e.matmul(
                            bank[bbk][:, 0:512],
                            a3ogf[a][:, ci * 128:(ci + 1) * 128],
                            wo_sb[:, ci * D + dh * 512: ci * D + (dh + 1) * 512],
                            start=(ci == 0), stop=(ci == 15))
                    return ins
                p.op("pe", emits, reads=[buf(f"a3ogf{a}"), buf("wo")],
                     writes=[bb_[bbk]])
                p.op("dve", lambda e, a=a, dh=dh, bbk=bbk: e.tensor_copy(
                    a3s[a][:, dh * 512:(dh + 1) * 512], bank[bbk][:, 0:512]),
                    reads=[bb_[bbk]], writes=[buf(f"a3s{a}")])
            dma("sp", "dsp", s_t[ti], a3s[a].ap(), rbufs=[buf(f"a3s{a}")],
                wbufs=[buf("s_dr")])
    c3b.close()
    c3.close()

    p.phase_join()

    c4 = ExitStack()
    S4 = lambda n, sh, d: c4.enter_context(nc.sbuf_tensor(n, sh, d))
    o1_sb = S4("o1_sb", [128, 8 * HID], BF16)
    o2_sb = S4("o2_sb", [128, 32 * D], BF16)
    ob1_sb = S4("ob1_sb", [128, 32], F32)
    ob2bc = S4("ob2bc", [128, D], BF16)
    b4x = [S4(f"b4x{i}", [128, D], F32) for i in range(2)]
    b4xf = S4("b4xf", [128, 4096], BF16)
    hfm4 = S4("hfm4", [128, 32 * 512], BF16)
    yov = S4("yov", [128, D], F32)
    b4s = [S4(f"b4s{i}", [128, D], BF16) for i in range(2)]
    b4g = S4("b4g", [128, D], BF16)
    b4o = [S4("b4o0", [128, D], F32)]

    dma("pool", "dpool", o1_sb.ap(),
        rowchunks(ow1_in, HID, 8), wbufs=[buf("o1")])
    dma("pool", "dpool", o2_sb.ap(),
        rowchunks(ow2_in, D, 32), wbufs=[buf("o2")])
    dma("sp", "dsp", ob1_sb.ap(), ob1_in[:, :], wbufs=[buf("ob1")])
    dma("sp", "dsp", ob2bc.ap(), bass.AP(ob2_in, 0, [[0, 128], [1, D]]),
        wbufs=[buf("ob2bc")])

    def b4_loads(ti, a2):
        dma("pool", "dpool", b4x[a2].ap(), x_t[ti], wbufs=[buf(f"b4x{a2}")])
        dma("pool", "dpool", b4s[a2].ap(), s_t[ti], rbufs=[buf("s_dr")],
            wbufs=[buf(f"b4s{a2}")])
        p.op("pool", lambda e, a2=a2: e.memset(ytok[a2].ap(), 0),
             reads=[], writes=[buf(f"ytok{a2}")])
        p.op("pool", lambda e, a2=a2, ti=ti: e.indirect_dma_start(
            out=ytok[a2].ap(), out_offset=None,
            in_=yfull_dr.ap(),
            in_offset=bass.IndirectOffsetOnAxis(idx_a[:, ti:ti + 1], 0),
            bounds_check=bc_reg, oob_is_err=False),
            reads=[buf("yfull"), idxb, buf(f"ytok{a2}")],
            writes=[buf(f"ytok{a2}")], dma=f"q_ytok{a2}")

    def b4_combine(ti, a2):
        p.op("dve", lambda e, ti=ti: e.tensor_scalar(
            yov.ap(), yov.ap(), inv_a[:, ti:ti + 1], None, OP.mult),
            reads=[buf("yov"), invb], writes=[buf("yov")])
        p.op("dve", lambda e, a2=a2: e.tensor_add(
            yov.ap(), yov.ap(), ytok[a2].ap()),
            reads=[buf("yov"), buf(f"ytok{a2}")], writes=[buf("yov")])
        p.op("act", lambda e: e.activation(b4g.ap(), yov.ap(), AF.Silu),
             reads=[buf("yov")], writes=[buf("b4g")])
        p.op("dve", lambda e, a2=a2: e.tensor_mul(
            b4g.ap(), b4g.ap(), b4s[a2].ap()),
            reads=[buf("b4g"), buf(f"b4s{a2}")], writes=[buf("b4g")])
        p.op("dve", lambda e, a2=a2: e.tensor_add(
            b4o[0].ap(), b4g.ap(), b4x[a2].ap()),
            reads=[buf("b4g"), buf(f"b4x{a2}")], writes=[buf("b4o0")])
        dma("sp", "dsp", out_t[ti], b4o[0].ap(), rbufs=[buf("b4o0")],
            wbufs=[buf("outw")])

    for g in range(NT // 4):
        t0 = 4 * g
        bxf = buf("b4xf")
        for a in range(4):
            p.op("sp", lambda e, a=a, ti=t0 + a: e.dma_start(
                bass.AP(b4xf, a * 128, [[4096, 128], [512, 8], [1, 128]]),
                bass.AP(xnf_dr, ti * 128 * D, [[D, 128], [128, 8], [1, 128]])),
                reads=[buf("xnf_dr")], writes=[bxf], dma="q_b4xf")
        for a in range(2):
            b4_loads(t0 + a, a)
        hbufs = []
        for j in range(32):
            bbk = j % 4
            prs = [(o1_sb[:, ci * HID + j * 128: ci * HID + (j + 1) * 128],
                    b4xf[:, ci * 512:(ci + 1) * 512]) for ci in range(8)]
            mmgroup(bank[bbk][:, 0:512], prs, [bxf, buf("o1")], bb_[bbk])
            hb = buf(f"h4_{j}")
            hbufs.append(hb)
            p.op("act", lambda e, j=j, bbk=bbk: e.activation(
                hfm4[:, j * 512:(j + 1) * 512], bank[bbk][:, 0:512],
                AF.Relu, bias=ob1_sb[:, j:j + 1]),
                reads=[bb_[bbk], buf("ob1")], writes=[hb])
        for ts in range(4):
            a2 = ts % 2
            for dh in range(2):
                bbk = 4 + (ts * 2 + dh) % 4

                def emity(e, ts=ts, dh=dh, bbk=bbk):
                    for j in range(32):
                        ins = e.matmul(
                            bank[bbk][:, 0:512],
                            hfm4[:, j * 512 + ts * 128:
                                 j * 512 + ts * 128 + 128],
                            o2_sb[:, j * D + dh * 512: j * D + (dh + 1) * 512],
                            start=(j == 0), stop=(j == 31))
                    return ins
                p.op("pe", emity, reads=hbufs + [buf("o2")], writes=[bb_[bbk]])
            for dh in range(2):
                bbk = 4 + (ts * 2 + dh) % 4
                p.op("dve", lambda e, dh=dh, bbk=bbk: e.tensor_tensor(
                    yov[:, dh * 512:(dh + 1) * 512], bank[bbk][:, 0:512],
                    ob2bc[:, dh * 512:(dh + 1) * 512], OP.add),
                    reads=[bb_[bbk], buf("ob2bc")], writes=[buf("yov")])
            b4_combine(t0 + ts, a2)
            if ts == 1:
                for a in range(2):
                    b4_loads(t0 + 2 + a, a)
    c4.close()

    if debug:
        dma("sp", "dsp", dbg["xn"].ap(), xn_dr.ap(), rbufs=[buf("xn_dr")])
        dma("sp", "dsp", dbg["eid"].ap(), eid_a.ap(), rbufs=[eidb])
        dma("sp", "dsp", dbg["lrank"].ap(), lrank_a.ap(), rbufs=[lrankb])
        dma("sp", "dsp", dbg["inv"].ap(), inv_a.ap(), rbufs=[invb])
        dma("sp", "dsp", dbg["idx"].ap(), idx_a.ap(), rbufs=[idxb])
        dma("sp", "dsp", dbg["osc"].ap(), osc_dr.ap(), rbufs=[buf("osc_dr")])
        dma("sp", "dsp", dbg["s"].ap(), s_dr.ap(), rbufs=[buf("s_dr")])
        dma("sp", "dsp", dbg["yfull"].ap(), yfull_dr.ap(), rbufs=[buf("yfull")])
        dma("sp", "dsp", dbg["bufc"].ap(), bufc_dr.ap(), rbufs=[buf("bufc")])
        dma("act", "dact", dbg["coff"].ap(), coff_dr.ap(), rbufs=[buf("coff_dr")])
        dma("sp", "dsp", dbg["rstd"].ap(), rstd_a.ap(), rbufs=[statsb])
        dma("sp", "dsp", dbg["nmr"].ap(), nmr_a.ap(), rbufs=[statsb])

    p.wait_all("sp", [s for s in p.cnt if s.startswith("q_")
                      or s.startswith("cc") or s in ("dact", "dpool", "dsp")])

    p.emit(sems)
    stack.close()
    return nc


_PROG_CACHE = {}


def _get_program(debug=False):
    if debug not in _PROG_CACHE:
        _PROG_CACHE[debug] = build_program(debug=debug)
    return _PROG_CACHE[debug]


def make_in_maps(inputs):
    t = _tables()
    x = np.asarray(inputs["L0_attn_x"], np.float32).reshape(16 * 4096, D)
    w1 = np.asarray(inputs["L0_ffn_w1"], np.float32)
    b1 = np.asarray(inputs["L0_ffn_b1"], np.float32)
    w2 = np.asarray(inputs["L0_ffn_w2"], np.float32)
    b2 = np.asarray(inputs["L0_ffn_b2"], np.float32)
    gn = np.asarray(inputs["L0_attn_gn_weight"], np.float32)
    gnw16 = np.ascontiguousarray(
        np.tile(gn, NH).reshape(16, 128).T)
    shared = dict(
        wq=np.ascontiguousarray(inputs["L0_attn_Wq"], np.float32),
        wk=np.ascontiguousarray(inputs["L0_attn_Wk"], np.float32),
        wv=np.ascontiguousarray(inputs["L0_attn_Wv"], np.float32),
        wg=np.ascontiguousarray(inputs["L0_attn_Wg"], np.float32),
        wo=np.ascontiguousarray(inputs["L0_attn_Wo"], np.float32),
        wr=np.ascontiguousarray(inputs["L0_ffn_router_w"], np.float32),
        rb=np.ascontiguousarray(inputs["L0_ffn_router_b"],
                                np.float32).reshape(1, E),
        gnw16=gnw16,
        ow1=np.ascontiguousarray(inputs["L0_ffn_ow1"], np.float32),
        ow2=np.ascontiguousarray(inputs["L0_ffn_ow2"], np.float32),
        ob1=np.ascontiguousarray(
            np.asarray(inputs["L0_ffn_ob1"], np.float32).reshape(32, 128).T),
        ob2=np.ascontiguousarray(inputs["L0_ffn_ob2"],
                                 np.float32).reshape(1, D).astype(
                                     ml_dtypes.bfloat16),
        cc2T=t["cc2T"], ss2T=t["ss2T"], DT=t["DT"], CQ=t["CQ"], CK=t["CK"],
        utri=t["utri"], iden=t["iden"], onescol=t["onescol"],
    )
    shared["io8_in"] = np.broadcast_to(np.arange(E, dtype=np.float32),
                                       (128, E)).copy()
    shared["io8m8_in"] = np.broadcast_to(np.arange(-8, 0, dtype=np.float32),
                                         (128, E)).copy()
    shared["iop_in"] = np.arange(128, dtype=np.float32).reshape(128, 1).copy()
    in_maps = []
    for c in range(NCORES):
        m = dict(shared)
        m["x"] = np.ascontiguousarray(x[c * TPB:(c + 1) * TPB])
        m["w1c"] = np.ascontiguousarray(w1[c])
        m["w2c"] = np.ascontiguousarray(w2[c])
        m["b1c"] = np.ascontiguousarray(b1[c].reshape(32, 128).T)
        m["b2c"] = np.ascontiguousarray(b2[c].reshape(1, D)).astype(ml_dtypes.bfloat16)
        cm = np.zeros((E, 1), np.float32)
        cm[:c] = 1.0
        m["cmask"] = cm
        in_maps.append(m)
    return in_maps


def kernel(**inputs):
    nc = _get_program(debug=False)
    in_maps = make_in_maps(inputs)
    res = run_bass_kernel_spmd(nc, in_maps, core_ids=list(range(NCORES)))
    out = np.concatenate([np.asarray(r["out"]) for r in res.results], axis=0)
    return out.reshape(16, 4096, D)


# version 16
# speedup vs baseline: 1.1375x; 1.0010x over previous
import numpy as np
import ml_dtypes

import concourse.bass as bass
import concourse.mybir as mybir
import concourse.bass_utils as _bu
from concourse.bass_utils import run_bass_kernel_spmd

F32 = mybir.dt.float32
BF16 = mybir.dt.bfloat16
I32 = mybir.dt.int32
U32 = mybir.dt.uint32
AF = mybir.ActivationFunctionType
OP = mybir.AluOpType
AX = mybir.AxisListType.X

NCORES = 8
D = 1024
HID = 4096
NH = 4
DK = 256
DV = 512
CH = 128
TPB = 8192
NT = TPB // CH
NTB = 32
E = 8
CAP = 1280
EPS = 1e-5
QSCALE = 0.0625
MAGIC = 0x5F3759DF
OOBBASE = 20000.0

GAMMAS = [1.0 - 2.0 ** (-5.0 - h) for h in range(NH)]

def _bvo_nobirsim(tmpdir, inp="bir.json", outp="file.neff", arch=None, *,
                  dve_root=None):
    from pathlib import Path
    from concourse.bass_utils import (get_walrus_driver, get_walrus_args,
                                      get_bir_arch, run_command)
    cmd = [
        get_walrus_driver(),
        "--pass", ",".join(["birverifier", "runtime_memory_reservation",
                            "lower_act", "lower_dve", "lower_ap_offset",
                            "codegen", "neff_packager"]),
        "-i", inp,
        "--neff-output-filename", outp,
        "--enable-birsim=false",
        "--mem-mode=physical", "--policy=0", "--enable-ldw-opt=false",
        "--assign-static-dmas-to-sp=false",
        "--dram-page-size=256", "--enable-neff-debug-info=true",
        "--jobs", "8",
        *get_walrus_args(get_bir_arch(tmpdir, inp) if arch is None else arch,
                         tmpdir, dve_root=dve_root),
    ]
    result = run_command(cmd, cwd=tmpdir)
    if result is not None:
        (Path(tmpdir) / "log.txt").write_text(result.stdout)
    return f"{tmpdir}/{outp}"

_bu.bir_verify_and_optimise = _bvo_nobirsim

ENGINES = ("pe", "dve", "act", "pool", "sp")
ENG_SEM = {e: "s_" + e for e in ENGINES}
BASE_SEMS = [ENG_SEM[e] for e in ENGINES] + ["cc0", "cc1", "cc2"]


class Buf:
    __slots__ = ("name", "w", "rs")

    def __init__(self, name=""):
        self.name, self.w, self.rs = name, None, []


class LM:
    __slots__ = ("eng", "sem", "val")

    def __init__(self, eng, sem, val):
        self.eng, self.sem, self.val = eng, sem, val


class Prog:
    def __init__(self, nc):
        self.nc = nc
        self.ops = {e: [] for e in ENGINES}
        self.cnt = {s: 0 for s in BASE_SEMS}

    def _sem(self, s):
        if s not in self.cnt:
            self.cnt[s] = 0
        return s

    def raw(self, eng, fn):
        self.ops[eng].append((fn, [], None, 0))

    def op(self, eng, fn, reads=(), writes=(), dma=None, cc=None, inc_n=1):
        deps = []
        own = ENG_SEM[eng]
        for b in reads:
            if b.w is not None:
                deps.append(b.w)
        for b in writes:
            deps.extend(b.rs)
            if b.w is not None:
                deps.append(b.w)
        if cc is not None:
            sem, amt = self._sem(cc), 1
        elif dma is not None:
            sem, amt = self._sem(dma), 16 * inc_n
        else:
            sem, amt = own, 1
        waits = {}
        for d in deps:
            if d.val > waits.get(d.sem, 0):
                waits[d.sem] = d.val
        self.cnt[sem] += amt
        lm = LM(eng, sem, self.cnt[sem])
        self.ops[eng].append((fn, sorted(waits.items()), sem, amt))
        for b in reads:
            b.rs.append(lm)
        for b in writes:
            b.w, b.rs = lm, []
        return lm

    def wait_all(self, eng, sems):
        marks = [(s, self.cnt[s]) for s in sems if self.cnt.get(s, 0) > 0]
        self.ops[eng].append((None, marks, None, 0))

    def phase_join(self, engines=ENGINES, dmas=None):
        if dmas is None:
            dmas = [s for s in self.cnt
                    if s.startswith("q_") or s in ("dact", "dpool", "dsp")]
        sl = [ENG_SEM[e] for e in engines] + list(dmas)
        for e in engines:
            self.wait_all(e, [s for s in sl if s != ENG_SEM[e]])

    def emit(self, sems):
        for s in self.cnt:
            if s not in sems:
                sems[s] = self._stack.enter_context(self.nc.semaphore(s))
        engmap = {"pe": "tensor", "dve": "vector", "act": "scalar",
                  "pool": "gpsimd", "sp": "sync"}
        with self.nc.Block() as blk:
            for ename in ENGINES:
                oplist = self.ops[ename]
                if not oplist:
                    continue

                def make(oplist=oplist):
                    def run(eng):
                        lw = {}
                        for (fn, waits, sem, amt) in oplist:
                            for (sname, val) in waits:
                                if lw.get(sname, 0) >= val:
                                    continue
                                eng.wait_ge(sems[sname], val)
                                lw[sname] = val
                            if fn is None:
                                continue
                            ins = fn(eng)
                            if sem is not None:
                                ins.then_inc(sems[sem], amt)
                    return run

                getattr(blk, engmap[ename])(make())


def _tables():
    t = {}
    inv = 1.0 / (10000.0 ** (np.arange(0, DK, 2, dtype=np.float64) / DK))
    fr = np.outer(np.arange(4096, dtype=np.float64), inv)
    cosT = np.cos(fr).T
    sinT = np.sin(fr).T
    NW = 4096 // 256
    cc2 = np.zeros((128, NW * 512))
    ss2 = np.zeros((128, NW * 512))
    for w in range(NW):
        cw = cosT[:, w * 256:(w + 1) * 256]
        sw = sinT[:, w * 256:(w + 1) * 256]
        cc2[:, w * 512:w * 512 + 256] = cw
        cc2[:, w * 512 + 256:(w + 1) * 512] = cw
        ss2[:, w * 512:w * 512 + 256] = -sw
        ss2[:, w * 512 + 256:(w + 1) * 512] = sw
    t["cc2T"] = cc2.astype(ml_dtypes.bfloat16)
    t["ss2T"] = ss2.astype(ml_dtypes.bfloat16)
    j = np.arange(CH)
    DT = np.zeros((CH, NH * CH), np.float64)
    CQ = np.zeros((CH, NH * 2 * CH), np.float64)
    CK = np.zeros((CH, NH), np.float64)
    for h, g in enumerate(GAMMAS):
        rel = j[None, :] - j[:, None]
        DT[:, h * CH:(h + 1) * CH] = np.where(rel >= 0, g ** np.maximum(rel, 0), 0.0) * QSCALE
        cqrow = np.broadcast_to((g ** (j + 1.0)) * QSCALE, (CH, CH))
        CQ[:, h * 2 * CH:h * 2 * CH + CH] = cqrow
        CQ[:, h * 2 * CH + CH:(h + 1) * 2 * CH] = cqrow
        CK[:, h] = g ** (CH - 1.0 - j)
    t["DT"] = DT.astype(ml_dtypes.bfloat16)
    t["CQ"] = CQ.astype(ml_dtypes.bfloat16)
    t["CK"] = CK.astype(np.float32)
    t["utri"] = np.triu(np.ones((128, 128))).astype(ml_dtypes.bfloat16)
    t["iden"] = np.eye(128).astype(ml_dtypes.bfloat16)
    t["onescol"] = np.ones((128, 1), ml_dtypes.bfloat16)
    return t


def build_program(debug=False):
    from contextlib import ExitStack
    nc = bass.Bass("TRN2", target_bir_lowering=False)

    def din(name, shape, dt=F32):
        return nc.dram_tensor(name, shape, dt, kind="ExternalInput")

    x_in = din("x", [TPB, D])
    wq_in = din("wq", [D, D]); wk_in = din("wk", [D, D])
    wv_in = din("wv", [D, 2048]); wg_in = din("wg", [D, 2048])
    wo_in = din("wo", [2048, D])
    wr_in = din("wr", [D, E]); rb_in = din("rb", [1, E])
    gnw16_in = din("gnw16", [128, 16])
    w1c_in = din("w1c", [D, HID]); w2c_in = din("w2c", [HID, D])
    b1c_in = din("b1c", [128, 32]); b2c_in = din("b2c", [1, D], BF16)
    ow1_in = din("ow1", [D, HID]); ow2_in = din("ow2", [HID, D])
    ob1_in = din("ob1", [128, 32]); ob2_in = din("ob2", [1, D], BF16)
    cc2_in = din("cc2T", [128, 16 * 512], BF16)
    ss2_in = din("ss2T", [128, 16 * 512], BF16)
    DT_in = din("DT", [128, NH * CH], BF16)
    CQ_in = din("CQ", [128, NH * 2 * CH], BF16)
    CK_in = din("CK", [128, NH])
    utri_in = din("utri", [128, 128], BF16)
    iden_in = din("iden", [128, 128], BF16)
    ones_in = din("onescol", [128, 1], BF16)
    cmask_in = din("cmask", [E, 1])
    io8_in = din("io8_in", [128, E])
    io8m8_in = din("io8m8_in", [128, E])
    iop_in = din("iop_in", [128, 1])

    out_dram = nc.dram_tensor("out", [TPB, D], F32, kind="ExternalOutput")
    dbg = {}
    if debug:
        dbg["xn"] = nc.dram_tensor("dbg_xn", [TPB, D], BF16, kind="ExternalOutput")
        dbg["eid"] = nc.dram_tensor("dbg_eid", [128, NT], F32, kind="ExternalOutput")
        dbg["lrank"] = nc.dram_tensor("dbg_lrank", [128, NT], F32, kind="ExternalOutput")
        dbg["inv"] = nc.dram_tensor("dbg_inv", [128, NT], F32, kind="ExternalOutput")
        dbg["idx"] = nc.dram_tensor("dbg_idx", [128, NT], U32, kind="ExternalOutput")
        dbg["osc"] = nc.dram_tensor("dbg_osc", [TPB, 2048], BF16, kind="ExternalOutput")
        dbg["s"] = nc.dram_tensor("dbg_s", [TPB, D], BF16, kind="ExternalOutput")
        dbg["yfull"] = nc.dram_tensor("dbg_yfull", [E * CAP, D], BF16, kind="ExternalOutput")
        dbg["bufc"] = nc.dram_tensor("dbg_bufc", [CAP, D], BF16, kind="ExternalOutput")
        dbg["coff"] = nc.dram_tensor("dbg_coff", [1, E], F32, kind="ExternalOutput")
        dbg["rstd"] = nc.dram_tensor("dbg_rstd", [128, NT], F32, kind="ExternalOutput")
        dbg["nmr"] = nc.dram_tensor("dbg_nmr", [128, NT], F32, kind="ExternalOutput")

    xn_dr = nc.dram_tensor("xn_dr", [TPB, D], BF16)
    xnf_dr = nc.dram_tensor("xnf_dr", [TPB, D], BF16)
    g_dr = nc.dram_tensor("g_dr", [TPB, 2048], BF16)
    osc_dr = nc.dram_tensor("osc_dr", [TPB, 2048], BF16)
    s_dr = nc.dram_tensor("s_dr", [TPB, D], BF16)
    bufg_dr = nc.dram_tensor("bufg_dr", [E * CAP, D], BF16)
    bufc_dr = nc.dram_tensor("bufc_dr", [CAP, D], BF16)
    yc_dr = nc.dram_tensor("yc_dr", [CAP, D], BF16)
    yfull_dr = nc.dram_tensor("yfull_dr", [E * CAP, D], BF16)
    ccc_in = nc.dram_tensor("ccc_in", [1, E], F32)
    ccc_out = nc.dram_tensor("ccc_out", [E, E], F32)
    offs_dr = nc.dram_tensor("offs_dr", [NT, E], F32)
    coff_dr = nc.dram_tensor("coff_dr", [1, E], F32)

    stack = ExitStack()
    sems = {s: stack.enter_context(nc.semaphore(s)) for s in BASE_SEMS}
    bc_reg = stack.enter_context(nc.gpsimd.register("bc_reg"))
    p = Prog(nc)
    p._stack = stack
    B = {}

    def buf(name):
        if name not in B:
            B[name] = Buf(name)
        return B[name]

    def dma(eng, q, dst, src, rbufs=(), wbufs=()):
        stream = "q_" + wbufs[0].name if wbufs else q
        return p.op(eng, lambda e, d=dst, s=src: e.dma_start(d, s),
                    reads=rbufs, writes=wbufs, dma=stream)

    def mmgroup(psum_ap, pairs, reads, wbank):
        def emit(e, psum_ap=psum_ap, pairs=pairs):
            n = len(pairs)
            for i, (l, r) in enumerate(pairs):
                ins = e.matmul(psum_ap, l, r, start=(i == 0), stop=(i == n - 1))
            return ins
        return p.op("pe", emit, reads=reads, writes=[wbank])

    def rowchunks(t_in, cols, nch):
        return bass.AP(t_in, 0, [[cols, 128], [128 * cols, nch], [1, cols]])

    AS = nc.alloc_sbuf_tensor
    rstd_a = AS("rstd_a", [128, NT], F32)
    nmr_a = AS("nmr_a", [128, NT], F32)
    eid_a = AS("eid_a", [128, NT], F32)
    lrank_a = AS("lrank_a", [128, NT], F32)
    inv_a = AS("inv_a", [128, NT], F32)
    idx_a = AS("idx_a", [128, NT], U32)
    io8 = AS("io8", [128, E], F32)
    io8m8 = AS("io8m8", [128, E], F32)
    iop = AS("iop", [128, 1], F32)
    utri = AS("utri_s", [128, 128], BF16)
    iden = AS("iden_s", [128, 128], BF16)
    onescol = AS("onescol_s", [128, 1], BF16)
    rb_bc = AS("rb_bc", [128, E], F32)
    coff_bc = AS("coff_bc", [128, E], F32)
    offs_bc = AS("offs_bc", [128, NT * E], F32)
    runtot = AS("runtot", [1, E], F32)
    offs_row = AS("offs_row", [1, NT * E], F32)
    counts_T = AS("counts_T", [E, E], F32)
    cmrow = AS("cmrow", [E, E], F32)
    coff_col = AS("coff_col", [E, 1], F32)
    wr_sb = AS("wr_sb", [128, 8 * E], BF16)
    rxn = [AS(f"rxn{i}", [128, D], BF16) for i in range(2)]
    r8 = AS("r8", [128, E], F32)
    r8b = AS("r8b", [128, E], F32)
    rc1 = AS("rc1", [128, 1], F32)
    rc2 = AS("rc2", [128, 1], F32)
    rc3 = AS("rc3", [128, 1], F32)
    ytok = [AS(f"ytok{i}", [128, D], BF16) for i in range(2)]

    bank = [stack.enter_context(nc.psum_tensor(f"bank{i}", [128, 512], F32))
            for i in range(8)]
    bankB = [b.ap().bitcast(BF16) for b in bank]
    bb_ = [buf(f"bank{i}") for i in range(8)]

    _rr = [0]

    def nxt():
        _rr[0] = (_rr[0] + 1) % 8
        return _rr[0]

    x_t = x_in.ap().rearrange("(n p) d -> n p d", p=128)
    xn_t = xn_dr.ap().rearrange("(n p) d -> n p d", p=128)
    xnf_t = xnf_dr.ap().rearrange("(n p) d -> n p d", p=128)
    g_t = g_dr.ap().rearrange("(n p) d -> n p d", p=128)
    osc_t = osc_dr.ap().rearrange("(n p) d -> n p d", p=128)
    s_t = s_dr.ap().rearrange("(n p) d -> n p d", p=128)
    out_t = out_dram.ap().rearrange("(n p) d -> n p d", p=128)
    bufc_t = bufc_dr.ap().rearrange("(n p) d -> n p d", p=128)
    yc_t = yc_dr.ap().rearrange("(n p) d -> n p d", p=128)

    statsb = buf("stats")
    eidb = buf("eid_a")
    lrankb = buf("lrank_a")
    invb = buf("inv_a")
    idxb = buf("idx_a")

    dma("sp", "dsp", utri.ap(), utri_in[:, :], wbufs=[buf("utri")])
    dma("sp", "dsp", iden.ap(), iden_in[:, :], wbufs=[buf("iden")])
    dma("sp", "dsp", onescol.ap(), ones_in[:, :], wbufs=[buf("onescol")])
    dma("sp", "dsp", rb_bc.ap(), bass.AP(rb_in, 0, [[0, 128], [1, E]]),
        wbufs=[buf("rb_bc")])
    dma("pool", "dpool", wr_sb.ap(),
        rowchunks(wr_in, E, 8), wbufs=[buf("wr_sb")])
    dma("sp", "dsp", io8.ap(), io8_in[:, :], wbufs=[buf("io8")])
    dma("sp", "dsp", io8m8.ap(), io8m8_in[:, :], wbufs=[buf("io8m8")])
    dma("sp", "dsp", iop.ap(), iop_in[:, :], wbufs=[buf("iop")])
    p.op("pool", lambda e: e.memset(runtot[:], 0), writes=[buf("runtot")])
    p.raw("pool", lambda e: e.reg_mov(bc_reg, E * CAP - 1))

    c1 = ExitStack()
    S1 = lambda n, sh, d: c1.enter_context(nc.sbuf_tensor(n, sh, d))
    wq_sb = S1("wq_sb", [128, 8 * D], BF16)
    wk_sb = S1("wk_sb", [128, 8 * D], BF16)
    wv_sb = S1("wv_sb", [128, 8 * 2048], BF16)
    DTt = S1("DTt", [128, NH * CH], BF16)
    CQt = S1("CQt", [128, NH * 2 * CH], BF16)
    CKt = S1("CKt", [128, NH], F32)
    dma("sp", "dsp", DTt.ap(), DT_in[:, :], wbufs=[buf("DTt")])
    dma("sp", "dsp", CQt.ap(), CQ_in[:, :], wbufs=[buf("CQt")])
    dma("sp", "dsp", CKt.ap(), CK_in[:, :], wbufs=[buf("CKt")])
    dma("pool", "dpool", wq_sb.ap(),
        rowchunks(wq_in, D, 8), wbufs=[buf("wq")])
    dma("pool", "dpool", wk_sb.ap(),
        rowchunks(wk_in, D, 8), wbufs=[buf("wk")])
    dma("pool", "dpool", wv_sb.ap(),
        rowchunks(wv_in, 2048, 8), wbufs=[buf("wv")])

    ppctx = ExitStack()
    S0 = lambda n, sh, d: ppctx.enter_context(nc.sbuf_tensor(n, sh, d))
    zero2k = S0("zero2k", [128, 2048], BF16)
    p.op("pool", lambda e: e.memset(zero2k[:], 0), writes=[buf("zero2k")])
    bz = bufg_dr.ap().rearrange("(n p) (a d) -> n p (a d)", p=256, a=1).rearrange("n p d -> n (p d)").rearrange("n (p q) -> n p q", p=128)
    for i in range(40):
        dma("sp", "dsp", bz[i], zero2k[:], rbufs=[buf("zero2k")],
            wbufs=[buf("bufg")])
    xraw = [S0(f"ppx{i}", [128, D], F32) for i in range(8)]
    xnb = [S0(f"ppxn{i}", [128, D], BF16) for i in range(8)]
    xfmp = [S0(f"ppxf{i}", [128, D], BF16) for i in range(8)]
    wg_sb = S0("wg_sb", [128, 8 * 2048], BF16)
    gsb = [S0(f"ppg{i}", [128, 2048], BF16) for i in range(4)]
    dma("pool", "dpool", wg_sb.ap(),
        rowchunks(wg_in, 2048, 8), wbufs=[buf("wg")])
    xsqd = S0("ppsq", [128, D], BF16)
    sumsq8 = S0("ppsumsq", [128, 8], F32)
    sumx8 = S0("ppsumx", [128, 8], F32)
    mu8 = S0("ppmu", [128, 8], F32)
    vv8 = S0("ppvv", [128, 8], F32)
    lgb = S0("pplg", [128, E], F32)
    lg2 = S0("pplg2", [128, E], F32)
    mrow = S0("ppm", [128, 1], F32)
    ohbf = S0("ppoh", [128, E], BF16)

    def newton8(v_ap, out_ap, rb_, wb_, scr0, scr1, scri):
        n0, n1, ni = buf(scr0.handle_name), buf(scr1.handle_name), buf(scri.handle_name)
        p.op("dve", lambda e: e.tensor_single_scalar(
            scri.ap(), v_ap.bitcast(I32), 1, OP.arith_shift_right),
            reads=[rb_], writes=[ni])
        p.op("dve", lambda e: e.tensor_scalar(
            scri.ap(), scri.ap(), -1, MAGIC, OP.mult, OP.add),
            reads=[ni], writes=[ni])
        p.op("dve", lambda e: e.tensor_copy(scr0.ap(), scri.ap().bitcast(F32)),
             reads=[ni], writes=[n0])
        for _ in range(2):
            p.op("dve", lambda e: e.tensor_mul(scr1.ap(), scr0.ap(), scr0.ap()),
                 reads=[n0], writes=[n1])
            p.op("dve", lambda e: e.tensor_tensor(scr1.ap(), scr1.ap(), v_ap,
                                                  OP.mult),
                 reads=[n1, rb_], writes=[n1])
            p.op("dve", lambda e: e.tensor_scalar(scr1.ap(), scr1.ap(), -0.5,
                                                  1.5, OP.mult, OP.add),
                 reads=[n1], writes=[n1])
            p.op("dve", lambda e: e.tensor_mul(scr0.ap(), scr0.ap(), scr1.ap()),
                 reads=[n0, n1], writes=[n0])
        p.op("dve", lambda e: e.tensor_copy(out_ap, scr0.ap()),
             reads=[n0], writes=[wb_])

    nwt8 = [S0(f"ppnw{i}", [128, 8], F32) for i in range(2)]
    nwt8i = S0("ppnwi", [128, 8], I32)
    for tn in (nwt8[0], nwt8[1], nwt8i, sumsq8, sumx8, mu8, vv8):
        pass

    for g8 in range(NT // 8):
        ts8 = list(range(8 * g8, 8 * g8 + 8))
        for k, t in enumerate(ts8):
            bx = buf(f"ppx{k}")
            dma("pool", "dpool", xraw[k].ap(), x_t[t], wbufs=[bx])
            p.op("act", lambda e, k=k: e.activation(
                xsqd.ap(), xraw[k].ap(), AF.Square, accum_out=sumsq8[:, k:k + 1]),
                reads=[bx], writes=[buf("ppsq"), buf("ppsumsq")])
            p.op("act", lambda e, k=k: e.activation(
                xsqd.ap(), xraw[k].ap(), AF.Copy, accum_out=sumx8[:, k:k + 1]),
                reads=[bx], writes=[buf("ppsq"), buf("ppsumx")])
        p.op("dve", lambda e: e.tensor_scalar(mu8.ap(), sumx8.ap(), 1.0 / D,
                                              None, OP.mult),
             reads=[buf("ppsumx")], writes=[buf("ppmu")])
        p.op("dve", lambda e: e.tensor_scalar(vv8.ap(), sumsq8.ap(), 1.0 / D,
                                              None, OP.mult),
             reads=[buf("ppsumsq")], writes=[buf("ppvv")])
        p.op("dve", lambda e: e.tensor_mul(nwt8[0].ap(), mu8.ap(), mu8.ap()),
             reads=[buf("ppmu")], writes=[buf("ppnw0")])
        p.op("dve", lambda e: e.tensor_sub(vv8.ap(), vv8.ap(), nwt8[0].ap()),
             reads=[buf("ppvv"), buf("ppnw0")], writes=[buf("ppvv")])
        p.op("dve", lambda e: e.tensor_scalar(vv8.ap(), vv8.ap(), EPS, None,
                                              OP.add),
             reads=[buf("ppvv")], writes=[buf("ppvv")])
        class _H:
            pass
        s0 = _H(); s0.ap = nwt8[0].ap; s0.handle_name = "ppnw0"
        s1 = _H(); s1.ap = nwt8[1].ap; s1.handle_name = "ppnw1"
        si = _H(); si.ap = nwt8i.ap; si.handle_name = "ppnwi"
        newton8(vv8.ap(), rstd_a[:, 8 * g8:8 * g8 + 8], buf("ppvv"), statsb,
                s0, s1, si)
        p.op("dve", lambda e, g8=g8: e.scalar_tensor_tensor(
            nmr_a[:, 8 * g8:8 * g8 + 8], mu8.ap(), -1.0,
            rstd_a[:, 8 * g8:8 * g8 + 8], OP.mult, OP.mult),
            reads=[buf("ppmu"), statsb], writes=[statsb])
        for k, t in enumerate(ts8):
            par = t % 8
            tbk = nxt()
            bx, bxn, bxf = buf(f"ppx{k}"), buf(f"ppxn{par}"), buf(f"ppxf{par}")
            p.op("dve", lambda e, k=k, par=par, t=t: e.tensor_scalar(
                xnb[par].ap(), xraw[k].ap(), rstd_a[:, t:t + 1],
                nmr_a[:, t:t + 1], OP.mult, OP.add),
                reads=[bx, statsb], writes=[bxn])
            dma("sp", "dsp", xn_t[t], xnb[par].ap(), rbufs=[bxn],
                wbufs=[buf("xn_dr")])
            for ci in range(8):
                p.op("pe", lambda e, par=par, ci=ci, tbk=tbk: e.transpose(
                    bankB[tbk][:, ci * 128:(ci + 1) * 128],
                    xnb[par][:, ci * 128:(ci + 1) * 128], iden.ap()),
                    reads=[bxn, buf("iden")], writes=[bb_[tbk]])
            p.op("dve", lambda e, par=par, tbk=tbk: e.tensor_copy(
                xfmp[par].ap(), bankB[tbk][:, 0:1024]),
                reads=[bb_[tbk]], writes=[bxf])
            dma("sp", "dsp", xnf_t[t], xfmp[par].ap(), rbufs=[bxf],
                wbufs=[buf("xnf_dr")])
            bg = buf(f"ppg{par % 4}")
            for dv in range(4):
                bbk = nxt()
                prs = [(xfmp[par][:, ci * 128:(ci + 1) * 128],
                        wg_sb[:, ci * 2048 + dv * 512: ci * 2048 + (dv + 1) * 512])
                       for ci in range(8)]
                mmgroup(bank[bbk][:, 0:512], prs, [bxf, buf("wg")], bb_[bbk])
                p.op("act", lambda e, par=par, dv=dv, bbk=bbk: e.activation(
                    gsb[par % 4][:, dv * 512:(dv + 1) * 512],
                    bank[bbk][:, 0:512], AF.Silu),
                    reads=[bb_[bbk]], writes=[bg])
            dma("act", "dact", g_t[t], gsb[par % 4].ap(), rbufs=[bg],
                wbufs=[buf("g_dr")])
            mmgroup(bank[0][:, 0:E],
                    [(xfmp[par][:, ci * 128:(ci + 1) * 128],
                      wr_sb[:, 8 * ci:8 * ci + 8]) for ci in range(8)],
                    [bxf, buf("wr_sb")], bb_[0])
            p.op("dve", lambda e: e.tensor_add(lgb.ap(), bank[0][:, 0:E],
                                               rb_bc.ap()),
                 reads=[bb_[0], buf("rb_bc")], writes=[buf("pplg")])
            p.op("dve", lambda e: e.reduce_max(mrow.ap(), lgb.ap(), axis=AX),
                 reads=[buf("pplg")], writes=[buf("ppm")])
            p.op("dve", lambda e: e.tensor_scalar(lg2.ap(), lgb.ap(), mrow.ap(),
                                                  None, OP.is_equal),
                 reads=[buf("pplg"), buf("ppm")], writes=[buf("pplg2")])
            p.op("dve", lambda e: e.tensor_mul(lg2.ap(), lg2.ap(), io8m8.ap()),
                 reads=[buf("pplg2"), buf("io8m8")], writes=[buf("pplg2")])
            p.op("dve", lambda e: e.tensor_reduce(mrow.ap(), lg2.ap(), AX,
                                                  OP.min),
                 reads=[buf("pplg2")], writes=[buf("ppm")])
            p.op("dve", lambda e, t=t: e.tensor_scalar(
                eid_a[:, t:t + 1], mrow.ap(), 8.0, None, OP.add),
                reads=[buf("ppm")], writes=[eidb])
            p.op("dve", lambda e, t=t: e.tensor_scalar(
                ohbf.ap(), io8.ap(), eid_a[:, t:t + 1], None, OP.is_equal),
                reads=[buf("io8"), eidb], writes=[buf("ppoh")])
            mmgroup(bank[1][:, 0:E], [(utri.ap(), ohbf.ap())],
                    [buf("utri"), buf("ppoh")], bb_[1])
            mmgroup(bank[2][0:1, 0:E], [(onescol.ap(), ohbf.ap())],
                    [buf("onescol"), buf("ppoh")], bb_[2])
            p.op("dve", lambda e: e.tensor_tensor(lg2.ap(), bank[1][:, 0:E],
                                                  ohbf.ap(), OP.mult),
                 reads=[bb_[1], buf("ppoh")], writes=[buf("pplg2")])
            p.op("dve", lambda e: e.tensor_reduce(mrow.ap(), lg2.ap(), AX,
                                                  OP.add),
                 reads=[buf("pplg2")], writes=[buf("ppm")])
            p.op("dve", lambda e, t=t: e.tensor_scalar(
                lrank_a[:, t:t + 1], mrow.ap(), -1.0, None, OP.add),
                reads=[buf("ppm")], writes=[lrankb])
            p.op("dve", lambda e, t=t: e.tensor_copy(
                offs_row[:, E * t:E * (t + 1)], runtot.ap()),
                reads=[buf("runtot")], writes=[buf("offs_row")])
            p.op("dve", lambda e: e.tensor_add(runtot.ap(), runtot.ap(),
                                               bank[2][0:1, 0:E]),
                 reads=[buf("runtot"), bb_[2]], writes=[buf("runtot")])

    dma("act", "dact", bass.AP(offs_dr, 0, [[512, 1], [1, 512]]), offs_row.ap(),
        rbufs=[buf("offs_row")], wbufs=[buf("offs_dr")])
    dma("pool", "dpool", ccc_in[:, :], runtot.ap(), rbufs=[buf("runtot")],
        wbufs=[buf("ccc_in")])
    p.op("pool", lambda e: e.collective_compute(
        "AllGather", OP.bypass, replica_groups=[list(range(NCORES))],
        ins=[ccc_in.ap().opt()], outs=[ccc_out.ap().opt()]),
        reads=[buf("ccc_in")], writes=[buf("ccc_out")], cc="cc0")
    p.op("pool", lambda e: e.dma_start(
        counts_T.ap(), bass.AP(ccc_out, 0, [[1, E], [E, E]]),
        allow_slow_non_contiguous=True),
        reads=[buf("ccc_out")], writes=[buf("counts_T")], dma="q_counts_T")
    dma("pool", "dpool", cmrow.ap(),
        bass.AP(cmask_in, 0, [[0, E], [1, E]]), wbufs=[buf("cmrow")])
    p.op("pool", lambda e: e.tensor_mul(counts_T.ap(), counts_T.ap(), cmrow.ap()),
         reads=[buf("counts_T"), buf("cmrow")], writes=[buf("counts_T")])
    p.op("pool", lambda e: e.tensor_add(counts_T[:, 0:4], counts_T[:, 0:4],
                                        counts_T[:, 4:8]),
         reads=[buf("counts_T")], writes=[buf("counts_T")])
    p.op("pool", lambda e: e.tensor_add(counts_T[:, 0:2], counts_T[:, 0:2],
                                        counts_T[:, 2:4]),
         reads=[buf("counts_T")], writes=[buf("counts_T")])
    p.op("pool", lambda e: e.tensor_add(coff_col.ap(), counts_T[:, 0:1],
                                        counts_T[:, 1:2]),
         reads=[buf("counts_T")], writes=[buf("coff_col")])
    p.op("pool", lambda e: e.dma_start(
        bass.AP(coff_dr, 0, [[1, E], [1, 1]]), coff_col.ap(),
        allow_slow_non_contiguous=True),
        reads=[buf("coff_col")], writes=[buf("coff_dr")], dma="q_coff_dr")
    dma("pool", "dpool", coff_bc.ap(), bass.AP(coff_dr, 0, [[0, 128], [1, E]]),
        rbufs=[buf("coff_dr")], wbufs=[buf("coff_bc")])
    dma("pool", "dpool", offs_bc.ap(),
        bass.AP(offs_dr, 0, [[0, 128], [E, NT], [1, E]]),
        rbufs=[buf("offs_dr")], wbufs=[buf("offs_bc")])

    for t in range(NT):
        par = t % 2
        rb_ = buf(f"rxn{par}")
        dma("pool", "dpool", rxn[par].ap(), xn_t[t], rbufs=[buf("xn_dr")],
            wbufs=[rb_])
        p.op("pool", lambda e, t=t: e.tensor_scalar(
            r8.ap(), io8.ap(), eid_a[:, t:t + 1], None, OP.is_equal),
            reads=[buf("io8"), eidb], writes=[buf("r8")])
        p.op("pool", lambda e, t=t: e.tensor_add(
            r8b.ap(), coff_bc.ap(), offs_bc[:, E * t:E * (t + 1)]),
            reads=[buf("coff_bc"), buf("offs_bc")], writes=[buf("r8b")])
        p.op("pool", lambda e: e.tensor_mul(r8.ap(), r8.ap(), r8b.ap()),
             reads=[buf("r8"), buf("r8b")], writes=[buf("r8")])
        p.op("pool", lambda e: e.tensor_add(r8[:, 0:4], r8[:, 0:4], r8[:, 4:8]),
             reads=[buf("r8")], writes=[buf("r8")])
        p.op("pool", lambda e: e.tensor_add(r8[:, 0:2], r8[:, 0:2], r8[:, 2:4]),
             reads=[buf("r8")], writes=[buf("r8")])
        p.op("pool", lambda e: e.tensor_add(rc1.ap(), r8[:, 0:1], r8[:, 1:2]),
             reads=[buf("r8")], writes=[buf("rc1")])
        p.op("pool", lambda e, t=t: e.tensor_add(rc1.ap(), rc1.ap(),
                                                 lrank_a[:, t:t + 1]),
             reads=[buf("rc1"), lrankb], writes=[buf("rc1")])
        p.op("pool", lambda e, t=t: e.tensor_single_scalar(
            inv_a[:, t:t + 1], rc1.ap(), float(CAP), OP.is_ge),
            reads=[buf("rc1")], writes=[invb])
        p.op("pool", lambda e, t=t: e.tensor_scalar(
            rc2.ap(), eid_a[:, t:t + 1], float(CAP), None, OP.mult),
            reads=[eidb], writes=[buf("rc2")])
        p.op("pool", lambda e: e.tensor_add(rc2.ap(), rc2.ap(), rc1.ap()),
             reads=[buf("rc2"), buf("rc1")], writes=[buf("rc2")])
        p.op("pool", lambda e, t=t: e.tensor_scalar(
            rc3.ap(), iop.ap(), 1.0, OOBBASE + 128.0 * t, OP.mult, OP.add),
            reads=[buf("iop")], writes=[buf("rc3")])
        p.op("pool", lambda e: e.tensor_sub(rc3.ap(), rc3.ap(), rc2.ap()),
             reads=[buf("rc3"), buf("rc2")], writes=[buf("rc3")])
        p.op("pool", lambda e, t=t: e.tensor_scalar(
            rc3.ap(), rc3.ap(), inv_a[:, t:t + 1], None, OP.mult),
            reads=[buf("rc3"), invb], writes=[buf("rc3")])
        p.op("pool", lambda e: e.tensor_add(rc2.ap(), rc2.ap(), rc3.ap()),
             reads=[buf("rc2"), buf("rc3")], writes=[buf("rc2")])
        p.op("pool", lambda e, t=t: e.tensor_copy(idx_a[:, t:t + 1], rc2.ap()),
             reads=[buf("rc2")], writes=[idxb])
        p.op("pool", lambda e, t=t, par=par: e.indirect_dma_start(
            out=bufg_dr.ap(),
            out_offset=bass.IndirectOffsetOnAxis(idx_a[:, t:t + 1], 0),
            in_=rxn[par].ap(), in_offset=None,
            bounds_check=bc_reg, oob_is_err=False),
            reads=[rb_, idxb, buf("bufg")],
            writes=[buf("bufg")], dma=f"q_sc{t % 2}")
    p.wait_all("pool", ["q_sc0", "q_sc1"])
    p.op("pool", lambda e: e.collective_compute(
        "ReduceScatter", OP.add, replica_groups=[list(range(NCORES))],
        ins=[bufg_dr.ap().opt()], outs=[bufc_dr.ap().opt()]),
        reads=[buf("bufg")], writes=[buf("bufc")], cc="cc1")

    p.phase_join(engines=("pe", "dve", "act", "sp"),
                 dmas=[f"q_ppx{i}" for i in range(8)]
                 + ["q_xn_dr", "q_xnf_dr", "q_g_dr", "q_bufg"])
    ppctx.close()
    Ssb = [[S1(f"S{b}_{h}", [128, 2 * DV], F32) for h in range(NH)]
           for b in range(2)]
    Sbf = [[S1(f"Sb{b}_{h}", [128, 2 * DV], BF16) for h in range(NH)]
           for b in range(2)]
    p1xf = [S1(f"p1xf{i}", [128, 2048], BF16) for i in range(2)]
    qfm = [S1(f"qfm{i}", [128, 2048], BF16) for i in range(2)]
    kfm = [S1(f"kfm{i}", [128, 2048], BF16) for i in range(2)]
    vsb = [S1(f"vsb{i}", [128, 2 * 2048], BF16) for i in range(2)]
    att4 = [S1(f"att4_{i}", [128, 512], BF16) for i in range(2)]
    kckb = [S1(f"kck{i}", [128, DK], BF16) for i in range(2 * NH)]
    qqb = [S1(f"qq{i}", [128, DK], BF16) for i in range(2 * NH)]
    rt1 = S1("rt1", [128, 512], F32)
    rt2 = S1("rt2", [128, 512], F32)
    oscraw = [S1(f"oscr{i}", [128, 2048], BF16) for i in range(2)]
    ccsl = [S1(f"ccsl{i}", [128, 512], BF16) for i in range(2)]
    sssl = [S1(f"sssl{i}", [128, 512], BF16) for i in range(2)]
    ovar8 = S1("ovar8", [128, 8], F32)
    rstd8 = S1("rstd8", [128, 8], F32)
    vv8p = S1("vv8p", [128, 8], F32)
    n8 = [S1(f"n8_{i}", [128, 8], F32) for i in range(2)]
    n8i = S1("n8i", [128, 8], I32)
    osq = S1("osq", [128, DV], BF16)

    def wblk(wsb, ci, j):
        return wsb[:, ci * D + j * 128: ci * D + (j + 1) * 128]

    def emit_qkv(g2):
        bat, gg = g2 % 2, g2 // 2
        gp = bat
        sl = gg % 2
        ta = bat * NTB + 2 * gg
        tb = ta + 1
        if bat == 0:
            dma("sp", "dsp", ccsl[sl].ap(), cc2_in[:, gg * 512:(gg + 1) * 512],
                wbufs=[buf(f"ccsl{sl}")])
            dma("sp", "dsp", sssl[sl].ap(), ss2_in[:, gg * 512:(gg + 1) * 512],
                wbufs=[buf(f"sssl{sl}")])
        bxf = buf(f"p1xf{gp}")
        for (ti, a) in ((ta, 0), (tb, 1)):
            p.op("sp", lambda e, gp=gp, a=a, ti=ti: e.dma_start(
                bass.AP(p1xf[gp], a * 128, [[2048, 128], [256, 8], [1, 128]]),
                bass.AP(xnf_dr, ti * 128 * D, [[D, 128], [128, 8], [1, 128]])),
                reads=[buf("xnf_dr")], writes=[bxf], dma=f"q_p1xf{gp}")
        for (wsb, wb_name, dstl, dname) in ((wq_sb, "wq", qfm, "qfm"),
                                            (wk_sb, "wk", kfm, "kfm")):
            dst = dstl[gp]
            bdst = buf(f"{dname}{gp}")
            for h in range(NH):
                bbk = nxt()
                prs = []
                for half in range(2):
                    j = 2 * h + half
                    prs += [(wblk(wsb, ci, j),
                             p1xf[gp][:, ci * 256:(ci + 1) * 256])
                            for ci in range(8)]

                def emitq(e, prs=prs, bbk=bbk):
                    for i, (l, r) in enumerate(prs):
                        ins = e.matmul(bank[bbk][:, (i // 8) * 256:
                                                 (i // 8) * 256 + 256],
                                       l, r, start=(i % 8 == 0),
                                       stop=(i % 8 == 7))
                    return ins
                p.op("pe", emitq, reads=[bxf, buf(wb_name)], writes=[bb_[bbk]])
                bcs, bsn = buf(f"ccsl{sl}"), buf(f"sssl{sl}")
                bk = bb_[bbk]
                p.op("dve", lambda e, bbk=bbk, sl=sl: e.tensor_tensor(
                    rt1.ap(), bank[bbk][:, 0:512], ccsl[sl].ap(), OP.mult),
                    reads=[bk, bcs], writes=[buf("rt1")])
                p.op("dve", lambda e, bbk=bbk, sl=sl: e.tensor_tensor(
                    rt2.ap(), bank[bbk][:, 0:512], sssl[sl].ap(), OP.mult),
                    reads=[bk, bsn], writes=[buf("rt2")])
                p.op("dve", lambda e, dst=dst, h=h: e.tensor_sub(
                    dst[:, (2 * h) * 256:(2 * h) * 256 + 256],
                    rt1[:, 0:256], rt2[:, 256:512]),
                    reads=[buf("rt1"), buf("rt2")], writes=[bdst])
                p.op("dve", lambda e, dst=dst, h=h: e.tensor_sub(
                    dst[:, (2 * h + 1) * 256:(2 * h + 1) * 256 + 256],
                    rt1[:, 256:512], rt2[:, 0:256]),
                    reads=[buf("rt1"), buf("rt2")], writes=[bdst])
        for a in range(2):
            for dv in range(4):
                bbk = nxt()
                prs = [(p1xf[gp][:, ci * 256 + a * 128: ci * 256 + a * 128 + 128],
                        wv_sb[:, ci * 2048 + dv * 512: ci * 2048 + (dv + 1) * 512])
                       for ci in range(8)]
                mmgroup(bank[bbk][:, 0:512], prs, [bxf, buf("wv")], bb_[bbk])
                p.op("act", lambda e, gp=gp, a=a, dv=dv, bbk=bbk: e.copy(
                    vsb[gp][:, a * 2048 + dv * 512: a * 2048 + (dv + 1) * 512],
                    bank[bbk][:, 0:512]),
                    reads=[bb_[bbk]], writes=[buf(f"vsb{gp}")])
    def emit_ret(g2):
        bat, gg = g2 % 2, g2 // 2
        gp = bat
        ta = bat * NTB + 2 * gg
        tb = ta + 1
        for a in range(2):
            for h in range(NH):
                hb = a * NH + h
                for half in range(2):
                    p.op("dve", lambda e, gp=gp, a=a, h=h, half=half, hb=hb:
                         e.tensor_tensor(
                        qqb[hb][:, half * 128:(half + 1) * 128],
                        qfm[gp][:, (2 * h + half) * 256 + a * 128:
                                (2 * h + half) * 256 + a * 128 + 128],
                        CQt[:, h * 256 + half * 128:
                            h * 256 + (half + 1) * 128], OP.mult),
                        reads=[buf(f"qfm{gp}"), buf("CQt")],
                        writes=[buf(f"qq{hb}")])
        for a, ti in ((0, ta), (1, tb)):
            c = ti % NTB
            batt = buf(f"att4_{a}")
            bk_at = nxt()

            def emit_attn(e, gp=gp, a=a, bk_at=bk_at):
                for h in range(NH):
                    for half in range(2):
                        ins = e.matmul(
                            bank[bk_at][:, h * 128:(h + 1) * 128],
                            kfm[gp][:, (2 * h + half) * 256 + a * 128:
                                    (2 * h + half) * 256 + a * 128 + 128],
                            qfm[gp][:, (2 * h + half) * 256 + a * 128:
                                    (2 * h + half) * 256 + a * 128 + 128],
                            start=(half == 0), stop=(half == 1))
                return ins
            p.op("pe", emit_attn, reads=[buf(f"kfm{gp}"), buf(f"qfm{gp}")],
                 writes=[bb_[bk_at]])
            p.op("dve", lambda e, a=a, bk_at=bk_at: e.tensor_tensor(
                att4[a].ap(), bank[bk_at][:, 0:512], DTt[:, 0:512], OP.mult),
                reads=[bb_[bk_at], buf("DTt")], writes=[batt])
            for h in range(NH):
                hb = a * NH + h
                bqq = buf(f"qq{hb}")
                bSb = buf(f"Sb{bat}_{h}")
                b_o = nxt()

                def emit_o(e, hb=hb, gp=gp, a=a, h=h, b_o=b_o, c=c, bat=bat):
                    ins = e.matmul(bank[b_o][:, 0:512],
                                   att4[a][:, h * 128:(h + 1) * 128],
                                   vsb[gp][:, a * 2048 + h * 512:
                                           a * 2048 + (h + 1) * 512],
                                   start=True, stop=(c == 0))
                    if c > 0:
                        for half in range(2):
                            ins = e.matmul(
                                bank[b_o][:, 0:512],
                                qqb[hb][:, half * 128:(half + 1) * 128],
                                Sbf[bat][h][:, half * DV:(half + 1) * DV],
                                start=False, stop=(half == 1))
                    return ins
                rds = [batt, buf(f"vsb{gp}")] + ([bqq, bSb] if c > 0 else [])
                p.op("pe", emit_o, reads=rds, writes=[bb_[b_o]])
                p.op("act", lambda e, a=a, h=h, b_o=b_o: e.activation(
                    osq.ap(), bank[b_o][:, 0:512], AF.Square,
                    accum_out=ovar8[:, a * 4 + h:a * 4 + h + 1]),
                    reads=[bb_[b_o]], writes=[buf("osq"), buf("ovar8")])
                p.op("act", lambda e, a=a, h=h, b_o=b_o: e.copy(
                    oscraw[a][:, h * 512:(h + 1) * 512], bank[b_o][:, 0:512]),
                    reads=[bb_[b_o]], writes=[buf(f"oscr{a}")])
            if c < NTB - 1:
                bk_t = nxt()

                def emit_kt(e, gp=gp, a=a, bk_t=bk_t):
                    for h in range(NH):
                        for half in range(2):
                            j2 = 2 * h + half
                            ins = e.transpose(
                                bankB[bk_t][:, j2 * 128:(j2 + 1) * 128],
                                kfm[gp][:, j2 * 256 + a * 128:
                                        j2 * 256 + a * 128 + 128],
                                iden.ap())
                    return ins
                p.op("pe", emit_kt, reads=[buf(f"kfm{gp}"), buf("iden")],
                     writes=[bb_[bk_t]])
                for h in range(NH):
                    hb = a * NH + h
                    p.op("act", lambda e, hb=hb, h=h, bk_t=bk_t: e.mul(
                        kckb[hb].ap(), bankB[bk_t][:, h * 256:(h + 1) * 256],
                        CKt[:, h:h + 1]),
                        reads=[bb_[bk_t], buf("CKt")], writes=[buf(f"kck{hb}")])
                for h in range(NH):
                    hb = a * NH + h
                    bkck = buf(f"kck{hb}")
                    bS = buf(f"S{bat}_{h}")
                    bSb = buf(f"Sb{bat}_{h}")
                    gC = GAMMAS[h] ** CH
                    ubanks = [nxt(), nxt()]
                    for half in range(2):
                        b_u = ubanks[half]
                        p.op("pe", lambda e, hb=hb, gp=gp, a=a, h=h, half=half,
                             b_u=b_u: e.matmul(
                            bank[b_u][:, 0:512],
                            kckb[hb][:, half * 128:(half + 1) * 128],
                            vsb[gp][:, a * 2048 + h * 512:
                                    a * 2048 + (h + 1) * 512],
                            start=True, stop=True),
                            reads=[bkck, buf(f"vsb{gp}")], writes=[bb_[b_u]])
                    for half in range(2):
                        b_u = ubanks[half]
                        if c == 0:
                            p.op("dve", lambda e, h=h, half=half, b_u=b_u,
                                 bat=bat: e.tensor_copy(
                                Ssb[bat][h][:, half * DV:(half + 1) * DV],
                                bank[b_u][:, 0:512]),
                                reads=[bb_[b_u]], writes=[bS])
                        else:
                            p.op("dve", lambda e, h=h, half=half, b_u=b_u,
                                 gC=gC, bat=bat: e.scalar_tensor_tensor(
                                Ssb[bat][h][:, half * DV:(half + 1) * DV],
                                Ssb[bat][h][:, half * DV:(half + 1) * DV], gC,
                                bank[b_u][:, 0:512], OP.mult, OP.add),
                                reads=[bb_[b_u], bS], writes=[bS])
                    p.op("act", lambda e, h=h, bat=bat: e.copy(
                        Sbf[bat][h].ap(), Ssb[bat][h].ap()),
                        reads=[bS], writes=[bSb])
        p.op("dve", lambda e: e.tensor_scalar(
            vv8p.ap(), ovar8.ap(), 1.0 / DV, EPS, OP.mult, OP.add),
            reads=[buf("ovar8")], writes=[buf("vv8p")])
        class _H8:
            pass
        s0 = _H8(); s0.ap = n8[0].ap; s0.handle_name = "n8_0"
        s1 = _H8(); s1.ap = n8[1].ap; s1.handle_name = "n8_1"
        si = _H8(); si.ap = n8i.ap; si.handle_name = "n8i"
        newton8(vv8p.ap(), rstd8.ap(), buf("vv8p"), buf("rstd8"), s0, s1, si)
        for a, ti in ((0, ta), (1, tb)):
            for h in range(NH):
                col = a * 4 + h
                p.op("dve", lambda e, a=a, h=h, col=col: e.tensor_scalar(
                    oscraw[a][:, h * 512:(h + 1) * 512],
                    oscraw[a][:, h * 512:(h + 1) * 512],
                    rstd8[:, col:col + 1], None, OP.mult),
                    reads=[buf(f"oscr{a}"), buf("rstd8")],
                    writes=[buf(f"oscr{a}")])
            dma("sp", "dsp", osc_t[ti], oscraw[a].ap(),
                rbufs=[buf(f"oscr{a}")], wbufs=[buf("osc_dr")])

    for g2 in range(NT // 2):
        emit_qkv(g2)
        if g2 >= 1:
            emit_ret(g2 - 1)
    emit_ret(NT // 2 - 1)
    c1.close()

    p.phase_join()

    c3 = ExitStack()
    S3 = lambda n, sh, d: c3.enter_context(nc.sbuf_tensor(n, sh, d))
    wo_sb = S3("wo_sb", [128, 16 * D], BF16)
    wos = S3("wos", [128, 512], F32)
    gnw_sb = S3("gnw_sb", [128, 16], F32)
    dma("sp", "dsp", gnw_sb.ap(), gnw16_in[:, :], wbufs=[buf("gnw")])
    wo_r2 = wo_in.ap().rearrange("(c k) (m n) -> c m k n", k=128, n=512)
    for ci in range(16):
        for m in range(2):
            dma("sp", "dsp", wos.ap(), wo_r2[ci, m], rbufs=[buf("wo_pre")],
                wbufs=[buf("wo_pre")])
            p.op("act", lambda e, ci=ci, m=m: e.mul(
                wo_sb[:, ci * D + m * 512:ci * D + (m + 1) * 512], wos.ap(),
                gnw_sb[:, ci:ci + 1]),
                reads=[buf("wo_pre"), buf("gnw")], writes=[buf("wo")])

    c2 = ExitStack()
    S2 = lambda n, sh, d: c2.enter_context(nc.sbuf_tensor(n, sh, d))
    w1_sb = S2("w1_sb", [128, 8 * HID], BF16)
    w2_sb = S2("w2_sb", [128, 32 * D], BF16)
    b1_sb = S2("b1_sb", [128, 32], F32)
    b2bc = S2("b2bc", [128, D], BF16)
    p2x = [S2(f"p2x{i}", [128, D], BF16) for i in range(2)]
    p2xf = S2("p2xf", [128, 2048], BF16)
    hfm = S2("hfm", [128, 32 * 256], BF16)
    ycsb = [S2(f"ycsb{i}", [128, D], BF16) for i in range(2)]

    dma("pool", "dpool", w1_sb.ap(),
        rowchunks(w1c_in, HID, 8), wbufs=[buf("w1")])
    dma("pool", "dpool", w2_sb.ap(),
        rowchunks(w2c_in, D, 32), wbufs=[buf("w2")])
    dma("sp", "dsp", b1_sb.ap(), b1c_in[:, :], wbufs=[buf("b1")])
    dma("sp", "dsp", b2bc.ap(), bass.AP(b2c_in, 0, [[0, 128], [1, D]]),
        wbufs=[buf("b2bc")])

    for gi in range(5):
        bxf = buf("p2xf")
        for a in range(2):
            dma("sp", "dsp", p2x[a].ap(), bufc_t[2 * gi + a],
                rbufs=[buf("bufc")], wbufs=[buf(f"p2x{a}")])
            bbk = 4 + a
            for ci in range(8):
                p.op("pe", lambda e, a=a, ci=ci, bbk=bbk: e.transpose(
                    bankB[bbk][:, ci * 128:(ci + 1) * 128],
                    p2x[a][:, ci * 128:(ci + 1) * 128], iden.ap()),
                    reads=[buf(f"p2x{a}"), buf("iden")], writes=[bb_[bbk]])
            p.op("dve", lambda e, a=a, bbk=bbk: e.tensor_copy(
                bass.AP(p2xf, a * 128, [[2048, 128], [256, 8], [1, 128]]),
                bankB[bbk][:, 0:1024]),
                reads=[bb_[bbk]], writes=[bxf])
        hbufs = []
        for j in range(32):
            bbk = j % 4
            prs = [(w1_sb[:, ci * HID + j * 128: ci * HID + (j + 1) * 128],
                    p2xf[:, ci * 256:(ci + 1) * 256]) for ci in range(8)]
            mmgroup(bank[bbk][:, 0:256], prs, [bxf, buf("w1")], bb_[bbk])
            hb = buf(f"hf2_{j}")
            hbufs.append(hb)
            p.op("act", lambda e, j=j, bbk=bbk: e.activation(
                hfm[:, j * 256:(j + 1) * 256], bank[bbk][:, 0:256], AF.Relu,
                bias=b1_sb[:, j:j + 1]),
                reads=[bb_[bbk], buf("b1")], writes=[hb])
        for ts in range(2):
            for dh in range(2):
                bbk = 4 + ts * 2 + dh

                def emity(e, ts=ts, dh=dh, bbk=bbk):
                    for j in range(32):
                        ins = e.matmul(
                            bank[bbk][:, 0:512],
                            hfm[:, j * 256 + ts * 128: j * 256 + ts * 128 + 128],
                            w2_sb[:, j * D + dh * 512: j * D + (dh + 1) * 512],
                            start=(j == 0), stop=(j == 31))
                    return ins
                p.op("pe", emity, reads=hbufs + [buf("w2")], writes=[bb_[bbk]])
            for dh in range(2):
                bbk = 4 + ts * 2 + dh
                p.op("dve", lambda e, ts=ts, dh=dh, bbk=bbk: e.tensor_tensor(
                    ycsb[ts][:, dh * 512:(dh + 1) * 512], bank[bbk][:, 0:512],
                    b2bc[:, dh * 512:(dh + 1) * 512], OP.add),
                    reads=[bb_[bbk], buf("b2bc")], writes=[buf(f"ycsb{ts}")])
        for a in range(2):
            dma("sp", "dsp", yc_t[2 * gi + a], ycsb[a].ap(),
                rbufs=[buf(f"ycsb{a}")], wbufs=[buf("yc_dr")])
    p.op("pool", lambda e: e.collective_compute(
        "AllGather", OP.bypass, replica_groups=[list(range(NCORES))],
        ins=[yc_dr.ap().opt()], outs=[yfull_dr.ap().opt()]),
        reads=[buf("yc_dr")], writes=[buf("yfull")], cc="cc2")
    c2.close()

    p.phase_join()

    c3b = ExitStack()
    S3b = lambda n, sh, d: c3b.enter_context(nc.sbuf_tensor(n, sh, d))
    a3g = [S3b(f"a3g{i}", [128, 2048], BF16) for i in range(2)]
    a3osc = [S3b(f"a3osc{i}", [128, 2048], BF16) for i in range(2)]
    a3og = [S3b(f"a3og{i}", [128, 2048], BF16) for i in range(2)]
    a3ogf = [S3b(f"a3ogf{i}", [128, 2048], BF16) for i in range(2)]
    a3s = [S3b(f"a3s{i}", [128, D], BF16) for i in range(2)]

    for g in range(NT // 2):
        ta, tb = 2 * g, 2 * g + 1
        for (ti, a) in ((ta, 0), (tb, 1)):
            dma("act", "dact", a3g[a].ap(), g_t[ti], rbufs=[buf("g_dr")],
                wbufs=[buf(f"a3g{a}")])
            dma("pool", "dpool", a3osc[a].ap(), osc_t[ti],
                rbufs=[buf("osc_dr")], wbufs=[buf(f"a3osc{a}")])
        for a, ti in ((0, ta), (1, tb)):
            p.op("dve", lambda e, a=a: e.tensor_mul(
                a3og[a].ap(), a3osc[a].ap(), a3g[a].ap()),
                reads=[buf(f"a3osc{a}"), buf(f"a3g{a}")],
                writes=[buf(f"a3og{a}")])
            for half in range(2):
                bbk = 4 + ((a * 2 + half) % 2)
                for ci in range(8):
                    cc = half * 8 + ci
                    p.op("pe", lambda e, a=a, cc=cc, ci=ci, bbk=bbk: e.transpose(
                        bankB[bbk][:, ci * 128:(ci + 1) * 128],
                        a3og[a][:, cc * 128:(cc + 1) * 128], iden.ap()),
                        reads=[buf(f"a3og{a}"), buf("iden")], writes=[bb_[bbk]])
                p.op("dve", lambda e, a=a, half=half, bbk=bbk: e.tensor_copy(
                    a3ogf[a][:, half * 1024:(half + 1) * 1024],
                    bankB[bbk][:, 0:1024]),
                    reads=[bb_[bbk]], writes=[buf(f"a3ogf{a}")])
            for dh in range(2):
                bbk = 6 + dh

                def emits(e, a=a, dh=dh, bbk=bbk):
                    for ci in range(16):
                        ins = e.matmul(
                            bank[bbk][:, 0:512],
                            a3ogf[a][:, ci * 128:(ci + 1) * 128],
                            wo_sb[:, ci * D + dh * 512: ci * D + (dh + 1) * 512],
                            start=(ci == 0), stop=(ci == 15))
                    return ins
                p.op("pe", emits, reads=[buf(f"a3ogf{a}"), buf("wo")],
                     writes=[bb_[bbk]])
                p.op("dve", lambda e, a=a, dh=dh, bbk=bbk: e.tensor_copy(
                    a3s[a][:, dh * 512:(dh + 1) * 512], bank[bbk][:, 0:512]),
                    reads=[bb_[bbk]], writes=[buf(f"a3s{a}")])
            dma("sp", "dsp", s_t[ti], a3s[a].ap(), rbufs=[buf(f"a3s{a}")],
                wbufs=[buf("s_dr")])
    c3b.close()
    c3.close()

    p.phase_join()

    c4 = ExitStack()
    S4 = lambda n, sh, d: c4.enter_context(nc.sbuf_tensor(n, sh, d))
    o1_sb = S4("o1_sb", [128, 8 * HID], BF16)
    o2_sb = S4("o2_sb", [128, 32 * D], BF16)
    ob1_sb = S4("ob1_sb", [128, 32], F32)
    ob2bc = S4("ob2bc", [128, D], BF16)
    b4x = [S4(f"b4x{i}", [128, D], F32) for i in range(2)]
    b4xf = S4("b4xf", [128, 4096], BF16)
    hfm4 = S4("hfm4", [128, 32 * 512], BF16)
    yov = S4("yov", [128, D], F32)
    b4s = [S4(f"b4s{i}", [128, D], BF16) for i in range(2)]
    b4g = S4("b4g", [128, D], BF16)
    b4o = [S4("b4o0", [128, D], F32)]

    dma("pool", "dpool", o1_sb.ap(),
        rowchunks(ow1_in, HID, 8), wbufs=[buf("o1")])
    dma("pool", "dpool", o2_sb.ap(),
        rowchunks(ow2_in, D, 32), wbufs=[buf("o2")])
    dma("sp", "dsp", ob1_sb.ap(), ob1_in[:, :], wbufs=[buf("ob1")])
    dma("sp", "dsp", ob2bc.ap(), bass.AP(ob2_in, 0, [[0, 128], [1, D]]),
        wbufs=[buf("ob2bc")])

    def b4_loads(ti, a2):
        dma("pool", "dpool", b4x[a2].ap(), x_t[ti], wbufs=[buf(f"b4x{a2}")])
        dma("pool", "dpool", b4s[a2].ap(), s_t[ti], rbufs=[buf("s_dr")],
            wbufs=[buf(f"b4s{a2}")])
        p.op("pool", lambda e, a2=a2: e.memset(ytok[a2].ap(), 0),
             reads=[], writes=[buf(f"ytok{a2}")])
        p.op("pool", lambda e, a2=a2, ti=ti: e.indirect_dma_start(
            out=ytok[a2].ap(), out_offset=None,
            in_=yfull_dr.ap(),
            in_offset=bass.IndirectOffsetOnAxis(idx_a[:, ti:ti + 1], 0),
            bounds_check=bc_reg, oob_is_err=False),
            reads=[buf("yfull"), idxb, buf(f"ytok{a2}")],
            writes=[buf(f"ytok{a2}")], dma=f"q_ytok{a2}")

    def b4_combine(ti, a2):
        p.op("dve", lambda e, ti=ti: e.tensor_scalar(
            yov.ap(), yov.ap(), inv_a[:, ti:ti + 1], None, OP.mult),
            reads=[buf("yov"), invb], writes=[buf("yov")])
        p.op("dve", lambda e, a2=a2: e.tensor_add(
            yov.ap(), yov.ap(), ytok[a2].ap()),
            reads=[buf("yov"), buf(f"ytok{a2}")], writes=[buf("yov")])
        p.op("act", lambda e: e.activation(b4g.ap(), yov.ap(), AF.Silu),
             reads=[buf("yov")], writes=[buf("b4g")])
        p.op("dve", lambda e, a2=a2: e.tensor_mul(
            b4g.ap(), b4g.ap(), b4s[a2].ap()),
            reads=[buf("b4g"), buf(f"b4s{a2}")], writes=[buf("b4g")])
        p.op("dve", lambda e, a2=a2: e.tensor_add(
            b4o[0].ap(), b4g.ap(), b4x[a2].ap()),
            reads=[buf("b4g"), buf(f"b4x{a2}")], writes=[buf("b4o0")])
        dma("sp", "dsp", out_t[ti], b4o[0].ap(), rbufs=[buf("b4o0")],
            wbufs=[buf("outw")])

    for g in range(NT // 4):
        t0 = 4 * g
        bxf = buf("b4xf")
        for a in range(4):
            p.op("sp", lambda e, a=a, ti=t0 + a: e.dma_start(
                bass.AP(b4xf, a * 128, [[4096, 128], [512, 8], [1, 128]]),
                bass.AP(xnf_dr, ti * 128 * D, [[D, 128], [128, 8], [1, 128]])),
                reads=[buf("xnf_dr")], writes=[bxf], dma="q_b4xf")
        for a in range(2):
            b4_loads(t0 + a, a)
        hbufs = []
        for j in range(32):
            bbk = j % 4
            prs = [(o1_sb[:, ci * HID + j * 128: ci * HID + (j + 1) * 128],
                    b4xf[:, ci * 512:(ci + 1) * 512]) for ci in range(8)]
            mmgroup(bank[bbk][:, 0:512], prs, [bxf, buf("o1")], bb_[bbk])
            hb = buf(f"h4_{j}")
            hbufs.append(hb)
            p.op("act", lambda e, j=j, bbk=bbk: e.activation(
                hfm4[:, j * 512:(j + 1) * 512], bank[bbk][:, 0:512],
                AF.Relu, bias=ob1_sb[:, j:j + 1]),
                reads=[bb_[bbk], buf("ob1")], writes=[hb])
        for ts in range(4):
            a2 = ts % 2
            for dh in range(2):
                bbk = 4 + (ts * 2 + dh) % 4

                def emity(e, ts=ts, dh=dh, bbk=bbk):
                    for j in range(32):
                        ins = e.matmul(
                            bank[bbk][:, 0:512],
                            hfm4[:, j * 512 + ts * 128:
                                 j * 512 + ts * 128 + 128],
                            o2_sb[:, j * D + dh * 512: j * D + (dh + 1) * 512],
                            start=(j == 0), stop=(j == 31))
                    return ins
                p.op("pe", emity, reads=hbufs + [buf("o2")], writes=[bb_[bbk]])
            for dh in range(2):
                bbk = 4 + (ts * 2 + dh) % 4
                p.op("dve", lambda e, dh=dh, bbk=bbk: e.tensor_tensor(
                    yov[:, dh * 512:(dh + 1) * 512], bank[bbk][:, 0:512],
                    ob2bc[:, dh * 512:(dh + 1) * 512], OP.add),
                    reads=[bb_[bbk], buf("ob2bc")], writes=[buf("yov")])
            b4_combine(t0 + ts, a2)
            if ts == 1:
                for a in range(2):
                    b4_loads(t0 + 2 + a, a)
    c4.close()

    if debug:
        dma("sp", "dsp", dbg["xn"].ap(), xn_dr.ap(), rbufs=[buf("xn_dr")])
        dma("sp", "dsp", dbg["eid"].ap(), eid_a.ap(), rbufs=[eidb])
        dma("sp", "dsp", dbg["lrank"].ap(), lrank_a.ap(), rbufs=[lrankb])
        dma("sp", "dsp", dbg["inv"].ap(), inv_a.ap(), rbufs=[invb])
        dma("sp", "dsp", dbg["idx"].ap(), idx_a.ap(), rbufs=[idxb])
        dma("sp", "dsp", dbg["osc"].ap(), osc_dr.ap(), rbufs=[buf("osc_dr")])
        dma("sp", "dsp", dbg["s"].ap(), s_dr.ap(), rbufs=[buf("s_dr")])
        dma("sp", "dsp", dbg["yfull"].ap(), yfull_dr.ap(), rbufs=[buf("yfull")])
        dma("sp", "dsp", dbg["bufc"].ap(), bufc_dr.ap(), rbufs=[buf("bufc")])
        dma("act", "dact", dbg["coff"].ap(), coff_dr.ap(), rbufs=[buf("coff_dr")])
        dma("sp", "dsp", dbg["rstd"].ap(), rstd_a.ap(), rbufs=[statsb])
        dma("sp", "dsp", dbg["nmr"].ap(), nmr_a.ap(), rbufs=[statsb])

    p.wait_all("sp", [s for s in p.cnt if s.startswith("q_")
                      or s.startswith("cc") or s in ("dact", "dpool", "dsp")])

    p.emit(sems)
    stack.close()
    return nc


_PROG_CACHE = {}


def _get_program(debug=False):
    if debug not in _PROG_CACHE:
        _PROG_CACHE[debug] = build_program(debug=debug)
    return _PROG_CACHE[debug]


def make_in_maps(inputs):
    t = _tables()
    x = np.asarray(inputs["L0_attn_x"], np.float32).reshape(16 * 4096, D)
    w1 = np.asarray(inputs["L0_ffn_w1"], np.float32)
    b1 = np.asarray(inputs["L0_ffn_b1"], np.float32)
    w2 = np.asarray(inputs["L0_ffn_w2"], np.float32)
    b2 = np.asarray(inputs["L0_ffn_b2"], np.float32)
    gn = np.asarray(inputs["L0_attn_gn_weight"], np.float32)
    gnw16 = np.ascontiguousarray(
        np.tile(gn, NH).reshape(16, 128).T)
    shared = dict(
        wq=np.ascontiguousarray(inputs["L0_attn_Wq"], np.float32),
        wk=np.ascontiguousarray(inputs["L0_attn_Wk"], np.float32),
        wv=np.ascontiguousarray(inputs["L0_attn_Wv"], np.float32),
        wg=np.ascontiguousarray(inputs["L0_attn_Wg"], np.float32),
        wo=np.ascontiguousarray(inputs["L0_attn_Wo"], np.float32),
        wr=np.ascontiguousarray(inputs["L0_ffn_router_w"], np.float32),
        rb=np.ascontiguousarray(inputs["L0_ffn_router_b"],
                                np.float32).reshape(1, E),
        gnw16=gnw16,
        ow1=np.ascontiguousarray(inputs["L0_ffn_ow1"], np.float32),
        ow2=np.ascontiguousarray(inputs["L0_ffn_ow2"], np.float32),
        ob1=np.ascontiguousarray(
            np.asarray(inputs["L0_ffn_ob1"], np.float32).reshape(32, 128).T),
        ob2=np.ascontiguousarray(inputs["L0_ffn_ob2"],
                                 np.float32).reshape(1, D).astype(
                                     ml_dtypes.bfloat16),
        cc2T=t["cc2T"], ss2T=t["ss2T"], DT=t["DT"], CQ=t["CQ"], CK=t["CK"],
        utri=t["utri"], iden=t["iden"], onescol=t["onescol"],
    )
    shared["io8_in"] = np.broadcast_to(np.arange(E, dtype=np.float32),
                                       (128, E)).copy()
    shared["io8m8_in"] = np.broadcast_to(np.arange(-8, 0, dtype=np.float32),
                                         (128, E)).copy()
    shared["iop_in"] = np.arange(128, dtype=np.float32).reshape(128, 1).copy()
    in_maps = []
    for c in range(NCORES):
        m = dict(shared)
        m["x"] = np.ascontiguousarray(x[c * TPB:(c + 1) * TPB])
        m["w1c"] = np.ascontiguousarray(w1[c])
        m["w2c"] = np.ascontiguousarray(w2[c])
        m["b1c"] = np.ascontiguousarray(b1[c].reshape(32, 128).T)
        m["b2c"] = np.ascontiguousarray(b2[c].reshape(1, D)).astype(ml_dtypes.bfloat16)
        cm = np.zeros((E, 1), np.float32)
        cm[:c] = 1.0
        m["cmask"] = cm
        in_maps.append(m)
    return in_maps


def kernel(**inputs):
    nc = _get_program(debug=False)
    in_maps = make_in_maps(inputs)
    res = run_bass_kernel_spmd(nc, in_maps, core_ids=list(range(NCORES)))
    out = np.concatenate([np.asarray(r["out"]) for r in res.results], axis=0)
    return out.reshape(16, 4096, D)


# version 17
# speedup vs baseline: 1.1422x; 1.0041x over previous
import numpy as np
import ml_dtypes

import concourse.bass as bass
import concourse.mybir as mybir
import concourse.bass_utils as _bu
from concourse.bass_utils import run_bass_kernel_spmd

F32 = mybir.dt.float32
BF16 = mybir.dt.bfloat16
I32 = mybir.dt.int32
U32 = mybir.dt.uint32
AF = mybir.ActivationFunctionType
OP = mybir.AluOpType
AX = mybir.AxisListType.X

NCORES = 8
D = 1024
HID = 4096
NH = 4
DK = 256
DV = 512
CH = 128
TPB = 8192
NT = TPB // CH
NTB = 32
E = 8
CAP = 1280
EPS = 1e-5
QSCALE = 0.0625
MAGIC = 0x5F3759DF
OOBBASE = 20000.0

GAMMAS = [1.0 - 2.0 ** (-5.0 - h) for h in range(NH)]

def _bvo_nobirsim(tmpdir, inp="bir.json", outp="file.neff", arch=None, *,
                  dve_root=None):
    from pathlib import Path
    from concourse.bass_utils import (get_walrus_driver, get_walrus_args,
                                      get_bir_arch, run_command)
    cmd = [
        get_walrus_driver(),
        "--pass", ",".join(["birverifier", "runtime_memory_reservation",
                            "lower_act", "lower_dve", "lower_ap_offset",
                            "codegen", "neff_packager"]),
        "-i", inp,
        "--neff-output-filename", outp,
        "--enable-birsim=false",
        "--mem-mode=physical", "--policy=0", "--enable-ldw-opt=false",
        "--assign-static-dmas-to-sp=false",
        "--dram-page-size=256", "--enable-neff-debug-info=true",
        "--jobs", "8",
        *get_walrus_args(get_bir_arch(tmpdir, inp) if arch is None else arch,
                         tmpdir, dve_root=dve_root),
    ]
    result = run_command(cmd, cwd=tmpdir)
    if result is not None:
        (Path(tmpdir) / "log.txt").write_text(result.stdout)
    return f"{tmpdir}/{outp}"

_bu.bir_verify_and_optimise = _bvo_nobirsim

ENGINES = ("pe", "dve", "act", "pool", "sp")
ENG_SEM = {e: "s_" + e for e in ENGINES}
BASE_SEMS = [ENG_SEM[e] for e in ENGINES] + ["cc0", "cc1", "cc2"]


class Buf:
    __slots__ = ("name", "w", "rs")

    def __init__(self, name=""):
        self.name, self.w, self.rs = name, None, []


class LM:
    __slots__ = ("eng", "sem", "val")

    def __init__(self, eng, sem, val):
        self.eng, self.sem, self.val = eng, sem, val


class Prog:
    def __init__(self, nc):
        self.nc = nc
        self.ops = {e: [] for e in ENGINES}
        self.cnt = {s: 0 for s in BASE_SEMS}

    def _sem(self, s):
        if s not in self.cnt:
            self.cnt[s] = 0
        return s

    def raw(self, eng, fn):
        self.ops[eng].append((fn, [], None, 0))

    def op(self, eng, fn, reads=(), writes=(), dma=None, cc=None, inc_n=1):
        deps = []
        own = ENG_SEM[eng]
        for b in reads:
            if b.w is not None:
                deps.append(b.w)
        for b in writes:
            deps.extend(b.rs)
            if b.w is not None:
                deps.append(b.w)
        if cc is not None:
            sem, amt = self._sem(cc), 1
        elif dma is not None:
            sem, amt = self._sem(dma), 16 * inc_n
        else:
            sem, amt = own, 1
        waits = {}
        for d in deps:
            if d.val > waits.get(d.sem, 0):
                waits[d.sem] = d.val
        self.cnt[sem] += amt
        lm = LM(eng, sem, self.cnt[sem])
        self.ops[eng].append((fn, sorted(waits.items()), sem, amt))
        for b in reads:
            b.rs.append(lm)
        for b in writes:
            b.w, b.rs = lm, []
        return lm

    def wait_all(self, eng, sems):
        marks = [(s, self.cnt[s]) for s in sems if self.cnt.get(s, 0) > 0]
        self.ops[eng].append((None, marks, None, 0))

    def phase_join(self, engines=ENGINES, dmas=None):
        if dmas is None:
            dmas = [s for s in self.cnt
                    if s.startswith("q_") or s in ("dact", "dpool", "dsp")]
        sl = [ENG_SEM[e] for e in engines] + list(dmas)
        for e in engines:
            self.wait_all(e, [s for s in sl if s != ENG_SEM[e]])

    def emit(self, sems):
        for s in self.cnt:
            if s not in sems:
                sems[s] = self._stack.enter_context(self.nc.semaphore(s))
        engmap = {"pe": "tensor", "dve": "vector", "act": "scalar",
                  "pool": "gpsimd", "sp": "sync"}
        with self.nc.Block() as blk:
            for ename in ENGINES:
                oplist = self.ops[ename]
                if not oplist:
                    continue

                def make(oplist=oplist):
                    def run(eng):
                        lw = {}
                        for (fn, waits, sem, amt) in oplist:
                            for (sname, val) in waits:
                                if lw.get(sname, 0) >= val:
                                    continue
                                eng.wait_ge(sems[sname], val)
                                lw[sname] = val
                            if fn is None:
                                continue
                            ins = fn(eng)
                            if sem is not None:
                                ins.then_inc(sems[sem], amt)
                    return run

                getattr(blk, engmap[ename])(make())


def _tables():
    t = {}
    inv = 1.0 / (10000.0 ** (np.arange(0, DK, 2, dtype=np.float64) / DK))
    fr = np.outer(np.arange(4096, dtype=np.float64), inv)
    cosT = np.cos(fr).T
    sinT = np.sin(fr).T
    NW = 4096 // 256
    cc2 = np.zeros((128, NW * 512))
    ss2 = np.zeros((128, NW * 512))
    for w in range(NW):
        cw = cosT[:, w * 256:(w + 1) * 256]
        sw = sinT[:, w * 256:(w + 1) * 256]
        cc2[:, w * 512:w * 512 + 256] = cw
        cc2[:, w * 512 + 256:(w + 1) * 512] = cw
        ss2[:, w * 512:w * 512 + 256] = -sw
        ss2[:, w * 512 + 256:(w + 1) * 512] = sw
    t["cc2T"] = cc2.astype(ml_dtypes.bfloat16)
    t["ss2T"] = ss2.astype(ml_dtypes.bfloat16)
    j = np.arange(CH)
    DT = np.zeros((CH, NH * CH), np.float64)
    CQ = np.zeros((CH, NH * 2 * CH), np.float64)
    CK = np.zeros((CH, NH), np.float64)
    for h, g in enumerate(GAMMAS):
        rel = j[None, :] - j[:, None]
        DT[:, h * CH:(h + 1) * CH] = np.where(rel >= 0, g ** np.maximum(rel, 0), 0.0) * QSCALE
        cqrow = np.broadcast_to((g ** (j + 1.0)) * QSCALE, (CH, CH))
        CQ[:, h * 2 * CH:h * 2 * CH + CH] = cqrow
        CQ[:, h * 2 * CH + CH:(h + 1) * 2 * CH] = cqrow
        CK[:, h] = g ** (CH - 1.0 - j)
    t["DT"] = DT.astype(ml_dtypes.bfloat16)
    t["CQ"] = CQ.astype(ml_dtypes.bfloat16)
    t["CK"] = CK.astype(np.float32)
    t["utri"] = np.triu(np.ones((128, 128))).astype(ml_dtypes.bfloat16)
    t["iden"] = np.eye(128).astype(ml_dtypes.bfloat16)
    t["onescol"] = np.ones((128, 1), ml_dtypes.bfloat16)
    return t


def build_program(debug=False):
    from contextlib import ExitStack
    nc = bass.Bass("TRN2", target_bir_lowering=False)

    def din(name, shape, dt=F32):
        return nc.dram_tensor(name, shape, dt, kind="ExternalInput")

    x_in = din("x", [TPB, D])
    wq_in = din("wq", [D, D]); wk_in = din("wk", [D, D])
    wv_in = din("wv", [D, 2048]); wg_in = din("wg", [D, 2048])
    wo_in = din("wo", [2048, D])
    wr_in = din("wr", [D, E]); rb_in = din("rb", [1, E])
    gnw16_in = din("gnw16", [128, 16])
    w1c_in = din("w1c", [D, HID]); w2c_in = din("w2c", [HID, D])
    b1c_in = din("b1c", [128, 32]); b2c_in = din("b2c", [1, D], BF16)
    ow1_in = din("ow1", [D, HID]); ow2_in = din("ow2", [HID, D])
    ob1_in = din("ob1", [128, 32]); ob2_in = din("ob2", [1, D], BF16)
    cc2_in = din("cc2T", [128, 16 * 512], BF16)
    ss2_in = din("ss2T", [128, 16 * 512], BF16)
    DT_in = din("DT", [128, NH * CH], BF16)
    CQ_in = din("CQ", [128, NH * 2 * CH], BF16)
    CK_in = din("CK", [128, NH])
    utri_in = din("utri", [128, 128], BF16)
    iden_in = din("iden", [128, 128], BF16)
    ones_in = din("onescol", [128, 1], BF16)
    cmask_in = din("cmask", [E, 1])
    io8_in = din("io8_in", [128, E])
    io8m8_in = din("io8m8_in", [128, E])
    iop_in = din("iop_in", [128, 1])

    out_dram = nc.dram_tensor("out", [TPB, D], F32, kind="ExternalOutput")
    dbg = {}
    if debug:
        dbg["xn"] = nc.dram_tensor("dbg_xn", [TPB, D], BF16, kind="ExternalOutput")
        dbg["eid"] = nc.dram_tensor("dbg_eid", [128, NT], F32, kind="ExternalOutput")
        dbg["lrank"] = nc.dram_tensor("dbg_lrank", [128, NT], F32, kind="ExternalOutput")
        dbg["inv"] = nc.dram_tensor("dbg_inv", [128, NT], F32, kind="ExternalOutput")
        dbg["idx"] = nc.dram_tensor("dbg_idx", [128, NT], U32, kind="ExternalOutput")
        dbg["osc"] = nc.dram_tensor("dbg_osc", [TPB, 2048], BF16, kind="ExternalOutput")
        dbg["s"] = nc.dram_tensor("dbg_s", [TPB, D], BF16, kind="ExternalOutput")
        dbg["yfull"] = nc.dram_tensor("dbg_yfull", [E * CAP, D], BF16, kind="ExternalOutput")
        dbg["bufc"] = nc.dram_tensor("dbg_bufc", [CAP, D], BF16, kind="ExternalOutput")
        dbg["coff"] = nc.dram_tensor("dbg_coff", [1, E], F32, kind="ExternalOutput")
        dbg["rstd"] = nc.dram_tensor("dbg_rstd", [128, NT], F32, kind="ExternalOutput")
        dbg["nmr"] = nc.dram_tensor("dbg_nmr", [128, NT], F32, kind="ExternalOutput")

    xn_dr = nc.dram_tensor("xn_dr", [TPB, D], BF16)
    xnf_dr = nc.dram_tensor("xnf_dr", [TPB, D], BF16)
    g_dr = nc.dram_tensor("g_dr", [TPB, 2048], BF16)
    osc_dr = nc.dram_tensor("osc_dr", [TPB, 2048], BF16)
    s_dr = nc.dram_tensor("s_dr", [TPB, D], BF16)
    bufg_dr = nc.dram_tensor("bufg_dr", [E * CAP, D], BF16)
    bufc_dr = nc.dram_tensor("bufc_dr", [CAP, D], BF16)
    yc_dr = nc.dram_tensor("yc_dr", [CAP, D], BF16)
    yfull_dr = nc.dram_tensor("yfull_dr", [E * CAP, D], BF16,
                              addr_space="Shared")
    ccc_in = nc.dram_tensor("ccc_in", [1, E], F32)
    ccc_out = nc.dram_tensor("ccc_out", [E, E], F32)
    offs_dr = nc.dram_tensor("offs_dr", [NT, E], F32)
    coff_dr = nc.dram_tensor("coff_dr", [1, E], F32)

    stack = ExitStack()
    sems = {s: stack.enter_context(nc.semaphore(s)) for s in BASE_SEMS}
    bc_reg = stack.enter_context(nc.gpsimd.register("bc_reg"))
    p = Prog(nc)
    p._stack = stack
    B = {}

    def buf(name):
        if name not in B:
            B[name] = Buf(name)
        return B[name]

    def dma(eng, q, dst, src, rbufs=(), wbufs=()):
        stream = "q_" + wbufs[0].name if wbufs else q
        return p.op(eng, lambda e, d=dst, s=src: e.dma_start(d, s),
                    reads=rbufs, writes=wbufs, dma=stream)

    def mmgroup(psum_ap, pairs, reads, wbank):
        def emit(e, psum_ap=psum_ap, pairs=pairs):
            n = len(pairs)
            for i, (l, r) in enumerate(pairs):
                ins = e.matmul(psum_ap, l, r, start=(i == 0), stop=(i == n - 1))
            return ins
        return p.op("pe", emit, reads=reads, writes=[wbank])

    def rowchunks(t_in, cols, nch):
        return bass.AP(t_in, 0, [[cols, 128], [128 * cols, nch], [1, cols]])

    AS = nc.alloc_sbuf_tensor
    rstd_a = AS("rstd_a", [128, NT], F32)
    nmr_a = AS("nmr_a", [128, NT], F32)
    eid_a = AS("eid_a", [128, NT], F32)
    lrank_a = AS("lrank_a", [128, NT], F32)
    inv_a = AS("inv_a", [128, NT], F32)
    idx_a = AS("idx_a", [128, NT], U32)
    io8 = AS("io8", [128, E], F32)
    io8m8 = AS("io8m8", [128, E], F32)
    iop = AS("iop", [128, 1], F32)
    utri = AS("utri_s", [128, 128], BF16)
    iden = AS("iden_s", [128, 128], BF16)
    onescol = AS("onescol_s", [128, 1], BF16)
    rb_bc = AS("rb_bc", [128, E], F32)
    coff_bc = AS("coff_bc", [128, E], F32)
    offs_bc = AS("offs_bc", [128, NT * E], F32)
    runtot = AS("runtot", [1, E], F32)
    offs_row = AS("offs_row", [1, NT * E], F32)
    counts_T = AS("counts_T", [E, E], F32)
    cmrow = AS("cmrow", [E, E], F32)
    coff_col = AS("coff_col", [E, 1], F32)
    wr_sb = AS("wr_sb", [128, 8 * E], BF16)
    rxn = [AS(f"rxn{i}", [128, D], BF16) for i in range(2)]
    r8 = AS("r8", [128, E], F32)
    r8b = AS("r8b", [128, E], F32)
    rc1 = AS("rc1", [128, 1], F32)
    rc2 = AS("rc2", [128, 1], F32)
    rc3 = AS("rc3", [128, 1], F32)
    ytok = [AS(f"ytok{i}", [128, D], BF16) for i in range(2)]

    bank = [stack.enter_context(nc.psum_tensor(f"bank{i}", [128, 512], F32))
            for i in range(8)]
    bankB = [b.ap().bitcast(BF16) for b in bank]
    bb_ = [buf(f"bank{i}") for i in range(8)]

    x_t = x_in.ap().rearrange("(n p) d -> n p d", p=128)
    xn_t = xn_dr.ap().rearrange("(n p) d -> n p d", p=128)
    xnf_t = xnf_dr.ap().rearrange("(n p) d -> n p d", p=128)
    g_t = g_dr.ap().rearrange("(n p) d -> n p d", p=128)
    osc_t = osc_dr.ap().rearrange("(n p) d -> n p d", p=128)
    s_t = s_dr.ap().rearrange("(n p) d -> n p d", p=128)
    out_t = out_dram.ap().rearrange("(n p) d -> n p d", p=128)
    bufc_t = bufc_dr.ap().rearrange("(n p) d -> n p d", p=128)
    yc_t = yc_dr.ap().rearrange("(n p) d -> n p d", p=128)

    statsb = buf("stats")
    eidb = buf("eid_a")
    lrankb = buf("lrank_a")
    invb = buf("inv_a")
    idxb = buf("idx_a")

    dma("sp", "dsp", utri.ap(), utri_in[:, :], wbufs=[buf("utri")])
    dma("sp", "dsp", iden.ap(), iden_in[:, :], wbufs=[buf("iden")])
    dma("sp", "dsp", onescol.ap(), ones_in[:, :], wbufs=[buf("onescol")])
    dma("sp", "dsp", rb_bc.ap(), bass.AP(rb_in, 0, [[0, 128], [1, E]]),
        wbufs=[buf("rb_bc")])
    dma("pool", "dpool", wr_sb.ap(),
        rowchunks(wr_in, E, 8), wbufs=[buf("wr_sb")])
    dma("sp", "dsp", io8.ap(), io8_in[:, :], wbufs=[buf("io8")])
    dma("sp", "dsp", io8m8.ap(), io8m8_in[:, :], wbufs=[buf("io8m8")])
    dma("sp", "dsp", iop.ap(), iop_in[:, :], wbufs=[buf("iop")])
    p.op("pool", lambda e: e.memset(runtot[:], 0), writes=[buf("runtot")])
    p.raw("pool", lambda e: e.reg_mov(bc_reg, E * CAP - 1))

    c1 = ExitStack()
    S1 = lambda n, sh, d: c1.enter_context(nc.sbuf_tensor(n, sh, d))
    wq_sb = S1("wq_sb", [128, 8 * D], BF16)
    wk_sb = S1("wk_sb", [128, 8 * D], BF16)
    wv_sb = S1("wv_sb", [128, 8 * 2048], BF16)
    DTt = S1("DTt", [128, NH * CH], BF16)
    CQt = S1("CQt", [128, NH * 2 * CH], BF16)
    CKt = S1("CKt", [128, NH], F32)
    dma("sp", "dsp", DTt.ap(), DT_in[:, :], wbufs=[buf("DTt")])
    dma("sp", "dsp", CQt.ap(), CQ_in[:, :], wbufs=[buf("CQt")])
    dma("sp", "dsp", CKt.ap(), CK_in[:, :], wbufs=[buf("CKt")])
    dma("pool", "dpool", wq_sb.ap(),
        rowchunks(wq_in, D, 8), wbufs=[buf("wq")])
    dma("pool", "dpool", wk_sb.ap(),
        rowchunks(wk_in, D, 8), wbufs=[buf("wk")])
    dma("pool", "dpool", wv_sb.ap(),
        rowchunks(wv_in, 2048, 8), wbufs=[buf("wv")])

    ppctx = ExitStack()
    S0 = lambda n, sh, d: ppctx.enter_context(nc.sbuf_tensor(n, sh, d))
    zero2k = S0("zero2k", [128, 2048], BF16)
    p.op("pool", lambda e: e.memset(zero2k[:], 0), writes=[buf("zero2k")])
    bz = bufg_dr.ap().rearrange("(n p) (a d) -> n p (a d)", p=256, a=1).rearrange("n p d -> n (p d)").rearrange("n (p q) -> n p q", p=128)
    for i in range(40):
        dma("sp", "dsp", bz[i], zero2k[:], rbufs=[buf("zero2k")],
            wbufs=[buf("bufg")])
    xraw = [S0(f"ppx{i}", [128, D], F32) for i in range(8)]
    xnb = [S0(f"ppxn{i}", [128, D], BF16) for i in range(4)]
    xfmp = [S0(f"ppxf{i}", [128, D], BF16) for i in range(4)]
    wg_sb = S0("wg_sb", [128, 8 * 2048], BF16)
    gsb = [S0(f"ppg{i}", [128, 2048], BF16) for i in range(4)]
    dma("pool", "dpool", wg_sb.ap(),
        rowchunks(wg_in, 2048, 8), wbufs=[buf("wg")])
    xsqd = S0("ppsq", [128, D], BF16)
    sumsq8 = S0("ppsumsq", [128, 8], F32)
    sumx8 = S0("ppsumx", [128, 8], F32)
    mu8 = S0("ppmu", [128, 8], F32)
    vv8 = S0("ppvv", [128, 8], F32)
    lgb = S0("pplg", [128, E], F32)
    lg2 = S0("pplg2", [128, E], F32)
    mrow = S0("ppm", [128, 1], F32)
    ohbf = S0("ppoh", [128, E], BF16)

    def newton8(v_ap, out_ap, rb_, wb_, scr0, scr1, scri):
        n0, n1, ni = buf(scr0.handle_name), buf(scr1.handle_name), buf(scri.handle_name)
        p.op("dve", lambda e: e.tensor_single_scalar(
            scri.ap(), v_ap.bitcast(I32), 1, OP.arith_shift_right),
            reads=[rb_], writes=[ni])
        p.op("dve", lambda e: e.tensor_scalar(
            scri.ap(), scri.ap(), -1, MAGIC, OP.mult, OP.add),
            reads=[ni], writes=[ni])
        p.op("dve", lambda e: e.tensor_copy(scr0.ap(), scri.ap().bitcast(F32)),
             reads=[ni], writes=[n0])
        for _ in range(2):
            p.op("dve", lambda e: e.tensor_mul(scr1.ap(), scr0.ap(), scr0.ap()),
                 reads=[n0], writes=[n1])
            p.op("dve", lambda e: e.tensor_tensor(scr1.ap(), scr1.ap(), v_ap,
                                                  OP.mult),
                 reads=[n1, rb_], writes=[n1])
            p.op("dve", lambda e: e.tensor_scalar(scr1.ap(), scr1.ap(), -0.5,
                                                  1.5, OP.mult, OP.add),
                 reads=[n1], writes=[n1])
            p.op("dve", lambda e: e.tensor_mul(scr0.ap(), scr0.ap(), scr1.ap()),
                 reads=[n0, n1], writes=[n0])
        p.op("dve", lambda e: e.tensor_copy(out_ap, scr0.ap()),
             reads=[n0], writes=[wb_])

    nwt8 = [S0(f"ppnw{i}", [128, 8], F32) for i in range(2)]
    nwt8i = S0("ppnwi", [128, 8], I32)
    for tn in (nwt8[0], nwt8[1], nwt8i, sumsq8, sumx8, mu8, vv8):
        pass

    for g8 in range(NT // 8):
        ts8 = list(range(8 * g8, 8 * g8 + 8))
        for k, t in enumerate(ts8):
            bx = buf(f"ppx{k}")
            dma("sp", "dsp", xraw[k].ap(), x_t[t], wbufs=[bx])
            p.op("act", lambda e, k=k: e.activation(
                xsqd.ap(), xraw[k].ap(), AF.Square, accum_out=sumsq8[:, k:k + 1]),
                reads=[bx], writes=[buf("ppsq"), buf("ppsumsq")])
            p.op("act", lambda e, k=k: e.activation(
                xsqd.ap(), xraw[k].ap(), AF.Copy, accum_out=sumx8[:, k:k + 1]),
                reads=[bx], writes=[buf("ppsq"), buf("ppsumx")])
        p.op("dve", lambda e: e.tensor_scalar(mu8.ap(), sumx8.ap(), 1.0 / D,
                                              None, OP.mult),
             reads=[buf("ppsumx")], writes=[buf("ppmu")])
        p.op("dve", lambda e: e.tensor_scalar(vv8.ap(), sumsq8.ap(), 1.0 / D,
                                              None, OP.mult),
             reads=[buf("ppsumsq")], writes=[buf("ppvv")])
        p.op("dve", lambda e: e.tensor_mul(nwt8[0].ap(), mu8.ap(), mu8.ap()),
             reads=[buf("ppmu")], writes=[buf("ppnw0")])
        p.op("dve", lambda e: e.tensor_sub(vv8.ap(), vv8.ap(), nwt8[0].ap()),
             reads=[buf("ppvv"), buf("ppnw0")], writes=[buf("ppvv")])
        p.op("dve", lambda e: e.tensor_scalar(vv8.ap(), vv8.ap(), EPS, None,
                                              OP.add),
             reads=[buf("ppvv")], writes=[buf("ppvv")])
        class _H:
            pass
        s0 = _H(); s0.ap = nwt8[0].ap; s0.handle_name = "ppnw0"
        s1 = _H(); s1.ap = nwt8[1].ap; s1.handle_name = "ppnw1"
        si = _H(); si.ap = nwt8i.ap; si.handle_name = "ppnwi"
        newton8(vv8.ap(), rstd_a[:, 8 * g8:8 * g8 + 8], buf("ppvv"), statsb,
                s0, s1, si)
        p.op("dve", lambda e, g8=g8: e.scalar_tensor_tensor(
            nmr_a[:, 8 * g8:8 * g8 + 8], mu8.ap(), -1.0,
            rstd_a[:, 8 * g8:8 * g8 + 8], OP.mult, OP.mult),
            reads=[buf("ppmu"), statsb], writes=[statsb])
        for k, t in enumerate(ts8):
            par = t % 4
            tbk = 7 - (t % 2)
            bx, bxn, bxf = buf(f"ppx{k}"), buf(f"ppxn{par}"), buf(f"ppxf{par}")
            p.op("dve", lambda e, k=k, par=par, t=t: e.tensor_scalar(
                xnb[par].ap(), xraw[k].ap(), rstd_a[:, t:t + 1],
                nmr_a[:, t:t + 1], OP.mult, OP.add),
                reads=[bx, statsb], writes=[bxn])
            dma("sp", "dsp", xn_t[t], xnb[par].ap(), rbufs=[bxn],
                wbufs=[buf("xn_dr")])
            for ci in range(8):
                p.op("pe", lambda e, par=par, ci=ci, tbk=tbk: e.transpose(
                    bankB[tbk][:, ci * 128:(ci + 1) * 128],
                    xnb[par][:, ci * 128:(ci + 1) * 128], iden.ap()),
                    reads=[bxn, buf("iden")], writes=[bb_[tbk]])
            p.op("dve", lambda e, par=par, tbk=tbk: e.tensor_copy(
                xfmp[par].ap(), bankB[tbk][:, 0:1024]),
                reads=[bb_[tbk]], writes=[bxf])
            dma("sp", "dsp", xnf_t[t], xfmp[par].ap(), rbufs=[bxf],
                wbufs=[buf("xnf_dr")])
            bg = buf(f"ppg{par}")
            for dv in range(4):
                bbk = 3 + (dv % 2)
                prs = [(xfmp[par][:, ci * 128:(ci + 1) * 128],
                        wg_sb[:, ci * 2048 + dv * 512: ci * 2048 + (dv + 1) * 512])
                       for ci in range(8)]
                mmgroup(bank[bbk][:, 0:512], prs, [bxf, buf("wg")], bb_[bbk])
                p.op("act", lambda e, par=par, dv=dv, bbk=bbk: e.activation(
                    gsb[par][:, dv * 512:(dv + 1) * 512], bank[bbk][:, 0:512],
                    AF.Silu),
                    reads=[bb_[bbk]], writes=[bg])
            dma("act", "dact", g_t[t], gsb[par].ap(), rbufs=[bg],
                wbufs=[buf("g_dr")])
            mmgroup(bank[0][:, 0:E],
                    [(xfmp[par][:, ci * 128:(ci + 1) * 128],
                      wr_sb[:, 8 * ci:8 * ci + 8]) for ci in range(8)],
                    [bxf, buf("wr_sb")], bb_[0])
            p.op("dve", lambda e: e.tensor_add(lgb.ap(), bank[0][:, 0:E],
                                               rb_bc.ap()),
                 reads=[bb_[0], buf("rb_bc")], writes=[buf("pplg")])
            p.op("dve", lambda e: e.reduce_max(mrow.ap(), lgb.ap(), axis=AX),
                 reads=[buf("pplg")], writes=[buf("ppm")])
            p.op("dve", lambda e: e.tensor_scalar(lg2.ap(), lgb.ap(), mrow.ap(),
                                                  None, OP.is_equal),
                 reads=[buf("pplg"), buf("ppm")], writes=[buf("pplg2")])
            p.op("dve", lambda e: e.tensor_mul(lg2.ap(), lg2.ap(), io8m8.ap()),
                 reads=[buf("pplg2"), buf("io8m8")], writes=[buf("pplg2")])
            p.op("dve", lambda e: e.tensor_reduce(mrow.ap(), lg2.ap(), AX,
                                                  OP.min),
                 reads=[buf("pplg2")], writes=[buf("ppm")])
            p.op("dve", lambda e, t=t: e.tensor_scalar(
                eid_a[:, t:t + 1], mrow.ap(), 8.0, None, OP.add),
                reads=[buf("ppm")], writes=[eidb])
            p.op("dve", lambda e, t=t: e.tensor_scalar(
                ohbf.ap(), io8.ap(), eid_a[:, t:t + 1], None, OP.is_equal),
                reads=[buf("io8"), eidb], writes=[buf("ppoh")])
            mmgroup(bank[1][:, 0:E], [(utri.ap(), ohbf.ap())],
                    [buf("utri"), buf("ppoh")], bb_[1])
            mmgroup(bank[2][0:1, 0:E], [(onescol.ap(), ohbf.ap())],
                    [buf("onescol"), buf("ppoh")], bb_[2])
            p.op("dve", lambda e: e.tensor_tensor(lg2.ap(), bank[1][:, 0:E],
                                                  ohbf.ap(), OP.mult),
                 reads=[bb_[1], buf("ppoh")], writes=[buf("pplg2")])
            p.op("dve", lambda e: e.tensor_reduce(mrow.ap(), lg2.ap(), AX,
                                                  OP.add),
                 reads=[buf("pplg2")], writes=[buf("ppm")])
            p.op("dve", lambda e, t=t: e.tensor_scalar(
                lrank_a[:, t:t + 1], mrow.ap(), -1.0, None, OP.add),
                reads=[buf("ppm")], writes=[lrankb])
            p.op("dve", lambda e, t=t: e.tensor_copy(
                offs_row[:, E * t:E * (t + 1)], runtot.ap()),
                reads=[buf("runtot")], writes=[buf("offs_row")])
            p.op("dve", lambda e: e.tensor_add(runtot.ap(), runtot.ap(),
                                               bank[2][0:1, 0:E]),
                 reads=[buf("runtot"), bb_[2]], writes=[buf("runtot")])

    dma("act", "dact", bass.AP(offs_dr, 0, [[512, 1], [1, 512]]), offs_row.ap(),
        rbufs=[buf("offs_row")], wbufs=[buf("offs_dr")])
    dma("pool", "dpool", ccc_in[:, :], runtot.ap(), rbufs=[buf("runtot")],
        wbufs=[buf("ccc_in")])
    p.op("pool", lambda e: e.collective_compute(
        "AllGather", OP.bypass, replica_groups=[list(range(NCORES))],
        ins=[ccc_in.ap().opt()], outs=[ccc_out.ap().opt()]),
        reads=[buf("ccc_in")], writes=[buf("ccc_out")], cc="cc0")
    p.op("pool", lambda e: e.dma_start(
        counts_T.ap(), bass.AP(ccc_out, 0, [[1, E], [E, E]]),
        allow_slow_non_contiguous=True),
        reads=[buf("ccc_out")], writes=[buf("counts_T")], dma="q_counts_T")
    dma("pool", "dpool", cmrow.ap(),
        bass.AP(cmask_in, 0, [[0, E], [1, E]]), wbufs=[buf("cmrow")])
    p.op("pool", lambda e: e.tensor_mul(counts_T.ap(), counts_T.ap(), cmrow.ap()),
         reads=[buf("counts_T"), buf("cmrow")], writes=[buf("counts_T")])
    p.op("pool", lambda e: e.tensor_add(counts_T[:, 0:4], counts_T[:, 0:4],
                                        counts_T[:, 4:8]),
         reads=[buf("counts_T")], writes=[buf("counts_T")])
    p.op("pool", lambda e: e.tensor_add(counts_T[:, 0:2], counts_T[:, 0:2],
                                        counts_T[:, 2:4]),
         reads=[buf("counts_T")], writes=[buf("counts_T")])
    p.op("pool", lambda e: e.tensor_add(coff_col.ap(), counts_T[:, 0:1],
                                        counts_T[:, 1:2]),
         reads=[buf("counts_T")], writes=[buf("coff_col")])
    p.op("pool", lambda e: e.dma_start(
        bass.AP(coff_dr, 0, [[1, E], [1, 1]]), coff_col.ap(),
        allow_slow_non_contiguous=True),
        reads=[buf("coff_col")], writes=[buf("coff_dr")], dma="q_coff_dr")
    dma("pool", "dpool", coff_bc.ap(), bass.AP(coff_dr, 0, [[0, 128], [1, E]]),
        rbufs=[buf("coff_dr")], wbufs=[buf("coff_bc")])
    dma("pool", "dpool", offs_bc.ap(),
        bass.AP(offs_dr, 0, [[0, 128], [E, NT], [1, E]]),
        rbufs=[buf("offs_dr")], wbufs=[buf("offs_bc")])

    for t in range(NT):
        par = t % 2
        rb_ = buf(f"rxn{par}")
        dma("pool", "dpool", rxn[par].ap(), xn_t[t], rbufs=[buf("xn_dr")],
            wbufs=[rb_])
        p.op("pool", lambda e, t=t: e.tensor_scalar(
            r8.ap(), io8.ap(), eid_a[:, t:t + 1], None, OP.is_equal),
            reads=[buf("io8"), eidb], writes=[buf("r8")])
        p.op("pool", lambda e, t=t: e.tensor_add(
            r8b.ap(), coff_bc.ap(), offs_bc[:, E * t:E * (t + 1)]),
            reads=[buf("coff_bc"), buf("offs_bc")], writes=[buf("r8b")])
        p.op("pool", lambda e: e.tensor_mul(r8.ap(), r8.ap(), r8b.ap()),
             reads=[buf("r8"), buf("r8b")], writes=[buf("r8")])
        p.op("pool", lambda e: e.tensor_add(r8[:, 0:4], r8[:, 0:4], r8[:, 4:8]),
             reads=[buf("r8")], writes=[buf("r8")])
        p.op("pool", lambda e: e.tensor_add(r8[:, 0:2], r8[:, 0:2], r8[:, 2:4]),
             reads=[buf("r8")], writes=[buf("r8")])
        p.op("pool", lambda e: e.tensor_add(rc1.ap(), r8[:, 0:1], r8[:, 1:2]),
             reads=[buf("r8")], writes=[buf("rc1")])
        p.op("pool", lambda e, t=t: e.tensor_add(rc1.ap(), rc1.ap(),
                                                 lrank_a[:, t:t + 1]),
             reads=[buf("rc1"), lrankb], writes=[buf("rc1")])
        p.op("pool", lambda e, t=t: e.tensor_single_scalar(
            inv_a[:, t:t + 1], rc1.ap(), float(CAP), OP.is_ge),
            reads=[buf("rc1")], writes=[invb])
        p.op("pool", lambda e, t=t: e.tensor_scalar(
            rc2.ap(), eid_a[:, t:t + 1], float(CAP), None, OP.mult),
            reads=[eidb], writes=[buf("rc2")])
        p.op("pool", lambda e: e.tensor_add(rc2.ap(), rc2.ap(), rc1.ap()),
             reads=[buf("rc2"), buf("rc1")], writes=[buf("rc2")])
        p.op("pool", lambda e, t=t: e.tensor_scalar(
            rc3.ap(), iop.ap(), 1.0, OOBBASE + 128.0 * t, OP.mult, OP.add),
            reads=[buf("iop")], writes=[buf("rc3")])
        p.op("pool", lambda e: e.tensor_sub(rc3.ap(), rc3.ap(), rc2.ap()),
             reads=[buf("rc3"), buf("rc2")], writes=[buf("rc3")])
        p.op("pool", lambda e, t=t: e.tensor_scalar(
            rc3.ap(), rc3.ap(), inv_a[:, t:t + 1], None, OP.mult),
            reads=[buf("rc3"), invb], writes=[buf("rc3")])
        p.op("pool", lambda e: e.tensor_add(rc2.ap(), rc2.ap(), rc3.ap()),
             reads=[buf("rc2"), buf("rc3")], writes=[buf("rc2")])
        p.op("pool", lambda e, t=t: e.tensor_copy(idx_a[:, t:t + 1], rc2.ap()),
             reads=[buf("rc2")], writes=[idxb])
        p.op("pool", lambda e, t=t, par=par: e.indirect_dma_start(
            out=bufg_dr.ap(),
            out_offset=bass.IndirectOffsetOnAxis(idx_a[:, t:t + 1], 0),
            in_=rxn[par].ap(), in_offset=None,
            bounds_check=bc_reg, oob_is_err=False),
            reads=[rb_, idxb, buf("bufg")],
            writes=[buf("bufg")], dma=f"q_sc{t % 2}")
    p.wait_all("pool", ["q_sc0", "q_sc1"])
    p.op("pool", lambda e: e.collective_compute(
        "ReduceScatter", OP.add, replica_groups=[list(range(NCORES))],
        ins=[bufg_dr.ap().opt()], outs=[bufc_dr.ap().opt()]),
        reads=[buf("bufg")], writes=[buf("bufc")], cc="cc1")

    p.phase_join(engines=("pe", "dve", "act", "sp"),
                 dmas=[f"q_ppx{i}" for i in range(8)]
                 + ["q_xn_dr", "q_xnf_dr", "q_g_dr", "q_bufg"])
    ppctx.close()
    Ssb = [[S1(f"S{b}_{h}", [128, 2 * DV], F32) for h in range(NH)]
           for b in range(2)]
    Sbf = [[S1(f"Sb{b}_{h}", [128, 2 * DV], BF16) for h in range(NH)]
           for b in range(2)]
    p1xf = [S1(f"p1xf{i}", [128, 2048], BF16) for i in range(2)]
    qfm = [S1(f"qfm{i}", [128, 2048], BF16) for i in range(2)]
    kfm = [S1(f"kfm{i}", [128, 2048], BF16) for i in range(2)]
    vsb = [S1(f"vsb{i}", [128, 2 * 2048], BF16) for i in range(2)]
    att4 = [S1(f"att4_{i}", [128, 512], BF16) for i in range(2)]
    kckb = [S1(f"kck{i}", [128, DK], BF16) for i in range(2 * NH)]
    qqb = [S1(f"qq{i}", [128, DK], BF16) for i in range(2 * NH)]
    rt1 = S1("rt1", [128, 512], F32)
    rt2 = S1("rt2", [128, 512], F32)
    oscraw = [S1(f"oscr{i}", [128, 2048], BF16) for i in range(2)]
    ccsl = [S1(f"ccsl{i}", [128, 512], BF16) for i in range(2)]
    sssl = [S1(f"sssl{i}", [128, 512], BF16) for i in range(2)]
    ovar8 = S1("ovar8", [128, 8], F32)
    rstd8 = S1("rstd8", [128, 8], F32)
    vv8p = S1("vv8p", [128, 8], F32)
    n8 = [S1(f"n8_{i}", [128, 8], F32) for i in range(2)]
    n8i = S1("n8i", [128, 8], I32)
    osq = S1("osq", [128, DV], BF16)

    def wblk(wsb, ci, j):
        return wsb[:, ci * D + j * 128: ci * D + (j + 1) * 128]

    _rr = [0]

    def nxt():
        _rr[0] = (_rr[0] + 1) % 8
        return _rr[0]

    def emit_qkv(g2):
        bat, gg = g2 % 2, g2 // 2
        gp = bat
        sl = gg % 2
        ta = bat * NTB + 2 * gg
        tb = ta + 1
        if bat == 0:
            dma("sp", "dsp", ccsl[sl].ap(), cc2_in[:, gg * 512:(gg + 1) * 512],
                wbufs=[buf(f"ccsl{sl}")])
            dma("sp", "dsp", sssl[sl].ap(), ss2_in[:, gg * 512:(gg + 1) * 512],
                wbufs=[buf(f"sssl{sl}")])
        bxf = buf(f"p1xf{gp}")
        for (ti, a) in ((ta, 0), (tb, 1)):
            p.op("sp", lambda e, gp=gp, a=a, ti=ti: e.dma_start(
                bass.AP(p1xf[gp], a * 128, [[2048, 128], [256, 8], [1, 128]]),
                bass.AP(xnf_dr, ti * 128 * D, [[D, 128], [128, 8], [1, 128]])),
                reads=[buf("xnf_dr")], writes=[bxf], dma=f"q_p1xf{gp}")
        for (wsb, wb_name, dstl, dname) in ((wq_sb, "wq", qfm, "qfm"),
                                            (wk_sb, "wk", kfm, "kfm")):
            dst = dstl[gp]
            bdst = buf(f"{dname}{gp}")
            for h in range(NH):
                bbk = nxt()
                prs = []
                for half in range(2):
                    j = 2 * h + half
                    prs += [(wblk(wsb, ci, j),
                             p1xf[gp][:, ci * 256:(ci + 1) * 256])
                            for ci in range(8)]

                def emitq(e, prs=prs, bbk=bbk):
                    for i, (l, r) in enumerate(prs):
                        ins = e.matmul(bank[bbk][:, (i // 8) * 256:
                                                 (i // 8) * 256 + 256],
                                       l, r, start=(i % 8 == 0),
                                       stop=(i % 8 == 7))
                    return ins
                p.op("pe", emitq, reads=[bxf, buf(wb_name)], writes=[bb_[bbk]])
                bcs, bsn = buf(f"ccsl{sl}"), buf(f"sssl{sl}")
                bk = bb_[bbk]
                p.op("dve", lambda e, bbk=bbk, sl=sl: e.tensor_tensor(
                    rt1.ap(), bank[bbk][:, 0:512], ccsl[sl].ap(), OP.mult),
                    reads=[bk, bcs], writes=[buf("rt1")])
                p.op("dve", lambda e, bbk=bbk, sl=sl: e.tensor_tensor(
                    rt2.ap(), bank[bbk][:, 0:512], sssl[sl].ap(), OP.mult),
                    reads=[bk, bsn], writes=[buf("rt2")])
                p.op("dve", lambda e, dst=dst, h=h: e.tensor_sub(
                    dst[:, (2 * h) * 256:(2 * h) * 256 + 256],
                    rt1[:, 0:256], rt2[:, 256:512]),
                    reads=[buf("rt1"), buf("rt2")], writes=[bdst])
                p.op("dve", lambda e, dst=dst, h=h: e.tensor_sub(
                    dst[:, (2 * h + 1) * 256:(2 * h + 1) * 256 + 256],
                    rt1[:, 256:512], rt2[:, 0:256]),
                    reads=[buf("rt1"), buf("rt2")], writes=[bdst])
        for a in range(2):
            for dv in range(4):
                bbk = nxt()
                prs = [(p1xf[gp][:, ci * 256 + a * 128: ci * 256 + a * 128 + 128],
                        wv_sb[:, ci * 2048 + dv * 512: ci * 2048 + (dv + 1) * 512])
                       for ci in range(8)]
                mmgroup(bank[bbk][:, 0:512], prs, [bxf, buf("wv")], bb_[bbk])
                p.op("act", lambda e, gp=gp, a=a, dv=dv, bbk=bbk: e.copy(
                    vsb[gp][:, a * 2048 + dv * 512: a * 2048 + (dv + 1) * 512],
                    bank[bbk][:, 0:512]),
                    reads=[bb_[bbk]], writes=[buf(f"vsb{gp}")])
    def emit_ret(g2):
        bat, gg = g2 % 2, g2 // 2
        gp = bat
        ta = bat * NTB + 2 * gg
        tb = ta + 1
        for a in range(2):
            for h in range(NH):
                hb = a * NH + h
                for half in range(2):
                    p.op("dve", lambda e, gp=gp, a=a, h=h, half=half, hb=hb:
                         e.tensor_tensor(
                        qqb[hb][:, half * 128:(half + 1) * 128],
                        qfm[gp][:, (2 * h + half) * 256 + a * 128:
                                (2 * h + half) * 256 + a * 128 + 128],
                        CQt[:, h * 256 + half * 128:
                            h * 256 + (half + 1) * 128], OP.mult),
                        reads=[buf(f"qfm{gp}"), buf("CQt")],
                        writes=[buf(f"qq{hb}")])
        for a, ti in ((0, ta), (1, tb)):
            c = ti % NTB
            batt = buf(f"att4_{a}")
            bk_at = nxt()

            def emit_attn(e, gp=gp, a=a, bk_at=bk_at):
                for h in range(NH):
                    for half in range(2):
                        ins = e.matmul(
                            bank[bk_at][:, h * 128:(h + 1) * 128],
                            kfm[gp][:, (2 * h + half) * 256 + a * 128:
                                    (2 * h + half) * 256 + a * 128 + 128],
                            qfm[gp][:, (2 * h + half) * 256 + a * 128:
                                    (2 * h + half) * 256 + a * 128 + 128],
                            start=(half == 0), stop=(half == 1))
                return ins
            p.op("pe", emit_attn, reads=[buf(f"kfm{gp}"), buf(f"qfm{gp}")],
                 writes=[bb_[bk_at]])
            p.op("dve", lambda e, a=a, bk_at=bk_at: e.tensor_tensor(
                att4[a].ap(), bank[bk_at][:, 0:512], DTt[:, 0:512], OP.mult),
                reads=[bb_[bk_at], buf("DTt")], writes=[batt])
            for h in range(NH):
                hb = a * NH + h
                bqq = buf(f"qq{hb}")
                bSb = buf(f"Sb{bat}_{h}")
                b_o = nxt()

                def emit_o(e, hb=hb, gp=gp, a=a, h=h, b_o=b_o, c=c, bat=bat):
                    ins = e.matmul(bank[b_o][:, 0:512],
                                   att4[a][:, h * 128:(h + 1) * 128],
                                   vsb[gp][:, a * 2048 + h * 512:
                                           a * 2048 + (h + 1) * 512],
                                   start=True, stop=(c == 0))
                    if c > 0:
                        for half in range(2):
                            ins = e.matmul(
                                bank[b_o][:, 0:512],
                                qqb[hb][:, half * 128:(half + 1) * 128],
                                Sbf[bat][h][:, half * DV:(half + 1) * DV],
                                start=False, stop=(half == 1))
                    return ins
                rds = [batt, buf(f"vsb{gp}")] + ([bqq, bSb] if c > 0 else [])
                p.op("pe", emit_o, reads=rds, writes=[bb_[b_o]])
                p.op("act", lambda e, a=a, h=h, b_o=b_o: e.activation(
                    osq.ap(), bank[b_o][:, 0:512], AF.Square,
                    accum_out=ovar8[:, a * 4 + h:a * 4 + h + 1]),
                    reads=[bb_[b_o]], writes=[buf("osq"), buf("ovar8")])
                p.op("act", lambda e, a=a, h=h, b_o=b_o: e.copy(
                    oscraw[a][:, h * 512:(h + 1) * 512], bank[b_o][:, 0:512]),
                    reads=[bb_[b_o]], writes=[buf(f"oscr{a}")])
            if c < NTB - 1:
                bk_t = nxt()

                def emit_kt(e, gp=gp, a=a, bk_t=bk_t):
                    for h in range(NH):
                        for half in range(2):
                            j2 = 2 * h + half
                            ins = e.transpose(
                                bankB[bk_t][:, j2 * 128:(j2 + 1) * 128],
                                kfm[gp][:, j2 * 256 + a * 128:
                                        j2 * 256 + a * 128 + 128],
                                iden.ap())
                    return ins
                p.op("pe", emit_kt, reads=[buf(f"kfm{gp}"), buf("iden")],
                     writes=[bb_[bk_t]])
                for h in range(NH):
                    hb = a * NH + h
                    p.op("act", lambda e, hb=hb, h=h, bk_t=bk_t: e.mul(
                        kckb[hb].ap(), bankB[bk_t][:, h * 256:(h + 1) * 256],
                        CKt[:, h:h + 1]),
                        reads=[bb_[bk_t], buf("CKt")], writes=[buf(f"kck{hb}")])
                for h in range(NH):
                    hb = a * NH + h
                    bkck = buf(f"kck{hb}")
                    bS = buf(f"S{bat}_{h}")
                    bSb = buf(f"Sb{bat}_{h}")
                    gC = GAMMAS[h] ** CH
                    ubanks = [nxt(), nxt()]
                    for half in range(2):
                        b_u = ubanks[half]
                        p.op("pe", lambda e, hb=hb, gp=gp, a=a, h=h, half=half,
                             b_u=b_u: e.matmul(
                            bank[b_u][:, 0:512],
                            kckb[hb][:, half * 128:(half + 1) * 128],
                            vsb[gp][:, a * 2048 + h * 512:
                                    a * 2048 + (h + 1) * 512],
                            start=True, stop=True),
                            reads=[bkck, buf(f"vsb{gp}")], writes=[bb_[b_u]])
                    for half in range(2):
                        b_u = ubanks[half]
                        if c == 0:
                            p.op("dve", lambda e, h=h, half=half, b_u=b_u,
                                 bat=bat: e.tensor_copy(
                                Ssb[bat][h][:, half * DV:(half + 1) * DV],
                                bank[b_u][:, 0:512]),
                                reads=[bb_[b_u]], writes=[bS])
                        else:
                            p.op("dve", lambda e, h=h, half=half, b_u=b_u,
                                 gC=gC, bat=bat: e.scalar_tensor_tensor(
                                Ssb[bat][h][:, half * DV:(half + 1) * DV],
                                Ssb[bat][h][:, half * DV:(half + 1) * DV], gC,
                                bank[b_u][:, 0:512], OP.mult, OP.add),
                                reads=[bb_[b_u], bS], writes=[bS])
                    p.op("act", lambda e, h=h, bat=bat: e.copy(
                        Sbf[bat][h].ap(), Ssb[bat][h].ap()),
                        reads=[bS], writes=[bSb])
        p.op("dve", lambda e: e.tensor_scalar(
            vv8p.ap(), ovar8.ap(), 1.0 / DV, EPS, OP.mult, OP.add),
            reads=[buf("ovar8")], writes=[buf("vv8p")])
        class _H8:
            pass
        s0 = _H8(); s0.ap = n8[0].ap; s0.handle_name = "n8_0"
        s1 = _H8(); s1.ap = n8[1].ap; s1.handle_name = "n8_1"
        si = _H8(); si.ap = n8i.ap; si.handle_name = "n8i"
        newton8(vv8p.ap(), rstd8.ap(), buf("vv8p"), buf("rstd8"), s0, s1, si)
        for a, ti in ((0, ta), (1, tb)):
            for h in range(NH):
                col = a * 4 + h
                p.op("dve", lambda e, a=a, h=h, col=col: e.tensor_scalar(
                    oscraw[a][:, h * 512:(h + 1) * 512],
                    oscraw[a][:, h * 512:(h + 1) * 512],
                    rstd8[:, col:col + 1], None, OP.mult),
                    reads=[buf(f"oscr{a}"), buf("rstd8")],
                    writes=[buf(f"oscr{a}")])
            dma("sp", "dsp", osc_t[ti], oscraw[a].ap(),
                rbufs=[buf(f"oscr{a}")], wbufs=[buf("osc_dr")])

    for g2 in range(NT // 2):
        emit_qkv(g2)
        if g2 >= 1:
            emit_ret(g2 - 1)
    emit_ret(NT // 2 - 1)
    c1.close()

    p.phase_join()

    c3 = ExitStack()
    S3 = lambda n, sh, d: c3.enter_context(nc.sbuf_tensor(n, sh, d))
    wo_sb = S3("wo_sb", [128, 16 * D], BF16)
    wos = S3("wos", [128, 512], F32)
    gnw_sb = S3("gnw_sb", [128, 16], F32)
    dma("sp", "dsp", gnw_sb.ap(), gnw16_in[:, :], wbufs=[buf("gnw")])
    wo_r2 = wo_in.ap().rearrange("(c k) (m n) -> c m k n", k=128, n=512)
    for ci in range(16):
        for m in range(2):
            dma("sp", "dsp", wos.ap(), wo_r2[ci, m], rbufs=[buf("wo_pre")],
                wbufs=[buf("wo_pre")])
            p.op("act", lambda e, ci=ci, m=m: e.mul(
                wo_sb[:, ci * D + m * 512:ci * D + (m + 1) * 512], wos.ap(),
                gnw_sb[:, ci:ci + 1]),
                reads=[buf("wo_pre"), buf("gnw")], writes=[buf("wo")])

    c2 = ExitStack()
    S2 = lambda n, sh, d: c2.enter_context(nc.sbuf_tensor(n, sh, d))
    w1_sb = S2("w1_sb", [128, 8 * HID], BF16)
    w2_sb = S2("w2_sb", [128, 32 * D], BF16)
    b1_sb = S2("b1_sb", [128, 32], F32)
    b2bc = S2("b2bc", [128, D], BF16)
    p2x = [S2(f"p2x{i}", [128, D], BF16) for i in range(2)]
    p2xf = S2("p2xf", [128, 2048], BF16)
    hfm = S2("hfm", [128, 32 * 256], BF16)
    ycsb = [S2(f"ycsb{i}", [128, D], BF16) for i in range(2)]

    dma("pool", "dpool", w1_sb.ap(),
        rowchunks(w1c_in, HID, 8), wbufs=[buf("w1")])
    dma("pool", "dpool", w2_sb.ap(),
        rowchunks(w2c_in, D, 32), wbufs=[buf("w2")])
    dma("sp", "dsp", b1_sb.ap(), b1c_in[:, :], wbufs=[buf("b1")])
    dma("sp", "dsp", b2bc.ap(), bass.AP(b2c_in, 0, [[0, 128], [1, D]]),
        wbufs=[buf("b2bc")])

    for gi in range(5):
        bxf = buf("p2xf")
        for a in range(2):
            dma("sp", "dsp", p2x[a].ap(), bufc_t[2 * gi + a],
                rbufs=[buf("bufc")], wbufs=[buf(f"p2x{a}")])
            bbk = 4 + a
            for ci in range(8):
                p.op("pe", lambda e, a=a, ci=ci, bbk=bbk: e.transpose(
                    bankB[bbk][:, ci * 128:(ci + 1) * 128],
                    p2x[a][:, ci * 128:(ci + 1) * 128], iden.ap()),
                    reads=[buf(f"p2x{a}"), buf("iden")], writes=[bb_[bbk]])
            p.op("dve", lambda e, a=a, bbk=bbk: e.tensor_copy(
                bass.AP(p2xf, a * 128, [[2048, 128], [256, 8], [1, 128]]),
                bankB[bbk][:, 0:1024]),
                reads=[bb_[bbk]], writes=[bxf])
        hbufs = []
        for j in range(32):
            bbk = j % 4
            prs = [(w1_sb[:, ci * HID + j * 128: ci * HID + (j + 1) * 128],
                    p2xf[:, ci * 256:(ci + 1) * 256]) for ci in range(8)]
            mmgroup(bank[bbk][:, 0:256], prs, [bxf, buf("w1")], bb_[bbk])
            hb = buf(f"hf2_{j}")
            hbufs.append(hb)
            p.op("act", lambda e, j=j, bbk=bbk: e.activation(
                hfm[:, j * 256:(j + 1) * 256], bank[bbk][:, 0:256], AF.Relu,
                bias=b1_sb[:, j:j + 1]),
                reads=[bb_[bbk], buf("b1")], writes=[hb])
        for ts in range(2):
            for dh in range(2):
                bbk = 4 + ts * 2 + dh

                def emity(e, ts=ts, dh=dh, bbk=bbk):
                    for j in range(32):
                        ins = e.matmul(
                            bank[bbk][:, 0:512],
                            hfm[:, j * 256 + ts * 128: j * 256 + ts * 128 + 128],
                            w2_sb[:, j * D + dh * 512: j * D + (dh + 1) * 512],
                            start=(j == 0), stop=(j == 31))
                    return ins
                p.op("pe", emity, reads=hbufs + [buf("w2")], writes=[bb_[bbk]])
            for dh in range(2):
                bbk = 4 + ts * 2 + dh
                p.op("dve", lambda e, ts=ts, dh=dh, bbk=bbk: e.tensor_tensor(
                    ycsb[ts][:, dh * 512:(dh + 1) * 512], bank[bbk][:, 0:512],
                    b2bc[:, dh * 512:(dh + 1) * 512], OP.add),
                    reads=[bb_[bbk], buf("b2bc")], writes=[buf(f"ycsb{ts}")])
        for a in range(2):
            dma("sp", "dsp", yc_t[2 * gi + a], ycsb[a].ap(),
                rbufs=[buf(f"ycsb{a}")], wbufs=[buf("yc_dr")])
    p.op("pool", lambda e: e.collective_compute(
        "AllGather", OP.bypass, replica_groups=[list(range(NCORES))],
        ins=[yc_dr.ap().opt()], outs=[yfull_dr.ap().opt()]),
        reads=[buf("yc_dr")], writes=[buf("yfull")], cc="cc2")
    c2.close()

    p.phase_join()

    c3b = ExitStack()
    S3b = lambda n, sh, d: c3b.enter_context(nc.sbuf_tensor(n, sh, d))
    a3g = [S3b(f"a3g{i}", [128, 2048], BF16) for i in range(2)]
    a3osc = [S3b(f"a3osc{i}", [128, 2048], BF16) for i in range(2)]
    a3og = [S3b(f"a3og{i}", [128, 2048], BF16) for i in range(2)]
    a3ogf = [S3b(f"a3ogf{i}", [128, 2048], BF16) for i in range(2)]
    a3s = [S3b(f"a3s{i}", [128, D], BF16) for i in range(2)]

    for g in range(NT // 2):
        ta, tb = 2 * g, 2 * g + 1
        for (ti, a) in ((ta, 0), (tb, 1)):
            dma("act", "dact", a3g[a].ap(), g_t[ti], rbufs=[buf("g_dr")],
                wbufs=[buf(f"a3g{a}")])
            dma("pool", "dpool", a3osc[a].ap(), osc_t[ti],
                rbufs=[buf("osc_dr")], wbufs=[buf(f"a3osc{a}")])
        for a, ti in ((0, ta), (1, tb)):
            p.op("dve", lambda e, a=a: e.tensor_mul(
                a3og[a].ap(), a3osc[a].ap(), a3g[a].ap()),
                reads=[buf(f"a3osc{a}"), buf(f"a3g{a}")],
                writes=[buf(f"a3og{a}")])
            for half in range(2):
                bbk = 4 + ((a * 2 + half) % 2)
                for ci in range(8):
                    cc = half * 8 + ci
                    p.op("pe", lambda e, a=a, cc=cc, ci=ci, bbk=bbk: e.transpose(
                        bankB[bbk][:, ci * 128:(ci + 1) * 128],
                        a3og[a][:, cc * 128:(cc + 1) * 128], iden.ap()),
                        reads=[buf(f"a3og{a}"), buf("iden")], writes=[bb_[bbk]])
                p.op("dve", lambda e, a=a, half=half, bbk=bbk: e.tensor_copy(
                    a3ogf[a][:, half * 1024:(half + 1) * 1024],
                    bankB[bbk][:, 0:1024]),
                    reads=[bb_[bbk]], writes=[buf(f"a3ogf{a}")])
            for dh in range(2):
                bbk = 6 + dh

                def emits(e, a=a, dh=dh, bbk=bbk):
                    for ci in range(16):
                        ins = e.matmul(
                            bank[bbk][:, 0:512],
                            a3ogf[a][:, ci * 128:(ci + 1) * 128],
                            wo_sb[:, ci * D + dh * 512: ci * D + (dh + 1) * 512],
                            start=(ci == 0), stop=(ci == 15))
                    return ins
                p.op("pe", emits, reads=[buf(f"a3ogf{a}"), buf("wo")],
                     writes=[bb_[bbk]])
                p.op("dve", lambda e, a=a, dh=dh, bbk=bbk: e.tensor_copy(
                    a3s[a][:, dh * 512:(dh + 1) * 512], bank[bbk][:, 0:512]),
                    reads=[bb_[bbk]], writes=[buf(f"a3s{a}")])
            dma("sp", "dsp", s_t[ti], a3s[a].ap(), rbufs=[buf(f"a3s{a}")],
                wbufs=[buf("s_dr")])
    c3b.close()
    c3.close()

    p.phase_join()

    c4 = ExitStack()
    S4 = lambda n, sh, d: c4.enter_context(nc.sbuf_tensor(n, sh, d))
    o1_sb = S4("o1_sb", [128, 8 * HID], BF16)
    o2_sb = S4("o2_sb", [128, 32 * D], BF16)
    ob1_sb = S4("ob1_sb", [128, 32], F32)
    ob2bc = S4("ob2bc", [128, D], BF16)
    b4x = [S4(f"b4x{i}", [128, D], F32) for i in range(2)]
    b4xf = S4("b4xf", [128, 4096], BF16)
    hfm4 = S4("hfm4", [128, 32 * 512], BF16)
    yov = S4("yov", [128, D], F32)
    b4s = [S4(f"b4s{i}", [128, D], BF16) for i in range(2)]
    b4g = S4("b4g", [128, D], BF16)
    b4o = [S4("b4o0", [128, D], F32)]

    dma("pool", "dpool", o1_sb.ap(),
        rowchunks(ow1_in, HID, 8), wbufs=[buf("o1")])
    dma("pool", "dpool", o2_sb.ap(),
        rowchunks(ow2_in, D, 32), wbufs=[buf("o2")])
    dma("sp", "dsp", ob1_sb.ap(), ob1_in[:, :], wbufs=[buf("ob1")])
    dma("sp", "dsp", ob2bc.ap(), bass.AP(ob2_in, 0, [[0, 128], [1, D]]),
        wbufs=[buf("ob2bc")])

    def b4_loads(ti, a2):
        dma("pool", "dpool", b4x[a2].ap(), x_t[ti], wbufs=[buf(f"b4x{a2}")])
        dma("pool", "dpool", b4s[a2].ap(), s_t[ti], rbufs=[buf("s_dr")],
            wbufs=[buf(f"b4s{a2}")])
        p.op("pool", lambda e, a2=a2: e.memset(ytok[a2].ap(), 0),
             reads=[], writes=[buf(f"ytok{a2}")])
        p.op("pool", lambda e, a2=a2, ti=ti: e.indirect_dma_start(
            out=ytok[a2].ap(), out_offset=None,
            in_=yfull_dr.ap(),
            in_offset=bass.IndirectOffsetOnAxis(idx_a[:, ti:ti + 1], 0),
            bounds_check=bc_reg, oob_is_err=False),
            reads=[buf("yfull"), idxb, buf(f"ytok{a2}")],
            writes=[buf(f"ytok{a2}")], dma=f"q_ytok{a2}")

    def b4_combine(ti, a2):
        p.op("dve", lambda e, ti=ti: e.tensor_scalar(
            yov.ap(), yov.ap(), inv_a[:, ti:ti + 1], None, OP.mult),
            reads=[buf("yov"), invb], writes=[buf("yov")])
        p.op("dve", lambda e, a2=a2: e.tensor_add(
            yov.ap(), yov.ap(), ytok[a2].ap()),
            reads=[buf("yov"), buf(f"ytok{a2}")], writes=[buf("yov")])
        p.op("act", lambda e: e.activation(b4g.ap(), yov.ap(), AF.Silu),
             reads=[buf("yov")], writes=[buf("b4g")])
        p.op("dve", lambda e, a2=a2: e.tensor_mul(
            b4g.ap(), b4g.ap(), b4s[a2].ap()),
            reads=[buf("b4g"), buf(f"b4s{a2}")], writes=[buf("b4g")])
        p.op("dve", lambda e, a2=a2: e.tensor_add(
            b4o[0].ap(), b4g.ap(), b4x[a2].ap()),
            reads=[buf("b4g"), buf(f"b4x{a2}")], writes=[buf("b4o0")])
        dma("sp", "dsp", out_t[ti], b4o[0].ap(), rbufs=[buf("b4o0")],
            wbufs=[buf("outw")])

    for g in range(NT // 4):
        t0 = 4 * g
        bxf = buf("b4xf")
        for a in range(4):
            p.op("sp", lambda e, a=a, ti=t0 + a: e.dma_start(
                bass.AP(b4xf, a * 128, [[4096, 128], [512, 8], [1, 128]]),
                bass.AP(xnf_dr, ti * 128 * D, [[D, 128], [128, 8], [1, 128]])),
                reads=[buf("xnf_dr")], writes=[bxf], dma="q_b4xf")
        for a in range(2):
            b4_loads(t0 + a, a)
        hbufs = []
        for j in range(32):
            bbk = j % 4
            prs = [(o1_sb[:, ci * HID + j * 128: ci * HID + (j + 1) * 128],
                    b4xf[:, ci * 512:(ci + 1) * 512]) for ci in range(8)]
            mmgroup(bank[bbk][:, 0:512], prs, [bxf, buf("o1")], bb_[bbk])
            hb = buf(f"h4_{j}")
            hbufs.append(hb)
            p.op("act", lambda e, j=j, bbk=bbk: e.activation(
                hfm4[:, j * 512:(j + 1) * 512], bank[bbk][:, 0:512],
                AF.Relu, bias=ob1_sb[:, j:j + 1]),
                reads=[bb_[bbk], buf("ob1")], writes=[hb])
        for ts in range(4):
            a2 = ts % 2
            for dh in range(2):
                bbk = 4 + (ts * 2 + dh) % 4

                def emity(e, ts=ts, dh=dh, bbk=bbk):
                    for j in range(32):
                        ins = e.matmul(
                            bank[bbk][:, 0:512],
                            hfm4[:, j * 512 + ts * 128:
                                 j * 512 + ts * 128 + 128],
                            o2_sb[:, j * D + dh * 512: j * D + (dh + 1) * 512],
                            start=(j == 0), stop=(j == 31))
                    return ins
                p.op("pe", emity, reads=hbufs + [buf("o2")], writes=[bb_[bbk]])
            for dh in range(2):
                bbk = 4 + (ts * 2 + dh) % 4
                p.op("dve", lambda e, dh=dh, bbk=bbk: e.tensor_tensor(
                    yov[:, dh * 512:(dh + 1) * 512], bank[bbk][:, 0:512],
                    ob2bc[:, dh * 512:(dh + 1) * 512], OP.add),
                    reads=[bb_[bbk], buf("ob2bc")], writes=[buf("yov")])
            b4_combine(t0 + ts, a2)
            if ts == 1:
                for a in range(2):
                    b4_loads(t0 + 2 + a, a)
    c4.close()

    if debug:
        dma("sp", "dsp", dbg["xn"].ap(), xn_dr.ap(), rbufs=[buf("xn_dr")])
        dma("sp", "dsp", dbg["eid"].ap(), eid_a.ap(), rbufs=[eidb])
        dma("sp", "dsp", dbg["lrank"].ap(), lrank_a.ap(), rbufs=[lrankb])
        dma("sp", "dsp", dbg["inv"].ap(), inv_a.ap(), rbufs=[invb])
        dma("sp", "dsp", dbg["idx"].ap(), idx_a.ap(), rbufs=[idxb])
        dma("sp", "dsp", dbg["osc"].ap(), osc_dr.ap(), rbufs=[buf("osc_dr")])
        dma("sp", "dsp", dbg["s"].ap(), s_dr.ap(), rbufs=[buf("s_dr")])
        dma("sp", "dsp", dbg["yfull"].ap(), yfull_dr.ap(), rbufs=[buf("yfull")])
        dma("sp", "dsp", dbg["bufc"].ap(), bufc_dr.ap(), rbufs=[buf("bufc")])
        dma("act", "dact", dbg["coff"].ap(), coff_dr.ap(), rbufs=[buf("coff_dr")])
        dma("sp", "dsp", dbg["rstd"].ap(), rstd_a.ap(), rbufs=[statsb])
        dma("sp", "dsp", dbg["nmr"].ap(), nmr_a.ap(), rbufs=[statsb])

    p.wait_all("sp", [s for s in p.cnt if s.startswith("q_")
                      or s.startswith("cc") or s in ("dact", "dpool", "dsp")])

    p.emit(sems)
    stack.close()
    return nc


_PROG_CACHE = {}


def _get_program(debug=False):
    if debug not in _PROG_CACHE:
        _PROG_CACHE[debug] = build_program(debug=debug)
    return _PROG_CACHE[debug]


def make_in_maps(inputs):
    t = _tables()
    x = np.asarray(inputs["L0_attn_x"], np.float32).reshape(16 * 4096, D)
    w1 = np.asarray(inputs["L0_ffn_w1"], np.float32)
    b1 = np.asarray(inputs["L0_ffn_b1"], np.float32)
    w2 = np.asarray(inputs["L0_ffn_w2"], np.float32)
    b2 = np.asarray(inputs["L0_ffn_b2"], np.float32)
    gn = np.asarray(inputs["L0_attn_gn_weight"], np.float32)
    gnw16 = np.ascontiguousarray(
        np.tile(gn, NH).reshape(16, 128).T)
    shared = dict(
        wq=np.ascontiguousarray(inputs["L0_attn_Wq"], np.float32),
        wk=np.ascontiguousarray(inputs["L0_attn_Wk"], np.float32),
        wv=np.ascontiguousarray(inputs["L0_attn_Wv"], np.float32),
        wg=np.ascontiguousarray(inputs["L0_attn_Wg"], np.float32),
        wo=np.ascontiguousarray(inputs["L0_attn_Wo"], np.float32),
        wr=np.ascontiguousarray(inputs["L0_ffn_router_w"], np.float32),
        rb=np.ascontiguousarray(inputs["L0_ffn_router_b"],
                                np.float32).reshape(1, E),
        gnw16=gnw16,
        ow1=np.ascontiguousarray(inputs["L0_ffn_ow1"], np.float32),
        ow2=np.ascontiguousarray(inputs["L0_ffn_ow2"], np.float32),
        ob1=np.ascontiguousarray(
            np.asarray(inputs["L0_ffn_ob1"], np.float32).reshape(32, 128).T),
        ob2=np.ascontiguousarray(inputs["L0_ffn_ob2"],
                                 np.float32).reshape(1, D).astype(
                                     ml_dtypes.bfloat16),
        cc2T=t["cc2T"], ss2T=t["ss2T"], DT=t["DT"], CQ=t["CQ"], CK=t["CK"],
        utri=t["utri"], iden=t["iden"], onescol=t["onescol"],
    )
    shared["io8_in"] = np.broadcast_to(np.arange(E, dtype=np.float32),
                                       (128, E)).copy()
    shared["io8m8_in"] = np.broadcast_to(np.arange(-8, 0, dtype=np.float32),
                                         (128, E)).copy()
    shared["iop_in"] = np.arange(128, dtype=np.float32).reshape(128, 1).copy()
    in_maps = []
    for c in range(NCORES):
        m = dict(shared)
        m["x"] = np.ascontiguousarray(x[c * TPB:(c + 1) * TPB])
        m["w1c"] = np.ascontiguousarray(w1[c])
        m["w2c"] = np.ascontiguousarray(w2[c])
        m["b1c"] = np.ascontiguousarray(b1[c].reshape(32, 128).T)
        m["b2c"] = np.ascontiguousarray(b2[c].reshape(1, D)).astype(ml_dtypes.bfloat16)
        cm = np.zeros((E, 1), np.float32)
        cm[:c] = 1.0
        m["cmask"] = cm
        in_maps.append(m)
    return in_maps


def kernel(**inputs):
    nc = _get_program(debug=False)
    in_maps = make_in_maps(inputs)
    res = run_bass_kernel_spmd(nc, in_maps, core_ids=list(range(NCORES)))
    out = np.concatenate([np.asarray(r["out"]) for r in res.results], axis=0)
    return out.reshape(16, 4096, D)
